# version 02
# baseline (speedup 1.0000x reference)
import numpy as np
import ml_dtypes
from contextlib import ExitStack

import concourse.bass as bass
import concourse.mybir as mybir
from concourse.bass_utils import run_bass_kernel_spmd

F32 = mybir.dt.float32
BF16 = mybir.dt.bfloat16
I32 = mybir.dt.int32
AX = mybir.AxisListType.X
AF = mybir.ActivationFunctionType
ALU = mybir.AluOpType
BF = ml_dtypes.bfloat16

D = 1024
DP = 8
T = 2048
NCH = 4
CH = 512
NTT = 16


def _key(x):
    if isinstance(x, str):
        return x
    if hasattr(x, "name"):
        return x.name
    if hasattr(x, "tensor"):
        return x.tensor.name
    raise TypeError(f"bad key {x!r}")


class Prog:
    ENGINES = ("tensor", "vector", "scalar", "gpsimd", "sync")

    def __init__(self, nc, stack):
        self.nc = nc
        self.stack = stack
        self.ops = {e: [] for e in self.ENGINES}
        self.sems = {}
        self.cnt = {}
        self.waited = {e: {} for e in self.ENGINES}
        for e in self.ENGINES:
            self._mksem("p_" + e)
        self.dma_sems = set()
        self.last_w = {}
        self.readers = {}

    def _mksem(self, name):
        self.sems[name] = self.stack.enter_context(self.nc.semaphore(name))
        self.cnt[name] = 0

    def _track_pre(self, rk, wk):
        auto = []
        for k in rk:
            t = self.last_w.get(k)
            if t:
                auto.append(t)
        for k in wk:
            t = self.last_w.get(k)
            if t:
                auto.append(t)
            auto.extend(self.readers.get(k, {}).values())
        return auto

    def _track_post(self, rk, wk, slot, tok):
        for k in wk:
            self.last_w[k] = tok
            self.readers[k] = {}
        for k in rk:
            if k not in wk:
                self.readers.setdefault(k, {})[slot] = tok

    def op(self, eng, fn, waits=(), r=(), w=(), sig=False):
        rk = [_key(x) for x in r]
        wk = [_key(x) for x in w]
        auto = self._track_pre(rk, wk)
        for t in waits:
            self._wait(eng, t)
        for t in auto:
            self._wait(eng, t)
        need_sig = sig or bool(rk) or bool(wk)
        if need_sig:
            sname = "p_" + eng
            self.cnt[sname] += 1
            v = self.cnt[sname]
            sh = self.sems[sname]
            self.ops[eng].append(lambda E, fn=fn, sh=sh: fn(E).then_inc(sh, 1))
            tok = (sname, v)
        else:
            self.ops[eng].append(fn)
            tok = None
        if tok:
            self._track_post(rk, wk, eng, tok)
        return tok

    def dma(self, q, out, in_, sem, waits=(), r=(), w=()):
        rk = [_key(x) for x in r]
        wk = [_key(x) for x in w]
        auto = self._track_pre(rk, wk)
        for t in waits:
            self._wait(q, t)
        for t in auto:
            self._wait(q, t)
        sname = "dma_" + sem
        if sname not in self.sems:
            self._mksem(sname)
            self.dma_sems.add(sname)
        self.cnt[sname] += 16
        v = self.cnt[sname]
        sh = self.sems[sname]
        self.ops[q].append(
            lambda E, out=out, in_=in_, sh=sh: E.dma_start(out=out, in_=in_).then_inc(sh, 16)
        )
        tok = (sname, v)
        self._track_post(rk, wk, sname, tok)
        return tok

    def _wait(self, eng, tok):
        if tok is None:
            return
        sname, v = tok
        if self.waited[eng].get(sname, 0) >= v:
            return
        self.waited[eng][sname] = v
        sh = self.sems[sname]
        self.ops[eng].append(lambda E, sh=sh, v=v: E.wait_ge(sh, v))

    def barrier(self):
        toks = [("p_" + e, self.cnt["p_" + e]) for e in self.ENGINES if self.cnt["p_" + e]]
        for s in sorted(self.dma_sems):
            if self.cnt[s]:
                toks.append((s, self.cnt[s]))
        for e in self.ENGINES:
            for t in toks:
                self._wait(e, t)

    def finish(self):
        for s in sorted(self.dma_sems):
            if self.cnt[s]:
                sh = self.sems[s]
                v = self.cnt[s]
                if self.waited["sync"].get(s, 0) < v:
                    self.waited["sync"][s] = v
                    self.ops["sync"].append(lambda E, sh=sh, v=v: E.wait_ge(sh, v))

    def emit(self):
        block = self.stack.enter_context(self.nc.Block())
        for e in self.ENGINES:
            ops = list(self.ops[e])

            def runner(E, ops=ops):
                for f in ops:
                    f(E)

            getattr(block, e)(runner)



def mm(out, lhsT, rhs, start, stop):
    return lambda E: E.matmul(out, lhsT, rhs, start=start, stop=stop)

def act(out, in_, func, bias=0.0, scale=1.0):
    return lambda E: E.activation(out, in_, func, bias=bias, scale=scale)

def tt(out, a, b, op):
    return lambda E: E.tensor_tensor(out, a, b, op)

def ts(out, a, s, op):
    return lambda E: E.tensor_single_scalar(out, a, s, op)

def stt(out, in0, scalar, in1, op0, op1):
    return lambda E: E.scalar_tensor_tensor(out, in0, scalar, in1, op0, op1)

def cp(out, in_):
    return lambda E: E.tensor_copy(out, in_)

def scp(out, in_):
    return lambda E: E.copy(out, in_)

def smul(out, in_, m):
    return lambda E: E.mul(out, in_, m)

def memset(ap, v):
    return lambda E: E.memset(ap, v)

def recip(out, in_):
    return lambda E: E.reciprocal(out, in_)

def rmax(out, in_, negate=False):
    return lambda E: E.reduce_max(out, in_, axis=AX, negate=negate)

def rsum(out, in_):
    return lambda E: E.reduce_sum(out, in_, axis=AX)

def max8(out, in_):
    return lambda E: E.max(out, in_)

def transp(out, in_, ident):
    return lambda E: E.transpose(out, in_, ident)



class Com:
    def __init__(self, nc, stack):
        sb = lambda n, s, d=F32: stack.enter_context(nc.sbuf_tensor(n, s, d))
        self.SCR_A = sb("com_scr_a", [128, T])
        self.SCR_B = sb("com_scr_b", [128, T])
        self.TMP = sb("com_tmp0", [128, T])
        self.R0 = self.TMP[0:1, :]
        self.R1 = self.SCR_B[0:1, :]
        self.SQS = [sb("com_sqs0", [128, CH]), sb("com_sqs1", [128, CH])]
        self.ones_col = sb("com_ones_col", [128, 1])
        self.ones_col_bf = sb("com_ones_col_bf", [128, 1], BF16)
        self._ones_rows = sb("com_ones_row", [65, 128])
        self.ones_row = self._ones_rows[0:1, :]
        self._cvals = sb("com_cvals", [128, 4])

    def cval(self, idx, base=0, n=1):
        return self._cvals[base:base + n, idx:idx + 1]

    def init(self, p):
        p.op("vector", memset(self.ones_col[:], 1.0), w=[self.ones_col])
        p.op("vector", memset(self.ones_col_bf[:], 1.0), w=[self.ones_col_bf])
        p.op("vector", memset(self._ones_rows[:], 1.0), w=[self._ones_rows])
        p.op("vector", memset(self._cvals[:, 0:1], 1e-5), w=["cval0"])
        p.op("vector", memset(self._cvals[:, 1:2], 1e-6), w=["cval1"])



def emit_ln(p, c, X, Y, psA, psB, psR, fold=None, eps=1e-5):
    for ci in range(NCH):
        sl = slice(ci * CH, (ci + 1) * CH)
        for i in range(DP):
            p.op("tensor", mm(psA[0:1, :], c.ones_col[:], X[i][:, sl], i == 0, i == DP - 1),
                 r=[c.ones_col, X[i]], w=[psA] if i in (0, DP - 1) else ())
        for i in range(DP):
            k = i % 2
            p.op("scalar", act(c.SQS[k][:], X[i][:, sl], AF.Square), r=[X[i]], w=[c.SQS[k]])
            p.op("tensor", mm(psB[0:1, :], c.ones_col[:], c.SQS[k][:], i == 0, i == DP - 1),
                 r=[c.ones_col, c.SQS[k]], w=[psB] if i in (0, DP - 1) else ())
        p.op("scalar", smul(c.R0[:, sl], psA[0:1, :], 1.0 / D), r=[psA], w=[c.R0])
        p.op("scalar", smul(c.R1[:, sl], psB[0:1, :], 1.0 / D), r=[psB], w=[c.R1])

    r_mean = c.R0[:]
    r_msq = c.R1[:]
    for ci in range(NCH):
        sl = slice(ci * CH, (ci + 1) * CH)
        p.op("scalar", act(psA[0:1, :], r_mean[:, sl], AF.Square), r=[c.R0], w=[psA])
        p.op("vector", tt(r_msq[:, sl], r_msq[:, sl], psA[0:1, :], ALU.subtract),
             r=[psA, c.R1], w=[c.R1])
    eps_idx = 0 if eps == 1e-5 else 1
    p.op("scalar", act(r_msq, r_msq, AF.Sqrt, bias=c.cval(eps_idx)),
         r=["cval0" if eps_idx == 0 else "cval1", c.R1], w=[c.R1])
    p.op("vector", recip(r_msq, r_msq), r=[c.R1], w=[c.R1])
    p.op("vector", tt(r_mean, r_mean, r_msq, ALU.mult), r=[c.R0, c.R1], w=[c.R0])

    for ci in range(NCH):
        sl = slice(ci * CH, (ci + 1) * CH)
        k = ci % 2
        p.op("tensor", mm(psR[k][:], c.ones_row, r_msq[:, sl], True, True),
             r=[c._ones_rows, c.R1], w=[psR[k]])
        p.op("scalar", scp(c.SCR_A[:, sl], psR[k][:]), r=[psR[k]], w=[c.SCR_A])
    for ci in range(NCH):
        sl = slice(ci * CH, (ci + 1) * CH)
        k = ci % 2
        p.op("tensor", mm(psR[k][:], c.ones_row, r_mean[:, sl], True, True),
             r=[c._ones_rows, c.R0], w=[psR[k]])
        p.op("scalar", scp(c.SCR_B[:, sl], psR[k][:]), r=[psR[k]], w=[c.SCR_B])

    y_toks = [None] * DP
    z_toks = [None] * DP
    for i in range(DP):
        p.op("gpsimd", tt(c.TMP[:], X[i][:], c.SCR_A[:], ALU.mult),
             r=[X[i], c.SCR_A], w=[c.TMP])
        y_toks[i] = p.op("vector", tt(Y[i][:], c.TMP[:], c.SCR_B[:], ALU.subtract),
                         r=[c.TMP, c.SCR_B], w=[Y[i]])
        if fold is not None:
            z_toks[i] = p.op("vector",
                             stt(X[i][:], Y[i][:], float(fold), X[i][:], ALU.mult, ALU.add),
                             r=[Y[i]], w=[X[i]])
    return y_toks, (z_toks if fold is not None else None)



def emit_rms(p, c, IN, psB, psR):
    for ci in range(NCH):
        sl = slice(ci * CH, (ci + 1) * CH)
        for i in range(DP):
            k = i % 2
            p.op("scalar", act(c.SQS[k][:], IN[i][:, sl], AF.Square), r=[IN[i]], w=[c.SQS[k]])
            p.op("tensor", mm(psB[0:1, :], c.ones_col[:], c.SQS[k][:], i == 0, i == DP - 1),
                 r=[c.ones_col, c.SQS[k]], w=[psB] if i in (0, DP - 1) else ())
        p.op("scalar", smul(c.R1[:, sl], psB[0:1, :], 1.0 / D), r=[psB], w=[c.R1])
    p.op("scalar", act(c.R1[:], c.R1[:], AF.Sqrt, bias=c.cval(1)),
         r=["cval1", c.R1], w=[c.R1])
    p.op("vector", recip(c.R1[:], c.R1[:]), r=[c.R1], w=[c.R1])
    for ci in range(NCH):
        sl = slice(ci * CH, (ci + 1) * CH)
        k = ci % 2
        p.op("tensor", mm(psR[k][:], c.ones_row, c.R1[:, sl], True, True),
             r=[c._ones_rows, c.R1], w=[psR[k]])
        p.op("scalar", scp(c.SCR_A[:, sl], psR[k][:]), r=[psR[k]], w=[c.SCR_A])



def emit_a1(p, c, nc, X, dr, uid=""):
    wq_d = dr["a1_wq"]; wk_d = dr["a1_wk"]; wv_d = dr["a1_wv"]; wo_d = dr["a1_wo"]
    anw_d = dr["a1_anw"]; cb_d = dr["a1_ropeC"]; sb_d = dr["a1_ropeS"]
    mask_d = dr["a1_dmask"]
    o_scr = dr["o_scr"]; y_scr = dr["y_scr"]; h_out = dr["h_scr"]
    GH = 4
    with ExitStack() as ss:
        sbS = lambda n, s, d=F32: ss.enter_context(nc.sbuf_tensor(uid + n, s, d))
        X1 = [sbS(f"x1_{i}", [128, T], BF16) for i in range(DP)]
        ANW = sbS("anw_sb", [128, DP])

        s0 = ExitStack()
        sb0 = lambda n, s, d=F32: s0.enter_context(nc.sbuf_tensor(uid + n, s, d))
        ps0 = lambda n, s, d=F32: s0.enter_context(nc.psum_tensor(uid + n, s, d))
        Y = [sb0(f"y{i}", [128, T], BF16) for i in range(DP)]
        psA = ps0("psA", [1, CH])
        psB = ps0("psB", [1, CH])
        psR = [ps0("psR0", [128, CH]), ps0("psR1", [128, CH])]
        p.dma("scalar", ANW[:], anw_d.ap(), sem="anwin", w=[ANW])
        emit_ln(p, c, X, Y, psA, psB, psR, fold=0.5)
        emit_rms(p, c, Y, psB, psR)
        for i in range(DP):
            p.op("vector", stt(X1[i][:], Y[i][:], ANW[:, i:i + 1], c.SCR_A[:],
                               ALU.mult, ALU.mult),
                 r=[Y[i], ANW, c.SCR_A], w=[X1[i]])
            p.dma("gpsimd", y_scr.ap()[128 * i:128 * (i + 1), :], Y[i][:],
                  sem=f"ysp{i % 2}", r=[Y[i]], w=["y_scr"])
        p.barrier()
        s0.close()

        s1 = ExitStack()
        sb1 = lambda n, s, d=F32: s1.enter_context(nc.sbuf_tensor(uid + n, s, d))
        ps1 = lambda n, s, d=F32: s1.enter_context(nc.psum_tensor(uid + n, s, d))
        CB = sb1("cb", [128, T], BF16)
        SBC = sb1("sbc", [128, T], BF16)
        MSK = sb1("msk", [128, 4 * CH], BF16)
        OSTG = [sb1("ostg0", [128, T], BF16), sb1("ostg1", [128, T], BF16)]
        p.dma("sync", CB[:], cb_d.ap(), sem="cbin", w=[CB])
        p.dma("sync", SBC[:], sb_d.ap(), sem="sbin", w=[SBC])
        p.dma("sync", MSK[:], mask_d.ap(), sem="mkin", w=[MSK])
        GW = GH * 64
        WSTG = [sb1("wstg0", [128, GW]), sb1("wstg1", [128, GW])]
        WQ = [sb1(f"wqb{d}", [128, GW], BF16) for d in range(DP)]
        WQR = [sb1(f"wqrb{d}", [128, GW], BF16) for d in range(DP)]
        WK = [sb1(f"wkb{d}", [128, GW], BF16) for d in range(DP)]
        WKR = [sb1(f"wkrb{d}", [128, GW], BF16) for d in range(DP)]
        WV = [sb1(f"wvb{d}", [128, GW], BF16) for d in range(DP)]
        QH = [sb1(f"qh{j}", [128, T], BF16) for j in range(GH // 2)]
        KH = [sb1(f"kh{j}", [128, T], BF16) for j in range(GH // 2)]
        VT = [sb1(f"vt{t2}", [128, GW], BF16) for t2 in range(NTT)]
        PB = [sb1("pb0", [128, CH], BF16), sb1("pb1", [128, CH], BF16)]
        REPS = sb1("reps", [128, CH])
        NEGBC = sb1("negbc", [128, 1])
        psS = [ps1("psS0", [128, CH]), ps1("psS1", [128, CH])]
        psO = ps1("psO", [64, CH])
        psDen = ps1("psDen", [1, CH])
        psRep = ps1("psRep", [128, CH])

        for g in range(4):
            gsl = slice(GW * g, GW * (g + 1))
            for d in range(DP):
                rs = slice(128 * d, 128 * (d + 1))
                p.dma("sync", WSTG[0][:], wq_d.ap()[rs, gsl], sem="wqs", w=[WSTG[0]])
                p.op("gpsimd", cp(WQ[d][:], WSTG[0][:]), r=[WSTG[0]], w=[WQ[d]])
                _emit_rot_weights_il(p, WQR[d], WQ[d], GH, 64)
                p.dma("scalar", WSTG[1][:], wk_d.ap()[rs, gsl], sem="wks", w=[WSTG[1]])
                p.op("gpsimd", cp(WK[d][:], WSTG[1][:]), r=[WSTG[1]], w=[WK[d]])
                _emit_rot_weights_il(p, WKR[d], WK[d], GH, 64)
                p.dma("sync", WSTG[0][:], wv_d.ap()[rs, gsl], sem="wqs", w=[WSTG[0]])
                p.op("gpsimd", cp(WV[d][:], WSTG[0][:]), r=[WSTG[0]], w=[WV[d]])
            for j in range(GH // 2):
                hsl = slice(128 * j, 128 * (j + 1))
                for qc in range(NCH):
                    tsl = slice(CH * qc, CH * (qc + 1))
                    for (psq, Wm, Wr, dstH) in ((psS[0], WQ, WQR, QH), (psS[1], WK, WKR, KH)):
                        for d in range(DP):
                            p.op("tensor", mm(psq[:], Wm[d][:, hsl], X1[d][:, tsl],
                                              d == 0, d == DP - 1),
                                 r=[Wm[d], X1[d]], w=[psq] if d in (0, DP - 1) else ())
                        for d in range(DP):
                            p.op("tensor", mm(psRep[:], Wr[d][:, hsl], X1[d][:, tsl],
                                              d == 0, d == DP - 1),
                                 r=[Wr[d], X1[d]], w=[psRep] if d in (0, DP - 1) else ())
                        p.op("vector", tt(c.SQS[0][:], psq[:], CB[:, tsl], ALU.mult),
                             r=[psq, CB], w=[c.SQS[0]])
                        p.op("vector", tt(c.SQS[1][:], psRep[:], SBC[:, tsl], ALU.mult),
                             r=[psRep, SBC], w=[c.SQS[1]])
                        p.op("vector", tt(dstH[j][:, tsl], c.SQS[0][:], c.SQS[1][:], ALU.add),
                             r=[c.SQS[0], c.SQS[1]], w=[f"{_key(dstH[j])}@{qc}"])
            for t2 in range(NTT):
                ssl = slice(128 * t2, 128 * (t2 + 1))
                for d in range(DP):
                    p.op("tensor", mm(psS[0][:, 0:GW], X1[d][:, ssl], WV[d][:],
                                      d == 0, d == DP - 1),
                         r=[X1[d], WV[d]], w=[psS[0]] if d in (0, DP - 1) else ())
                p.op("scalar", scp(VT[t2][:], psS[0][:, 0:GW]), r=[psS[0]], w=[VT[t2]])
            for hh in range(GH):
                hg = GH * g + hh
                j = hh // 2
                half = hh % 2
                rows = slice(64 * half, 64 * (half + 1))
                for (HT, o0) in ((QH[j], 0), (KH[j], 4)):
                    p.op("scalar", act(c.SCR_B[:], HT[:], AF.Square),
                         r=[f"{_key(HT)}@{q2}" for q2 in range(NCH)], w=[c.SCR_B])
                    for ci in range(NCH):
                        sl2 = slice(ci * CH, (ci + 1) * CH)
                        p.op("tensor", mm(psDen[0:1, :], c.ones_col[rows, :],
                                          c.SCR_B[rows, sl2], True, True),
                             r=[c.ones_col, c.SCR_B], w=[psDen])
                        p.op("vector", rmax(c.R0[:, o0 + ci:o0 + ci + 1], psDen[0:1, :]),
                             r=[psDen], w=[c.TMP])
                p.op("vector", rmax(c.R0[:, 8:9], c.R0[:, 0:4]), r=[c.TMP], w=[c.TMP])
                p.op("vector", rmax(c.R0[:, 9:10], c.R0[:, 4:8]), r=[c.TMP], w=[c.TMP])
                p.op("vector", tt(c.R0[:, 10:11], c.R0[:, 8:9], c.R0[:, 9:10], ALU.mult),
                     r=[c.TMP], w=[c.TMP])
                p.op("scalar", act(c.R0[:, 11:12], c.R0[:, 10:11], AF.Sqrt),
                     r=[c.TMP], w=[c.TMP])
                p.op("vector", ts(c.R0[:, 12:13], c.R0[:, 11:12], -1.0 / 8.0, ALU.mult),
                     r=[c.TMP], w=[c.TMP])
                p.op("tensor", mm(psRep[:, 0:1], c.ones_row, c.R0[:, 12:13], True, True),
                     r=[c._ones_rows, c.TMP], w=[psRep])
                p.op("scalar", scp(NEGBC[:], psRep[:, 0:1]), r=[psRep], w=[NEGBC])
                for qc in range(NCH):
                    tsl = slice(CH * qc, CH * (qc + 1))
                    nkt = 4 * qc + 4
                    for kt in range(nkt):
                        k2 = kt % 2
                        ksl = slice(128 * kt, 128 * (kt + 1))
                        p.op("tensor", mm(psS[k2][:, 0:CH], KH[j][rows, ksl], QH[j][rows, tsl],
                                          True, True),
                             r=[f"{_key(KH[j])}@{q2}" for q2 in range(NCH)]
                               + [f"{_key(QH[j])}@{qc}"],
                             w=[psS[k2]])
                        r_rel = kt - 4 * qc
                        if r_rel >= 0:
                            p.op("vector", tt(c.SQS[k2][:], psS[k2][:, 0:CH],
                                              MSK[:, CH * r_rel:CH * (r_rel + 1)], ALU.add),
                                 r=[psS[k2], MSK], w=[c.SQS[k2]])
                            src = c.SQS[k2][0:128, :]
                            srck = [c.SQS[k2]]
                        else:
                            src = psS[k2][0:128, 0:CH]
                            srck = [psS[k2]]
                        p.op("scalar", act(PB[k2][:], src, AF.Exp, bias=NEGBC[:], scale=1.0 / 8.0),
                             r=srck + [NEGBC], w=[PB[k2]])
                        p.op("tensor", mm(psO[:, 0:CH], VT[kt][:, 64 * hh:64 * (hh + 1)],
                                          PB[k2][:], kt == 0, kt == nkt - 1),
                             r=[VT[kt], PB[k2]], w=[psO] if kt in (0, nkt - 1) else ())
                        p.op("tensor", mm(psDen[0:1, :], c.ones_col_bf[:], PB[k2][:],
                                          kt == 0, kt == nkt - 1),
                             r=[c.ones_col_bf, PB[k2]], w=[psDen] if kt in (0, nkt - 1) else ())
                    p.op("vector", recip(c.R0[:, 0:CH], psDen[0:1, :]), r=[psDen], w=[c.TMP])
                    p.op("tensor", mm(psRep[:], c.ones_row, c.R0[:, 0:CH], True, True),
                         r=[c._ones_rows, c.TMP], w=[psRep])
                    p.op("scalar", scp(REPS[:], psRep[:]), r=[psRep], w=[REPS])
                    p.op("vector", tt(OSTG[j % 2][rows, tsl], psO[:, 0:CH], REPS[0:64, :],
                                      ALU.mult),
                         r=[psO, REPS], w=[f"ostg{j % 2}@{half}@{qc}"])
                if half == 1:
                    hp = hg // 2
                    p.dma("gpsimd", o_scr.ap()[128 * hp:128 * (hp + 1), :], OSTG[j % 2][:],
                          sem=f"osp{j % 2}",
                          r=[f"ostg{j % 2}@{h2}@{q2}" for h2 in range(2) for q2 in range(NCH)],
                          w=["o_scr"])
        p.barrier()
        s1.close()

        s2 = ExitStack()
        sb2 = lambda n, s, d=F32: s2.enter_context(nc.sbuf_tensor(uid + n, s, d))
        ps2 = lambda n, s, d=F32: s2.enter_context(nc.psum_tensor(uid + n, s, d))
        OIN = [sb2(f"oin{ot}", [128, T], BF16) for ot in range(DP)]
        WOSTG = [sb2("wostg0", [128, D]), sb2("wostg1", [128, D])]
        WO = [sb2(f"wob{ot}", [128, D], BF16) for ot in range(DP)]
        YW = [sb2("yw0", [128, T], BF16), sb2("yw1", [128, T], BF16)]
        HST = [sb2("hst0", [128, T], BF16), sb2("hst1", [128, T], BF16)]
        psD = [ps2("psD0", [128, CH]), ps2("psD1", [128, CH])]
        for i in range(DP):
            p.dma("scalar", OIN[i][:], o_scr.ap()[128 * i:128 * (i + 1), :], sem=f"oin{i}",
                  r=["o_scr"], w=[OIN[i]])
        for ot in range(DP):
            k = ot % 2
            p.dma("sync", WOSTG[k][:], wo_d.ap()[128 * ot:128 * (ot + 1), :], sem=f"wos{k}",
                  w=[WOSTG[k]])
            p.op("gpsimd", cp(WO[ot][:], WOSTG[k][:]), r=[WOSTG[k]], w=[WO[ot]])
        for jj in range(DP):
            jsl = slice(128 * jj, 128 * (jj + 1))
            kj = jj % 2
            p.dma("sync", YW[kj][:], y_scr.ap()[jsl, :], sem=f"ywin{kj}",
                  r=["y_scr"], w=[YW[kj]])
            for tc in range(NCH):
                tsl = slice(CH * tc, CH * (tc + 1))
                k = (jj * NCH + tc) % 2
                for ot in range(DP):
                    p.op("tensor", mm(psD[k][:], WO[ot][:, jsl], OIN[ot][:, tsl],
                                      ot == 0, ot == DP - 1),
                         r=[WO[ot], OIN[ot]], w=[psD[k]] if ot in (0, DP - 1) else ())
                p.op("vector", tt(HST[kj][:, tsl], YW[kj][:, tsl], psD[k][:], ALU.add),
                     r=[YW[kj], psD[k]], w=[HST[kj]])
                p.op("vector", stt(X[jj][:, tsl], psD[k][:], 0.5, X[jj][:, tsl],
                                   ALU.mult, ALU.add),
                     r=[psD[k]], w=[X[jj]])
            p.dma("gpsimd", h_out.ap()[jsl, :], HST[kj][:], sem=f"hout{kj}",
                  r=[HST[kj]], w=["h_scr"])
        s2.close()
    p.barrier()


def build_prog_a1():
    nc = bass.Bass("TRN2", target_bir_lowering=False)
    x_in = nc.dram_tensor("x_in", [D, T], F32, kind="ExternalInput")
    dr = {
        "a1_wq": nc.dram_tensor("wq", [D, D], F32, kind="ExternalInput"),
        "a1_wk": nc.dram_tensor("wk", [D, D], F32, kind="ExternalInput"),
        "a1_wv": nc.dram_tensor("wv", [D, D], F32, kind="ExternalInput"),
        "a1_wo": nc.dram_tensor("wo", [D, D], F32, kind="ExternalInput"),
        "a1_anw": nc.dram_tensor("anw", [128, DP], F32, kind="ExternalInput"),
        "a1_ropeC": nc.dram_tensor("ropeC", [128, T], BF16, kind="ExternalInput"),
        "a1_ropeS": nc.dram_tensor("ropeS", [128, T], BF16, kind="ExternalInput"),
        "a1_dmask": nc.dram_tensor("dmask", [128, 4 * CH], BF16, kind="ExternalInput"),
        "o_scr": nc.dram_tensor("o_scr", [D, T], BF16, kind="Internal"),
        "y_scr": nc.dram_tensor("y_scr", [D, T], BF16, kind="Internal"),
        "h_scr": nc.dram_tensor("h_out", [D, T], BF16, kind="ExternalOutput"),
    }
    z_out = nc.dram_tensor("z_out", [D, T], F32, kind="ExternalOutput")
    stack = ExitStack()
    with stack:
        sb = lambda n, s, d=F32: stack.enter_context(nc.sbuf_tensor(n, s, d))
        p = Prog(nc, stack)
        c = Com(nc, stack)
        X = [sb(f"x{i}", [128, T]) for i in range(DP)]
        c.init(p)
        for i in range(DP):
            p.dma("sync", X[i][:], x_in.ap()[128 * i:128 * (i + 1), :], sem=f"xin{i}", w=[X[i]])
        emit_a1(p, c, nc, X, dr)
        for jj in range(DP):
            p.dma("sync", z_out.ap()[128 * jj:128 * (jj + 1), :], X[jj][:], sem="zout",
                  r=[X[jj]])
        p.finish()
        p.emit()
    return nc


def _emit_rot_weights_il(p, dst, src, nh, dh):
    sv = src.ap().rearrange("p (h i two) -> p h i two", h=nh, i=dh // 2, two=2)
    dv = dst.ap().rearrange("p (h i two) -> p h i two", h=nh, i=dh // 2, two=2)
    p.op("vector", ts(dv[:, :, :, 0:1], sv[:, :, :, 1:2], -1.0, ALU.mult), r=[src], w=[dst])
    p.op("vector", cp(dv[:, :, :, 1:2], sv[:, :, :, 0:1]), r=[src], w=[dst])



def emit_a2(p, c, nc, Z, dr, uid=""):
    h_in = dr["h_scr"]; fnw_d = dr["a2_fnw"]; rw_d = dr["a2_rw"]
    w1_d = dr["a2_w1"]; w3_d = dr["a2_w3"]; w2_d = dr["a2_w2"]
    ident_d = dr["a2_ident"]; sel_d = dr["a2_sel8"]
    with ExitStack() as ss:
        sbS = lambda n, s, d=F32: ss.enter_context(nc.sbuf_tensor(uid + n, s, d))
        X2 = [sbS(f"x2_{i}", [128, T], BF16) for i in range(DP)]
        CTT = sbS("ctt", [8, T])
        SEL8 = sbS("sel8_sb", [8, D])
        p.dma("scalar", SEL8[:], sel_d.ap(), sem="selin", w=[SEL8])

        s0 = ExitStack()
        sb0 = lambda n, s, d=F32: s0.enter_context(nc.sbuf_tensor(uid + n, s, d))
        ps0 = lambda n, s, d=F32: s0.enter_context(nc.psum_tensor(uid + n, s, d))
        H = [sb0(f"h{i}", [128, T], BF16) for i in range(DP)]
        FNW = sb0("fnw_sb", [128, DP])
        RWB = sb0("rwb", [128, 8 * DP], BF16)
        IDENT = sb0("ident_sb", [128, 128])
        LROW = sb0("lrow", [8, CH])
        LTOK = sb0("ltok", [128, 128])
        M8 = sb0("m8", [128, 8])
        N1 = sb0("n1", [128, 1])
        E8 = sb0("e8", [128, 8])
        IND = sb0("ind", [128, 8])
        NUM = sb0("num", [128, 8])
        E2 = sb0("e2", [128, 1])
        DINV = sb0("dinv", [128, 1])
        CT = sb0("ct", [128, 128])
        psB = ps0("psB", [1, CH])
        psR = [ps0("psR0", [128, CH]), ps0("psR1", [128, CH])]
        psL = ps0("psL", [8, CH])
        psT = ps0("psT", [128, 8])
        psT2 = ps0("psT2", [8, 128])
        for i in range(DP):
            p.dma("sync", H[i][:], h_in.ap()[128 * i:128 * (i + 1), :], sem=f"hin{i}", w=[H[i]])
        p.dma("scalar", FNW[:], fnw_d.ap(), sem="fnwin", w=[FNW])
        p.dma("scalar", IDENT[:], ident_d.ap(), sem="idin", w=[IDENT])
        emit_rms(p, c, H, psB, psR)
        for i in range(DP):
            p.op("vector", stt(X2[i][:], H[i][:], FNW[:, i:i + 1], c.SCR_A[:],
                               ALU.mult, ALU.mult),
                 r=[H[i], FNW, c.SCR_A], w=[X2[i]])
        for d in range(DP):
            p.dma("gpsimd", c.SQS[0][:, 0:8], rw_d.ap()[128 * d:128 * (d + 1), :],
                  sem="rwin", w=[c.SQS[0]])
            p.op("vector", cp(RWB[:, 8 * d:8 * (d + 1)], c.SQS[0][:, 0:8]),
                 r=[c.SQS[0]], w=[RWB])
        for ci in range(NCH):
            csl = slice(CH * ci, CH * (ci + 1))
            for d in range(DP):
                p.op("tensor", mm(psL[0:8, :], RWB[:, 8 * d:8 * (d + 1)], X2[d][:, csl],
                                  d == 0, d == DP - 1),
                     r=[RWB, X2[d]], w=[psL] if d in (0, DP - 1) else ())
            p.op("scalar", scp(LROW[:], psL[0:8, :]), r=[psL], w=[LROW])
            for t2 in range(4):
                tt_g = 4 * ci + t2
                p.op("tensor", transp(psT[:, 0:8], LROW[0:8, 128 * t2:128 * (t2 + 1)],
                                      IDENT[0:8, 0:8]),
                     r=[LROW, IDENT], w=[psT])
                p.op("scalar", scp(LTOK[:, 8 * tt_g:8 * (tt_g + 1)], psT[:, 0:8]),
                     r=[psT], w=[LTOK])
        for tt_g in range(NTT):
            lt = LTOK[:, 8 * tt_g:8 * (tt_g + 1)]
            p.op("vector", max8(M8[:], lt), r=[LTOK], w=[M8])
            p.op("vector", ts(N1[:], M8[:, 0:1], -1.0, ALU.mult), r=[M8], w=[N1])
            p.op("scalar", act(E8[:], lt, AF.Exp, bias=N1[:]), r=[LTOK, N1], w=[E8])
            p.op("vector", ts(IND[:], lt, M8[:, 1:2], ALU.is_ge), r=[LTOK, M8], w=[IND])
            p.op("vector", tt(NUM[:], E8[:], IND[:], ALU.mult), r=[E8, IND], w=[NUM])
            p.op("scalar", act(E2[:], M8[:, 1:2], AF.Exp, bias=N1[:]), r=[M8, N1], w=[E2])
            p.op("vector", ts(E2[:], E2[:], 1.0, ALU.add), r=[E2], w=[E2])
            p.op("vector", recip(DINV[:], E2[:]), r=[E2], w=[DINV])
            p.op("vector",
                 lambda E, NUM=NUM, DINV=DINV, tt_g=tt_g:
                     E.tensor_scalar(CT[:, 8 * tt_g:8 * (tt_g + 1)], NUM[:], DINV[:], 0.5,
                                     ALU.mult, ALU.mult),
                 r=[NUM, DINV], w=[CT])
        for tt_g in range(NTT):
            p.op("tensor", transp(psT2[0:8, :], CT[:, 8 * tt_g:8 * (tt_g + 1)], IDENT[:]),
                 r=[CT, IDENT], w=[psT2])
            p.op("scalar", scp(CTT[:, 128 * tt_g:128 * (tt_g + 1)], psT2[0:8, :]),
                 r=[psT2], w=[CTT])
        p.barrier()
        s0.close()

        s1 = ExitStack()
        sb1 = lambda n, s, d=F32: s1.enter_context(nc.sbuf_tensor(uid + n, s, d))
        ps1 = lambda n, s, d=F32: s1.enter_context(nc.psum_tensor(uid + n, s, d))
        W1 = [sb1(f"w1b{d}", [128, D], BF16) for d in range(DP)]
        W3 = [sb1(f"w3b{d}", [128, D], BF16) for d in range(DP)]
        W2H = [sb1(f"w2b{it}", [128, D], BF16) for it in range(DP)]
        WSTG1 = sb1("wstg1", [128, D])
        WSTG3 = sb1("wstg3", [128, D])
        WSTG2 = WSTG1
        GT = [sb1(f"gt{it}", [128, CH], BF16) for it in range(DP)]
        TSC = c.SQS
        psH1 = [ps1("psH1a", [128, CH]), ps1("psH1b", [128, CH])]
        psH3 = [ps1("psH3a", [128, CH]), ps1("psH3b", [128, CH])]
        psY = [ps1("psYa", [128, CH]), ps1("psYb", [128, CH])]
        psRep = ps1("psRep2", [128, CH])

        for e in range(8):
            for ci in range(NCH):
                csl = slice(CH * ci, CH * (ci + 1))
                p.op("tensor", mm(psRep[:], SEL8[0:8, 128 * e:128 * (e + 1)],
                                  CTT[0:8, csl], True, True),
                     r=[SEL8, CTT], w=[psRep])
                p.op("scalar", scp(c.SCR_B[:, csl], psRep[:]), r=[psRep], w=[c.SCR_B])
            for ih in range(2):
                isl = slice(1024 * ih, 1024 * (ih + 1))
                for d in range(DP):
                    p.dma("sync", WSTG1[:], w1_d.ap()[e, 128 * d:128 * (d + 1), isl],
                          sem="w1s", w=[WSTG1])
                    p.op("gpsimd", cp(W1[d][:], WSTG1[:]), r=[WSTG1], w=[W1[d]])
                    p.dma("scalar", WSTG3[:], w3_d.ap()[e, 128 * d:128 * (d + 1), isl],
                          sem="w3s", w=[WSTG3])
                    p.op("gpsimd", cp(W3[d][:], WSTG3[:]), r=[WSTG3], w=[W3[d]])
                for it in range(DP):
                    p.dma("sync", WSTG2[:], w2_d.ap()[e, 1024 * ih + 128 * it:
                                                      1024 * ih + 128 * (it + 1), :],
                          sem="w1s", w=[WSTG2])
                    p.op("gpsimd", cp(W2H[it][:], WSTG2[:]), r=[WSTG2], w=[W2H[it]])
                for tc in range(NCH):
                    tsl = slice(CH * tc, CH * (tc + 1))
                    for it in range(DP):
                        k = it % 2
                        col = slice(128 * it, 128 * (it + 1))
                        for d in range(DP):
                            p.op("tensor", mm(psH1[k][:], W1[d][:, col], X2[d][:, tsl],
                                              d == 0, d == DP - 1),
                                 r=[W1[d], X2[d]], w=[psH1[k]] if d in (0, DP - 1) else ())
                        for d in range(DP):
                            p.op("tensor", mm(psH3[k][:], W3[d][:, col], X2[d][:, tsl],
                                              d == 0, d == DP - 1),
                                 r=[W3[d], X2[d]], w=[psH3[k]] if d in (0, DP - 1) else ())
                        p.op("scalar", act(c.SQS[k][:], psH1[k][:], AF.Silu),
                             r=[psH1[k]], w=[c.SQS[k]])
                        p.op("vector", tt(GT[it][:], c.SQS[k][:], psH3[k][:], ALU.mult),
                             r=[c.SQS[k], psH3[k]], w=[GT[it]])
                    for j in range(DP):
                        k = j % 2
                        jcol = slice(128 * j, 128 * (j + 1))
                        for it in range(DP):
                            p.op("tensor", mm(psY[k][:], W2H[it][:, jcol], GT[it][:],
                                              it == 0, it == DP - 1),
                                 r=[W2H[it], GT[it]], w=[psY[k]] if it in (0, DP - 1) else ())
                        p.op("vector", tt(TSC[k][:], psY[k][:], c.SCR_B[:, tsl], ALU.mult),
                             r=[psY[k], c.SCR_B], w=[TSC[k]])
                        p.op("gpsimd", tt(Z[j][:, tsl], Z[j][:, tsl], TSC[k][:], ALU.add),
                             r=[TSC[k]], w=[Z[j]])
        s1.close()
    p.barrier()


def build_prog_a2():
    nc = bass.Bass("TRN2", target_bir_lowering=False)
    z_in = nc.dram_tensor("z_in", [D, T], F32, kind="ExternalInput")
    dr = {
        "h_scr": nc.dram_tensor("h_in", [D, T], BF16, kind="ExternalInput"),
        "a2_fnw": nc.dram_tensor("fnw", [128, DP], F32, kind="ExternalInput"),
        "a2_rw": nc.dram_tensor("rw", [D, 8], F32, kind="ExternalInput"),
        "a2_w1": nc.dram_tensor("w1", [8, D, 2048], F32, kind="ExternalInput"),
        "a2_w3": nc.dram_tensor("w3", [8, D, 2048], F32, kind="ExternalInput"),
        "a2_w2": nc.dram_tensor("w2", [8, 2048, D], F32, kind="ExternalInput"),
        "a2_ident": nc.dram_tensor("ident", [128, 128], F32, kind="ExternalInput"),
        "a2_sel8": nc.dram_tensor("sel8", [8, D], F32, kind="ExternalInput"),
    }
    x_out = nc.dram_tensor("x_out", [D, T], F32, kind="ExternalOutput")
    stack = ExitStack()
    with stack:
        sb = lambda n, s, d=F32: stack.enter_context(nc.sbuf_tensor(n, s, d))
        p = Prog(nc, stack)
        c = Com(nc, stack)
        Z = [sb(f"z{i}", [128, T]) for i in range(DP)]
        c.init(p)
        for i in range(DP):
            p.dma("sync", Z[i][:], z_in.ap()[128 * i:128 * (i + 1), :], sem=f"zin{i}", w=[Z[i]])
        emit_a2(p, c, nc, Z, dr)
        for j in range(DP):
            p.dma("sync", x_out.ap()[128 * j:128 * (j + 1), :], Z[j][:], sem="xout", r=[Z[j]])
        p.finish()
        p.emit()
    return nc


def _prep_l0_ffn_moe(inputs):
    sel = np.zeros((8, D), np.float32)
    for e in range(8):
        sel[e, 128 * e:128 * (e + 1)] = 1.0
    return {
        "fnw": np.ascontiguousarray(np.asarray(inputs["L0_ffn_ffn_norm_w"]).reshape(DP, 128).T),
        "rw": np.ascontiguousarray(inputs["L0_ffn_router_w"]),
        "w1": np.ascontiguousarray(inputs["L0_ffn_w1"]),
        "w3": np.ascontiguousarray(inputs["L0_ffn_w3"]),
        "w2": np.ascontiguousarray(inputs["L0_ffn_w2"]),
        "ident": np.eye(128, dtype=np.float32),
        "sel8": sel,
    }


def run_a2(hs, zs, W):
    nc = get_prog("a2")
    in_maps = [dict(h_in=hs[b], z_in=zs[b], **W) for b in range(8)]
    res = run_bass_kernel_spmd(nc, in_maps, core_ids=list(range(8)))
    return [np.asarray(r["x_out"]) for r in res.results]



def emit_e(p, c, nc, X, dr, uid=""):
    wq_d = dr["e_wq"]; wk_d = dr["e_wk"]; wv_d = dr["e_wv"]
    wqf_d = dr["e_wqf"]; wkf_d = dr["e_wkf"]; wout_d = dr["e_wout"]
    nw_d = dr["e_nw"]; ga_d = dr["e_gha"]; gb_d = dr["e_ghb"]
    triu_d = dr["e_triu"]; wmask_d = dr["e_wmask"]
    lin_scr = dr["lin_scr"]; win_scr = dr["win_scr"]
    with ExitStack() as ss:
        sbS = lambda n, s, d=F32: ss.enter_context(nc.sbuf_tensor(uid + n, s, d))
        XH = [sbS(f"xh{i}", [128, T], BF16) for i in range(DP)]

        s0 = ExitStack()
        sb0 = lambda n, s, d=F32: s0.enter_context(nc.sbuf_tensor(uid + n, s, d))
        ps0 = lambda n, s, d=F32: s0.enter_context(nc.psum_tensor(uid + n, s, d))
        Y = [sb0(f"y{i}", [128, T], BF16) for i in range(DP)]
        NW = sb0("nw_sb", [128, DP])
        psA = ps0("psA", [1, CH])
        psB = ps0("psB", [1, CH])
        psR = [ps0("psR0", [128, CH]), ps0("psR1", [128, CH])]
        p.dma("scalar", NW[:], nw_d.ap(), sem="nwin", w=[NW])
        emit_ln(p, c, X, Y, psA, psB, psR, fold=1.0)
        emit_rms(p, c, Y, psB, psR)
        for i in range(DP):
            p.op("vector", stt(XH[i][:], Y[i][:], NW[:, i:i + 1], c.SCR_A[:],
                               ALU.mult, ALU.mult),
                 r=[Y[i], NW, c.SCR_A], w=[XH[i]])
        p.barrier()
        s0.close()

        s1 = ExitStack()
        sb1 = lambda n, s, d=F32: s1.enter_context(nc.sbuf_tensor(uid + n, s, d))
        ps1 = lambda n, s, d=F32: s1.enter_context(nc.psum_tensor(uid + n, s, d))
        WQB = [sb1(f"wqb{d}", [128, 128], BF16) for d in range(DP)]
        WKB = [sb1(f"wkb{d}", [128, 128], BF16) for d in range(DP)]
        WVB = [sb1(f"wvb{d}", [128, 128], BF16) for d in range(DP)]
        WSTG = [sb1("wstg0", [128, 128]), sb1("wstg1", [128, 128])]
        WQF = sb1("wqf_sb", [64, 16], BF16)
        WKF = sb1("wkf_sb", [64, 16], BF16)
        WFS = sb1("wfs", [64, 32])
        GA = sb1("gha_sb", [33, 153], BF16)
        GB = sb1("ghb_sb", [33, 153], BF16)
        TRIU = sb1("triu_sb", [128, 128])
        WMSK = sb1("wmsk_sb", [128, 256], BF16)
        QH = sb1("qh", [64, T], BF16)
        KH = sb1("kh", [64, T], BF16)
        VA = [sb1("va0", [128, NTT * 65], BF16), sb1("va1", [128, NTT * 65], BF16)]
        SQ = sb1("sq", [33, T], BF16)
        SK = sb1("sk", [33, T], BF16)
        PQ1 = sb1("pq1", [128, T], BF16)
        PQ2 = sb1("pq2", [25, T], BF16)
        PK1 = sb1("pk1", [128, T], BF16)
        PK2 = sb1("pk2", [25, T], BF16)
        KTOK = sb1("ktok", [128, NTT * 153], BF16)
        KVH = sb1("kvh", [128, 65], BF16)
        KVL = sb1("kvl", [25, 65], BF16)
        KVFH = sb1("kvfh", [128, 65])
        KVFL = sb1("kvfl", [25, 65])
        LINH = sb1("linh", [64, T], BF16)
        WINH = sb1("winh", [64, T], BF16)
        ASB = sb1("asb", [128, 128], BF16)
        PBW = sb1("pbw", [128, 128], BF16)
        REPE = sb1("repe", [128, 128])
        NEGWC = sb1("negwc", [128, 1])
        psU = ps1("psU", [128, CH])
        psW = ps1("psW", [128, CH])
        psA2 = ps1("psA2", [128, 128])
        psY2 = ps1("psY2", [65, 128])
        psOW = ps1("psOW", [65, 128])
        psRepE = ps1("psRepE", [128, 128])
        kvh_ps = ps1("kvh_ps", [128, 65])
        kvl_ps = ps1("kvl_ps", [25, 65])

        p.dma("sync", WFS[:, 0:16], wqf_d.ap(), sem="wfin", w=[WFS])
        p.dma("sync", WFS[:, 16:32], wkf_d.ap(), sem="wfin2", w=[WFS])
        p.op("vector", cp(WQF[:], WFS[:, 0:16]), r=[WFS], w=[WQF])
        p.op("vector", cp(WKF[:], WFS[:, 16:32]), r=[WFS], w=[WKF])
        p.dma("sync", GA[:], ga_d.ap(), sem="gain", w=[GA])
        p.dma("sync", GB[:], gb_d.ap(), sem="gbin", w=[GB])
        p.dma("sync", TRIU[:], triu_d.ap(), sem="trin", w=[TRIU])
        p.dma("sync", WMSK[:], wmask_d.ap(), sem="wmin", w=[WMSK])
        p.op("vector", memset(SQ[:], 0.0), w=[SQ])
        p.op("vector", memset(SQ[32:33, :], 1.0), w=[SQ])
        p.op("vector", memset(SK[:], 0.0), w=[SK])
        p.op("vector", memset(SK[32:33, :], 1.0), w=[SK])

        for hp in range(8):
            psl = slice(128 * hp, 128 * (hp + 1))
            for d in range(DP):
                rs = slice(128 * d, 128 * (d + 1))
                p.dma("sync", WSTG[0][:], wq_d.ap()[rs, psl], sem="wqs", w=[WSTG[0]])
                p.op("gpsimd", cp(WQB[d][:], WSTG[0][:]), r=[WSTG[0]], w=[WQB[d]])
                p.dma("scalar", WSTG[1][:], wk_d.ap()[rs, psl], sem="wks", w=[WSTG[1]])
                p.op("gpsimd", cp(WKB[d][:], WSTG[1][:]), r=[WSTG[1]], w=[WKB[d]])
                p.dma("sync", WSTG[0][:], wv_d.ap()[rs, psl], sem="wqs", w=[WSTG[0]])
                p.op("gpsimd", cp(WVB[d][:], WSTG[0][:]), r=[WSTG[0]], w=[WVB[d]])
            for h2 in range(2):
                p.op("vector", memset(
                    VA[h2].ap().rearrange("p (t c) -> p t c", t=NTT, c=65)[:, :, 64:65], 1.0),
                    w=[VA[h2]])
            for t2 in range(NTT):
                ssl = slice(128 * t2, 128 * (t2 + 1))
                for d in range(DP):
                    p.op("tensor", mm(psA2[:], XH[d][:, ssl], WVB[d][:],
                                      d == 0, d == DP - 1),
                         r=[XH[d], WVB[d]], w=[psA2] if d in (0, DP - 1) else ())
                for h2 in range(2):
                    p.op("vector", cp(
                        VA[h2].ap().rearrange("p (t c) -> p t c", t=NTT, c=65)[:, t2, 0:64],
                        psA2[:, 64 * h2:64 * (h2 + 1)]),
                        r=[psA2], w=[VA[h2]])
            for h2 in range(2):
                h = 2 * hp + h2
                hsl = slice(64 * h2, 64 * (h2 + 1))
                vaR = VA[h2].ap().rearrange("p (t c) -> p t c", t=NTT, c=65)
                for ci in range(NCH):
                    csl = slice(CH * ci, CH * (ci + 1))
                    for (ps_, Wm, dst) in ((psU, WQB, QH), (psW, WKB, KH)):
                        for d in range(DP):
                            p.op("tensor", mm(ps_[0:64, :], Wm[d][:, hsl], XH[d][:, csl],
                                              d == 0, d == DP - 1),
                                 r=[Wm[d], XH[d]], w=[ps_] if d in (0, DP - 1) else ())
                        p.op("scalar", scp(dst[:, csl], ps_[0:64, :]), r=[ps_], w=[dst])
                for ci in range(NCH):
                    csl = slice(CH * ci, CH * (ci + 1))
                    p.op("tensor", mm(psU[0:16, :], WQF[:], QH[:, csl], True, True),
                         r=[WQF, QH], w=[psU])
                    p.op("scalar", scp(SQ[0:16, csl], psU[0:16, :]), r=[psU], w=[SQ])
                    p.op("tensor", mm(psW[0:16, :], WKF[:], KH[:, csl], True, True),
                         r=[WKF, KH], w=[psW])
                    p.op("scalar", scp(SK[0:16, csl], psW[0:16, :]), r=[psW], w=[SK])
                for (S_, P1, P2) in ((SQ, PQ1, PQ2), (SK, PK1, PK2)):
                    for ci in range(NCH):
                        csl = slice(CH * ci, CH * (ci + 1))
                        for (P_, fsl, np_) in ((P1, slice(0, 128), 128),
                                               (P2, slice(128, 153), 25)):
                            p.op("tensor", mm(psU[0:np_, :], GA[:, fsl], S_[:, csl],
                                              True, True),
                                 r=[GA, S_], w=[psU])
                            p.op("tensor", mm(psW[0:np_, :], GB[:, fsl], S_[:, csl],
                                              True, True),
                                 r=[GB, S_], w=[psW])
                            p.op("scalar", scp(c.SQS[0][0:np_, :], psW[0:np_, :]),
                                 r=[psW], w=[c.SQS[0]])
                            p.op("vector", tt(P_[0:np_, csl], psU[0:np_, :],
                                              c.SQS[0][0:np_, :], ALU.mult),
                                 r=[psU, c.SQS[0]], w=[P_])
                for ch in range(NTT):
                    p.op("tensor", mm(psU[:, 0:153], SK[:, 128 * ch:128 * (ch + 1)], GA[:],
                                      True, True),
                         r=[SK, GA], w=[psU])
                    p.op("tensor", mm(psW[:, 0:153], SK[:, 128 * ch:128 * (ch + 1)], GB[:],
                                      True, True),
                         r=[SK, GB], w=[psW])
                    p.op("scalar", scp(c.SQS[0][:, 0:153], psW[:, 0:153]),
                         r=[psW], w=[c.SQS[0]])
                    p.op("vector", tt(KTOK[:, 153 * ch:153 * (ch + 1)], psU[:, 0:153],
                                      c.SQS[0][:, 0:153], ALU.mult),
                         r=[psU, c.SQS[0]], w=[KTOK])
                p.op("vector", memset(KVH[:], 0.0), w=[KVH])
                p.op("vector", memset(KVL[:], 0.0), w=[KVL])
                p.op("vector", memset(KVFH[:], 0.0), w=[KVFH])
                p.op("vector", memset(KVFL[:], 0.0), w=[KVFL])
                for ch in range(NTT):
                    cs128 = slice(128 * ch, 128 * (ch + 1))
                    p.op("tensor", mm(psA2[:], PK1[:, cs128], PQ1[:, cs128], True, False),
                         r=[PK1, PQ1], w=[psA2])
                    p.op("tensor", mm(psA2[:], PK2[0:25, cs128], PQ2[0:25, cs128],
                                      False, True),
                         r=[PK2, PQ2], w=[psA2])
                    p.op("vector", tt(ASB[:], psA2[:], TRIU[:], ALU.mult),
                         r=[psA2, TRIU], w=[ASB])
                    p.op("tensor", mm(psY2[:, :], vaR[:, ch, :], ASB[:], True, False),
                         r=[VA[h2], ASB], w=[psY2])
                    p.op("tensor", mm(psY2[:, :], KVH[:], PQ1[:, cs128], False, False),
                         r=[KVH, PQ1], w=[psY2])
                    p.op("tensor", mm(psY2[:, :], KVL[:], PQ2[0:25, cs128], False, True),
                         r=[KVL, PQ2], w=[psY2])
                    p.op("vector", ts(c.R0[:, 0:128], psY2[64:65, :], c.cval(1), ALU.add),
                         r=[psY2, "cval1"], w=[c.TMP])
                    p.op("vector", recip(c.R0[:, 0:128], c.R0[:, 0:128]), r=[c.TMP], w=[c.TMP])
                    p.op("tensor", mm(psRepE[:], c.ones_row, c.R0[:, 0:128], True, True),
                         r=[c._ones_rows, c.TMP], w=[psRepE])
                    p.op("scalar", scp(REPE[:], psRepE[:]), r=[psRepE], w=[REPE])
                    p.op("vector", tt(LINH[:, cs128], psY2[0:64, :], REPE[0:64, :], ALU.mult),
                         r=[psY2, REPE], w=[LINH])
                    if ch < NTT - 1:
                        p.op("tensor", mm(kvh_ps[:], KTOK[:, 153 * ch:153 * ch + 128],
                                          vaR[:, ch, :], True, True),
                             r=[KTOK, VA[h2]], w=[kvh_ps])
                        p.op("tensor", mm(kvl_ps[:], KTOK[:, 153 * ch + 128:153 * (ch + 1)],
                                          vaR[:, ch, :], True, True),
                             r=[KTOK, VA[h2]], w=[kvl_ps])
                        p.op("vector", tt(KVFH[:], KVFH[:], kvh_ps[:], ALU.add),
                             r=[kvh_ps], w=[KVFH])
                        p.op("vector", tt(KVFL[:], KVFL[:], kvl_ps[:], ALU.add),
                             r=[kvl_ps], w=[KVFL])
                        p.op("scalar", scp(KVH[:], KVFH[:]), r=[KVFH], w=[KVH])
                        p.op("scalar", scp(KVL[:], KVFL[:]), r=[KVFL], w=[KVL])
                for (HT, o0) in ((QH, 0), (KH, 4)):
                    p.op("scalar", act(c.SCR_B[0:64, :], HT[:], AF.Square),
                         r=[HT], w=[c.SCR_B])
                    for ci in range(NCH):
                        sl2 = slice(ci * CH, (ci + 1) * CH)
                        p.op("tensor", mm(psU[0:1, 0:CH], c.ones_col[0:64, :],
                                          c.SCR_B[0:64, sl2], True, True),
                             r=[c.ones_col, c.SCR_B], w=[psU])
                        p.op("vector", rmax(c.R0[:, o0 + ci:o0 + ci + 1], psU[0:1, 0:CH]),
                             r=[psU], w=[c.TMP])
                p.op("vector", rmax(c.R0[:, 8:9], c.R0[:, 0:4]), r=[c.TMP], w=[c.TMP])
                p.op("vector", rmax(c.R0[:, 9:10], c.R0[:, 4:8]), r=[c.TMP], w=[c.TMP])
                p.op("vector", tt(c.R0[:, 10:11], c.R0[:, 8:9], c.R0[:, 9:10], ALU.mult),
                     r=[c.TMP], w=[c.TMP])
                p.op("scalar", act(c.R0[:, 11:12], c.R0[:, 10:11], AF.Sqrt),
                     r=[c.TMP], w=[c.TMP])
                p.op("vector", ts(c.R0[:, 12:13], c.R0[:, 11:12], -1.0 / 8.0, ALU.mult),
                     r=[c.TMP], w=[c.TMP])
                p.op("tensor", mm(psRepE[:, 0:1], c.ones_row, c.R0[:, 12:13], True, True),
                     r=[c._ones_rows, c.TMP], w=[psRepE])
                p.op("scalar", scp(NEGWC[:], psRepE[:, 0:1]), r=[psRepE], w=[NEGWC])
                for qc in range(NTT):
                    qsl = slice(128 * qc, 128 * (qc + 1))
                    kts = [kt for kt in (qc - 1, qc) if kt >= 0]
                    for ki, kt in enumerate(kts):
                        ksl = slice(128 * kt, 128 * (kt + 1))
                        p.op("tensor", mm(psA2[:], KH[:, ksl], QH[:, qsl], True, True),
                             r=[KH, QH], w=[psA2])
                        wm = WMSK[:, 0:128] if kt < qc else WMSK[:, 128:256]
                        p.op("vector", tt(c.SQS[1][:, 0:128], psA2[:], wm, ALU.add),
                             r=[psA2, WMSK], w=[c.SQS[1]])
                        p.op("scalar", act(PBW[:], c.SQS[1][:, 0:128], AF.Exp,
                                           bias=NEGWC[:], scale=1.0 / 8.0),
                             r=[c.SQS[1], NEGWC], w=[PBW])
                        p.op("tensor", mm(psOW[:, :], vaR[:, kt, :], PBW[:],
                                          ki == 0, ki == len(kts) - 1),
                             r=[VA[h2], PBW], w=[psOW])
                    p.op("vector", recip(c.R0[:, 0:128], psOW[64:65, :]), r=[psOW], w=[c.TMP])
                    p.op("tensor", mm(psRepE[:], c.ones_row, c.R0[:, 0:128], True, True),
                         r=[c._ones_rows, c.TMP], w=[psRepE])
                    p.op("scalar", scp(REPE[:], psRepE[:]), r=[psRepE], w=[REPE])
                    p.op("vector", tt(WINH[:, qsl], psOW[0:64, :], REPE[0:64, :], ALU.mult),
                         r=[psOW, REPE], w=[WINH])
                p.dma("gpsimd", lin_scr.ap()[64 * h:64 * (h + 1), :], LINH[:],
                      sem="linsp", r=[LINH], w=["lin_scr"])
                p.dma("gpsimd", win_scr.ap()[64 * h:64 * (h + 1), :], WINH[:],
                      sem="winsp", r=[WINH], w=["win_scr"])
        p.barrier()
        s1.close()

        s2 = ExitStack()
        sb2 = lambda n, s, d=F32: s2.enter_context(nc.sbuf_tensor(uid + n, s, d))
        ps2 = lambda n, s, d=F32: s2.enter_context(nc.psum_tensor(uid + n, s, d))
        Z = X
        LW2 = [sb2("lw2a", [128, CH], BF16), sb2("lw2b", [128, CH], BF16)]
        WOSTG = [sb2("wostg0", [128, D]), sb2("wostg1", [128, D])]
        WOUT = [sb2(f"wob{ot}", [128, D], BF16) for ot in range(16)]
        psDE = [ps2(f"psDE{j}", [128, CH]) for j in range(DP)]
        for ot in range(16):
            k = ot % 2
            p.dma("sync", WOSTG[k][:], wout_d.ap()[128 * ot:128 * (ot + 1), :],
                  sem=f"wos{k}", w=[WOSTG[k]])
            p.op("gpsimd", cp(WOUT[ot][:], WOSTG[k][:]), r=[WOSTG[k]], w=[WOUT[ot]])
        for tc in range(NCH):
            tsl = slice(CH * tc, CH * (tc + 1))
            for ot in range(16):
                k = ot % 2
                src = lin_scr if ot < DP else win_scr
                ro = ot if ot < DP else ot - DP
                p.dma("scalar", LW2[k][:],
                      src.ap()[128 * ro:128 * (ro + 1), tsl], sem=f"lw2{k}",
                      r=["lin_scr" if ot < DP else "win_scr"], w=[LW2[k]])
                for j in range(DP):
                    p.op("tensor", mm(psDE[j][:], WOUT[ot][:, 128 * j:128 * (j + 1)],
                                      LW2[k][:], ot == 0, ot == 15),
                         r=[WOUT[ot], LW2[k]], w=[psDE[j]] if ot in (0, 15) else ())
            for j in range(DP):
                p.op("vector", tt(Z[j][:, tsl], Z[j][:, tsl], psDE[j][:], ALU.add),
                     r=[psDE[j]], w=[Z[j]])
        s2.close()
    p.barrier()


def build_prog_e():
    nc = bass.Bass("TRN2", target_bir_lowering=False)
    x_in = nc.dram_tensor("x_in", [D, T], F32, kind="ExternalInput")
    dr = {
        "e_wq": nc.dram_tensor("wq", [D, D], F32, kind="ExternalInput"),
        "e_wk": nc.dram_tensor("wk", [D, D], F32, kind="ExternalInput"),
        "e_wv": nc.dram_tensor("wv", [D, D], F32, kind="ExternalInput"),
        "e_wqf": nc.dram_tensor("wqf", [64, 16], F32, kind="ExternalInput"),
        "e_wkf": nc.dram_tensor("wkf", [64, 16], F32, kind="ExternalInput"),
        "e_wout": nc.dram_tensor("wout", [2 * D, D], F32, kind="ExternalInput"),
        "e_nw": nc.dram_tensor("nw", [128, DP], F32, kind="ExternalInput"),
        "e_gha": nc.dram_tensor("gha", [33, 153], BF16, kind="ExternalInput"),
        "e_ghb": nc.dram_tensor("ghb", [33, 153], BF16, kind="ExternalInput"),
        "e_triu": nc.dram_tensor("triu", [128, 128], F32, kind="ExternalInput"),
        "e_wmask": nc.dram_tensor("wmask", [128, 256], BF16, kind="ExternalInput"),
        "lin_scr": nc.dram_tensor("lin_scr", [D, T], BF16, kind="Internal"),
        "win_scr": nc.dram_tensor("win_scr", [D, T], BF16, kind="Internal"),
    }
    x_out = nc.dram_tensor("x_out", [D, T], F32, kind="ExternalOutput")
    stack = ExitStack()
    with stack:
        sb = lambda n, s, d=F32: stack.enter_context(nc.sbuf_tensor(n, s, d))
        p = Prog(nc, stack)
        c = Com(nc, stack)
        X = [sb(f"x{i}", [128, T]) for i in range(DP)]
        c.init(p)
        for i in range(DP):
            p.dma("sync", X[i][:], x_in.ap()[128 * i:128 * (i + 1), :], sem=f"xin{i}", w=[X[i]])
        emit_e(p, c, nc, X, dr)
        for jj in range(DP):
            p.dma("sync", x_out.ap()[128 * jj:128 * (jj + 1), :], X[jj][:], sem="xout",
                  r=[X[jj]])
        p.finish()
        p.emit()
    return nc


def _prep_l1_attn(inputs):
    import math
    iu, ju = np.triu_indices(16)
    scale = np.where(iu == ju, 0.5, 1.0 / math.sqrt(2.0)).astype(np.float32)
    GA = np.zeros((33, 153), np.float32)
    GB = np.zeros((33, 153), np.float32)
    GA[32, 0] = 1.0
    GB[32, 0] = 1.0
    for i in range(16):
        GA[i, 1 + i] = 1.0
        GB[32, 1 + i] = 1.0
    for qn in range(136):
        GA[iu[qn], 17 + qn] = 1.0
        GB[ju[qn], 17 + qn] = scale[qn]
    rk = np.arange(128)[:, None]
    cq = np.arange(128)[None, :]
    wm_prev = np.where(rk >= cq + 64, 0.0, -1e30).astype(np.float32)
    wm_diag = np.where((rk <= cq) & (rk >= cq - 64), 0.0, -1e30).astype(np.float32)
    triu = np.triu(np.ones((128, 128), np.float32))
    return {
        "wq": np.ascontiguousarray(inputs["L1_attn_Wq"]),
        "wk": np.ascontiguousarray(inputs["L1_attn_Wk"]),
        "wv": np.ascontiguousarray(inputs["L1_attn_Wv"]),
        "wqf": np.ascontiguousarray(inputs["L1_attn_Wqf"]),
        "wkf": np.ascontiguousarray(inputs["L1_attn_Wkf"]),
        "wout": np.ascontiguousarray(inputs["L1_attn_Wout"]),
        "nw": np.ascontiguousarray(np.asarray(inputs["L1_attn_norm_w"]).reshape(DP, 128).T),
        "gha": np.ascontiguousarray(GA.astype(BF)),
        "ghb": np.ascontiguousarray(GB.astype(BF)),
        "triu": triu,
        "wmask": np.ascontiguousarray(
            np.concatenate([wm_prev, wm_diag], axis=1).astype(BF)),
    }


def run_e(xs, W):
    nc = get_prog("e")
    in_maps = [dict(x_in=xs[b], **W) for b in range(8)]
    res = run_bass_kernel_spmd(nc, in_maps, core_ids=list(range(8)))
    return [np.asarray(r["x_out"]) for r in res.results]



_PROG_CACHE = {}

def emit_d(p, c, nc, X, dr, uid=""):
    wg = dr["d_wg"]; wu = dr["d_wu"]; wd = dr["d_wd"]; g_scr = dr["g_scr"]
    with ExitStack() as ss:
        sbS = lambda n, s, d=F32: ss.enter_context(nc.sbuf_tensor(uid + n, s, d))
        Y = [sbS(f"y{i}", [128, T], BF16) for i in range(DP)]

        s1 = ExitStack()
        sb1 = lambda n, s, d=F32: s1.enter_context(nc.sbuf_tensor(uid + n, s, d))
        ps1 = lambda n, s, d=F32: s1.enter_context(nc.psum_tensor(uid + n, s, d))
        psA = ps1("psA", [1, CH])
        psB = ps1("psB", [1, CH])
        psR = [ps1("psR0", [128, CH]), ps1("psR1", [128, CH])]
        GPS = [ps1("gate0", [128, CH]), ps1("gate1", [128, CH])]
        UPS = [ps1("up0", [128, CH]), ps1("up1", [128, CH])]
        stg_g = sb1("stg_g0", [128, 1536], I32)
        stg_u = sb1("stg_u0", [128, 1536], I32)
        gst = sb1("gstage0", [128, T], BF16)
        WG = [sb1(f"wgb{i}", [128, 1536], BF16) for i in range(DP)]
        WU = [sb1(f"wub{i}", [128, 1536], BF16) for i in range(DP)]

        emit_ln(p, c, X, Y, psA, psB, psR, fold=None)

        for ih in range(2):
            csl = slice(1536 * ih, 1536 * (ih + 1))
            for d in range(DP):
                p.dma("sync", stg_g[:], wg.ap()[128 * d:128 * (d + 1), csl], sem="wgs",
                      w=[stg_g])
                p.op("gpsimd", ts(WG[d][:], stg_g[:], 1.0 / 127.0, ALU.mult),
                     r=[stg_g], w=[WG[d]])
                p.dma("scalar", stg_u[:], wu.ap()[128 * d:128 * (d + 1), csl], sem="wus",
                      w=[stg_u])
                p.op("gpsimd", ts(WU[d][:], stg_u[:], 1.0 / 127.0, ALU.mult),
                     r=[stg_u], w=[WU[d]])
            for hp in range(12):
                hpg = 12 * ih + hp
                col = slice(128 * hp, 128 * (hp + 1))
                for tc in range(NCH):
                    seq = hp * NCH + tc
                    tsl = slice(CH * tc, CH * (tc + 1))
                    k = seq % 2
                    for d in range(DP):
                        p.op("tensor", mm(GPS[k][:], WG[d][:, col], Y[d][:, tsl],
                                          d == 0, d == DP - 1),
                             r=[WG[d], Y[d]], w=[GPS[k]] if d in (0, DP - 1) else ())
                    for d in range(DP):
                        p.op("tensor", mm(UPS[k][:], WU[d][:, col], Y[d][:, tsl],
                                          d == 0, d == DP - 1),
                             r=[WU[d], Y[d]], w=[UPS[k]] if d in (0, DP - 1) else ())
                    p.op("scalar", act(c.SQS[k][:], GPS[k][:], AF.Silu),
                         r=[GPS[k]], w=[c.SQS[k]])
                    p.op("vector", tt(gst[:, tsl], c.SQS[k][:], UPS[k][:], ALU.mult),
                         r=[c.SQS[k], UPS[k]], w=[f"gst@{tc}"])
                p.dma("gpsimd", g_scr.ap()[128 * hpg:128 * (hpg + 1), :], gst[:], sem="gsp",
                      r=[f"gst@{t2}" for t2 in range(NCH)], w=["g_scr"])
            p.barrier()

        s1.close()

        s2 = ExitStack()
        sb2 = lambda n, s, d=F32: s2.enter_context(nc.sbuf_tensor(uid + n, s, d))
        ps2 = lambda n, s, d=F32: s2.enter_context(nc.psum_tensor(uid + n, s, d))
        WDSTG = [sb2("wdstg0", [128, D], I32), sb2("wdstg1", [128, D], I32)]
        GIN = [sb2("gin0", [128, CH], BF16), sb2("gin1", [128, CH], BF16)]
        WD = [sb2(f"wdb{i}", [128, D], BF16) for i in range(24)]
        dps = [ps2(f"dps{j}", [128, CH]) for j in range(DP)]
        for it in range(24):
            k = it % 2
            p.dma("sync", WDSTG[k][:], wd.ap()[128 * it:128 * (it + 1), :],
                  sem=f"wds{k}", w=[WDSTG[k]])
            p.op("gpsimd", ts(WD[it][:], WDSTG[k][:], 1.0 / 127.0, ALU.mult),
                 r=[WDSTG[k]], w=[WD[it]])
        for tc in range(NCH):
            tsl = slice(CH * tc, CH * (tc + 1))
            for it in range(24):
                k = it % 2
                p.dma("scalar", GIN[k][:], g_scr.ap()[128 * it:128 * (it + 1), tsl],
                      sem=f"gin{k}", r=["g_scr"], w=[GIN[k]])
                for j in range(DP):
                    p.op("tensor", mm(dps[j][:], WD[it][:, 128 * j:128 * (j + 1)], GIN[k][:],
                                      it == 0, it == 23),
                         r=[WD[it], GIN[k]], w=[dps[j]] if it in (0, 23) else ())
            for j in range(DP):
                p.op("vector", stt(X[j][:, tsl], dps[j][:], 0.5, X[j][:, tsl],
                                   ALU.mult, ALU.add),
                     r=[dps[j]], w=[X[j]])
        s2.close()
    p.barrier()


def build_prog_d():
    nc = bass.Bass("TRN2", target_bir_lowering=False)
    x_in = nc.dram_tensor("x_in", [D, T], F32, kind="ExternalInput")
    dr = {
        "d_wg": nc.dram_tensor("wg", [D, 3072], I32, kind="ExternalInput"),
        "d_wu": nc.dram_tensor("wu", [D, 3072], I32, kind="ExternalInput"),
        "d_wd": nc.dram_tensor("wd", [3072, D], I32, kind="ExternalInput"),
        "g_scr": nc.dram_tensor("g_scr", [3072, T], BF16, kind="Internal"),
    }
    x_out = nc.dram_tensor("x_out", [D, T], F32, kind="ExternalOutput")
    stack = ExitStack()
    with stack:
        sb = lambda n, s, d=F32: stack.enter_context(nc.sbuf_tensor(n, s, d))
        p = Prog(nc, stack)
        c = Com(nc, stack)
        X = [sb(f"x{i}", [128, T]) for i in range(DP)]
        c.init(p)
        for i in range(DP):
            p.dma("sync", X[i][:], x_in.ap()[128 * i:128 * (i + 1), :], sem=f"xin{i}", w=[X[i]])
        emit_d(p, c, nc, X, dr)
        for j in range(DP):
            p.dma("sync", x_out.ap()[128 * j:128 * (j + 1), :], X[j][:], sem="xout", r=[X[j]])
        p.finish()
        p.emit()
    return nc



def _emit_rot_weights(p, dst, src, nh, dh, rot_half):
    rh = rot_half
    sv = src.ap().rearrange("p (h c) -> p h c", h=nh, c=dh)
    dv = dst.ap().rearrange("p (h c) -> p h c", h=nh, c=dh)
    p.op("vector", ts(dv[:, :, 0:rh], sv[:, :, rh:2 * rh], -1.0, ALU.mult),
         r=[src], w=[dst])
    p.op("vector", cp(dv[:, :, rh:2 * rh], sv[:, :, 0:rh]), r=[src], w=[dst])
    if 2 * rh < dh:
        p.op("vector", cp(dv[:, :, 2 * rh:dh], sv[:, :, 2 * rh:dh]), r=[src], w=[dst])


def emit_b(p, c, nc, X, dr, uid=""):
    wq_d = dr["b_wq"]; wk_d = dr["b_wk"]; wv_d = dr["b_wv"]; wo_d = dr["b_wo"]
    cb_d = dr["b_ropeC"]; sb_d = dr["b_ropeS"]; mask_d = dr["b_dmask"]
    o_scr = dr["o_scr"]
    NHB = 8
    GH = 2
    with ExitStack() as ss:
        sbS = lambda n, s, d=F32: ss.enter_context(nc.sbuf_tensor(uid + n, s, d))
        Y = [sbS(f"y{i}", [128, T], BF16) for i in range(DP)]

        s0 = ExitStack()
        sb0 = lambda n, s, d=F32: s0.enter_context(nc.sbuf_tensor(uid + n, s, d))
        ps0 = lambda n, s, d=F32: s0.enter_context(nc.psum_tensor(uid + n, s, d))
        psA = ps0("psA", [1, CH])
        psB = ps0("psB", [1, CH])
        psR = [ps0("psR0", [128, CH]), ps0("psR1", [128, CH])]
        emit_ln(p, c, X, Y, psA, psB, psR, fold=None)
        p.barrier()
        s0.close()

        s1 = ExitStack()
        sb1 = lambda n, s, d=F32: s1.enter_context(nc.sbuf_tensor(uid + n, s, d))
        ps1 = lambda n, s, d=F32: s1.enter_context(nc.psum_tensor(uid + n, s, d))
        CB = sb1("cb", [128, T], BF16)
        SBC = sb1("sbc", [128, T], BF16)
        MSK = sb1("msk", [128, 4 * CH], BF16)
        OSTG = [sb1("ostg0", [128, T], BF16), sb1("ostg1", [128, T], BF16)]
        p.dma("sync", CB[:], cb_d.ap(), sem="cbin", w=[CB])
        p.dma("sync", SBC[:], sb_d.ap(), sem="sbin", w=[SBC])
        p.dma("sync", MSK[:], mask_d.ap(), sem="mkin", w=[MSK])
        WSTG = [sb1("wstg0", [128, GH * 128]), sb1("wstg1", [128, GH * 128])]
        WQ = [sb1(f"wqb{d}", [128, GH * 128], BF16) for d in range(DP)]
        WQR = [sb1(f"wqrb{d}", [128, GH * 128], BF16) for d in range(DP)]
        WK = [sb1(f"wkb{d}", [128, GH * 128], BF16) for d in range(DP)]
        WKR = [sb1(f"wkrb{d}", [128, GH * 128], BF16) for d in range(DP)]
        WV = [sb1(f"wvb{d}", [128, GH * 128], BF16) for d in range(DP)]
        QH = [sb1(f"qh{h}", [128, T], BF16) for h in range(GH)]
        KH = [sb1(f"kh{h}", [128, T], BF16) for h in range(GH)]
        VT = [sb1(f"vt{t2}", [128, GH * 128], BF16) for t2 in range(NTT)]
        PB = [sb1("pb0", [128, CH], BF16), sb1("pb1", [128, CH], BF16)]
        REPS = sb1("reps", [128, CH])
        psS = [ps1("psS0", [128, CH]), ps1("psS1", [128, CH])]
        psO = ps1("psO", [128, CH])
        psDen = ps1("psDen", [1, CH])
        psRep = ps1("psRep", [128, CH])

        for g in range(NHB // GH):
            hbase = GH * g
            gsl = slice(128 * GH * g, 128 * GH * (g + 1))
            for d in range(DP):
                rs = slice(128 * d, 128 * (d + 1))
                p.dma("sync", WSTG[0][:], wq_d.ap()[rs, gsl], sem="wqs", w=[WSTG[0]])
                p.op("gpsimd", cp(WQ[d][:], WSTG[0][:]), r=[WSTG[0]], w=[WQ[d]])
                _emit_rot_weights(p, WQR[d], WQ[d], GH, 128, 32)
                p.dma("scalar", WSTG[1][:], wk_d.ap()[rs, gsl], sem="wks", w=[WSTG[1]])
                p.op("gpsimd", cp(WK[d][:], WSTG[1][:]), r=[WSTG[1]], w=[WK[d]])
                _emit_rot_weights(p, WKR[d], WK[d], GH, 128, 32)
                p.dma("sync", WSTG[0][:], wv_d.ap()[rs, gsl], sem="wqs", w=[WSTG[0]])
                p.op("gpsimd", cp(WV[d][:], WSTG[0][:]), r=[WSTG[0]], w=[WV[d]])
            for h in range(GH):
                hsl = slice(128 * h, 128 * (h + 1))
                for qc in range(NCH):
                    tsl = slice(CH * qc, CH * (qc + 1))
                    for (psq, Wm, Wr, dstH) in ((psS[0], WQ, WQR, QH), (psS[1], WK, WKR, KH)):
                        for d in range(DP):
                            p.op("tensor", mm(psq[:], Wm[d][:, hsl], Y[d][:, tsl],
                                              d == 0, d == DP - 1),
                                 r=[Wm[d], Y[d]], w=[psq] if d in (0, DP - 1) else ())
                        for d in range(DP):
                            p.op("tensor", mm(psO[:], Wr[d][:, hsl], Y[d][:, tsl],
                                              d == 0, d == DP - 1),
                                 r=[Wr[d], Y[d]], w=[psO] if d in (0, DP - 1) else ())
                        p.op("vector", tt(c.SQS[0][:], psq[:], CB[:, tsl], ALU.mult),
                             r=[psq, CB], w=[c.SQS[0]])
                        p.op("vector", tt(c.SQS[1][:], psO[:], SBC[:, tsl], ALU.mult),
                             r=[psO, SBC], w=[c.SQS[1]])
                        p.op("vector", tt(dstH[h][:, tsl], c.SQS[0][:], c.SQS[1][:], ALU.add),
                             r=[c.SQS[0], c.SQS[1]], w=[f"{_key(dstH[h])}@{qc}"])
            for t2 in range(NTT):
                ssl = slice(128 * t2, 128 * (t2 + 1))
                for d in range(DP):
                    p.op("tensor", mm(psS[0][:, 0:128 * GH], Y[d][:, ssl], WV[d][:],
                                      d == 0, d == DP - 1),
                         r=[Y[d], WV[d]], w=[psS[0]] if d in (0, DP - 1) else ())
                p.op("scalar", scp(VT[t2][:], psS[0][:, 0:128 * GH]), r=[psS[0]], w=[VT[t2]])
            for h in range(GH):
                hg = hbase + h
                for qc in range(NCH):
                    tsl = slice(CH * qc, CH * (qc + 1))
                    qkeys = [f"{_key(QH[h])}@{qc}"]
                    nkt = 4 * qc + 4
                    for kt in range(nkt):
                        k2 = kt % 2
                        ksl = slice(128 * kt, 128 * (kt + 1))
                        p.op("tensor", mm(psS[k2][:], KH[h][:, ksl], QH[h][:, tsl],
                                          True, True),
                             r=[f"{_key(KH[h])}@{j}" for j in range(NCH)] + qkeys,
                             w=[psS[k2]])
                        r_rel = kt - 4 * qc
                        if r_rel >= 0:
                            p.op("vector", tt(c.SQS[k2][:], psS[k2][:],
                                              MSK[:, CH * r_rel:CH * (r_rel + 1)], ALU.add),
                                 r=[psS[k2], MSK], w=[c.SQS[k2]])
                            src = c.SQS[k2][:]
                            srck = [c.SQS[k2]]
                        else:
                            src = psS[k2][:]
                            srck = [psS[k2]]
                        p.op("scalar", act(PB[k2][:], src, AF.Exp, scale=1.0 / 128.0),
                             r=srck, w=[PB[k2]])
                        p.op("tensor", mm(psO[:], VT[kt][:, 128 * h:128 * (h + 1)], PB[k2][:],
                                          kt == 0, kt == nkt - 1),
                             r=[VT[kt], PB[k2]], w=[psO] if kt in (0, nkt - 1) else ())
                        p.op("tensor", mm(psDen[0:1, :], c.ones_col_bf[:], PB[k2][:],
                                          kt == 0, kt == nkt - 1),
                             r=[c.ones_col_bf, PB[k2]], w=[psDen] if kt in (0, nkt - 1) else ())
                    p.op("vector", recip(c.R0[:, 0:CH], psDen[0:1, :]), r=[psDen], w=[c.R0])
                    p.op("tensor", mm(psRep[:], c.ones_row, c.R0[:, 0:CH], True, True),
                         r=[c._ones_rows, c.R0], w=[psRep])
                    p.op("scalar", scp(REPS[:], psRep[:]), r=[psRep], w=[REPS])
                    p.op("vector", tt(OSTG[hg % 2][:, tsl], psO[:], REPS[:], ALU.mult),
                         r=[psO, REPS], w=[f"ostg{hg % 2}@{qc}"])
                p.dma("gpsimd", o_scr.ap()[128 * hg:128 * (hg + 1), :], OSTG[hg % 2][:],
                      sem=f"osp{hg % 2}",
                      r=[f"ostg{hg % 2}@{j}" for j in range(NCH)], w=["o_scr"])
        p.barrier()
        s1.close()

        s2 = ExitStack()
        sb2 = lambda n, s, d=F32: s2.enter_context(nc.sbuf_tensor(uid + n, s, d))
        ps2 = lambda n, s, d=F32: s2.enter_context(nc.psum_tensor(uid + n, s, d))
        OIN = [sb2(f"oin{ot}", [128, T], BF16) for ot in range(DP)]
        WOSTG = [sb2("wostg0", [128, D]), sb2("wostg1", [128, D])]
        WO = [sb2(f"wob{ot}", [128, D], BF16) for ot in range(DP)]
        psD = [ps2("psD0", [128, CH]), ps2("psD1", [128, CH])]
        for i in range(DP):
            p.dma("scalar", OIN[i][:], o_scr.ap()[128 * i:128 * (i + 1), :], sem=f"oin{i}",
                  r=["o_scr"], w=[OIN[i]])
        for ot in range(DP):
            k = ot % 2
            p.dma("sync", WOSTG[k][:], wo_d.ap()[128 * ot:128 * (ot + 1), :], sem=f"wos{k}",
                  w=[WOSTG[k]])
            p.op("gpsimd", cp(WO[ot][:], WOSTG[k][:]), r=[WOSTG[k]], w=[WO[ot]])
        for j in range(DP):
            jsl = slice(128 * j, 128 * (j + 1))
            for tc in range(NCH):
                tsl = slice(CH * tc, CH * (tc + 1))
                k = (j * NCH + tc) % 2
                for ot in range(DP):
                    p.op("tensor", mm(psD[k][:], WO[ot][:, jsl], OIN[ot][:, tsl],
                                      ot == 0, ot == DP - 1),
                         r=[WO[ot], OIN[ot]], w=[psD[k]] if ot in (0, DP - 1) else ())
                p.op("vector", tt(X[j][:, tsl], X[j][:, tsl], psD[k][:], ALU.add),
                     r=[psD[k]], w=[X[j]])
        s2.close()
    p.barrier()


def build_prog_b():
    nc = bass.Bass("TRN2", target_bir_lowering=False)
    x_in = nc.dram_tensor("x_in", [D, T], F32, kind="ExternalInput")
    dr = {
        "b_wq": nc.dram_tensor("wq", [D, D], F32, kind="ExternalInput"),
        "b_wk": nc.dram_tensor("wk", [D, D], F32, kind="ExternalInput"),
        "b_wv": nc.dram_tensor("wv", [D, D], F32, kind="ExternalInput"),
        "b_wo": nc.dram_tensor("wo", [D, D], F32, kind="ExternalInput"),
        "b_ropeC": nc.dram_tensor("ropeC", [128, T], BF16, kind="ExternalInput"),
        "b_ropeS": nc.dram_tensor("ropeS", [128, T], BF16, kind="ExternalInput"),
        "b_dmask": nc.dram_tensor("dmask", [128, 4 * CH], BF16, kind="ExternalInput"),
        "o_scr": nc.dram_tensor("o_scr", [D, T], BF16, kind="Internal"),
    }
    x_out = nc.dram_tensor("x_out", [D, T], F32, kind="ExternalOutput")
    stack = ExitStack()
    with stack:
        sb = lambda n, s, d=F32: stack.enter_context(nc.sbuf_tensor(n, s, d))
        p = Prog(nc, stack)
        c = Com(nc, stack)
        X = [sb(f"x{i}", [128, T]) for i in range(DP)]
        c.init(p)
        for i in range(DP):
            p.dma("sync", X[i][:], x_in.ap()[128 * i:128 * (i + 1), :], sem=f"xin{i}", w=[X[i]])
        emit_b(p, c, nc, X, dr)
        for j in range(DP):
            p.dma("sync", x_out.ap()[128 * j:128 * (j + 1), :], X[j][:], sem="xout", r=[X[j]])
        p.finish()
        p.emit()
    return nc


def get_prog(name):
    if name not in _PROG_CACHE:
        _PROG_CACHE[name] = _BUILDERS[name]()
    return _PROG_CACHE[name]



def _prep_l0_attn(inputs):
    wq = np.ascontiguousarray(np.transpose(inputs["L0_attn_wq"], (1, 0, 2)).reshape(D, D))
    wk = np.ascontiguousarray(np.transpose(inputs["L0_attn_wk"], (1, 0, 2)).reshape(D, D))
    wv = np.ascontiguousarray(np.transpose(inputs["L0_attn_wv"], (1, 0, 2)).reshape(D, D))
    wo = np.ascontiguousarray(inputs["L0_attn_wo"])
    t = np.arange(T, dtype=np.float64)[None, :]
    i = np.arange(32, dtype=np.float64)[:, None]
    rad = t * np.power(10000.0, -i / 32.0)
    cosr = np.cos(rad).astype(np.float32)
    sinr = np.sin(rad).astype(np.float32)
    ropeC = np.concatenate([cosr, cosr, np.ones((64, T), np.float32)], 0).astype(BF)
    ropeS = np.concatenate([sinr, sinr, np.zeros((64, T), np.float32)], 0).astype(BF)
    rk = np.arange(128)[:, None]
    cq = np.arange(CH)[None, :]
    dmask = np.concatenate(
        [np.where(128 * rr + rk <= cq, 0.0, -1e30).astype(np.float32) for rr in range(4)],
        axis=1).astype(BF)
    return {"wq": wq, "wk": wk, "wv": wv, "wo": wo,
            "ropeC": np.ascontiguousarray(ropeC), "ropeS": np.ascontiguousarray(ropeS),
            "dmask": np.ascontiguousarray(dmask)}


def _prep_l0_ffn_attn(inputs):
    F = np.asarray(inputs["L0_ffn_freqs_cis"])
    idx = np.repeat(np.arange(0, 64, 2), 2)
    crow = F[:, idx].T
    srow = F[:, idx + 1].T
    ropeC = np.tile(crow, (2, 1)).astype(BF)
    ropeS = np.tile(srow, (2, 1)).astype(BF)
    rk = np.arange(128)[:, None]
    cq = np.arange(CH)[None, :]
    dmask = np.concatenate(
        [np.where(128 * rr + rk <= cq, 0.0, -1e30).astype(np.float32) for rr in range(4)],
        axis=1).astype(BF)
    return {
        "wq": np.ascontiguousarray(inputs["L0_ffn_wq"]),
        "wk": np.ascontiguousarray(inputs["L0_ffn_wk"]),
        "wv": np.ascontiguousarray(inputs["L0_ffn_wv"]),
        "wo": np.ascontiguousarray(inputs["L0_ffn_wo"]),
        "anw": np.ascontiguousarray(np.asarray(inputs["L0_ffn_attn_norm_w"]).reshape(DP, 128).T),
        "ropeC": np.ascontiguousarray(ropeC),
        "ropeS": np.ascontiguousarray(ropeS),
        "dmask": np.ascontiguousarray(dmask),
    }


def run_a1(xs, W):
    nc = get_prog("a1")
    in_maps = [dict(x_in=xs[b], **W) for b in range(8)]
    res = run_bass_kernel_spmd(nc, in_maps, core_ids=list(range(8)))
    return ([np.asarray(r["z_out"]) for r in res.results],
            [np.asarray(r["h_out"]) for r in res.results])


def run_b(xs, W):
    nc = get_prog("b")
    in_maps = [dict(x_in=xs[b], **W) for b in range(8)]
    res = run_bass_kernel_spmd(nc, in_maps, core_ids=list(range(8)))
    return [np.asarray(r["x_out"]) for r in res.results]


def _prep_l1_ffn(inputs):
    return {
        "wg": np.ascontiguousarray(inputs["L1_ffn_w_gate"].T),
        "wu": np.ascontiguousarray(inputs["L1_ffn_w_up"].T),
        "wd": np.ascontiguousarray(inputs["L1_ffn_w_down"].T),
    }


def run_d(xs, W):
    nc = get_prog("d")
    in_maps = [dict(x_in=xs[b], **W) for b in range(8)]
    res = run_bass_kernel_spmd(nc, in_maps, core_ids=list(range(8)))
    return [np.asarray(r["x_out"]) for r in res.results]




def build_mega():
    nc = bass.Bass("TRN2", target_bir_lowering=False)
    x_in = nc.dram_tensor("x_in", [D, T], F32, kind="ExternalInput")
    x_out = nc.dram_tensor("x_out", [D, T], F32, kind="ExternalOutput")
    dr = {}
    for nm, shape, dt in [
        ("a1_wq", [D, D], F32), ("a1_wk", [D, D], F32), ("a1_wv", [D, D], F32),
        ("a1_wo", [D, D], F32), ("a1_anw", [128, DP], F32),
        ("a1_ropeC", [128, T], BF16), ("a1_ropeS", [128, T], BF16),
        ("a1_dmask", [128, 4 * CH], BF16),
        ("a2_fnw", [128, DP], F32), ("a2_rw", [D, 8], F32),
        ("a2_w1", [8, D, 2048], F32), ("a2_w3", [8, D, 2048], F32),
        ("a2_w2", [8, 2048, D], F32), ("a2_ident", [128, 128], F32),
        ("a2_sel8", [8, D], F32),
        ("b_wq", [D, D], F32), ("b_wk", [D, D], F32), ("b_wv", [D, D], F32),
        ("b_wo", [D, D], F32), ("b_ropeC", [128, T], BF16),
        ("b_ropeS", [128, T], BF16), ("b_dmask", [128, 4 * CH], BF16),
        ("d_wg", [D, 3072], I32), ("d_wu", [D, 3072], I32), ("d_wd", [3072, D], I32),
        ("e_wq", [D, D], F32), ("e_wk", [D, D], F32), ("e_wv", [D, D], F32),
        ("e_wqf", [64, 16], F32), ("e_wkf", [64, 16], F32),
        ("e_wout", [2 * D, D], F32), ("e_nw", [128, DP], F32),
        ("e_gha", [33, 153], BF16), ("e_ghb", [33, 153], BF16),
        ("e_triu", [128, 128], F32), ("e_wmask", [128, 256], BF16),
    ]:
        dr[nm] = nc.dram_tensor(nm, shape, dt, kind="ExternalInput")
    for nm, shape, dt in [
        ("o_scr", [D, T], BF16), ("y_scr", [D, T], BF16), ("h_scr", [D, T], BF16),
        ("g_scr", [3072, T], BF16), ("lin_scr", [D, T], BF16), ("win_scr", [D, T], BF16),
    ]:
        dr[nm] = nc.dram_tensor(nm, shape, dt, kind="Internal")

    stack = ExitStack()
    with stack:
        sb = lambda n, s, d=F32: stack.enter_context(nc.sbuf_tensor(n, s, d))
        p = Prog(nc, stack)
        c = Com(nc, stack)
        X = [sb(f"x{i}", [128, T]) for i in range(DP)]
        c.init(p)
        for i in range(DP):
            p.dma("sync", X[i][:], x_in.ap()[128 * i:128 * (i + 1), :], sem=f"xin{i}", w=[X[i]])
        emit_a1(p, c, nc, X, dr, uid="s1_")
        emit_a2(p, c, nc, X, dr, uid="s2_")
        emit_b(p, c, nc, X, dr, uid="s3_")
        emit_a1(p, c, nc, X, dr, uid="s4_")
        emit_a2(p, c, nc, X, dr, uid="s5_")
        emit_d(p, c, nc, X, dr, uid="s6_")
        emit_e(p, c, nc, X, dr, uid="s7_")
        emit_d(p, c, nc, X, dr, uid="s8_")
        for j in range(DP):
            p.dma("sync", x_out.ap()[128 * j:128 * (j + 1), :], X[j][:], sem="xout", r=[X[j]])
        p.finish()
        p.emit()
    return nc


def _mega_inmap(inputs):
    W_a1 = _prep_l0_ffn_attn(inputs)
    W_a2 = _prep_l0_ffn_moe(inputs)
    W_b = _prep_l0_attn(inputs)
    W_d = _prep_l1_ffn(inputs)
    W_e = _prep_l1_attn(inputs)
    m = {}
    for pre, Wd in (("a1_", W_a1), ("a2_", W_a2), ("b_", W_b), ("d_", W_d), ("e_", W_e)):
        for k, v in Wd.items():
            m[pre + k] = v
    return m


FUSED = True


def kernel(**inputs):
    inputs = {k: np.asarray(v) for k, v in inputs.items()}
    xs = [np.ascontiguousarray(inputs["L0_attn_x"][b].T) for b in range(8)]
    if FUSED:
        base = _mega_inmap(inputs)
        nc = get_prog("mega")
        in_maps = [dict(x_in=xs[b], **base) for b in range(8)]
        res = run_bass_kernel_spmd(nc, in_maps, core_ids=list(range(8)))
        return np.stack([np.asarray(r["x_out"]).T for r in res.results]).astype(np.float32)
    W_a1 = _prep_l0_ffn_attn(inputs)
    W_a2 = _prep_l0_ffn_moe(inputs)
    W_b = _prep_l0_attn(inputs)
    W_d = _prep_l1_ffn(inputs)
    W_e = _prep_l1_attn(inputs)
    zs, hs = run_a1(xs, W_a1)
    xs = run_a2(hs, zs, W_a2)
    xs = run_b(xs, W_b)
    zs, hs = run_a1(xs, W_a1)
    xs = run_a2(hs, zs, W_a2)
    xs = run_d(xs, W_d)
    xs = run_e(xs, W_e)
    xs = run_d(xs, W_d)
    return np.stack([x.T for x in xs]).astype(np.float32)


_BUILDERS = {"d": build_prog_d, "b": build_prog_b, "a1": build_prog_a1, "a2": build_prog_a2, "e": build_prog_e, "mega": build_mega}


# version 03
# speedup vs baseline: 1.1878x; 1.1878x over previous
import numpy as np
import ml_dtypes
from contextlib import ExitStack

import concourse.bass as bass
import concourse.mybir as mybir
from concourse.bass_utils import run_bass_kernel_spmd

F32 = mybir.dt.float32
BF16 = mybir.dt.bfloat16
I32 = mybir.dt.int32
AX = mybir.AxisListType.X
AF = mybir.ActivationFunctionType
ALU = mybir.AluOpType
BF = ml_dtypes.bfloat16

D = 1024
DP = 8
T = 2048
NCH = 4
CH = 512
NTT = 16


def _key(x):
    if isinstance(x, str):
        return x
    if hasattr(x, "name"):
        return x.name
    if hasattr(x, "tensor"):
        return x.tensor.name
    raise TypeError(f"bad key {x!r}")


class Prog:
    ENGINES = ("tensor", "vector", "scalar", "gpsimd", "sync")

    def __init__(self, nc, stack):
        self.nc = nc
        self.stack = stack
        self.ops = {e: [] for e in self.ENGINES}
        self.sems = {}
        self.cnt = {}
        self.waited = {e: {} for e in self.ENGINES}
        for e in self.ENGINES:
            self._mksem("p_" + e)
        self.dma_sems = set()
        self.last_w = {}
        self.readers = {}

    def _mksem(self, name):
        self.sems[name] = self.stack.enter_context(self.nc.semaphore(name))
        self.cnt[name] = 0

    def _track_pre(self, rk, wk):
        auto = []
        for k in rk:
            t = self.last_w.get(k)
            if t:
                auto.append(t)
        for k in wk:
            t = self.last_w.get(k)
            if t:
                auto.append(t)
            auto.extend(self.readers.get(k, {}).values())
        return auto

    def _track_post(self, rk, wk, slot, tok):
        for k in wk:
            self.last_w[k] = tok
            self.readers[k] = {}
        for k in rk:
            if k not in wk:
                self.readers.setdefault(k, {})[slot] = tok

    def op(self, eng, fn, waits=(), r=(), w=(), sig=False):
        rk = [_key(x) for x in r]
        wk = [_key(x) for x in w]
        auto = self._track_pre(rk, wk)
        for t in waits:
            self._wait(eng, t)
        for t in auto:
            self._wait(eng, t)
        need_sig = sig or bool(rk) or bool(wk)
        if need_sig:
            sname = "p_" + eng
            self.cnt[sname] += 1
            v = self.cnt[sname]
            sh = self.sems[sname]
            self.ops[eng].append(lambda E, fn=fn, sh=sh: fn(E).then_inc(sh, 1))
            tok = (sname, v)
        else:
            self.ops[eng].append(fn)
            tok = None
        if tok:
            self._track_post(rk, wk, eng, tok)
        return tok

    def dma(self, q, out, in_, sem, waits=(), r=(), w=()):
        rk = [_key(x) for x in r]
        wk = [_key(x) for x in w]
        auto = self._track_pre(rk, wk)
        for t in waits:
            self._wait(q, t)
        for t in auto:
            self._wait(q, t)
        sname = "dma_" + sem
        if sname not in self.sems:
            self._mksem(sname)
            self.dma_sems.add(sname)
        self.cnt[sname] += 16
        v = self.cnt[sname]
        sh = self.sems[sname]
        self.ops[q].append(
            lambda E, out=out, in_=in_, sh=sh: E.dma_start(out=out, in_=in_).then_inc(sh, 16)
        )
        tok = (sname, v)
        self._track_post(rk, wk, sname, tok)
        return tok

    def retag_writers(self, keys, tok):
        for k in keys:
            self.last_w[_key(k)] = tok

    def _wait(self, eng, tok):
        if tok is None:
            return
        sname, v = tok
        if self.waited[eng].get(sname, 0) >= v:
            return
        self.waited[eng][sname] = v
        sh = self.sems[sname]
        self.ops[eng].append(lambda E, sh=sh, v=v: E.wait_ge(sh, v))

    def barrier(self):
        toks = [("p_" + e, self.cnt["p_" + e]) for e in self.ENGINES if self.cnt["p_" + e]]
        for s in sorted(self.dma_sems):
            if self.cnt[s]:
                toks.append((s, self.cnt[s]))
        for e in self.ENGINES:
            for t in toks:
                self._wait(e, t)

    def finish(self):
        for s in sorted(self.dma_sems):
            if self.cnt[s]:
                sh = self.sems[s]
                v = self.cnt[s]
                if self.waited["sync"].get(s, 0) < v:
                    self.waited["sync"][s] = v
                    self.ops["sync"].append(lambda E, sh=sh, v=v: E.wait_ge(sh, v))

    def emit(self):
        block = self.stack.enter_context(self.nc.Block())
        for e in self.ENGINES:
            ops = list(self.ops[e])

            def runner(E, ops=ops):
                for f in ops:
                    f(E)

            getattr(block, e)(runner)



def mm(out, lhsT, rhs, start, stop):
    return lambda E: E.matmul(out, lhsT, rhs, start=start, stop=stop)

def act(out, in_, func, bias=0.0, scale=1.0):
    return lambda E: E.activation(out, in_, func, bias=bias, scale=scale)

def tt(out, a, b, op):
    return lambda E: E.tensor_tensor(out, a, b, op)

def ts(out, a, s, op):
    return lambda E: E.tensor_single_scalar(out, a, s, op)

def stt(out, in0, scalar, in1, op0, op1):
    return lambda E: E.scalar_tensor_tensor(out, in0, scalar, in1, op0, op1)

def cp(out, in_):
    return lambda E: E.tensor_copy(out, in_)

def scp(out, in_):
    return lambda E: E.copy(out, in_)

def smul(out, in_, m):
    return lambda E: E.mul(out, in_, m)

def memset(ap, v):
    return lambda E: E.memset(ap, v)

def recip(out, in_):
    return lambda E: E.reciprocal(out, in_)

def rmax(out, in_, negate=False):
    return lambda E: E.reduce_max(out, in_, axis=AX, negate=negate)

def rsum(out, in_):
    return lambda E: E.reduce_sum(out, in_, axis=AX)

def max8(out, in_):
    return lambda E: E.max(out, in_)

def transp(out, in_, ident):
    return lambda E: E.transpose(out, in_, ident)



class Com:
    def __init__(self, nc, stack):
        sb = lambda n, s, d=F32: stack.enter_context(nc.sbuf_tensor(n, s, d))
        self.SCR_A = sb("com_scr_a", [128, T])
        self.SCR_B = sb("com_scr_b", [128, T])
        self.TMP = sb("com_tmp0", [128, T])
        self.R0 = self.TMP[0:1, :]
        self.R1 = self.SCR_B[0:1, :]
        self.SQS = [sb("com_sqs0", [128, CH]), sb("com_sqs1", [128, CH])]
        self.ones_col = sb("com_ones_col", [128, 1])
        self.ones_col_bf = sb("com_ones_col_bf", [128, 1], BF16)
        self._ones_rows = sb("com_ones_row", [65, 128])
        self.ones_row = self._ones_rows[0:1, :]
        self._cvals = sb("com_cvals", [128, 4])

    def cval(self, idx, base=0, n=1):
        return self._cvals[base:base + n, idx:idx + 1]

    def init(self, p):
        p.op("vector", memset(self.ones_col[:], 1.0), w=[self.ones_col])
        p.op("vector", memset(self.ones_col_bf[:], 1.0), w=[self.ones_col_bf])
        p.op("vector", memset(self._ones_rows[:], 1.0), w=[self._ones_rows])
        p.op("vector", memset(self._cvals[:, 0:1], 1e-5), w=["cval0"])
        p.op("vector", memset(self._cvals[:, 1:2], 1e-6), w=["cval1"])



def emit_ln(p, c, X, Y, psA, psB, psR, fold=None, eps=1e-5):
    for ci in range(NCH):
        sl = slice(ci * CH, (ci + 1) * CH)
        for i in range(DP):
            p.op("tensor", mm(psA[0:1, :], c.ones_col[:], X[i][:, sl], i == 0, i == DP - 1),
                 r=[c.ones_col, X[i]], w=[psA] if i in (0, DP - 1) else ())
        for i in range(DP):
            k = i % 2
            p.op("scalar", act(c.SQS[k][:], X[i][:, sl], AF.Square), r=[X[i]], w=[c.SQS[k]])
            p.op("tensor", mm(psB[0:1, :], c.ones_col[:], c.SQS[k][:], i == 0, i == DP - 1),
                 r=[c.ones_col, c.SQS[k]], w=[psB] if i in (0, DP - 1) else ())
        p.op("scalar", smul(c.R0[:, sl], psA[0:1, :], 1.0 / D), r=[psA], w=[c.R0])
        p.op("scalar", smul(c.R1[:, sl], psB[0:1, :], 1.0 / D), r=[psB], w=[c.R1])

    r_mean = c.R0[:]
    r_msq = c.R1[:]
    for ci in range(NCH):
        sl = slice(ci * CH, (ci + 1) * CH)
        p.op("scalar", act(psA[0:1, :], r_mean[:, sl], AF.Square), r=[c.R0], w=[psA])
        p.op("vector", tt(r_msq[:, sl], r_msq[:, sl], psA[0:1, :], ALU.subtract),
             r=[psA, c.R1], w=[c.R1])
    eps_idx = 0 if eps == 1e-5 else 1
    p.op("scalar", act(r_msq, r_msq, AF.Sqrt, bias=c.cval(eps_idx)),
         r=["cval0" if eps_idx == 0 else "cval1", c.R1], w=[c.R1])
    p.op("vector", recip(r_msq, r_msq), r=[c.R1], w=[c.R1])
    p.op("vector", tt(r_mean, r_mean, r_msq, ALU.mult), r=[c.R0, c.R1], w=[c.R0])

    for ci in range(NCH):
        sl = slice(ci * CH, (ci + 1) * CH)
        k = ci % 2
        p.op("tensor", mm(psR[k][:], c.ones_row, r_msq[:, sl], True, True),
             r=[c._ones_rows, c.R1], w=[psR[k]])
        p.op("scalar", scp(c.SCR_A[:, sl], psR[k][:]), r=[psR[k]], w=[c.SCR_A])
    for ci in range(NCH):
        sl = slice(ci * CH, (ci + 1) * CH)
        k = ci % 2
        p.op("tensor", mm(psR[k][:], c.ones_row, r_mean[:, sl], True, True),
             r=[c._ones_rows, c.R0], w=[psR[k]])
        p.op("scalar", scp(c.SCR_B[:, sl], psR[k][:]), r=[psR[k]], w=[c.SCR_B])

    y_toks = [None] * DP
    z_toks = [None] * DP
    for i in range(DP):
        p.op("gpsimd", tt(c.TMP[:], X[i][:], c.SCR_A[:], ALU.mult),
             r=[X[i], c.SCR_A], w=[c.TMP])
        y_toks[i] = p.op("vector", tt(Y[i][:], c.TMP[:], c.SCR_B[:], ALU.subtract),
                         r=[c.TMP, c.SCR_B], w=[Y[i]])
        if fold is not None:
            z_toks[i] = p.op("vector",
                             stt(X[i][:], Y[i][:], float(fold), X[i][:], ALU.mult, ALU.add),
                             r=[Y[i]], w=[X[i]])
    return y_toks, (z_toks if fold is not None else None)



def emit_rms(p, c, IN, psB, psR):
    for ci in range(NCH):
        sl = slice(ci * CH, (ci + 1) * CH)
        for i in range(DP):
            k = i % 2
            p.op("scalar", act(c.SQS[k][:], IN[i][:, sl], AF.Square), r=[IN[i]], w=[c.SQS[k]])
            p.op("tensor", mm(psB[0:1, :], c.ones_col[:], c.SQS[k][:], i == 0, i == DP - 1),
                 r=[c.ones_col, c.SQS[k]], w=[psB] if i in (0, DP - 1) else ())
        p.op("scalar", smul(c.R1[:, sl], psB[0:1, :], 1.0 / D), r=[psB], w=[c.R1])
    p.op("scalar", act(c.R1[:], c.R1[:], AF.Sqrt, bias=c.cval(1)),
         r=["cval1", c.R1], w=[c.R1])
    p.op("vector", recip(c.R1[:], c.R1[:]), r=[c.R1], w=[c.R1])
    for ci in range(NCH):
        sl = slice(ci * CH, (ci + 1) * CH)
        k = ci % 2
        p.op("tensor", mm(psR[k][:], c.ones_row, c.R1[:, sl], True, True),
             r=[c._ones_rows, c.R1], w=[psR[k]])
        p.op("scalar", scp(c.SCR_A[:, sl], psR[k][:]), r=[psR[k]], w=[c.SCR_A])



def emit_a1(p, c, nc, X, dr, uid=""):
    wq_d = dr["a1_wq"]; wk_d = dr["a1_wk"]; wv_d = dr["a1_wv"]; wo_d = dr["a1_wo"]
    anw_d = dr["a1_anw"]; cb_d = dr["a1_ropeC"]; sb_d = dr["a1_ropeS"]
    mask_d = dr["a1_dmask"]
    o_scr = dr["o_scr"]; y_scr = dr["y_scr"]; h_out = dr["h_scr"]
    GH = 4
    with ExitStack() as ss:
        sbS = lambda n, s, d=F32: ss.enter_context(nc.sbuf_tensor(uid + n, s, d))
        X1 = [sbS(f"x1_{i}", [128, T], BF16) for i in range(DP)]
        ANW = sbS("anw_sb", [128, DP])

        s0 = ExitStack()
        sb0 = lambda n, s, d=F32: s0.enter_context(nc.sbuf_tensor(uid + n, s, d))
        ps0 = lambda n, s, d=F32: s0.enter_context(nc.psum_tensor(uid + n, s, d))
        Y = [sb0(f"y{i}", [128, T], BF16) for i in range(DP)]
        psA = ps0("psA", [1, CH])
        psB = ps0("psB", [1, CH])
        psR = [ps0("psR0", [128, CH]), ps0("psR1", [128, CH])]
        p.dma("scalar", ANW[:], anw_d.ap(), sem="anwin", w=[ANW])
        emit_ln(p, c, X, Y, psA, psB, psR, fold=0.5)
        emit_rms(p, c, Y, psB, psR)
        for i in range(DP):
            p.op("vector", stt(X1[i][:], Y[i][:], ANW[:, i:i + 1], c.SCR_A[:],
                               ALU.mult, ALU.mult),
                 r=[Y[i], ANW, c.SCR_A], w=[X1[i]])
            p.dma("gpsimd", y_scr.ap()[128 * i:128 * (i + 1), :], Y[i][:],
                  sem=f"ysp{i % 2}", r=[Y[i]], w=["y_scr"])
        p.barrier()
        s0.close()

        s1 = ExitStack()
        sb1 = lambda n, s, d=F32: s1.enter_context(nc.sbuf_tensor(uid + n, s, d))
        ps1 = lambda n, s, d=F32: s1.enter_context(nc.psum_tensor(uid + n, s, d))
        CB = sb1("cb", [128, T], BF16)
        SBC = sb1("sbc", [128, T], BF16)
        MSK = sb1("msk", [128, 4 * CH], BF16)
        OSTG = [sb1("ostg0", [128, T], BF16), sb1("ostg1", [128, T], BF16)]
        p.dma("sync", CB[:], cb_d.ap(), sem="cbin", w=[CB])
        p.dma("sync", SBC[:], sb_d.ap(), sem="sbin", w=[SBC])
        p.dma("sync", MSK[:], mask_d.ap(), sem="mkin", w=[MSK])
        GW = GH * 64
        WQ = [sb1(f"wqb{d}", [128, GW], BF16) for d in range(DP)]
        WQR = [sb1(f"wqrb{d}", [128, GW], BF16) for d in range(DP)]
        WK = [sb1(f"wkb{d}", [128, GW], BF16) for d in range(DP)]
        WKR = [sb1(f"wkrb{d}", [128, GW], BF16) for d in range(DP)]
        WV = [sb1(f"wvb{d}", [128, GW], BF16) for d in range(DP)]
        QH = [sb1(f"qh{j}", [128, T], BF16) for j in range(GH // 2)]
        KH = [sb1(f"kh{j}", [128, T], BF16) for j in range(GH // 2)]
        VT = [sb1(f"vt{t2}", [128, GW], BF16) for t2 in range(NTT)]
        PB = [sb1("pb0", [128, CH], BF16), sb1("pb1", [128, CH], BF16)]
        REPS = sb1("reps", [128, CH])
        NEGBC = sb1("negbc", [128, 1])
        psS = [ps1("psS0", [128, CH]), ps1("psS1", [128, CH])]
        psO = ps1("psO", [64, CH])
        psDen = ps1("psDen", [1, CH])
        psRep = ps1("psRep", [128, CH])

        for g in range(4):
            gsl = slice(GW * g, GW * (g + 1))
            for d in range(DP):
                rs = slice(128 * d, 128 * (d + 1))
                tq = p.dma("gpsimd", WQ[d][:], wq_d.ap()[rs, gsl], sem="wqg", w=[WQ[d]])
                tk = p.dma("gpsimd", WK[d][:], wk_d.ap()[rs, gsl], sem="wkg", w=[WK[d]])
                tv = p.dma("gpsimd", WV[d][:], wv_d.ap()[rs, gsl], sem="wvg", w=[WV[d]])
            p.retag_writers(WQ, tq)
            p.retag_writers(WK, tk)
            p.retag_writers(WV, tv)
            for d in range(DP):
                _emit_rot_weights_il(p, WQR[d], WQ[d], GH, 64)
                _emit_rot_weights_il(p, WKR[d], WK[d], GH, 64)
            for j in range(GH // 2):
                hsl = slice(128 * j, 128 * (j + 1))
                for qc in range(NCH):
                    tsl = slice(CH * qc, CH * (qc + 1))
                    for (psq, Wm, Wr, dstH) in ((psS[0], WQ, WQR, QH), (psS[1], WK, WKR, KH)):
                        for d in range(DP):
                            p.op("tensor", mm(psq[:], Wm[d][:, hsl], X1[d][:, tsl],
                                              d == 0, d == DP - 1),
                                 r=[Wm[d], X1[d]], w=[psq] if d in (0, DP - 1) else ())
                        for d in range(DP):
                            p.op("tensor", mm(psRep[:], Wr[d][:, hsl], X1[d][:, tsl],
                                              d == 0, d == DP - 1),
                                 r=[Wr[d], X1[d]], w=[psRep] if d in (0, DP - 1) else ())
                        p.op("vector", tt(c.SQS[0][:], psq[:], CB[:, tsl], ALU.mult),
                             r=[psq, CB], w=[c.SQS[0]])
                        p.op("vector", tt(c.SQS[1][:], psRep[:], SBC[:, tsl], ALU.mult),
                             r=[psRep, SBC], w=[c.SQS[1]])
                        p.op("vector", tt(dstH[j][:, tsl], c.SQS[0][:], c.SQS[1][:], ALU.add),
                             r=[c.SQS[0], c.SQS[1]], w=[f"{_key(dstH[j])}@{qc}"])
            for t2 in range(NTT):
                ssl = slice(128 * t2, 128 * (t2 + 1))
                for d in range(DP):
                    p.op("tensor", mm(psS[0][:, 0:GW], X1[d][:, ssl], WV[d][:],
                                      d == 0, d == DP - 1),
                         r=[X1[d], WV[d]], w=[psS[0]] if d in (0, DP - 1) else ())
                p.op("scalar", scp(VT[t2][:], psS[0][:, 0:GW]), r=[psS[0]], w=[VT[t2]])
            for hh in range(GH):
                hg = GH * g + hh
                j = hh // 2
                half = hh % 2
                rows = slice(64 * half, 64 * (half + 1))
                for (HT, o0) in ((QH[j], 0), (KH[j], 4)):
                    p.op("scalar", act(c.SCR_B[:], HT[:], AF.Square),
                         r=[f"{_key(HT)}@{q2}" for q2 in range(NCH)], w=[c.SCR_B])
                    for ci in range(NCH):
                        sl2 = slice(ci * CH, (ci + 1) * CH)
                        p.op("tensor", mm(psDen[0:1, :], c.ones_col[rows, :],
                                          c.SCR_B[rows, sl2], True, True),
                             r=[c.ones_col, c.SCR_B], w=[psDen])
                        p.op("vector", rmax(c.R0[:, o0 + ci:o0 + ci + 1], psDen[0:1, :]),
                             r=[psDen], w=[c.TMP])
                p.op("vector", rmax(c.R0[:, 8:9], c.R0[:, 0:4]), r=[c.TMP], w=[c.TMP])
                p.op("vector", rmax(c.R0[:, 9:10], c.R0[:, 4:8]), r=[c.TMP], w=[c.TMP])
                p.op("vector", tt(c.R0[:, 10:11], c.R0[:, 8:9], c.R0[:, 9:10], ALU.mult),
                     r=[c.TMP], w=[c.TMP])
                p.op("scalar", act(c.R0[:, 11:12], c.R0[:, 10:11], AF.Sqrt),
                     r=[c.TMP], w=[c.TMP])
                p.op("vector", ts(c.R0[:, 12:13], c.R0[:, 11:12], -1.0 / 8.0, ALU.mult),
                     r=[c.TMP], w=[c.TMP])
                p.op("tensor", mm(psRep[:, 0:1], c.ones_row, c.R0[:, 12:13], True, True),
                     r=[c._ones_rows, c.TMP], w=[psRep])
                p.op("scalar", scp(NEGBC[:], psRep[:, 0:1]), r=[psRep], w=[NEGBC])
                for qc in range(NCH):
                    tsl = slice(CH * qc, CH * (qc + 1))
                    nkt = 4 * qc + 4
                    for kt in range(nkt):
                        k2 = kt % 2
                        ksl = slice(128 * kt, 128 * (kt + 1))
                        p.op("tensor", mm(psS[k2][:, 0:CH], KH[j][rows, ksl], QH[j][rows, tsl],
                                          True, True),
                             r=[f"{_key(KH[j])}@{q2}" for q2 in range(NCH)]
                               + [f"{_key(QH[j])}@{qc}"],
                             w=[psS[k2]])
                        r_rel = kt - 4 * qc
                        if r_rel >= 0:
                            p.op("vector", tt(c.SQS[k2][:], psS[k2][:, 0:CH],
                                              MSK[:, CH * r_rel:CH * (r_rel + 1)], ALU.add),
                                 r=[psS[k2], MSK], w=[c.SQS[k2]])
                            src = c.SQS[k2][0:128, :]
                            srck = [c.SQS[k2]]
                        else:
                            src = psS[k2][0:128, 0:CH]
                            srck = [psS[k2]]
                        p.op("scalar", act(PB[k2][:], src, AF.Exp, bias=NEGBC[:], scale=1.0 / 8.0),
                             r=srck + [NEGBC], w=[PB[k2]])
                        p.op("tensor", mm(psO[:, 0:CH], VT[kt][:, 64 * hh:64 * (hh + 1)],
                                          PB[k2][:], kt == 0, kt == nkt - 1),
                             r=[VT[kt], PB[k2]], w=[psO] if kt in (0, nkt - 1) else ())
                        p.op("tensor", mm(psDen[0:1, :], c.ones_col_bf[:], PB[k2][:],
                                          kt == 0, kt == nkt - 1),
                             r=[c.ones_col_bf, PB[k2]], w=[psDen] if kt in (0, nkt - 1) else ())
                    p.op("vector", recip(c.R0[:, 0:CH], psDen[0:1, :]), r=[psDen], w=[c.TMP])
                    p.op("tensor", mm(psRep[:], c.ones_row, c.R0[:, 0:CH], True, True),
                         r=[c._ones_rows, c.TMP], w=[psRep])
                    p.op("scalar", scp(REPS[:], psRep[:]), r=[psRep], w=[REPS])
                    p.op("vector", tt(OSTG[j % 2][rows, tsl], psO[:, 0:CH], REPS[0:64, :],
                                      ALU.mult),
                         r=[psO, REPS], w=[f"ostg{j % 2}@{half}@{qc}"])
                if half == 1:
                    hp = hg // 2
                    p.dma("gpsimd", o_scr.ap()[128 * hp:128 * (hp + 1), :], OSTG[j % 2][:],
                          sem=f"osp{j % 2}",
                          r=[f"ostg{j % 2}@{h2}@{q2}" for h2 in range(2) for q2 in range(NCH)],
                          w=["o_scr"])
        p.barrier()
        s1.close()

        s2 = ExitStack()
        sb2 = lambda n, s, d=F32: s2.enter_context(nc.sbuf_tensor(uid + n, s, d))
        ps2 = lambda n, s, d=F32: s2.enter_context(nc.psum_tensor(uid + n, s, d))
        OIN = [sb2(f"oin{ot}", [128, T], BF16) for ot in range(DP)]
        WO = [sb2(f"wob{ot}", [128, D], BF16) for ot in range(DP)]
        YW = [sb2("yw0", [128, T], BF16), sb2("yw1", [128, T], BF16)]
        HST = [sb2("hst0", [128, T], BF16), sb2("hst1", [128, T], BF16)]
        psD = [ps2("psD0", [128, CH]), ps2("psD1", [128, CH])]
        for i in range(DP):
            to = p.dma("scalar", OIN[i][:], o_scr.ap()[128 * i:128 * (i + 1), :], sem="oing",
                       r=["o_scr"], w=[OIN[i]])
        p.retag_writers(OIN, to)
        for ot in range(DP):
            tw = p.dma("gpsimd", WO[ot][:], wo_d.ap()[128 * ot:128 * (ot + 1), :], sem="wog",
                       w=[WO[ot]])
        p.retag_writers(WO, tw)
        for jj in range(DP):
            jsl = slice(128 * jj, 128 * (jj + 1))
            kj = jj % 2
            p.dma("sync", YW[kj][:], y_scr.ap()[jsl, :], sem=f"ywin{kj}",
                  r=["y_scr"], w=[YW[kj]])
            for tc in range(NCH):
                tsl = slice(CH * tc, CH * (tc + 1))
                k = (jj * NCH + tc) % 2
                for ot in range(DP):
                    p.op("tensor", mm(psD[k][:], WO[ot][:, jsl], OIN[ot][:, tsl],
                                      ot == 0, ot == DP - 1),
                         r=[WO[ot], OIN[ot]], w=[psD[k]] if ot in (0, DP - 1) else ())
                p.op("vector", tt(HST[kj][:, tsl], YW[kj][:, tsl], psD[k][:], ALU.add),
                     r=[YW[kj], psD[k]], w=[HST[kj]])
                p.op("vector", stt(X[jj][:, tsl], psD[k][:], 0.5, X[jj][:, tsl],
                                   ALU.mult, ALU.add),
                     r=[psD[k]], w=[X[jj]])
            p.dma("gpsimd", h_out.ap()[jsl, :], HST[kj][:], sem=f"hout{kj}",
                  r=[HST[kj]], w=["h_scr"])
        s2.close()
    p.barrier()


def build_prog_a1():
    nc = bass.Bass("TRN2", target_bir_lowering=False)
    x_in = nc.dram_tensor("x_in", [D, T], F32, kind="ExternalInput")
    dr = {
        "a1_wq": nc.dram_tensor("wq", [D, D], F32, kind="ExternalInput"),
        "a1_wk": nc.dram_tensor("wk", [D, D], F32, kind="ExternalInput"),
        "a1_wv": nc.dram_tensor("wv", [D, D], F32, kind="ExternalInput"),
        "a1_wo": nc.dram_tensor("wo", [D, D], F32, kind="ExternalInput"),
        "a1_anw": nc.dram_tensor("anw", [128, DP], F32, kind="ExternalInput"),
        "a1_ropeC": nc.dram_tensor("ropeC", [128, T], BF16, kind="ExternalInput"),
        "a1_ropeS": nc.dram_tensor("ropeS", [128, T], BF16, kind="ExternalInput"),
        "a1_dmask": nc.dram_tensor("dmask", [128, 4 * CH], BF16, kind="ExternalInput"),
        "o_scr": nc.dram_tensor("o_scr", [D, T], BF16, kind="Internal"),
        "y_scr": nc.dram_tensor("y_scr", [D, T], BF16, kind="Internal"),
        "h_scr": nc.dram_tensor("h_out", [D, T], BF16, kind="ExternalOutput"),
    }
    z_out = nc.dram_tensor("z_out", [D, T], F32, kind="ExternalOutput")
    stack = ExitStack()
    with stack:
        sb = lambda n, s, d=F32: stack.enter_context(nc.sbuf_tensor(n, s, d))
        p = Prog(nc, stack)
        c = Com(nc, stack)
        X = [sb(f"x{i}", [128, T]) for i in range(DP)]
        c.init(p)
        for i in range(DP):
            tx = p.dma("sync", X[i][:], x_in.ap()[128 * i:128 * (i + 1), :], sem="xing", w=[X[i]])
        p.retag_writers(X, tx)
        emit_a1(p, c, nc, X, dr)
        for jj in range(DP):
            p.dma("sync", z_out.ap()[128 * jj:128 * (jj + 1), :], X[jj][:], sem="zout",
                  r=[X[jj]])
        p.finish()
        p.emit()
    return nc


def _emit_rot_weights_il(p, dst, src, nh, dh):
    sv = src.ap().rearrange("p (h i two) -> p h i two", h=nh, i=dh // 2, two=2)
    dv = dst.ap().rearrange("p (h i two) -> p h i two", h=nh, i=dh // 2, two=2)
    p.op("vector", ts(dv[:, :, :, 0:1], sv[:, :, :, 1:2], -1.0, ALU.mult), r=[src], w=[dst])
    p.op("vector", cp(dv[:, :, :, 1:2], sv[:, :, :, 0:1]), r=[src], w=[dst])



def emit_a2(p, c, nc, Z, dr, uid=""):
    h_in = dr["h_scr"]; fnw_d = dr["a2_fnw"]; rw_d = dr["a2_rw"]
    w1_d = dr["a2_w1"]; w3_d = dr["a2_w3"]; w2_d = dr["a2_w2"]
    ident_d = dr["a2_ident"]; sel_d = dr["a2_sel8"]
    with ExitStack() as ss:
        sbS = lambda n, s, d=F32: ss.enter_context(nc.sbuf_tensor(uid + n, s, d))
        X2 = [sbS(f"x2_{i}", [128, T], BF16) for i in range(DP)]
        CTT = sbS("ctt", [8, T])
        SEL8 = sbS("sel8_sb", [8, D])
        p.dma("scalar", SEL8[:], sel_d.ap(), sem="selin", w=[SEL8])

        s0 = ExitStack()
        sb0 = lambda n, s, d=F32: s0.enter_context(nc.sbuf_tensor(uid + n, s, d))
        ps0 = lambda n, s, d=F32: s0.enter_context(nc.psum_tensor(uid + n, s, d))
        H = [sb0(f"h{i}", [128, T], BF16) for i in range(DP)]
        FNW = sb0("fnw_sb", [128, DP])
        RWB = sb0("rwb", [128, 8 * DP], BF16)
        IDENT = sb0("ident_sb", [128, 128])
        LROW = sb0("lrow", [8, CH])
        LTOK = sb0("ltok", [128, 128])
        M8 = sb0("m8", [128, 8])
        N1 = sb0("n1", [128, 1])
        E8 = sb0("e8", [128, 8])
        IND = sb0("ind", [128, 8])
        NUM = sb0("num", [128, 8])
        E2 = sb0("e2", [128, 1])
        DINV = sb0("dinv", [128, 1])
        CT = sb0("ct", [128, 128])
        psB = ps0("psB", [1, CH])
        psR = [ps0("psR0", [128, CH]), ps0("psR1", [128, CH])]
        psL = ps0("psL", [8, CH])
        psT = ps0("psT", [128, 8])
        psT2 = ps0("psT2", [8, 128])
        for i in range(DP):
            th = p.dma("sync", H[i][:], h_in.ap()[128 * i:128 * (i + 1), :], sem="hing", w=[H[i]])
        p.retag_writers(H, th)
        p.dma("scalar", FNW[:], fnw_d.ap(), sem="fnwin", w=[FNW])
        p.dma("scalar", IDENT[:], ident_d.ap(), sem="idin", w=[IDENT])
        emit_rms(p, c, H, psB, psR)
        for i in range(DP):
            p.op("vector", stt(X2[i][:], H[i][:], FNW[:, i:i + 1], c.SCR_A[:],
                               ALU.mult, ALU.mult),
                 r=[H[i], FNW, c.SCR_A], w=[X2[i]])
        for d in range(DP):
            p.dma("gpsimd", c.SQS[0][:, 0:8], rw_d.ap()[128 * d:128 * (d + 1), :],
                  sem="rwin", w=[c.SQS[0]])
            p.op("vector", cp(RWB[:, 8 * d:8 * (d + 1)], c.SQS[0][:, 0:8]),
                 r=[c.SQS[0]], w=[RWB])
        for ci in range(NCH):
            csl = slice(CH * ci, CH * (ci + 1))
            for d in range(DP):
                p.op("tensor", mm(psL[0:8, :], RWB[:, 8 * d:8 * (d + 1)], X2[d][:, csl],
                                  d == 0, d == DP - 1),
                     r=[RWB, X2[d]], w=[psL] if d in (0, DP - 1) else ())
            p.op("scalar", scp(LROW[:], psL[0:8, :]), r=[psL], w=[LROW])
            for t2 in range(4):
                tt_g = 4 * ci + t2
                p.op("tensor", transp(psT[:, 0:8], LROW[0:8, 128 * t2:128 * (t2 + 1)],
                                      IDENT[0:8, 0:8]),
                     r=[LROW, IDENT], w=[psT])
                p.op("scalar", scp(LTOK[:, 8 * tt_g:8 * (tt_g + 1)], psT[:, 0:8]),
                     r=[psT], w=[LTOK])
        for tt_g in range(NTT):
            lt = LTOK[:, 8 * tt_g:8 * (tt_g + 1)]
            p.op("vector", max8(M8[:], lt), r=[LTOK], w=[M8])
            p.op("vector", ts(N1[:], M8[:, 0:1], -1.0, ALU.mult), r=[M8], w=[N1])
            p.op("scalar", act(E8[:], lt, AF.Exp, bias=N1[:]), r=[LTOK, N1], w=[E8])
            p.op("vector", ts(IND[:], lt, M8[:, 1:2], ALU.is_ge), r=[LTOK, M8], w=[IND])
            p.op("vector", tt(NUM[:], E8[:], IND[:], ALU.mult), r=[E8, IND], w=[NUM])
            p.op("scalar", act(E2[:], M8[:, 1:2], AF.Exp, bias=N1[:]), r=[M8, N1], w=[E2])
            p.op("vector", ts(E2[:], E2[:], 1.0, ALU.add), r=[E2], w=[E2])
            p.op("vector", recip(DINV[:], E2[:]), r=[E2], w=[DINV])
            p.op("vector",
                 lambda E, NUM=NUM, DINV=DINV, tt_g=tt_g:
                     E.tensor_scalar(CT[:, 8 * tt_g:8 * (tt_g + 1)], NUM[:], DINV[:], 0.5,
                                     ALU.mult, ALU.mult),
                 r=[NUM, DINV], w=[CT])
        for tt_g in range(NTT):
            p.op("tensor", transp(psT2[0:8, :], CT[:, 8 * tt_g:8 * (tt_g + 1)], IDENT[:]),
                 r=[CT, IDENT], w=[psT2])
            p.op("scalar", scp(CTT[:, 128 * tt_g:128 * (tt_g + 1)], psT2[0:8, :]),
                 r=[psT2], w=[CTT])
        p.barrier()
        s0.close()

        s1 = ExitStack()
        sb1 = lambda n, s, d=F32: s1.enter_context(nc.sbuf_tensor(uid + n, s, d))
        ps1 = lambda n, s, d=F32: s1.enter_context(nc.psum_tensor(uid + n, s, d))
        W1 = [sb1(f"w1b{d}", [128, D], BF16) for d in range(DP)]
        W3 = [sb1(f"w3b{d}", [128, D], BF16) for d in range(DP)]
        W2H = [sb1(f"w2b{it}", [128, D], BF16) for it in range(DP)]
        GT = [sb1(f"gt{it}", [128, CH], BF16) for it in range(DP)]
        TSC = c.SQS
        psH1 = [ps1("psH1a", [128, CH]), ps1("psH1b", [128, CH])]
        psH3 = [ps1("psH3a", [128, CH]), ps1("psH3b", [128, CH])]
        psY = [ps1("psYa", [128, CH]), ps1("psYb", [128, CH])]
        psRep = ps1("psRep2", [128, CH])

        for e in range(8):
            for ci in range(NCH):
                csl = slice(CH * ci, CH * (ci + 1))
                p.op("tensor", mm(psRep[:], SEL8[0:8, 128 * e:128 * (e + 1)],
                                  CTT[0:8, csl], True, True),
                     r=[SEL8, CTT], w=[psRep])
                p.op("scalar", scp(c.SCR_B[:, csl], psRep[:]), r=[psRep], w=[c.SCR_B])
            for ih in range(2):
                isl = slice(1024 * ih, 1024 * (ih + 1))
                for d in range(DP):
                    t1 = p.dma("gpsimd", W1[d][:], w1_d.ap()[e, 128 * d:128 * (d + 1), isl],
                               sem="w1g", w=[W1[d]])
                    t3 = p.dma("gpsimd", W3[d][:], w3_d.ap()[e, 128 * d:128 * (d + 1), isl],
                               sem="w3g", w=[W3[d]])
                p.retag_writers(W1, t1)
                p.retag_writers(W3, t3)
                for it in range(DP):
                    t2 = p.dma("gpsimd", W2H[it][:], w2_d.ap()[e, 1024 * ih + 128 * it:
                                                               1024 * ih + 128 * (it + 1), :],
                               sem="w2g", w=[W2H[it]])
                p.retag_writers(W2H, t2)
                for tc in range(NCH):
                    tsl = slice(CH * tc, CH * (tc + 1))
                    for it in range(DP):
                        k = it % 2
                        col = slice(128 * it, 128 * (it + 1))
                        for d in range(DP):
                            p.op("tensor", mm(psH1[k][:], W1[d][:, col], X2[d][:, tsl],
                                              d == 0, d == DP - 1),
                                 r=[W1[d], X2[d]], w=[psH1[k]] if d in (0, DP - 1) else ())
                        for d in range(DP):
                            p.op("tensor", mm(psH3[k][:], W3[d][:, col], X2[d][:, tsl],
                                              d == 0, d == DP - 1),
                                 r=[W3[d], X2[d]], w=[psH3[k]] if d in (0, DP - 1) else ())
                        p.op("scalar", act(c.SQS[k][:], psH1[k][:], AF.Silu),
                             r=[psH1[k]], w=[c.SQS[k]])
                        p.op("vector", tt(GT[it][:], c.SQS[k][:], psH3[k][:], ALU.mult),
                             r=[c.SQS[k], psH3[k]], w=[GT[it]])
                    for j in range(DP):
                        k = j % 2
                        jcol = slice(128 * j, 128 * (j + 1))
                        for it in range(DP):
                            p.op("tensor", mm(psY[k][:], W2H[it][:, jcol], GT[it][:],
                                              it == 0, it == DP - 1),
                                 r=[W2H[it], GT[it]], w=[psY[k]] if it in (0, DP - 1) else ())
                        p.op("vector", tt(TSC[k][:], psY[k][:], c.SCR_B[:, tsl], ALU.mult),
                             r=[psY[k], c.SCR_B], w=[TSC[k]])
                        p.op("vector", tt(Z[j][:, tsl], Z[j][:, tsl], TSC[k][:], ALU.add),
                             r=[TSC[k]], w=[Z[j]])
        s1.close()
    p.barrier()


def build_prog_a2():
    nc = bass.Bass("TRN2", target_bir_lowering=False)
    z_in = nc.dram_tensor("z_in", [D, T], F32, kind="ExternalInput")
    dr = {
        "h_scr": nc.dram_tensor("h_in", [D, T], BF16, kind="ExternalInput"),
        "a2_fnw": nc.dram_tensor("fnw", [128, DP], F32, kind="ExternalInput"),
        "a2_rw": nc.dram_tensor("rw", [D, 8], F32, kind="ExternalInput"),
        "a2_w1": nc.dram_tensor("w1", [8, D, 2048], F32, kind="ExternalInput"),
        "a2_w3": nc.dram_tensor("w3", [8, D, 2048], F32, kind="ExternalInput"),
        "a2_w2": nc.dram_tensor("w2", [8, 2048, D], F32, kind="ExternalInput"),
        "a2_ident": nc.dram_tensor("ident", [128, 128], F32, kind="ExternalInput"),
        "a2_sel8": nc.dram_tensor("sel8", [8, D], F32, kind="ExternalInput"),
    }
    x_out = nc.dram_tensor("x_out", [D, T], F32, kind="ExternalOutput")
    stack = ExitStack()
    with stack:
        sb = lambda n, s, d=F32: stack.enter_context(nc.sbuf_tensor(n, s, d))
        p = Prog(nc, stack)
        c = Com(nc, stack)
        Z = [sb(f"z{i}", [128, T]) for i in range(DP)]
        c.init(p)
        for i in range(DP):
            tz = p.dma("sync", Z[i][:], z_in.ap()[128 * i:128 * (i + 1), :], sem="zing", w=[Z[i]])
        p.retag_writers(Z, tz)
        emit_a2(p, c, nc, Z, dr)
        for j in range(DP):
            p.dma("sync", x_out.ap()[128 * j:128 * (j + 1), :], Z[j][:], sem="xout", r=[Z[j]])
        p.finish()
        p.emit()
    return nc


def _prep_l0_ffn_moe(inputs):
    sel = np.zeros((8, D), np.float32)
    for e in range(8):
        sel[e, 128 * e:128 * (e + 1)] = 1.0
    return {
        "fnw": np.ascontiguousarray(np.asarray(inputs["L0_ffn_ffn_norm_w"]).reshape(DP, 128).T),
        "rw": np.ascontiguousarray(inputs["L0_ffn_router_w"]),
        "w1": np.ascontiguousarray(inputs["L0_ffn_w1"]),
        "w3": np.ascontiguousarray(inputs["L0_ffn_w3"]),
        "w2": np.ascontiguousarray(inputs["L0_ffn_w2"]),
        "ident": np.eye(128, dtype=np.float32),
        "sel8": sel,
    }


def run_a2(hs, zs, W):
    nc = get_prog("a2")
    in_maps = [dict(h_in=hs[b], z_in=zs[b], **W) for b in range(8)]
    res = run_bass_kernel_spmd(nc, in_maps, core_ids=list(range(8)))
    return [np.asarray(r["x_out"]) for r in res.results]



def emit_e(p, c, nc, X, dr, uid=""):
    wq_d = dr["e_wq"]; wk_d = dr["e_wk"]; wv_d = dr["e_wv"]
    wqf_d = dr["e_wqf"]; wkf_d = dr["e_wkf"]; wout_d = dr["e_wout"]
    nw_d = dr["e_nw"]; ga_d = dr["e_gha"]; gb_d = dr["e_ghb"]
    triu_d = dr["e_triu"]; wmask_d = dr["e_wmask"]
    lin_scr = dr["lin_scr"]; win_scr = dr["win_scr"]
    with ExitStack() as ss:
        sbS = lambda n, s, d=F32: ss.enter_context(nc.sbuf_tensor(uid + n, s, d))
        XH = [sbS(f"xh{i}", [128, T], BF16) for i in range(DP)]

        s0 = ExitStack()
        sb0 = lambda n, s, d=F32: s0.enter_context(nc.sbuf_tensor(uid + n, s, d))
        ps0 = lambda n, s, d=F32: s0.enter_context(nc.psum_tensor(uid + n, s, d))
        Y = [sb0(f"y{i}", [128, T], BF16) for i in range(DP)]
        NW = sb0("nw_sb", [128, DP])
        psA = ps0("psA", [1, CH])
        psB = ps0("psB", [1, CH])
        psR = [ps0("psR0", [128, CH]), ps0("psR1", [128, CH])]
        p.dma("scalar", NW[:], nw_d.ap(), sem="nwin", w=[NW])
        emit_ln(p, c, X, Y, psA, psB, psR, fold=1.0)
        emit_rms(p, c, Y, psB, psR)
        for i in range(DP):
            p.op("vector", stt(XH[i][:], Y[i][:], NW[:, i:i + 1], c.SCR_A[:],
                               ALU.mult, ALU.mult),
                 r=[Y[i], NW, c.SCR_A], w=[XH[i]])
        p.barrier()
        s0.close()

        s1 = ExitStack()
        sb1 = lambda n, s, d=F32: s1.enter_context(nc.sbuf_tensor(uid + n, s, d))
        ps1 = lambda n, s, d=F32: s1.enter_context(nc.psum_tensor(uid + n, s, d))
        WQB = [sb1(f"wqb{d}", [128, 128], BF16) for d in range(DP)]
        WKB = [sb1(f"wkb{d}", [128, 128], BF16) for d in range(DP)]
        WVB = [sb1(f"wvb{d}", [128, 128], BF16) for d in range(DP)]
        WQF = sb1("wqf_sb", [64, 16], BF16)
        WKF = sb1("wkf_sb", [64, 16], BF16)
        WFS = sb1("wfs", [64, 32])
        GA = sb1("gha_sb", [33, 153], BF16)
        GB = sb1("ghb_sb", [33, 153], BF16)
        TRIU = sb1("triu_sb", [128, 128])
        WMSK = sb1("wmsk_sb", [128, 256], BF16)
        QH = sb1("qh", [64, T], BF16)
        KH = sb1("kh", [64, T], BF16)
        VA = [sb1("va0", [128, NTT * 65], BF16), sb1("va1", [128, NTT * 65], BF16)]
        SQ = sb1("sq", [33, T], BF16)
        SK = sb1("sk", [33, T], BF16)
        PQ1 = sb1("pq1", [128, T], BF16)
        PQ2 = sb1("pq2", [25, T], BF16)
        PK1 = sb1("pk1", [128, T], BF16)
        PK2 = sb1("pk2", [25, T], BF16)
        KTOK = sb1("ktok", [128, NTT * 153], BF16)
        KVH = sb1("kvh", [128, 65], BF16)
        KVL = sb1("kvl", [25, 65], BF16)
        KVFH = sb1("kvfh", [128, 65])
        KVFL = sb1("kvfl", [25, 65])
        LINH = sb1("linh", [64, T], BF16)
        WINH = sb1("winh", [64, T], BF16)
        ASB = sb1("asb", [128, 128], BF16)
        PBW = sb1("pbw", [128, 128], BF16)
        REPE = sb1("repe", [128, 128])
        NEGWC = sb1("negwc", [128, 1])
        psU = ps1("psU", [128, CH])
        psW = ps1("psW", [128, CH])
        psA2 = ps1("psA2", [128, 128])
        psY2 = ps1("psY2", [65, 128])
        psOW = ps1("psOW", [65, 128])
        psRepE = ps1("psRepE", [128, 128])
        kvh_ps = ps1("kvh_ps", [128, 65])
        kvl_ps = ps1("kvl_ps", [25, 65])

        p.dma("sync", WFS[:, 0:16], wqf_d.ap(), sem="wfin", w=[WFS])
        p.dma("sync", WFS[:, 16:32], wkf_d.ap(), sem="wfin2", w=[WFS])
        p.op("vector", cp(WQF[:], WFS[:, 0:16]), r=[WFS], w=[WQF])
        p.op("vector", cp(WKF[:], WFS[:, 16:32]), r=[WFS], w=[WKF])
        p.dma("sync", GA[:], ga_d.ap(), sem="gain", w=[GA])
        p.dma("sync", GB[:], gb_d.ap(), sem="gbin", w=[GB])
        p.dma("sync", TRIU[:], triu_d.ap(), sem="trin", w=[TRIU])
        p.dma("sync", WMSK[:], wmask_d.ap(), sem="wmin", w=[WMSK])
        p.op("vector", memset(SQ[:], 0.0), w=[SQ])
        p.op("vector", memset(SQ[32:33, :], 1.0), w=[SQ])
        p.op("vector", memset(SK[:], 0.0), w=[SK])
        p.op("vector", memset(SK[32:33, :], 1.0), w=[SK])

        for hp in range(8):
            psl = slice(128 * hp, 128 * (hp + 1))
            for d in range(DP):
                rs = slice(128 * d, 128 * (d + 1))
                tq = p.dma("gpsimd", WQB[d][:], wq_d.ap()[rs, psl], sem="wqg", w=[WQB[d]])
                tk = p.dma("gpsimd", WKB[d][:], wk_d.ap()[rs, psl], sem="wkg", w=[WKB[d]])
                tv = p.dma("gpsimd", WVB[d][:], wv_d.ap()[rs, psl], sem="wvg", w=[WVB[d]])
            p.retag_writers(WQB, tq)
            p.retag_writers(WKB, tk)
            p.retag_writers(WVB, tv)
            for h2 in range(2):
                p.op("vector", memset(
                    VA[h2].ap().rearrange("p (t c) -> p t c", t=NTT, c=65)[:, :, 64:65], 1.0),
                    w=[VA[h2]])
            for t2 in range(NTT):
                ssl = slice(128 * t2, 128 * (t2 + 1))
                for d in range(DP):
                    p.op("tensor", mm(psA2[:], XH[d][:, ssl], WVB[d][:],
                                      d == 0, d == DP - 1),
                         r=[XH[d], WVB[d]], w=[psA2] if d in (0, DP - 1) else ())
                for h2 in range(2):
                    p.op("vector", cp(
                        VA[h2].ap().rearrange("p (t c) -> p t c", t=NTT, c=65)[:, t2, 0:64],
                        psA2[:, 64 * h2:64 * (h2 + 1)]),
                        r=[psA2], w=[VA[h2]])
            for h2 in range(2):
                h = 2 * hp + h2
                hsl = slice(64 * h2, 64 * (h2 + 1))
                vaR = VA[h2].ap().rearrange("p (t c) -> p t c", t=NTT, c=65)
                for ci in range(NCH):
                    csl = slice(CH * ci, CH * (ci + 1))
                    for (ps_, Wm, dst) in ((psU, WQB, QH), (psW, WKB, KH)):
                        for d in range(DP):
                            p.op("tensor", mm(ps_[0:64, :], Wm[d][:, hsl], XH[d][:, csl],
                                              d == 0, d == DP - 1),
                                 r=[Wm[d], XH[d]], w=[ps_] if d in (0, DP - 1) else ())
                        p.op("scalar", scp(dst[:, csl], ps_[0:64, :]), r=[ps_], w=[dst])
                for ci in range(NCH):
                    csl = slice(CH * ci, CH * (ci + 1))
                    p.op("tensor", mm(psU[0:16, :], WQF[:], QH[:, csl], True, True),
                         r=[WQF, QH], w=[psU])
                    p.op("scalar", scp(SQ[0:16, csl], psU[0:16, :]), r=[psU], w=[SQ])
                    p.op("tensor", mm(psW[0:16, :], WKF[:], KH[:, csl], True, True),
                         r=[WKF, KH], w=[psW])
                    p.op("scalar", scp(SK[0:16, csl], psW[0:16, :]), r=[psW], w=[SK])
                for (S_, P1, P2) in ((SQ, PQ1, PQ2), (SK, PK1, PK2)):
                    for ci in range(NCH):
                        csl = slice(CH * ci, CH * (ci + 1))
                        for (P_, fsl, np_) in ((P1, slice(0, 128), 128),
                                               (P2, slice(128, 153), 25)):
                            p.op("tensor", mm(psU[0:np_, :], GA[:, fsl], S_[:, csl],
                                              True, True),
                                 r=[GA, S_], w=[psU])
                            p.op("tensor", mm(psW[0:np_, :], GB[:, fsl], S_[:, csl],
                                              True, True),
                                 r=[GB, S_], w=[psW])
                            p.op("scalar", scp(c.SQS[0][0:np_, :], psW[0:np_, :]),
                                 r=[psW], w=[c.SQS[0]])
                            p.op("vector", tt(P_[0:np_, csl], psU[0:np_, :],
                                              c.SQS[0][0:np_, :], ALU.mult),
                                 r=[psU, c.SQS[0]], w=[P_])
                for ch in range(NTT):
                    p.op("tensor", mm(psU[:, 0:153], SK[:, 128 * ch:128 * (ch + 1)], GA[:],
                                      True, True),
                         r=[SK, GA], w=[psU])
                    p.op("tensor", mm(psW[:, 0:153], SK[:, 128 * ch:128 * (ch + 1)], GB[:],
                                      True, True),
                         r=[SK, GB], w=[psW])
                    p.op("scalar", scp(c.SQS[0][:, 0:153], psW[:, 0:153]),
                         r=[psW], w=[c.SQS[0]])
                    p.op("vector", tt(KTOK[:, 153 * ch:153 * (ch + 1)], psU[:, 0:153],
                                      c.SQS[0][:, 0:153], ALU.mult),
                         r=[psU, c.SQS[0]], w=[KTOK])
                p.op("vector", memset(KVH[:], 0.0), w=[KVH])
                p.op("vector", memset(KVL[:], 0.0), w=[KVL])
                p.op("vector", memset(KVFH[:], 0.0), w=[KVFH])
                p.op("vector", memset(KVFL[:], 0.0), w=[KVFL])
                for ch in range(NTT):
                    cs128 = slice(128 * ch, 128 * (ch + 1))
                    p.op("tensor", mm(psA2[:], PK1[:, cs128], PQ1[:, cs128], True, False),
                         r=[PK1, PQ1], w=[psA2])
                    p.op("tensor", mm(psA2[:], PK2[0:25, cs128], PQ2[0:25, cs128],
                                      False, True),
                         r=[PK2, PQ2], w=[psA2])
                    p.op("vector", tt(ASB[:], psA2[:], TRIU[:], ALU.mult),
                         r=[psA2, TRIU], w=[ASB])
                    p.op("tensor", mm(psY2[:, :], vaR[:, ch, :], ASB[:], True, False),
                         r=[VA[h2], ASB], w=[psY2])
                    p.op("tensor", mm(psY2[:, :], KVH[:], PQ1[:, cs128], False, False),
                         r=[KVH, PQ1], w=[psY2])
                    p.op("tensor", mm(psY2[:, :], KVL[:], PQ2[0:25, cs128], False, True),
                         r=[KVL, PQ2], w=[psY2])
                    p.op("vector", ts(c.R0[:, 0:128], psY2[64:65, :], c.cval(1), ALU.add),
                         r=[psY2, "cval1"], w=[c.TMP])
                    p.op("vector", recip(c.R0[:, 0:128], c.R0[:, 0:128]), r=[c.TMP], w=[c.TMP])
                    p.op("tensor", mm(psRepE[:], c.ones_row, c.R0[:, 0:128], True, True),
                         r=[c._ones_rows, c.TMP], w=[psRepE])
                    p.op("scalar", scp(REPE[:], psRepE[:]), r=[psRepE], w=[REPE])
                    p.op("vector", tt(LINH[:, cs128], psY2[0:64, :], REPE[0:64, :], ALU.mult),
                         r=[psY2, REPE], w=[LINH])
                    if ch < NTT - 1:
                        p.op("tensor", mm(kvh_ps[:], KTOK[:, 153 * ch:153 * ch + 128],
                                          vaR[:, ch, :], True, True),
                             r=[KTOK, VA[h2]], w=[kvh_ps])
                        p.op("tensor", mm(kvl_ps[:], KTOK[:, 153 * ch + 128:153 * (ch + 1)],
                                          vaR[:, ch, :], True, True),
                             r=[KTOK, VA[h2]], w=[kvl_ps])
                        p.op("vector", tt(KVFH[:], KVFH[:], kvh_ps[:], ALU.add),
                             r=[kvh_ps], w=[KVFH])
                        p.op("vector", tt(KVFL[:], KVFL[:], kvl_ps[:], ALU.add),
                             r=[kvl_ps], w=[KVFL])
                        p.op("scalar", scp(KVH[:], KVFH[:]), r=[KVFH], w=[KVH])
                        p.op("scalar", scp(KVL[:], KVFL[:]), r=[KVFL], w=[KVL])
                for (HT, o0) in ((QH, 0), (KH, 4)):
                    p.op("scalar", act(c.SCR_B[0:64, :], HT[:], AF.Square),
                         r=[HT], w=[c.SCR_B])
                    for ci in range(NCH):
                        sl2 = slice(ci * CH, (ci + 1) * CH)
                        p.op("tensor", mm(psU[0:1, 0:CH], c.ones_col[0:64, :],
                                          c.SCR_B[0:64, sl2], True, True),
                             r=[c.ones_col, c.SCR_B], w=[psU])
                        p.op("vector", rmax(c.R0[:, o0 + ci:o0 + ci + 1], psU[0:1, 0:CH]),
                             r=[psU], w=[c.TMP])
                p.op("vector", rmax(c.R0[:, 8:9], c.R0[:, 0:4]), r=[c.TMP], w=[c.TMP])
                p.op("vector", rmax(c.R0[:, 9:10], c.R0[:, 4:8]), r=[c.TMP], w=[c.TMP])
                p.op("vector", tt(c.R0[:, 10:11], c.R0[:, 8:9], c.R0[:, 9:10], ALU.mult),
                     r=[c.TMP], w=[c.TMP])
                p.op("scalar", act(c.R0[:, 11:12], c.R0[:, 10:11], AF.Sqrt),
                     r=[c.TMP], w=[c.TMP])
                p.op("vector", ts(c.R0[:, 12:13], c.R0[:, 11:12], -1.0 / 8.0, ALU.mult),
                     r=[c.TMP], w=[c.TMP])
                p.op("tensor", mm(psRepE[:, 0:1], c.ones_row, c.R0[:, 12:13], True, True),
                     r=[c._ones_rows, c.TMP], w=[psRepE])
                p.op("scalar", scp(NEGWC[:], psRepE[:, 0:1]), r=[psRepE], w=[NEGWC])
                for qc in range(NTT):
                    qsl = slice(128 * qc, 128 * (qc + 1))
                    kts = [kt for kt in (qc - 1, qc) if kt >= 0]
                    for ki, kt in enumerate(kts):
                        ksl = slice(128 * kt, 128 * (kt + 1))
                        p.op("tensor", mm(psA2[:], KH[:, ksl], QH[:, qsl], True, True),
                             r=[KH, QH], w=[psA2])
                        wm = WMSK[:, 0:128] if kt < qc else WMSK[:, 128:256]
                        p.op("vector", tt(c.SQS[1][:, 0:128], psA2[:], wm, ALU.add),
                             r=[psA2, WMSK], w=[c.SQS[1]])
                        p.op("scalar", act(PBW[:], c.SQS[1][:, 0:128], AF.Exp,
                                           bias=NEGWC[:], scale=1.0 / 8.0),
                             r=[c.SQS[1], NEGWC], w=[PBW])
                        p.op("tensor", mm(psOW[:, :], vaR[:, kt, :], PBW[:],
                                          ki == 0, ki == len(kts) - 1),
                             r=[VA[h2], PBW], w=[psOW])
                    p.op("vector", recip(c.R0[:, 0:128], psOW[64:65, :]), r=[psOW], w=[c.TMP])
                    p.op("tensor", mm(psRepE[:], c.ones_row, c.R0[:, 0:128], True, True),
                         r=[c._ones_rows, c.TMP], w=[psRepE])
                    p.op("scalar", scp(REPE[:], psRepE[:]), r=[psRepE], w=[REPE])
                    p.op("vector", tt(WINH[:, qsl], psOW[0:64, :], REPE[0:64, :], ALU.mult),
                         r=[psOW, REPE], w=[WINH])
                p.dma("gpsimd", lin_scr.ap()[64 * h:64 * (h + 1), :], LINH[:],
                      sem="linsp", r=[LINH], w=["lin_scr"])
                p.dma("gpsimd", win_scr.ap()[64 * h:64 * (h + 1), :], WINH[:],
                      sem="winsp", r=[WINH], w=["win_scr"])
        p.barrier()
        s1.close()

        s2 = ExitStack()
        sb2 = lambda n, s, d=F32: s2.enter_context(nc.sbuf_tensor(uid + n, s, d))
        ps2 = lambda n, s, d=F32: s2.enter_context(nc.psum_tensor(uid + n, s, d))
        Z = X
        LW2 = [sb2("lw2a", [128, CH], BF16), sb2("lw2b", [128, CH], BF16)]
        WOUT = [sb2(f"wob{ot}", [128, D], BF16) for ot in range(16)]
        psDE = [ps2(f"psDE{j}", [128, CH]) for j in range(DP)]
        for ot in range(16):
            tw = p.dma("gpsimd", WOUT[ot][:], wout_d.ap()[128 * ot:128 * (ot + 1), :],
                       sem="wog", w=[WOUT[ot]])
        p.retag_writers(WOUT, tw)
        for tc in range(NCH):
            tsl = slice(CH * tc, CH * (tc + 1))
            for ot in range(16):
                k = ot % 2
                src = lin_scr if ot < DP else win_scr
                ro = ot if ot < DP else ot - DP
                p.dma("scalar", LW2[k][:],
                      src.ap()[128 * ro:128 * (ro + 1), tsl], sem=f"lw2{k}",
                      r=["lin_scr" if ot < DP else "win_scr"], w=[LW2[k]])
                for j in range(DP):
                    p.op("tensor", mm(psDE[j][:], WOUT[ot][:, 128 * j:128 * (j + 1)],
                                      LW2[k][:], ot == 0, ot == 15),
                         r=[WOUT[ot], LW2[k]], w=[psDE[j]] if ot in (0, 15) else ())
            for j in range(DP):
                p.op("vector", tt(Z[j][:, tsl], Z[j][:, tsl], psDE[j][:], ALU.add),
                     r=[psDE[j]], w=[Z[j]])
        s2.close()
    p.barrier()


def build_prog_e():
    nc = bass.Bass("TRN2", target_bir_lowering=False)
    x_in = nc.dram_tensor("x_in", [D, T], F32, kind="ExternalInput")
    dr = {
        "e_wq": nc.dram_tensor("wq", [D, D], F32, kind="ExternalInput"),
        "e_wk": nc.dram_tensor("wk", [D, D], F32, kind="ExternalInput"),
        "e_wv": nc.dram_tensor("wv", [D, D], F32, kind="ExternalInput"),
        "e_wqf": nc.dram_tensor("wqf", [64, 16], F32, kind="ExternalInput"),
        "e_wkf": nc.dram_tensor("wkf", [64, 16], F32, kind="ExternalInput"),
        "e_wout": nc.dram_tensor("wout", [2 * D, D], F32, kind="ExternalInput"),
        "e_nw": nc.dram_tensor("nw", [128, DP], F32, kind="ExternalInput"),
        "e_gha": nc.dram_tensor("gha", [33, 153], BF16, kind="ExternalInput"),
        "e_ghb": nc.dram_tensor("ghb", [33, 153], BF16, kind="ExternalInput"),
        "e_triu": nc.dram_tensor("triu", [128, 128], F32, kind="ExternalInput"),
        "e_wmask": nc.dram_tensor("wmask", [128, 256], BF16, kind="ExternalInput"),
        "lin_scr": nc.dram_tensor("lin_scr", [D, T], BF16, kind="Internal"),
        "win_scr": nc.dram_tensor("win_scr", [D, T], BF16, kind="Internal"),
    }
    x_out = nc.dram_tensor("x_out", [D, T], F32, kind="ExternalOutput")
    stack = ExitStack()
    with stack:
        sb = lambda n, s, d=F32: stack.enter_context(nc.sbuf_tensor(n, s, d))
        p = Prog(nc, stack)
        c = Com(nc, stack)
        X = [sb(f"x{i}", [128, T]) for i in range(DP)]
        c.init(p)
        for i in range(DP):
            tx = p.dma("sync", X[i][:], x_in.ap()[128 * i:128 * (i + 1), :], sem="xing", w=[X[i]])
        p.retag_writers(X, tx)
        emit_e(p, c, nc, X, dr)
        for jj in range(DP):
            p.dma("sync", x_out.ap()[128 * jj:128 * (jj + 1), :], X[jj][:], sem="xout",
                  r=[X[jj]])
        p.finish()
        p.emit()
    return nc


def _prep_l1_attn(inputs):
    import math
    iu, ju = np.triu_indices(16)
    scale = np.where(iu == ju, 0.5, 1.0 / math.sqrt(2.0)).astype(np.float32)
    GA = np.zeros((33, 153), np.float32)
    GB = np.zeros((33, 153), np.float32)
    GA[32, 0] = 1.0
    GB[32, 0] = 1.0
    for i in range(16):
        GA[i, 1 + i] = 1.0
        GB[32, 1 + i] = 1.0
    for qn in range(136):
        GA[iu[qn], 17 + qn] = 1.0
        GB[ju[qn], 17 + qn] = scale[qn]
    rk = np.arange(128)[:, None]
    cq = np.arange(128)[None, :]
    wm_prev = np.where(rk >= cq + 64, 0.0, -1e30).astype(np.float32)
    wm_diag = np.where((rk <= cq) & (rk >= cq - 64), 0.0, -1e30).astype(np.float32)
    triu = np.triu(np.ones((128, 128), np.float32))
    return {
        "wq": np.ascontiguousarray(inputs["L1_attn_Wq"]),
        "wk": np.ascontiguousarray(inputs["L1_attn_Wk"]),
        "wv": np.ascontiguousarray(inputs["L1_attn_Wv"]),
        "wqf": np.ascontiguousarray(inputs["L1_attn_Wqf"]),
        "wkf": np.ascontiguousarray(inputs["L1_attn_Wkf"]),
        "wout": np.ascontiguousarray(inputs["L1_attn_Wout"]),
        "nw": np.ascontiguousarray(np.asarray(inputs["L1_attn_norm_w"]).reshape(DP, 128).T),
        "gha": np.ascontiguousarray(GA.astype(BF)),
        "ghb": np.ascontiguousarray(GB.astype(BF)),
        "triu": triu,
        "wmask": np.ascontiguousarray(
            np.concatenate([wm_prev, wm_diag], axis=1).astype(BF)),
    }


def run_e(xs, W):
    nc = get_prog("e")
    in_maps = [dict(x_in=xs[b], **W) for b in range(8)]
    res = run_bass_kernel_spmd(nc, in_maps, core_ids=list(range(8)))
    return [np.asarray(r["x_out"]) for r in res.results]



_PROG_CACHE = {}

def emit_d(p, c, nc, X, dr, uid=""):
    wg = dr["d_wg"]; wu = dr["d_wu"]; wd = dr["d_wd"]; g_scr = dr["g_scr"]
    with ExitStack() as ss:
        sbS = lambda n, s, d=F32: ss.enter_context(nc.sbuf_tensor(uid + n, s, d))
        Y = [sbS(f"y{i}", [128, T], BF16) for i in range(DP)]

        s1 = ExitStack()
        sb1 = lambda n, s, d=F32: s1.enter_context(nc.sbuf_tensor(uid + n, s, d))
        ps1 = lambda n, s, d=F32: s1.enter_context(nc.psum_tensor(uid + n, s, d))
        psA = ps1("psA", [1, CH])
        psB = ps1("psB", [1, CH])
        psR = [ps1("psR0", [128, CH]), ps1("psR1", [128, CH])]
        GPS = [ps1("gate0", [128, CH]), ps1("gate1", [128, CH])]
        UPS = [ps1("up0", [128, CH]), ps1("up1", [128, CH])]
        stg_g = sb1("stg_g0", [128, 1536], I32)
        stg_u = sb1("stg_u0", [128, 1536], I32)
        gst = sb1("gstage0", [128, T], BF16)
        WG = [sb1(f"wgb{i}", [128, 1536], BF16) for i in range(DP)]
        WU = [sb1(f"wub{i}", [128, 1536], BF16) for i in range(DP)]

        emit_ln(p, c, X, Y, psA, psB, psR, fold=None)

        for ih in range(2):
            csl = slice(1536 * ih, 1536 * (ih + 1))
            for d in range(DP):
                p.dma("sync", stg_g[:], wg.ap()[128 * d:128 * (d + 1), csl], sem="wgs",
                      w=[stg_g])
                p.op("gpsimd", ts(WG[d][:], stg_g[:], 1.0 / 127.0, ALU.mult),
                     r=[stg_g], w=[WG[d]])
                p.dma("scalar", stg_u[:], wu.ap()[128 * d:128 * (d + 1), csl], sem="wus",
                      w=[stg_u])
                p.op("vector", ts(WU[d][:], stg_u[:], 1.0 / 127.0, ALU.mult),
                     r=[stg_u], w=[WU[d]])
            for hp in range(12):
                hpg = 12 * ih + hp
                col = slice(128 * hp, 128 * (hp + 1))
                for tc in range(NCH):
                    seq = hp * NCH + tc
                    tsl = slice(CH * tc, CH * (tc + 1))
                    k = seq % 2
                    for d in range(DP):
                        p.op("tensor", mm(GPS[k][:], WG[d][:, col], Y[d][:, tsl],
                                          d == 0, d == DP - 1),
                             r=[WG[d], Y[d]], w=[GPS[k]] if d in (0, DP - 1) else ())
                    for d in range(DP):
                        p.op("tensor", mm(UPS[k][:], WU[d][:, col], Y[d][:, tsl],
                                          d == 0, d == DP - 1),
                             r=[WU[d], Y[d]], w=[UPS[k]] if d in (0, DP - 1) else ())
                    p.op("scalar", act(c.SQS[k][:], GPS[k][:], AF.Silu),
                         r=[GPS[k]], w=[c.SQS[k]])
                    p.op("vector", tt(gst[:, tsl], c.SQS[k][:], UPS[k][:], ALU.mult),
                         r=[c.SQS[k], UPS[k]], w=[f"gst@{tc}"])
                p.dma("gpsimd", g_scr.ap()[128 * hpg:128 * (hpg + 1), :], gst[:], sem="gsp",
                      r=[f"gst@{t2}" for t2 in range(NCH)], w=["g_scr"])
            p.barrier()

        s1.close()

        s2 = ExitStack()
        sb2 = lambda n, s, d=F32: s2.enter_context(nc.sbuf_tensor(uid + n, s, d))
        ps2 = lambda n, s, d=F32: s2.enter_context(nc.psum_tensor(uid + n, s, d))
        WDSTG = [sb2("wdstg0", [128, D], I32), sb2("wdstg1", [128, D], I32)]
        GIN = [sb2("gin0", [128, CH], BF16), sb2("gin1", [128, CH], BF16)]
        WD = [sb2(f"wdb{i}", [128, D], BF16) for i in range(24)]
        dps = [ps2(f"dps{j}", [128, CH]) for j in range(DP)]
        for it in range(24):
            k = it % 2
            p.dma("sync", WDSTG[k][:], wd.ap()[128 * it:128 * (it + 1), :],
                  sem=f"wds{k}", w=[WDSTG[k]])
            p.op("gpsimd" if it % 2 else "vector", ts(WD[it][:], WDSTG[k][:], 1.0 / 127.0, ALU.mult),
                 r=[WDSTG[k]], w=[WD[it]])
        for tc in range(NCH):
            tsl = slice(CH * tc, CH * (tc + 1))
            for it in range(24):
                k = it % 2
                p.dma("scalar", GIN[k][:], g_scr.ap()[128 * it:128 * (it + 1), tsl],
                      sem=f"gin{k}", r=["g_scr"], w=[GIN[k]])
                for j in range(DP):
                    p.op("tensor", mm(dps[j][:], WD[it][:, 128 * j:128 * (j + 1)], GIN[k][:],
                                      it == 0, it == 23),
                         r=[WD[it], GIN[k]], w=[dps[j]] if it in (0, 23) else ())
            for j in range(DP):
                p.op("vector", stt(X[j][:, tsl], dps[j][:], 0.5, X[j][:, tsl],
                                   ALU.mult, ALU.add),
                     r=[dps[j]], w=[X[j]])
        s2.close()
    p.barrier()


def build_prog_d():
    nc = bass.Bass("TRN2", target_bir_lowering=False)
    x_in = nc.dram_tensor("x_in", [D, T], F32, kind="ExternalInput")
    dr = {
        "d_wg": nc.dram_tensor("wg", [D, 3072], I32, kind="ExternalInput"),
        "d_wu": nc.dram_tensor("wu", [D, 3072], I32, kind="ExternalInput"),
        "d_wd": nc.dram_tensor("wd", [3072, D], I32, kind="ExternalInput"),
        "g_scr": nc.dram_tensor("g_scr", [3072, T], BF16, kind="Internal"),
    }
    x_out = nc.dram_tensor("x_out", [D, T], F32, kind="ExternalOutput")
    stack = ExitStack()
    with stack:
        sb = lambda n, s, d=F32: stack.enter_context(nc.sbuf_tensor(n, s, d))
        p = Prog(nc, stack)
        c = Com(nc, stack)
        X = [sb(f"x{i}", [128, T]) for i in range(DP)]
        c.init(p)
        for i in range(DP):
            tx = p.dma("sync", X[i][:], x_in.ap()[128 * i:128 * (i + 1), :], sem="xing", w=[X[i]])
        p.retag_writers(X, tx)
        emit_d(p, c, nc, X, dr)
        for j in range(DP):
            p.dma("sync", x_out.ap()[128 * j:128 * (j + 1), :], X[j][:], sem="xout", r=[X[j]])
        p.finish()
        p.emit()
    return nc



def _emit_rot_weights(p, dst, src, nh, dh, rot_half):
    rh = rot_half
    sv = src.ap().rearrange("p (h c) -> p h c", h=nh, c=dh)
    dv = dst.ap().rearrange("p (h c) -> p h c", h=nh, c=dh)
    p.op("vector", ts(dv[:, :, 0:rh], sv[:, :, rh:2 * rh], -1.0, ALU.mult),
         r=[src], w=[dst])
    p.op("vector", cp(dv[:, :, rh:2 * rh], sv[:, :, 0:rh]), r=[src], w=[dst])
    if 2 * rh < dh:
        p.op("vector", cp(dv[:, :, 2 * rh:dh], sv[:, :, 2 * rh:dh]), r=[src], w=[dst])


def emit_b(p, c, nc, X, dr, uid=""):
    wq_d = dr["b_wq"]; wk_d = dr["b_wk"]; wv_d = dr["b_wv"]; wo_d = dr["b_wo"]
    cb_d = dr["b_ropeC"]; sb_d = dr["b_ropeS"]; mask_d = dr["b_dmask"]
    o_scr = dr["o_scr"]
    NHB = 8
    GH = 2
    with ExitStack() as ss:
        sbS = lambda n, s, d=F32: ss.enter_context(nc.sbuf_tensor(uid + n, s, d))
        Y = [sbS(f"y{i}", [128, T], BF16) for i in range(DP)]

        s0 = ExitStack()
        sb0 = lambda n, s, d=F32: s0.enter_context(nc.sbuf_tensor(uid + n, s, d))
        ps0 = lambda n, s, d=F32: s0.enter_context(nc.psum_tensor(uid + n, s, d))
        psA = ps0("psA", [1, CH])
        psB = ps0("psB", [1, CH])
        psR = [ps0("psR0", [128, CH]), ps0("psR1", [128, CH])]
        emit_ln(p, c, X, Y, psA, psB, psR, fold=None)
        p.barrier()
        s0.close()

        s1 = ExitStack()
        sb1 = lambda n, s, d=F32: s1.enter_context(nc.sbuf_tensor(uid + n, s, d))
        ps1 = lambda n, s, d=F32: s1.enter_context(nc.psum_tensor(uid + n, s, d))
        CB = sb1("cb", [128, T], BF16)
        SBC = sb1("sbc", [128, T], BF16)
        MSK = sb1("msk", [128, 4 * CH], BF16)
        OSTG = [sb1("ostg0", [128, T], BF16), sb1("ostg1", [128, T], BF16)]
        p.dma("sync", CB[:], cb_d.ap(), sem="cbin", w=[CB])
        p.dma("sync", SBC[:], sb_d.ap(), sem="sbin", w=[SBC])
        p.dma("sync", MSK[:], mask_d.ap(), sem="mkin", w=[MSK])
        WQ = [sb1(f"wqb{d}", [128, GH * 128], BF16) for d in range(DP)]
        WQR = [sb1(f"wqrb{d}", [128, GH * 128], BF16) for d in range(DP)]
        WK = [sb1(f"wkb{d}", [128, GH * 128], BF16) for d in range(DP)]
        WKR = [sb1(f"wkrb{d}", [128, GH * 128], BF16) for d in range(DP)]
        WV = [sb1(f"wvb{d}", [128, GH * 128], BF16) for d in range(DP)]
        QH = [sb1(f"qh{h}", [128, T], BF16) for h in range(GH)]
        KH = [sb1(f"kh{h}", [128, T], BF16) for h in range(GH)]
        VT = [sb1(f"vt{t2}", [128, GH * 128], BF16) for t2 in range(NTT)]
        PB = [sb1("pb0", [128, CH], BF16), sb1("pb1", [128, CH], BF16)]
        REPS = sb1("reps", [128, CH])
        psS = [ps1("psS0", [128, CH]), ps1("psS1", [128, CH])]
        psO = ps1("psO", [128, CH])
        psDen = ps1("psDen", [1, CH])
        psRep = ps1("psRep", [128, CH])

        for g in range(NHB // GH):
            hbase = GH * g
            gsl = slice(128 * GH * g, 128 * GH * (g + 1))
            for d in range(DP):
                rs = slice(128 * d, 128 * (d + 1))
                tq = p.dma("gpsimd", WQ[d][:], wq_d.ap()[rs, gsl], sem="wqg", w=[WQ[d]])
                tk = p.dma("gpsimd", WK[d][:], wk_d.ap()[rs, gsl], sem="wkg", w=[WK[d]])
                tv = p.dma("gpsimd", WV[d][:], wv_d.ap()[rs, gsl], sem="wvg", w=[WV[d]])
            p.retag_writers(WQ, tq)
            p.retag_writers(WK, tk)
            p.retag_writers(WV, tv)
            for d in range(DP):
                _emit_rot_weights(p, WQR[d], WQ[d], GH, 128, 32)
                _emit_rot_weights(p, WKR[d], WK[d], GH, 128, 32)
            for h in range(GH):
                hsl = slice(128 * h, 128 * (h + 1))
                for qc in range(NCH):
                    tsl = slice(CH * qc, CH * (qc + 1))
                    for (psq, Wm, Wr, dstH) in ((psS[0], WQ, WQR, QH), (psS[1], WK, WKR, KH)):
                        for d in range(DP):
                            p.op("tensor", mm(psq[:], Wm[d][:, hsl], Y[d][:, tsl],
                                              d == 0, d == DP - 1),
                                 r=[Wm[d], Y[d]], w=[psq] if d in (0, DP - 1) else ())
                        for d in range(DP):
                            p.op("tensor", mm(psO[:], Wr[d][:, hsl], Y[d][:, tsl],
                                              d == 0, d == DP - 1),
                                 r=[Wr[d], Y[d]], w=[psO] if d in (0, DP - 1) else ())
                        p.op("vector", tt(c.SQS[0][:], psq[:], CB[:, tsl], ALU.mult),
                             r=[psq, CB], w=[c.SQS[0]])
                        p.op("vector", tt(c.SQS[1][:], psO[:], SBC[:, tsl], ALU.mult),
                             r=[psO, SBC], w=[c.SQS[1]])
                        p.op("vector", tt(dstH[h][:, tsl], c.SQS[0][:], c.SQS[1][:], ALU.add),
                             r=[c.SQS[0], c.SQS[1]], w=[f"{_key(dstH[h])}@{qc}"])
            for t2 in range(NTT):
                ssl = slice(128 * t2, 128 * (t2 + 1))
                for d in range(DP):
                    p.op("tensor", mm(psS[0][:, 0:128 * GH], Y[d][:, ssl], WV[d][:],
                                      d == 0, d == DP - 1),
                         r=[Y[d], WV[d]], w=[psS[0]] if d in (0, DP - 1) else ())
                p.op("scalar", scp(VT[t2][:], psS[0][:, 0:128 * GH]), r=[psS[0]], w=[VT[t2]])
            for h in range(GH):
                hg = hbase + h
                for qc in range(NCH):
                    tsl = slice(CH * qc, CH * (qc + 1))
                    qkeys = [f"{_key(QH[h])}@{qc}"]
                    nkt = 4 * qc + 4
                    for kt in range(nkt):
                        k2 = kt % 2
                        ksl = slice(128 * kt, 128 * (kt + 1))
                        p.op("tensor", mm(psS[k2][:], KH[h][:, ksl], QH[h][:, tsl],
                                          True, True),
                             r=[f"{_key(KH[h])}@{j}" for j in range(NCH)] + qkeys,
                             w=[psS[k2]])
                        r_rel = kt - 4 * qc
                        if r_rel >= 0:
                            p.op("vector", tt(c.SQS[k2][:], psS[k2][:],
                                              MSK[:, CH * r_rel:CH * (r_rel + 1)], ALU.add),
                                 r=[psS[k2], MSK], w=[c.SQS[k2]])
                            src = c.SQS[k2][:]
                            srck = [c.SQS[k2]]
                        else:
                            src = psS[k2][:]
                            srck = [psS[k2]]
                        p.op("scalar", act(PB[k2][:], src, AF.Exp, scale=1.0 / 128.0),
                             r=srck, w=[PB[k2]])
                        p.op("tensor", mm(psO[:], VT[kt][:, 128 * h:128 * (h + 1)], PB[k2][:],
                                          kt == 0, kt == nkt - 1),
                             r=[VT[kt], PB[k2]], w=[psO] if kt in (0, nkt - 1) else ())
                        p.op("tensor", mm(psDen[0:1, :], c.ones_col_bf[:], PB[k2][:],
                                          kt == 0, kt == nkt - 1),
                             r=[c.ones_col_bf, PB[k2]], w=[psDen] if kt in (0, nkt - 1) else ())
                    p.op("vector", recip(c.R0[:, 0:CH], psDen[0:1, :]), r=[psDen], w=[c.R0])
                    p.op("tensor", mm(psRep[:], c.ones_row, c.R0[:, 0:CH], True, True),
                         r=[c._ones_rows, c.R0], w=[psRep])
                    p.op("scalar", scp(REPS[:], psRep[:]), r=[psRep], w=[REPS])
                    p.op("vector", tt(OSTG[hg % 2][:, tsl], psO[:], REPS[:], ALU.mult),
                         r=[psO, REPS], w=[f"ostg{hg % 2}@{qc}"])
                p.dma("gpsimd", o_scr.ap()[128 * hg:128 * (hg + 1), :], OSTG[hg % 2][:],
                      sem=f"osp{hg % 2}",
                      r=[f"ostg{hg % 2}@{j}" for j in range(NCH)], w=["o_scr"])
        p.barrier()
        s1.close()

        s2 = ExitStack()
        sb2 = lambda n, s, d=F32: s2.enter_context(nc.sbuf_tensor(uid + n, s, d))
        ps2 = lambda n, s, d=F32: s2.enter_context(nc.psum_tensor(uid + n, s, d))
        OIN = [sb2(f"oin{ot}", [128, T], BF16) for ot in range(DP)]
        WO = [sb2(f"wob{ot}", [128, D], BF16) for ot in range(DP)]
        psD = [ps2("psD0", [128, CH]), ps2("psD1", [128, CH])]
        for i in range(DP):
            to = p.dma("scalar", OIN[i][:], o_scr.ap()[128 * i:128 * (i + 1), :], sem="oing",
                       r=["o_scr"], w=[OIN[i]])
        p.retag_writers(OIN, to)
        for ot in range(DP):
            tw = p.dma("gpsimd", WO[ot][:], wo_d.ap()[128 * ot:128 * (ot + 1), :], sem="wog",
                       w=[WO[ot]])
        p.retag_writers(WO, tw)
        for j in range(DP):
            jsl = slice(128 * j, 128 * (j + 1))
            for tc in range(NCH):
                tsl = slice(CH * tc, CH * (tc + 1))
                k = (j * NCH + tc) % 2
                for ot in range(DP):
                    p.op("tensor", mm(psD[k][:], WO[ot][:, jsl], OIN[ot][:, tsl],
                                      ot == 0, ot == DP - 1),
                         r=[WO[ot], OIN[ot]], w=[psD[k]] if ot in (0, DP - 1) else ())
                p.op("vector", tt(X[j][:, tsl], X[j][:, tsl], psD[k][:], ALU.add),
                     r=[psD[k]], w=[X[j]])
        s2.close()
    p.barrier()


def build_prog_b():
    nc = bass.Bass("TRN2", target_bir_lowering=False)
    x_in = nc.dram_tensor("x_in", [D, T], F32, kind="ExternalInput")
    dr = {
        "b_wq": nc.dram_tensor("wq", [D, D], F32, kind="ExternalInput"),
        "b_wk": nc.dram_tensor("wk", [D, D], F32, kind="ExternalInput"),
        "b_wv": nc.dram_tensor("wv", [D, D], F32, kind="ExternalInput"),
        "b_wo": nc.dram_tensor("wo", [D, D], F32, kind="ExternalInput"),
        "b_ropeC": nc.dram_tensor("ropeC", [128, T], BF16, kind="ExternalInput"),
        "b_ropeS": nc.dram_tensor("ropeS", [128, T], BF16, kind="ExternalInput"),
        "b_dmask": nc.dram_tensor("dmask", [128, 4 * CH], BF16, kind="ExternalInput"),
        "o_scr": nc.dram_tensor("o_scr", [D, T], BF16, kind="Internal"),
    }
    x_out = nc.dram_tensor("x_out", [D, T], F32, kind="ExternalOutput")
    stack = ExitStack()
    with stack:
        sb = lambda n, s, d=F32: stack.enter_context(nc.sbuf_tensor(n, s, d))
        p = Prog(nc, stack)
        c = Com(nc, stack)
        X = [sb(f"x{i}", [128, T]) for i in range(DP)]
        c.init(p)
        for i in range(DP):
            tx = p.dma("sync", X[i][:], x_in.ap()[128 * i:128 * (i + 1), :], sem="xing", w=[X[i]])
        p.retag_writers(X, tx)
        emit_b(p, c, nc, X, dr)
        for j in range(DP):
            p.dma("sync", x_out.ap()[128 * j:128 * (j + 1), :], X[j][:], sem="xout", r=[X[j]])
        p.finish()
        p.emit()
    return nc


def get_prog(name):
    if name not in _PROG_CACHE:
        _PROG_CACHE[name] = _BUILDERS[name]()
    return _PROG_CACHE[name]



def _prep_l0_attn(inputs):
    wq = np.ascontiguousarray(np.transpose(inputs["L0_attn_wq"], (1, 0, 2)).reshape(D, D))
    wk = np.ascontiguousarray(np.transpose(inputs["L0_attn_wk"], (1, 0, 2)).reshape(D, D))
    wv = np.ascontiguousarray(np.transpose(inputs["L0_attn_wv"], (1, 0, 2)).reshape(D, D))
    wo = np.ascontiguousarray(inputs["L0_attn_wo"])
    t = np.arange(T, dtype=np.float64)[None, :]
    i = np.arange(32, dtype=np.float64)[:, None]
    rad = t * np.power(10000.0, -i / 32.0)
    cosr = np.cos(rad).astype(np.float32)
    sinr = np.sin(rad).astype(np.float32)
    ropeC = np.concatenate([cosr, cosr, np.ones((64, T), np.float32)], 0).astype(BF)
    ropeS = np.concatenate([sinr, sinr, np.zeros((64, T), np.float32)], 0).astype(BF)
    rk = np.arange(128)[:, None]
    cq = np.arange(CH)[None, :]
    dmask = np.concatenate(
        [np.where(128 * rr + rk <= cq, 0.0, -1e30).astype(np.float32) for rr in range(4)],
        axis=1).astype(BF)
    return {"wq": wq, "wk": wk, "wv": wv, "wo": wo,
            "ropeC": np.ascontiguousarray(ropeC), "ropeS": np.ascontiguousarray(ropeS),
            "dmask": np.ascontiguousarray(dmask)}


def _prep_l0_ffn_attn(inputs):
    F = np.asarray(inputs["L0_ffn_freqs_cis"])
    idx = np.repeat(np.arange(0, 64, 2), 2)
    crow = F[:, idx].T
    srow = F[:, idx + 1].T
    ropeC = np.tile(crow, (2, 1)).astype(BF)
    ropeS = np.tile(srow, (2, 1)).astype(BF)
    rk = np.arange(128)[:, None]
    cq = np.arange(CH)[None, :]
    dmask = np.concatenate(
        [np.where(128 * rr + rk <= cq, 0.0, -1e30).astype(np.float32) for rr in range(4)],
        axis=1).astype(BF)
    return {
        "wq": np.ascontiguousarray(inputs["L0_ffn_wq"]),
        "wk": np.ascontiguousarray(inputs["L0_ffn_wk"]),
        "wv": np.ascontiguousarray(inputs["L0_ffn_wv"]),
        "wo": np.ascontiguousarray(inputs["L0_ffn_wo"]),
        "anw": np.ascontiguousarray(np.asarray(inputs["L0_ffn_attn_norm_w"]).reshape(DP, 128).T),
        "ropeC": np.ascontiguousarray(ropeC),
        "ropeS": np.ascontiguousarray(ropeS),
        "dmask": np.ascontiguousarray(dmask),
    }


def run_a1(xs, W):
    nc = get_prog("a1")
    in_maps = [dict(x_in=xs[b], **W) for b in range(8)]
    res = run_bass_kernel_spmd(nc, in_maps, core_ids=list(range(8)))
    return ([np.asarray(r["z_out"]) for r in res.results],
            [np.asarray(r["h_out"]) for r in res.results])


def run_b(xs, W):
    nc = get_prog("b")
    in_maps = [dict(x_in=xs[b], **W) for b in range(8)]
    res = run_bass_kernel_spmd(nc, in_maps, core_ids=list(range(8)))
    return [np.asarray(r["x_out"]) for r in res.results]


def _prep_l1_ffn(inputs):
    return {
        "wg": np.ascontiguousarray(inputs["L1_ffn_w_gate"].T),
        "wu": np.ascontiguousarray(inputs["L1_ffn_w_up"].T),
        "wd": np.ascontiguousarray(inputs["L1_ffn_w_down"].T),
    }


def run_d(xs, W):
    nc = get_prog("d")
    in_maps = [dict(x_in=xs[b], **W) for b in range(8)]
    res = run_bass_kernel_spmd(nc, in_maps, core_ids=list(range(8)))
    return [np.asarray(r["x_out"]) for r in res.results]




def build_mega():
    nc = bass.Bass("TRN2", target_bir_lowering=False)
    x_in = nc.dram_tensor("x_in", [D, T], F32, kind="ExternalInput")
    x_out = nc.dram_tensor("x_out", [D, T], F32, kind="ExternalOutput")
    dr = {}
    for nm, shape, dt in [
        ("a1_wq", [D, D], F32), ("a1_wk", [D, D], F32), ("a1_wv", [D, D], F32),
        ("a1_wo", [D, D], F32), ("a1_anw", [128, DP], F32),
        ("a1_ropeC", [128, T], BF16), ("a1_ropeS", [128, T], BF16),
        ("a1_dmask", [128, 4 * CH], BF16),
        ("a2_fnw", [128, DP], F32), ("a2_rw", [D, 8], F32),
        ("a2_w1", [8, D, 2048], F32), ("a2_w3", [8, D, 2048], F32),
        ("a2_w2", [8, 2048, D], F32), ("a2_ident", [128, 128], F32),
        ("a2_sel8", [8, D], F32),
        ("b_wq", [D, D], F32), ("b_wk", [D, D], F32), ("b_wv", [D, D], F32),
        ("b_wo", [D, D], F32), ("b_ropeC", [128, T], BF16),
        ("b_ropeS", [128, T], BF16), ("b_dmask", [128, 4 * CH], BF16),
        ("d_wg", [D, 3072], I32), ("d_wu", [D, 3072], I32), ("d_wd", [3072, D], I32),
        ("e_wq", [D, D], F32), ("e_wk", [D, D], F32), ("e_wv", [D, D], F32),
        ("e_wqf", [64, 16], F32), ("e_wkf", [64, 16], F32),
        ("e_wout", [2 * D, D], F32), ("e_nw", [128, DP], F32),
        ("e_gha", [33, 153], BF16), ("e_ghb", [33, 153], BF16),
        ("e_triu", [128, 128], F32), ("e_wmask", [128, 256], BF16),
    ]:
        dr[nm] = nc.dram_tensor(nm, shape, dt, kind="ExternalInput")
    for nm, shape, dt in [
        ("o_scr", [D, T], BF16), ("y_scr", [D, T], BF16), ("h_scr", [D, T], BF16),
        ("g_scr", [3072, T], BF16), ("lin_scr", [D, T], BF16), ("win_scr", [D, T], BF16),
    ]:
        dr[nm] = nc.dram_tensor(nm, shape, dt, kind="Internal")

    stack = ExitStack()
    with stack:
        sb = lambda n, s, d=F32: stack.enter_context(nc.sbuf_tensor(n, s, d))
        p = Prog(nc, stack)
        c = Com(nc, stack)
        X = [sb(f"x{i}", [128, T]) for i in range(DP)]
        c.init(p)
        for i in range(DP):
            tx = p.dma("sync", X[i][:], x_in.ap()[128 * i:128 * (i + 1), :], sem="xing", w=[X[i]])
        p.retag_writers(X, tx)
        emit_a1(p, c, nc, X, dr, uid="s1_")
        emit_a2(p, c, nc, X, dr, uid="s2_")
        emit_b(p, c, nc, X, dr, uid="s3_")
        emit_a1(p, c, nc, X, dr, uid="s4_")
        emit_a2(p, c, nc, X, dr, uid="s5_")
        emit_d(p, c, nc, X, dr, uid="s6_")
        emit_e(p, c, nc, X, dr, uid="s7_")
        emit_d(p, c, nc, X, dr, uid="s8_")
        for j in range(DP):
            p.dma("sync", x_out.ap()[128 * j:128 * (j + 1), :], X[j][:], sem="xout", r=[X[j]])
        p.finish()
        p.emit()
    return nc


def _mega_inmap(inputs):
    W_a1 = _prep_l0_ffn_attn(inputs)
    W_a2 = _prep_l0_ffn_moe(inputs)
    W_b = _prep_l0_attn(inputs)
    W_d = _prep_l1_ffn(inputs)
    W_e = _prep_l1_attn(inputs)
    m = {}
    for pre, Wd in (("a1_", W_a1), ("a2_", W_a2), ("b_", W_b), ("d_", W_d), ("e_", W_e)):
        for k, v in Wd.items():
            m[pre + k] = v
    return m


FUSED = True


def kernel(**inputs):
    inputs = {k: np.asarray(v) for k, v in inputs.items()}
    xs = [np.ascontiguousarray(inputs["L0_attn_x"][b].T) for b in range(8)]
    if FUSED:
        base = _mega_inmap(inputs)
        nc = get_prog("mega")
        in_maps = [dict(x_in=xs[b], **base) for b in range(8)]
        res = run_bass_kernel_spmd(nc, in_maps, core_ids=list(range(8)))
        return np.stack([np.asarray(r["x_out"]).T for r in res.results]).astype(np.float32)
    W_a1 = _prep_l0_ffn_attn(inputs)
    W_a2 = _prep_l0_ffn_moe(inputs)
    W_b = _prep_l0_attn(inputs)
    W_d = _prep_l1_ffn(inputs)
    W_e = _prep_l1_attn(inputs)
    zs, hs = run_a1(xs, W_a1)
    xs = run_a2(hs, zs, W_a2)
    xs = run_b(xs, W_b)
    zs, hs = run_a1(xs, W_a1)
    xs = run_a2(hs, zs, W_a2)
    xs = run_d(xs, W_d)
    xs = run_e(xs, W_e)
    xs = run_d(xs, W_d)
    return np.stack([x.T for x in xs]).astype(np.float32)


_BUILDERS = {"d": build_prog_d, "b": build_prog_b, "a1": build_prog_a1, "a2": build_prog_a2, "e": build_prog_e, "mega": build_mega}


# version 04
# speedup vs baseline: 1.2358x; 1.0405x over previous
import numpy as np
import ml_dtypes
from contextlib import ExitStack

import concourse.bass as bass
import concourse.mybir as mybir
from concourse.bass_utils import run_bass_kernel_spmd

F32 = mybir.dt.float32
BF16 = mybir.dt.bfloat16
I32 = mybir.dt.int32
AX = mybir.AxisListType.X
AF = mybir.ActivationFunctionType
ALU = mybir.AluOpType
BF = ml_dtypes.bfloat16

D = 1024
DP = 8
T = 2048
NCH = 4
CH = 512
NTT = 16


def _key(x):
    if isinstance(x, str):
        return x
    if hasattr(x, "name"):
        return x.name
    if hasattr(x, "tensor"):
        return x.tensor.name
    raise TypeError(f"bad key {x!r}")


_NEEDED = None


class Prog:
    ENGINES = ("tensor", "vector", "scalar", "gpsimd", "sync")
    last = None

    def __init__(self, nc, stack):
        self.nc = nc
        self.stack = stack
        self.ops = {e: [] for e in self.ENGINES}
        self.sems = {}
        self.cnt = {}
        self.waited = {e: {} for e in self.ENGINES}
        for e in self.ENGINES:
            self._mksem("p_" + e)
        self.dma_sems = set()
        self.last_w = {}
        self.readers = {}
        self.needed = _NEEDED
        self.attached = {}
        self.emitted_waits = {}
        Prog.last = self

    def _mksem(self, name):
        self.sems[name] = self.stack.enter_context(self.nc.semaphore(name))
        self.cnt[name] = 0

    def _track_pre(self, rk, wk):
        auto = []
        for k in rk:
            t = self.last_w.get(k)
            if t:
                auto.append(t)
        for k in wk:
            t = self.last_w.get(k)
            if t:
                auto.append(t)
            auto.extend(self.readers.get(k, {}).values())
        return auto

    def _track_post(self, rk, wk, slot, tok):
        for k in wk:
            self.last_w[k] = tok
            self.readers[k] = {}
        for k in rk:
            if k not in wk:
                self.readers.setdefault(k, {})[slot] = tok

    def op(self, eng, fn, waits=(), r=(), w=(), sig=False):
        rk = [_key(x) for x in r]
        wk = [_key(x) for x in w]
        auto = self._track_pre(rk, wk)
        for t in waits:
            self._wait(eng, t)
        for t in auto:
            self._wait(eng, t)
        need_sig = sig or bool(rk) or bool(wk)
        if need_sig:
            sname = "p_" + eng
            self.cnt[sname] += 1
            v = self.cnt[sname]
            tok = (sname, v)
            if self.needed is None or v in self.needed.get(sname, ()):
                delta = v - self.attached.get(sname, 0)
                self.attached[sname] = v
                sh = self.sems[sname]
                self.ops[eng].append(
                    lambda E, fn=fn, sh=sh, delta=delta: fn(E).then_inc(sh, delta))
            else:
                self.ops[eng].append(fn)
        else:
            self.ops[eng].append(fn)
            tok = None
        if tok:
            self._track_post(rk, wk, eng, tok)
        return tok

    def dma(self, q, out, in_, sem, waits=(), r=(), w=()):
        rk = [_key(x) for x in r]
        wk = [_key(x) for x in w]
        auto = self._track_pre(rk, wk)
        for t in waits:
            self._wait(q, t)
        for t in auto:
            self._wait(q, t)
        sname = "dma_" + sem
        if sname not in self.sems:
            self._mksem(sname)
            self.dma_sems.add(sname)
        self.cnt[sname] += 16
        v = self.cnt[sname]
        sh = self.sems[sname]
        self.ops[q].append(
            lambda E, out=out, in_=in_, sh=sh: E.dma_start(out=out, in_=in_).then_inc(sh, 16)
        )
        tok = (sname, v)
        self._track_post(rk, wk, sname, tok)
        return tok

    def retag_writers(self, keys, tok):
        for k in keys:
            self.last_w[_key(k)] = tok

    def _wait(self, eng, tok):
        if tok is None:
            return
        sname, v = tok
        if self.waited[eng].get(sname, 0) >= v:
            return
        self.waited[eng][sname] = v
        self.emitted_waits.setdefault(sname, set()).add(v)
        sh = self.sems[sname]
        self.ops[eng].append(lambda E, sh=sh, v=v: E.wait_ge(sh, v))

    def barrier(self):
        toks = [("p_" + e, self.cnt["p_" + e]) for e in self.ENGINES if self.cnt["p_" + e]]
        for s in sorted(self.dma_sems):
            if self.cnt[s]:
                toks.append((s, self.cnt[s]))
        for e in self.ENGINES:
            for t in toks:
                self._wait(e, t)

    def finish(self):
        for s in sorted(self.dma_sems):
            if self.cnt[s]:
                sh = self.sems[s]
                v = self.cnt[s]
                if self.waited["sync"].get(s, 0) < v:
                    self.waited["sync"][s] = v
                    self.ops["sync"].append(lambda E, sh=sh, v=v: E.wait_ge(sh, v))

    def emit(self):
        block = self.stack.enter_context(self.nc.Block())
        for e in self.ENGINES:
            ops = list(self.ops[e])

            def runner(E, ops=ops):
                for f in ops:
                    f(E)

            getattr(block, e)(runner)



def mm(out, lhsT, rhs, start, stop):
    return lambda E: E.matmul(out, lhsT, rhs, start=start, stop=stop)

def act(out, in_, func, bias=0.0, scale=1.0):
    return lambda E: E.activation(out, in_, func, bias=bias, scale=scale)

def tt(out, a, b, op):
    return lambda E: E.tensor_tensor(out, a, b, op)

def ts(out, a, s, op):
    return lambda E: E.tensor_single_scalar(out, a, s, op)

def stt(out, in0, scalar, in1, op0, op1):
    return lambda E: E.scalar_tensor_tensor(out, in0, scalar, in1, op0, op1)

def cp(out, in_):
    return lambda E: E.tensor_copy(out, in_)

def scp(out, in_):
    return lambda E: E.copy(out, in_)

def smul(out, in_, m):
    return lambda E: E.mul(out, in_, m)

def memset(ap, v):
    return lambda E: E.memset(ap, v)

def recip(out, in_):
    return lambda E: E.reciprocal(out, in_)

def rmax(out, in_, negate=False):
    return lambda E: E.reduce_max(out, in_, axis=AX, negate=negate)

def rsum(out, in_):
    return lambda E: E.reduce_sum(out, in_, axis=AX)

def max8(out, in_):
    return lambda E: E.max(out, in_)

def transp(out, in_, ident):
    return lambda E: E.transpose(out, in_, ident)



class Com:
    def __init__(self, nc, stack):
        sb = lambda n, s, d=F32: stack.enter_context(nc.sbuf_tensor(n, s, d))
        self.SCR_A = sb("com_scr_a", [128, T])
        self.SCR_B = sb("com_scr_b", [128, T])
        self.TMP = sb("com_tmp0", [128, T])
        self.R0 = self.TMP[0:1, :]
        self.R1 = self.SCR_B[0:1, :]
        self.SQS = [sb("com_sqs0", [128, CH]), sb("com_sqs1", [128, CH])]
        self.ones_col = sb("com_ones_col", [128, 1])
        self.ones_col_bf = sb("com_ones_col_bf", [128, 1], BF16)
        self._ones_rows = sb("com_ones_row", [65, 128])
        self.ones_row = self._ones_rows[0:1, :]
        self._cvals = sb("com_cvals", [128, 4])

    def cval(self, idx, base=0, n=1):
        return self._cvals[base:base + n, idx:idx + 1]

    def init(self, p):
        p.op("vector", memset(self.ones_col[:], 1.0), w=[self.ones_col])
        p.op("vector", memset(self.ones_col_bf[:], 1.0), w=[self.ones_col_bf])
        p.op("vector", memset(self._ones_rows[:], 1.0), w=[self._ones_rows])
        p.op("vector", memset(self._cvals[:, 0:1], 1e-5), w=["cval0"])
        p.op("vector", memset(self._cvals[:, 1:2], 1e-6), w=["cval1"])



def emit_ln(p, c, X, Y, psA, psB, psR, fold=None, eps=1e-5):
    for ci in range(NCH):
        sl = slice(ci * CH, (ci + 1) * CH)
        for i in range(DP):
            p.op("tensor", mm(psA[0:1, :], c.ones_col[:], X[i][:, sl], i == 0, i == DP - 1),
                 r=[c.ones_col, X[i]], w=[psA] if i in (0, DP - 1) else ())
        for i in range(DP):
            k = i % 2
            p.op("scalar", act(c.SQS[k][:], X[i][:, sl], AF.Square), r=[X[i]], w=[c.SQS[k]])
            p.op("tensor", mm(psB[0:1, :], c.ones_col[:], c.SQS[k][:], i == 0, i == DP - 1),
                 r=[c.ones_col, c.SQS[k]], w=[psB] if i in (0, DP - 1) else ())
        p.op("scalar", smul(c.R0[:, sl], psA[0:1, :], 1.0 / D), r=[psA], w=[c.R0])
        p.op("scalar", smul(c.R1[:, sl], psB[0:1, :], 1.0 / D), r=[psB], w=[c.R1])

    r_mean = c.R0[:]
    r_msq = c.R1[:]
    for ci in range(NCH):
        sl = slice(ci * CH, (ci + 1) * CH)
        p.op("scalar", act(psA[0:1, :], r_mean[:, sl], AF.Square), r=[c.R0], w=[psA])
        p.op("vector", tt(r_msq[:, sl], r_msq[:, sl], psA[0:1, :], ALU.subtract),
             r=[psA, c.R1], w=[c.R1])
    eps_idx = 0 if eps == 1e-5 else 1
    p.op("scalar", act(r_msq, r_msq, AF.Sqrt, bias=c.cval(eps_idx)),
         r=["cval0" if eps_idx == 0 else "cval1", c.R1], w=[c.R1])
    p.op("vector", recip(r_msq, r_msq), r=[c.R1], w=[c.R1])
    p.op("vector", tt(r_mean, r_mean, r_msq, ALU.mult), r=[c.R0, c.R1], w=[c.R0])

    for ci in range(NCH):
        sl = slice(ci * CH, (ci + 1) * CH)
        k = ci % 2
        p.op("tensor", mm(psR[k][:], c.ones_row, r_msq[:, sl], True, True),
             r=[c._ones_rows, c.R1], w=[psR[k]])
        p.op("scalar", scp(c.SCR_A[:, sl], psR[k][:]), r=[psR[k]], w=[c.SCR_A])
    for ci in range(NCH):
        sl = slice(ci * CH, (ci + 1) * CH)
        k = ci % 2
        p.op("tensor", mm(psR[k][:], c.ones_row, r_mean[:, sl], True, True),
             r=[c._ones_rows, c.R0], w=[psR[k]])
        p.op("scalar", scp(c.SCR_B[:, sl], psR[k][:]), r=[psR[k]], w=[c.SCR_B])

    y_toks = [None] * DP
    z_toks = [None] * DP
    for i in range(DP):
        p.op("gpsimd", tt(c.TMP[:], X[i][:], c.SCR_A[:], ALU.mult),
             r=[X[i], c.SCR_A], w=[c.TMP])
        y_toks[i] = p.op("vector", tt(Y[i][:], c.TMP[:], c.SCR_B[:], ALU.subtract),
                         r=[c.TMP, c.SCR_B], w=[Y[i]])
        if fold is not None:
            z_toks[i] = p.op("vector",
                             stt(X[i][:], Y[i][:], float(fold), X[i][:], ALU.mult, ALU.add),
                             r=[Y[i]], w=[X[i]])
    return y_toks, (z_toks if fold is not None else None)



def emit_rms(p, c, IN, psB, psR):
    for ci in range(NCH):
        sl = slice(ci * CH, (ci + 1) * CH)
        for i in range(DP):
            k = i % 2
            p.op("scalar", act(c.SQS[k][:], IN[i][:, sl], AF.Square), r=[IN[i]], w=[c.SQS[k]])
            p.op("tensor", mm(psB[0:1, :], c.ones_col[:], c.SQS[k][:], i == 0, i == DP - 1),
                 r=[c.ones_col, c.SQS[k]], w=[psB] if i in (0, DP - 1) else ())
        p.op("scalar", smul(c.R1[:, sl], psB[0:1, :], 1.0 / D), r=[psB], w=[c.R1])
    p.op("scalar", act(c.R1[:], c.R1[:], AF.Sqrt, bias=c.cval(1)),
         r=["cval1", c.R1], w=[c.R1])
    p.op("vector", recip(c.R1[:], c.R1[:]), r=[c.R1], w=[c.R1])
    for ci in range(NCH):
        sl = slice(ci * CH, (ci + 1) * CH)
        k = ci % 2
        p.op("tensor", mm(psR[k][:], c.ones_row, c.R1[:, sl], True, True),
             r=[c._ones_rows, c.R1], w=[psR[k]])
        p.op("scalar", scp(c.SCR_A[:, sl], psR[k][:]), r=[psR[k]], w=[c.SCR_A])



def emit_a1(p, c, nc, X, dr, uid=""):
    wq_d = dr["a1_wq"]; wk_d = dr["a1_wk"]; wv_d = dr["a1_wv"]; wo_d = dr["a1_wo"]
    anw_d = dr["a1_anw"]; cb_d = dr["a1_ropeC"]; sb_d = dr["a1_ropeS"]
    mask_d = dr["a1_dmask"]
    o_scr = dr["o_scr"]; y_scr = dr["y_scr"]; h_out = dr["h_scr"]
    GH = 4
    with ExitStack() as ss:
        sbS = lambda n, s, d=F32: ss.enter_context(nc.sbuf_tensor(uid + n, s, d))
        X1 = [sbS(f"x1_{i}", [128, T], BF16) for i in range(DP)]
        ANW = sbS("anw_sb", [128, DP])

        s0 = ExitStack()
        sb0 = lambda n, s, d=F32: s0.enter_context(nc.sbuf_tensor(uid + n, s, d))
        ps0 = lambda n, s, d=F32: s0.enter_context(nc.psum_tensor(uid + n, s, d))
        Y = [sb0(f"y{i}", [128, T], BF16) for i in range(DP)]
        psA = ps0("psA", [1, CH])
        psB = ps0("psB", [1, CH])
        psR = [ps0("psR0", [128, CH]), ps0("psR1", [128, CH])]
        p.dma("scalar", ANW[:], anw_d.ap(), sem="anwin", w=[ANW])
        emit_ln(p, c, X, Y, psA, psB, psR, fold=0.5)
        emit_rms(p, c, Y, psB, psR)
        for i in range(DP):
            p.op("vector", stt(X1[i][:], Y[i][:], ANW[:, i:i + 1], c.SCR_A[:],
                               ALU.mult, ALU.mult),
                 r=[Y[i], ANW, c.SCR_A], w=[X1[i]])
            p.dma("gpsimd", y_scr.ap()[128 * i:128 * (i + 1), :], Y[i][:],
                  sem=f"ysp{i % 2}", r=[Y[i]], w=["y_scr"])
        p.barrier()
        s0.close()

        s1 = ExitStack()
        sb1 = lambda n, s, d=F32: s1.enter_context(nc.sbuf_tensor(uid + n, s, d))
        ps1 = lambda n, s, d=F32: s1.enter_context(nc.psum_tensor(uid + n, s, d))
        CB = sb1("cb", [128, T], BF16)
        SBC = sb1("sbc", [128, T], BF16)
        MSK = sb1("msk", [128, 4 * CH], BF16)
        OSTG = [sb1("ostg0", [128, T], BF16), sb1("ostg1", [128, T], BF16)]
        p.dma("sync", CB[:], cb_d.ap(), sem="cbin", w=[CB])
        p.dma("sync", SBC[:], sb_d.ap(), sem="sbin", w=[SBC])
        p.dma("sync", MSK[:], mask_d.ap(), sem="mkin", w=[MSK])
        GW = GH * 64
        WQ = [sb1(f"wqb{d}", [128, GW], BF16) for d in range(DP)]
        WQR = [sb1(f"wqrb{d}", [128, GW], BF16) for d in range(DP)]
        WK = [sb1(f"wkb{d}", [128, GW], BF16) for d in range(DP)]
        WKR = [sb1(f"wkrb{d}", [128, GW], BF16) for d in range(DP)]
        WV = [sb1(f"wvb{d}", [128, GW], BF16) for d in range(DP)]
        QH = [sb1(f"qh{j}", [128, T], BF16) for j in range(GH // 2)]
        KH = [sb1(f"kh{j}", [128, T], BF16) for j in range(GH // 2)]
        VT = [sb1(f"vt{t2}", [128, GW], BF16) for t2 in range(NTT)]
        PB = [sb1("pb0", [128, CH], BF16), sb1("pb1", [128, CH], BF16)]
        REPS = sb1("reps", [128, CH])
        NEGBC = sb1("negbc", [128, 1])
        psS = [ps1("psS0", [128, CH]), ps1("psS1", [128, CH])]
        psO = ps1("psO", [64, CH])
        psDen = ps1("psDen", [1, CH])
        psRep = ps1("psRep", [128, CH])

        for g in range(4):
            gsl = slice(GW * g, GW * (g + 1))
            for d in range(DP):
                rs = slice(128 * d, 128 * (d + 1))
                tq = p.dma("gpsimd", WQ[d][:], wq_d.ap()[rs, gsl], sem="wqg", w=[WQ[d]])
                tk = p.dma("gpsimd", WK[d][:], wk_d.ap()[rs, gsl], sem="wkg", w=[WK[d]])
                tv = p.dma("gpsimd", WV[d][:], wv_d.ap()[rs, gsl], sem="wvg", w=[WV[d]])
            p.retag_writers(WQ, tq)
            p.retag_writers(WK, tk)
            p.retag_writers(WV, tv)
            for d in range(DP):
                _emit_rot_weights_il(p, WQR[d], WQ[d], GH, 64)
                _emit_rot_weights_il(p, WKR[d], WK[d], GH, 64)
            for j in range(GH // 2):
                hsl = slice(128 * j, 128 * (j + 1))
                for qc in range(NCH):
                    tsl = slice(CH * qc, CH * (qc + 1))
                    for (psq, Wm, Wr, dstH) in ((psS[0], WQ, WQR, QH), (psS[1], WK, WKR, KH)):
                        for d in range(DP):
                            p.op("tensor", mm(psq[:], Wm[d][:, hsl], X1[d][:, tsl],
                                              d == 0, d == DP - 1),
                                 r=[Wm[d], X1[d]], w=[psq] if d in (0, DP - 1) else ())
                        for d in range(DP):
                            p.op("tensor", mm(psRep[:], Wr[d][:, hsl], X1[d][:, tsl],
                                              d == 0, d == DP - 1),
                                 r=[Wr[d], X1[d]], w=[psRep] if d in (0, DP - 1) else ())
                        p.op("vector", tt(c.SQS[0][:], psq[:], CB[:, tsl], ALU.mult),
                             r=[psq, CB], w=[c.SQS[0]])
                        p.op("vector", tt(c.SQS[1][:], psRep[:], SBC[:, tsl], ALU.mult),
                             r=[psRep, SBC], w=[c.SQS[1]])
                        p.op("vector", tt(dstH[j][:, tsl], c.SQS[0][:], c.SQS[1][:], ALU.add),
                             r=[c.SQS[0], c.SQS[1]], w=[f"{_key(dstH[j])}@{qc}"])
            for t2 in range(NTT):
                ssl = slice(128 * t2, 128 * (t2 + 1))
                for d in range(DP):
                    p.op("tensor", mm(psS[0][:, 0:GW], X1[d][:, ssl], WV[d][:],
                                      d == 0, d == DP - 1),
                         r=[X1[d], WV[d]], w=[psS[0]] if d in (0, DP - 1) else ())
                p.op("scalar", scp(VT[t2][:], psS[0][:, 0:GW]), r=[psS[0]], w=[VT[t2]])
            for hh in range(GH):
                hg = GH * g + hh
                j = hh // 2
                half = hh % 2
                rows = slice(64 * half, 64 * (half + 1))
                for (HT, o0) in ((QH[j], 0), (KH[j], 4)):
                    p.op("scalar", act(c.SCR_B[:], HT[:], AF.Square),
                         r=[f"{_key(HT)}@{q2}" for q2 in range(NCH)], w=[c.SCR_B])
                    for ci in range(NCH):
                        sl2 = slice(ci * CH, (ci + 1) * CH)
                        p.op("tensor", mm(psDen[0:1, :], c.ones_col[rows, :],
                                          c.SCR_B[rows, sl2], True, True),
                             r=[c.ones_col, c.SCR_B], w=[psDen])
                        p.op("vector", rmax(c.R0[:, o0 + ci:o0 + ci + 1], psDen[0:1, :]),
                             r=[psDen], w=[c.TMP])
                p.op("vector", rmax(c.R0[:, 8:9], c.R0[:, 0:4]), r=[c.TMP], w=[c.TMP])
                p.op("vector", rmax(c.R0[:, 9:10], c.R0[:, 4:8]), r=[c.TMP], w=[c.TMP])
                p.op("vector", tt(c.R0[:, 10:11], c.R0[:, 8:9], c.R0[:, 9:10], ALU.mult),
                     r=[c.TMP], w=[c.TMP])
                p.op("scalar", act(c.R0[:, 11:12], c.R0[:, 10:11], AF.Sqrt),
                     r=[c.TMP], w=[c.TMP])
                p.op("vector", ts(c.R0[:, 12:13], c.R0[:, 11:12], -1.0 / 8.0, ALU.mult),
                     r=[c.TMP], w=[c.TMP])
                p.op("tensor", mm(psRep[:, 0:1], c.ones_row, c.R0[:, 12:13], True, True),
                     r=[c._ones_rows, c.TMP], w=[psRep])
                p.op("scalar", scp(NEGBC[:], psRep[:, 0:1]), r=[psRep], w=[NEGBC])
                for qc in range(NCH):
                    tsl = slice(CH * qc, CH * (qc + 1))
                    nkt = 4 * qc + 4
                    for kt in range(nkt):
                        k2 = kt % 2
                        ksl = slice(128 * kt, 128 * (kt + 1))
                        p.op("tensor", mm(psS[k2][:, 0:CH], KH[j][rows, ksl], QH[j][rows, tsl],
                                          True, True),
                             r=[f"{_key(KH[j])}@{q2}" for q2 in range(NCH)]
                               + [f"{_key(QH[j])}@{qc}"],
                             w=[psS[k2]])
                        r_rel = kt - 4 * qc
                        if r_rel >= 0:
                            p.op("vector", tt(c.SQS[k2][:], psS[k2][:, 0:CH],
                                              MSK[:, CH * r_rel:CH * (r_rel + 1)], ALU.add),
                                 r=[psS[k2], MSK], w=[c.SQS[k2]])
                            src = c.SQS[k2][0:128, :]
                            srck = [c.SQS[k2]]
                        else:
                            src = psS[k2][0:128, 0:CH]
                            srck = [psS[k2]]
                        p.op("scalar", act(PB[k2][:], src, AF.Exp, bias=NEGBC[:], scale=1.0 / 8.0),
                             r=srck + [NEGBC], w=[PB[k2]])
                        p.op("tensor", mm(psO[:, 0:CH], VT[kt][:, 64 * hh:64 * (hh + 1)],
                                          PB[k2][:], kt == 0, kt == nkt - 1),
                             r=[VT[kt], PB[k2]], w=[psO] if kt in (0, nkt - 1) else ())
                        p.op("tensor", mm(psDen[0:1, :], c.ones_col_bf[:], PB[k2][:],
                                          kt == 0, kt == nkt - 1),
                             r=[c.ones_col_bf, PB[k2]], w=[psDen] if kt in (0, nkt - 1) else ())
                    p.op("vector", recip(c.R0[:, 0:CH], psDen[0:1, :]), r=[psDen], w=[c.TMP])
                    p.op("tensor", mm(psRep[:], c.ones_row, c.R0[:, 0:CH], True, True),
                         r=[c._ones_rows, c.TMP], w=[psRep])
                    p.op("scalar", scp(REPS[:], psRep[:]), r=[psRep], w=[REPS])
                    p.op("vector", tt(OSTG[j % 2][rows, tsl], psO[:, 0:CH], REPS[0:64, :],
                                      ALU.mult),
                         r=[psO, REPS], w=[f"ostg{j % 2}@{half}@{qc}"])
                if half == 1:
                    hp = hg // 2
                    p.dma("gpsimd", o_scr.ap()[128 * hp:128 * (hp + 1), :], OSTG[j % 2][:],
                          sem=f"osp{j % 2}",
                          r=[f"ostg{j % 2}@{h2}@{q2}" for h2 in range(2) for q2 in range(NCH)],
                          w=["o_scr"])
        p.barrier()
        s1.close()

        s2 = ExitStack()
        sb2 = lambda n, s, d=F32: s2.enter_context(nc.sbuf_tensor(uid + n, s, d))
        ps2 = lambda n, s, d=F32: s2.enter_context(nc.psum_tensor(uid + n, s, d))
        OIN = [sb2(f"oin{ot}", [128, T], BF16) for ot in range(DP)]
        WO = [sb2(f"wob{ot}", [128, D], BF16) for ot in range(DP)]
        YW = [sb2("yw0", [128, T], BF16), sb2("yw1", [128, T], BF16)]
        HST = [sb2("hst0", [128, T], BF16), sb2("hst1", [128, T], BF16)]
        psD = [ps2("psD0", [128, CH]), ps2("psD1", [128, CH])]
        for i in range(DP):
            to = p.dma("scalar", OIN[i][:], o_scr.ap()[128 * i:128 * (i + 1), :], sem="oing",
                       r=["o_scr"], w=[OIN[i]])
        p.retag_writers(OIN, to)
        for ot in range(DP):
            tw = p.dma("gpsimd", WO[ot][:], wo_d.ap()[128 * ot:128 * (ot + 1), :], sem="wog",
                       w=[WO[ot]])
        p.retag_writers(WO, tw)
        for jj in range(DP):
            jsl = slice(128 * jj, 128 * (jj + 1))
            kj = jj % 2
            p.dma("sync", YW[kj][:], y_scr.ap()[jsl, :], sem=f"ywin{kj}",
                  r=["y_scr"], w=[YW[kj]])
            for tc in range(NCH):
                tsl = slice(CH * tc, CH * (tc + 1))
                k = (jj * NCH + tc) % 2
                for ot in range(DP):
                    p.op("tensor", mm(psD[k][:], WO[ot][:, jsl], OIN[ot][:, tsl],
                                      ot == 0, ot == DP - 1),
                         r=[WO[ot], OIN[ot]], w=[psD[k]] if ot in (0, DP - 1) else ())
                p.op("vector", tt(HST[kj][:, tsl], YW[kj][:, tsl], psD[k][:], ALU.add),
                     r=[YW[kj], psD[k]], w=[HST[kj]])
                p.op("vector", stt(X[jj][:, tsl], psD[k][:], 0.5, X[jj][:, tsl],
                                   ALU.mult, ALU.add),
                     r=[psD[k]], w=[X[jj]])
            p.dma("gpsimd", h_out.ap()[jsl, :], HST[kj][:], sem=f"hout{kj}",
                  r=[HST[kj]], w=["h_scr"])
        s2.close()
    p.barrier()


def build_prog_a1():
    nc = bass.Bass("TRN2", target_bir_lowering=False)
    x_in = nc.dram_tensor("x_in", [D, T], F32, kind="ExternalInput")
    dr = {
        "a1_wq": nc.dram_tensor("wq", [D, D], F32, kind="ExternalInput"),
        "a1_wk": nc.dram_tensor("wk", [D, D], F32, kind="ExternalInput"),
        "a1_wv": nc.dram_tensor("wv", [D, D], F32, kind="ExternalInput"),
        "a1_wo": nc.dram_tensor("wo", [D, D], F32, kind="ExternalInput"),
        "a1_anw": nc.dram_tensor("anw", [128, DP], F32, kind="ExternalInput"),
        "a1_ropeC": nc.dram_tensor("ropeC", [128, T], BF16, kind="ExternalInput"),
        "a1_ropeS": nc.dram_tensor("ropeS", [128, T], BF16, kind="ExternalInput"),
        "a1_dmask": nc.dram_tensor("dmask", [128, 4 * CH], BF16, kind="ExternalInput"),
        "o_scr": nc.dram_tensor("o_scr", [D, T], BF16, kind="Internal"),
        "y_scr": nc.dram_tensor("y_scr", [D, T], BF16, kind="Internal"),
        "h_scr": nc.dram_tensor("h_out", [D, T], BF16, kind="ExternalOutput"),
    }
    z_out = nc.dram_tensor("z_out", [D, T], F32, kind="ExternalOutput")
    stack = ExitStack()
    with stack:
        sb = lambda n, s, d=F32: stack.enter_context(nc.sbuf_tensor(n, s, d))
        p = Prog(nc, stack)
        c = Com(nc, stack)
        X = [sb(f"x{i}", [128, T]) for i in range(DP)]
        c.init(p)
        for i in range(DP):
            tx = p.dma("sync", X[i][:], x_in.ap()[128 * i:128 * (i + 1), :], sem="xing", w=[X[i]])
        p.retag_writers(X, tx)
        emit_a1(p, c, nc, X, dr)
        for jj in range(DP):
            p.dma("sync", z_out.ap()[128 * jj:128 * (jj + 1), :], X[jj][:], sem="zout",
                  r=[X[jj]])
        p.finish()
        p.emit()
    return nc


def _emit_rot_weights_il(p, dst, src, nh, dh):
    sv = src.ap().rearrange("p (h i two) -> p h i two", h=nh, i=dh // 2, two=2)
    dv = dst.ap().rearrange("p (h i two) -> p h i two", h=nh, i=dh // 2, two=2)
    p.op("vector", ts(dv[:, :, :, 0:1], sv[:, :, :, 1:2], -1.0, ALU.mult), r=[src], w=[dst])
    p.op("vector", cp(dv[:, :, :, 1:2], sv[:, :, :, 0:1]), r=[src], w=[dst])



def emit_a2(p, c, nc, Z, dr, uid=""):
    h_in = dr["h_scr"]; fnw_d = dr["a2_fnw"]; rw_d = dr["a2_rw"]
    w1_d = dr["a2_w1"]; w3_d = dr["a2_w3"]; w2_d = dr["a2_w2"]
    ident_d = dr["a2_ident"]; sel_d = dr["a2_sel8"]
    with ExitStack() as ss:
        sbS = lambda n, s, d=F32: ss.enter_context(nc.sbuf_tensor(uid + n, s, d))
        X2 = [sbS(f"x2_{i}", [128, T], BF16) for i in range(DP)]
        CTT = sbS("ctt", [8, T])
        SEL8 = sbS("sel8_sb", [8, D])
        p.dma("scalar", SEL8[:], sel_d.ap(), sem="selin", w=[SEL8])

        s0 = ExitStack()
        sb0 = lambda n, s, d=F32: s0.enter_context(nc.sbuf_tensor(uid + n, s, d))
        ps0 = lambda n, s, d=F32: s0.enter_context(nc.psum_tensor(uid + n, s, d))
        H = [sb0(f"h{i}", [128, T], BF16) for i in range(DP)]
        FNW = sb0("fnw_sb", [128, DP])
        RWB = sb0("rwb", [128, 8 * DP], BF16)
        IDENT = sb0("ident_sb", [128, 128])
        LROW = sb0("lrow", [8, CH])
        LTOK = sb0("ltok", [128, 128])
        M8 = sb0("m8", [128, 8])
        N1 = sb0("n1", [128, 1])
        E8 = sb0("e8", [128, 8])
        IND = sb0("ind", [128, 8])
        NUM = sb0("num", [128, 8])
        E2 = sb0("e2", [128, 1])
        DINV = sb0("dinv", [128, 1])
        CT = sb0("ct", [128, 128])
        psB = ps0("psB", [1, CH])
        psR = [ps0("psR0", [128, CH]), ps0("psR1", [128, CH])]
        psL = ps0("psL", [8, CH])
        psT = ps0("psT", [128, 8])
        psT2 = ps0("psT2", [8, 128])
        for i in range(DP):
            th = p.dma("sync", H[i][:], h_in.ap()[128 * i:128 * (i + 1), :], sem="hing", w=[H[i]])
        p.retag_writers(H, th)
        p.dma("scalar", FNW[:], fnw_d.ap(), sem="fnwin", w=[FNW])
        p.dma("scalar", IDENT[:], ident_d.ap(), sem="idin", w=[IDENT])
        emit_rms(p, c, H, psB, psR)
        for i in range(DP):
            p.op("vector", stt(X2[i][:], H[i][:], FNW[:, i:i + 1], c.SCR_A[:],
                               ALU.mult, ALU.mult),
                 r=[H[i], FNW, c.SCR_A], w=[X2[i]])
        for d in range(DP):
            p.dma("gpsimd", c.SQS[0][:, 0:8], rw_d.ap()[128 * d:128 * (d + 1), :],
                  sem="rwin", w=[c.SQS[0]])
            p.op("vector", cp(RWB[:, 8 * d:8 * (d + 1)], c.SQS[0][:, 0:8]),
                 r=[c.SQS[0]], w=[RWB])
        for ci in range(NCH):
            csl = slice(CH * ci, CH * (ci + 1))
            for d in range(DP):
                p.op("tensor", mm(psL[0:8, :], RWB[:, 8 * d:8 * (d + 1)], X2[d][:, csl],
                                  d == 0, d == DP - 1),
                     r=[RWB, X2[d]], w=[psL] if d in (0, DP - 1) else ())
            p.op("scalar", scp(LROW[:], psL[0:8, :]), r=[psL], w=[LROW])
            for t2 in range(4):
                tt_g = 4 * ci + t2
                p.op("tensor", transp(psT[:, 0:8], LROW[0:8, 128 * t2:128 * (t2 + 1)],
                                      IDENT[0:8, 0:8]),
                     r=[LROW, IDENT], w=[psT])
                p.op("scalar", scp(LTOK[:, 8 * tt_g:8 * (tt_g + 1)], psT[:, 0:8]),
                     r=[psT], w=[LTOK])
        for tt_g in range(NTT):
            lt = LTOK[:, 8 * tt_g:8 * (tt_g + 1)]
            p.op("vector", max8(M8[:], lt), r=[LTOK], w=[M8])
            p.op("vector", ts(N1[:], M8[:, 0:1], -1.0, ALU.mult), r=[M8], w=[N1])
            p.op("scalar", act(E8[:], lt, AF.Exp, bias=N1[:]), r=[LTOK, N1], w=[E8])
            p.op("vector", ts(IND[:], lt, M8[:, 1:2], ALU.is_ge), r=[LTOK, M8], w=[IND])
            p.op("vector", tt(NUM[:], E8[:], IND[:], ALU.mult), r=[E8, IND], w=[NUM])
            p.op("scalar", act(E2[:], M8[:, 1:2], AF.Exp, bias=N1[:]), r=[M8, N1], w=[E2])
            p.op("vector", ts(E2[:], E2[:], 1.0, ALU.add), r=[E2], w=[E2])
            p.op("vector", recip(DINV[:], E2[:]), r=[E2], w=[DINV])
            p.op("vector",
                 lambda E, NUM=NUM, DINV=DINV, tt_g=tt_g:
                     E.tensor_scalar(CT[:, 8 * tt_g:8 * (tt_g + 1)], NUM[:], DINV[:], 0.5,
                                     ALU.mult, ALU.mult),
                 r=[NUM, DINV], w=[CT])
        for tt_g in range(NTT):
            p.op("tensor", transp(psT2[0:8, :], CT[:, 8 * tt_g:8 * (tt_g + 1)], IDENT[:]),
                 r=[CT, IDENT], w=[psT2])
            p.op("scalar", scp(CTT[:, 128 * tt_g:128 * (tt_g + 1)], psT2[0:8, :]),
                 r=[psT2], w=[CTT])
        p.barrier()
        s0.close()

        s1 = ExitStack()
        sb1 = lambda n, s, d=F32: s1.enter_context(nc.sbuf_tensor(uid + n, s, d))
        ps1 = lambda n, s, d=F32: s1.enter_context(nc.psum_tensor(uid + n, s, d))
        W1 = [sb1(f"w1b{d}", [128, D], BF16) for d in range(DP)]
        W3 = [sb1(f"w3b{d}", [128, D], BF16) for d in range(DP)]
        W2H = [sb1(f"w2b{it}", [128, D], BF16) for it in range(DP)]
        GT = [sb1(f"gt{it}", [128, CH], BF16) for it in range(DP)]
        TSC = c.SQS
        psH1 = [ps1("psH1a", [128, CH]), ps1("psH1b", [128, CH])]
        psH3 = [ps1("psH3a", [128, CH]), ps1("psH3b", [128, CH])]
        psY = [ps1("psYa", [128, CH]), ps1("psYb", [128, CH])]
        psRep = ps1("psRep2", [128, CH])

        for e in range(8):
            for ci in range(NCH):
                csl = slice(CH * ci, CH * (ci + 1))
                p.op("tensor", mm(psRep[:], SEL8[0:8, 128 * e:128 * (e + 1)],
                                  CTT[0:8, csl], True, True),
                     r=[SEL8, CTT], w=[psRep])
                p.op("scalar", scp(c.SCR_B[:, csl], psRep[:]), r=[psRep], w=[c.SCR_B])
            for ih in range(2):
                isl = slice(1024 * ih, 1024 * (ih + 1))
                for d in range(DP):
                    t1 = p.dma("gpsimd", W1[d][:], w1_d.ap()[e, 128 * d:128 * (d + 1), isl],
                               sem="w1g", w=[W1[d]])
                    t3 = p.dma("gpsimd", W3[d][:], w3_d.ap()[e, 128 * d:128 * (d + 1), isl],
                               sem="w3g", w=[W3[d]])
                p.retag_writers(W1, t1)
                p.retag_writers(W3, t3)
                for it in range(DP):
                    t2 = p.dma("gpsimd", W2H[it][:], w2_d.ap()[e, 1024 * ih + 128 * it:
                                                               1024 * ih + 128 * (it + 1), :],
                               sem="w2g", w=[W2H[it]])
                p.retag_writers(W2H, t2)
                for tc in range(NCH):
                    tsl = slice(CH * tc, CH * (tc + 1))
                    for it in range(DP):
                        k = it % 2
                        col = slice(128 * it, 128 * (it + 1))
                        for d in range(DP):
                            p.op("tensor", mm(psH1[k][:], W1[d][:, col], X2[d][:, tsl],
                                              d == 0, d == DP - 1),
                                 r=[W1[d], X2[d]], w=[psH1[k]] if d in (0, DP - 1) else ())
                        for d in range(DP):
                            p.op("tensor", mm(psH3[k][:], W3[d][:, col], X2[d][:, tsl],
                                              d == 0, d == DP - 1),
                                 r=[W3[d], X2[d]], w=[psH3[k]] if d in (0, DP - 1) else ())
                        p.op("scalar", act(c.SQS[k][:], psH1[k][:], AF.Silu),
                             r=[psH1[k]], w=[c.SQS[k]])
                        p.op("vector", tt(GT[it][:], c.SQS[k][:], psH3[k][:], ALU.mult),
                             r=[c.SQS[k], psH3[k]], w=[GT[it]])
                    for j in range(DP):
                        k = j % 2
                        jcol = slice(128 * j, 128 * (j + 1))
                        for it in range(DP):
                            p.op("tensor", mm(psY[k][:], W2H[it][:, jcol], GT[it][:],
                                              it == 0, it == DP - 1),
                                 r=[W2H[it], GT[it]], w=[psY[k]] if it in (0, DP - 1) else ())
                        p.op("vector", tt(TSC[k][:], psY[k][:], c.SCR_B[:, tsl], ALU.mult),
                             r=[psY[k], c.SCR_B], w=[TSC[k]])
                        p.op("vector", tt(Z[j][:, tsl], Z[j][:, tsl], TSC[k][:], ALU.add),
                             r=[TSC[k]], w=[Z[j]])
        s1.close()
    p.barrier()


def build_prog_a2():
    nc = bass.Bass("TRN2", target_bir_lowering=False)
    z_in = nc.dram_tensor("z_in", [D, T], F32, kind="ExternalInput")
    dr = {
        "h_scr": nc.dram_tensor("h_in", [D, T], BF16, kind="ExternalInput"),
        "a2_fnw": nc.dram_tensor("fnw", [128, DP], F32, kind="ExternalInput"),
        "a2_rw": nc.dram_tensor("rw", [D, 8], F32, kind="ExternalInput"),
        "a2_w1": nc.dram_tensor("w1", [8, D, 2048], F32, kind="ExternalInput"),
        "a2_w3": nc.dram_tensor("w3", [8, D, 2048], F32, kind="ExternalInput"),
        "a2_w2": nc.dram_tensor("w2", [8, 2048, D], F32, kind="ExternalInput"),
        "a2_ident": nc.dram_tensor("ident", [128, 128], F32, kind="ExternalInput"),
        "a2_sel8": nc.dram_tensor("sel8", [8, D], F32, kind="ExternalInput"),
    }
    x_out = nc.dram_tensor("x_out", [D, T], F32, kind="ExternalOutput")
    stack = ExitStack()
    with stack:
        sb = lambda n, s, d=F32: stack.enter_context(nc.sbuf_tensor(n, s, d))
        p = Prog(nc, stack)
        c = Com(nc, stack)
        Z = [sb(f"z{i}", [128, T]) for i in range(DP)]
        c.init(p)
        for i in range(DP):
            tz = p.dma("sync", Z[i][:], z_in.ap()[128 * i:128 * (i + 1), :], sem="zing", w=[Z[i]])
        p.retag_writers(Z, tz)
        emit_a2(p, c, nc, Z, dr)
        for j in range(DP):
            p.dma("sync", x_out.ap()[128 * j:128 * (j + 1), :], Z[j][:], sem="xout", r=[Z[j]])
        p.finish()
        p.emit()
    return nc


def _prep_l0_ffn_moe(inputs):
    sel = np.zeros((8, D), np.float32)
    for e in range(8):
        sel[e, 128 * e:128 * (e + 1)] = 1.0
    return {
        "fnw": np.ascontiguousarray(np.asarray(inputs["L0_ffn_ffn_norm_w"]).reshape(DP, 128).T),
        "rw": np.ascontiguousarray(inputs["L0_ffn_router_w"]),
        "w1": np.ascontiguousarray(inputs["L0_ffn_w1"]),
        "w3": np.ascontiguousarray(inputs["L0_ffn_w3"]),
        "w2": np.ascontiguousarray(inputs["L0_ffn_w2"]),
        "ident": np.eye(128, dtype=np.float32),
        "sel8": sel,
    }


def run_a2(hs, zs, W):
    nc = get_prog("a2")
    in_maps = [dict(h_in=hs[b], z_in=zs[b], **W) for b in range(8)]
    res = run_bass_kernel_spmd(nc, in_maps, core_ids=list(range(8)))
    return [np.asarray(r["x_out"]) for r in res.results]



def emit_e(p, c, nc, X, dr, uid=""):
    wq_d = dr["e_wq"]; wk_d = dr["e_wk"]; wv_d = dr["e_wv"]
    wqf_d = dr["e_wqf"]; wkf_d = dr["e_wkf"]; wout_d = dr["e_wout"]
    nw_d = dr["e_nw"]; ga_d = dr["e_gha"]; gb_d = dr["e_ghb"]
    triu_d = dr["e_triu"]; wmask_d = dr["e_wmask"]
    lin_scr = dr["lin_scr"]; win_scr = dr["win_scr"]
    with ExitStack() as ss:
        sbS = lambda n, s, d=F32: ss.enter_context(nc.sbuf_tensor(uid + n, s, d))
        XH = [sbS(f"xh{i}", [128, T], BF16) for i in range(DP)]

        s0 = ExitStack()
        sb0 = lambda n, s, d=F32: s0.enter_context(nc.sbuf_tensor(uid + n, s, d))
        ps0 = lambda n, s, d=F32: s0.enter_context(nc.psum_tensor(uid + n, s, d))
        Y = [sb0(f"y{i}", [128, T], BF16) for i in range(DP)]
        NW = sb0("nw_sb", [128, DP])
        psA = ps0("psA", [1, CH])
        psB = ps0("psB", [1, CH])
        psR = [ps0("psR0", [128, CH]), ps0("psR1", [128, CH])]
        p.dma("scalar", NW[:], nw_d.ap(), sem="nwin", w=[NW])
        emit_ln(p, c, X, Y, psA, psB, psR, fold=1.0)
        emit_rms(p, c, Y, psB, psR)
        for i in range(DP):
            p.op("vector", stt(XH[i][:], Y[i][:], NW[:, i:i + 1], c.SCR_A[:],
                               ALU.mult, ALU.mult),
                 r=[Y[i], NW, c.SCR_A], w=[XH[i]])
        p.barrier()
        s0.close()

        s1 = ExitStack()
        sb1 = lambda n, s, d=F32: s1.enter_context(nc.sbuf_tensor(uid + n, s, d))
        ps1 = lambda n, s, d=F32: s1.enter_context(nc.psum_tensor(uid + n, s, d))
        WQB = [sb1(f"wqb{d}", [128, 128], BF16) for d in range(DP)]
        WKB = [sb1(f"wkb{d}", [128, 128], BF16) for d in range(DP)]
        WVB = [sb1(f"wvb{d}", [128, 128], BF16) for d in range(DP)]
        WQF = sb1("wqf_sb", [64, 16], BF16)
        WKF = sb1("wkf_sb", [64, 16], BF16)
        WFS = sb1("wfs", [64, 32])
        GA = sb1("gha_sb", [33, 153], BF16)
        GB = sb1("ghb_sb", [33, 153], BF16)
        TRIU = sb1("triu_sb", [128, 128])
        WMSK = sb1("wmsk_sb", [128, 256], BF16)
        QH = sb1("qh", [64, T], BF16)
        KH = sb1("kh", [64, T], BF16)
        VA = [sb1("va0", [128, NTT * 65], BF16), sb1("va1", [128, NTT * 65], BF16)]
        SQ = sb1("sq", [33, T], BF16)
        SK = sb1("sk", [33, T], BF16)
        PQ1 = sb1("pq1", [128, T], BF16)
        PQ2 = sb1("pq2", [25, T], BF16)
        PK1 = sb1("pk1", [128, T], BF16)
        PK2 = sb1("pk2", [25, T], BF16)
        KTOK = sb1("ktok", [128, NTT * 153], BF16)
        KVH = sb1("kvh", [128, 65], BF16)
        KVL = sb1("kvl", [25, 65], BF16)
        KVFH = sb1("kvfh", [128, 65])
        KVFL = sb1("kvfl", [25, 65])
        LINH = sb1("linh", [64, T], BF16)
        WINH = sb1("winh", [64, T], BF16)
        ASB = sb1("asb", [128, 128], BF16)
        PBW = sb1("pbw", [128, 128], BF16)
        REPE = sb1("repe", [128, 128])
        NEGWC = sb1("negwc", [128, 1])
        psU = ps1("psU", [128, CH])
        psW = ps1("psW", [128, CH])
        psA2 = ps1("psA2", [128, 128])
        psY2 = ps1("psY2", [65, 128])
        psOW = ps1("psOW", [65, 128])
        psRepE = ps1("psRepE", [128, 128])
        kvh_ps = ps1("kvh_ps", [128, 65])
        kvl_ps = ps1("kvl_ps", [25, 65])

        p.dma("sync", WFS[:, 0:16], wqf_d.ap(), sem="wfin", w=[WFS])
        p.dma("sync", WFS[:, 16:32], wkf_d.ap(), sem="wfin2", w=[WFS])
        p.op("vector", cp(WQF[:], WFS[:, 0:16]), r=[WFS], w=[WQF])
        p.op("vector", cp(WKF[:], WFS[:, 16:32]), r=[WFS], w=[WKF])
        p.dma("sync", GA[:], ga_d.ap(), sem="gain", w=[GA])
        p.dma("sync", GB[:], gb_d.ap(), sem="gbin", w=[GB])
        p.dma("sync", TRIU[:], triu_d.ap(), sem="trin", w=[TRIU])
        p.dma("sync", WMSK[:], wmask_d.ap(), sem="wmin", w=[WMSK])
        p.op("vector", memset(SQ[:], 0.0), w=[SQ])
        p.op("vector", memset(SQ[32:33, :], 1.0), w=[SQ])
        p.op("vector", memset(SK[:], 0.0), w=[SK])
        p.op("vector", memset(SK[32:33, :], 1.0), w=[SK])

        for hp in range(8):
            psl = slice(128 * hp, 128 * (hp + 1))
            for d in range(DP):
                rs = slice(128 * d, 128 * (d + 1))
                tq = p.dma("gpsimd", WQB[d][:], wq_d.ap()[rs, psl], sem="wqg", w=[WQB[d]])
                tk = p.dma("gpsimd", WKB[d][:], wk_d.ap()[rs, psl], sem="wkg", w=[WKB[d]])
                tv = p.dma("gpsimd", WVB[d][:], wv_d.ap()[rs, psl], sem="wvg", w=[WVB[d]])
            p.retag_writers(WQB, tq)
            p.retag_writers(WKB, tk)
            p.retag_writers(WVB, tv)
            for h2 in range(2):
                p.op("vector", memset(
                    VA[h2].ap().rearrange("p (t c) -> p t c", t=NTT, c=65)[:, :, 64:65], 1.0),
                    w=[VA[h2]])
            for t2 in range(NTT):
                ssl = slice(128 * t2, 128 * (t2 + 1))
                for d in range(DP):
                    p.op("tensor", mm(psA2[:], XH[d][:, ssl], WVB[d][:],
                                      d == 0, d == DP - 1),
                         r=[XH[d], WVB[d]], w=[psA2] if d in (0, DP - 1) else ())
                for h2 in range(2):
                    p.op("vector", cp(
                        VA[h2].ap().rearrange("p (t c) -> p t c", t=NTT, c=65)[:, t2, 0:64],
                        psA2[:, 64 * h2:64 * (h2 + 1)]),
                        r=[psA2], w=[VA[h2]])
            for h2 in range(2):
                h = 2 * hp + h2
                hsl = slice(64 * h2, 64 * (h2 + 1))
                vaR = VA[h2].ap().rearrange("p (t c) -> p t c", t=NTT, c=65)
                for ci in range(NCH):
                    csl = slice(CH * ci, CH * (ci + 1))
                    for (ps_, Wm, dst) in ((psU, WQB, QH), (psW, WKB, KH)):
                        for d in range(DP):
                            p.op("tensor", mm(ps_[0:64, :], Wm[d][:, hsl], XH[d][:, csl],
                                              d == 0, d == DP - 1),
                                 r=[Wm[d], XH[d]], w=[ps_] if d in (0, DP - 1) else ())
                        p.op("scalar", scp(dst[:, csl], ps_[0:64, :]), r=[ps_], w=[dst])
                for ci in range(NCH):
                    csl = slice(CH * ci, CH * (ci + 1))
                    p.op("tensor", mm(psU[0:16, :], WQF[:], QH[:, csl], True, True),
                         r=[WQF, QH], w=[psU])
                    p.op("scalar", scp(SQ[0:16, csl], psU[0:16, :]), r=[psU], w=[SQ])
                    p.op("tensor", mm(psW[0:16, :], WKF[:], KH[:, csl], True, True),
                         r=[WKF, KH], w=[psW])
                    p.op("scalar", scp(SK[0:16, csl], psW[0:16, :]), r=[psW], w=[SK])
                for (S_, P1, P2) in ((SQ, PQ1, PQ2), (SK, PK1, PK2)):
                    for ci in range(NCH):
                        csl = slice(CH * ci, CH * (ci + 1))
                        for (P_, fsl, np_) in ((P1, slice(0, 128), 128),
                                               (P2, slice(128, 153), 25)):
                            p.op("tensor", mm(psU[0:np_, :], GA[:, fsl], S_[:, csl],
                                              True, True),
                                 r=[GA, S_], w=[psU])
                            p.op("tensor", mm(psW[0:np_, :], GB[:, fsl], S_[:, csl],
                                              True, True),
                                 r=[GB, S_], w=[psW])
                            p.op("scalar", scp(c.SQS[0][0:np_, :], psW[0:np_, :]),
                                 r=[psW], w=[c.SQS[0]])
                            p.op("vector", tt(P_[0:np_, csl], psU[0:np_, :],
                                              c.SQS[0][0:np_, :], ALU.mult),
                                 r=[psU, c.SQS[0]], w=[P_])
                for ch in range(NTT):
                    p.op("tensor", mm(psU[:, 0:153], SK[:, 128 * ch:128 * (ch + 1)], GA[:],
                                      True, True),
                         r=[SK, GA], w=[psU])
                    p.op("tensor", mm(psW[:, 0:153], SK[:, 128 * ch:128 * (ch + 1)], GB[:],
                                      True, True),
                         r=[SK, GB], w=[psW])
                    p.op("scalar", scp(c.SQS[0][:, 0:153], psW[:, 0:153]),
                         r=[psW], w=[c.SQS[0]])
                    p.op("vector", tt(KTOK[:, 153 * ch:153 * (ch + 1)], psU[:, 0:153],
                                      c.SQS[0][:, 0:153], ALU.mult),
                         r=[psU, c.SQS[0]], w=[KTOK])
                p.op("vector", memset(KVH[:], 0.0), w=[KVH])
                p.op("vector", memset(KVL[:], 0.0), w=[KVL])
                p.op("vector", memset(KVFH[:], 0.0), w=[KVFH])
                p.op("vector", memset(KVFL[:], 0.0), w=[KVFL])
                for ch in range(NTT):
                    cs128 = slice(128 * ch, 128 * (ch + 1))
                    p.op("tensor", mm(psA2[:], PK1[:, cs128], PQ1[:, cs128], True, False),
                         r=[PK1, PQ1], w=[psA2])
                    p.op("tensor", mm(psA2[:], PK2[0:25, cs128], PQ2[0:25, cs128],
                                      False, True),
                         r=[PK2, PQ2], w=[psA2])
                    p.op("vector", tt(ASB[:], psA2[:], TRIU[:], ALU.mult),
                         r=[psA2, TRIU], w=[ASB])
                    p.op("tensor", mm(psY2[:, :], vaR[:, ch, :], ASB[:], True, False),
                         r=[VA[h2], ASB], w=[psY2])
                    p.op("tensor", mm(psY2[:, :], KVH[:], PQ1[:, cs128], False, False),
                         r=[KVH, PQ1], w=[psY2])
                    p.op("tensor", mm(psY2[:, :], KVL[:], PQ2[0:25, cs128], False, True),
                         r=[KVL, PQ2], w=[psY2])
                    p.op("vector", ts(c.R0[:, 0:128], psY2[64:65, :], c.cval(1), ALU.add),
                         r=[psY2, "cval1"], w=[c.TMP])
                    p.op("vector", recip(c.R0[:, 0:128], c.R0[:, 0:128]), r=[c.TMP], w=[c.TMP])
                    p.op("tensor", mm(psRepE[:], c.ones_row, c.R0[:, 0:128], True, True),
                         r=[c._ones_rows, c.TMP], w=[psRepE])
                    p.op("scalar", scp(REPE[:], psRepE[:]), r=[psRepE], w=[REPE])
                    p.op("vector", tt(LINH[:, cs128], psY2[0:64, :], REPE[0:64, :], ALU.mult),
                         r=[psY2, REPE], w=[LINH])
                    if ch < NTT - 1:
                        p.op("tensor", mm(kvh_ps[:], KTOK[:, 153 * ch:153 * ch + 128],
                                          vaR[:, ch, :], True, True),
                             r=[KTOK, VA[h2]], w=[kvh_ps])
                        p.op("tensor", mm(kvl_ps[:], KTOK[:, 153 * ch + 128:153 * (ch + 1)],
                                          vaR[:, ch, :], True, True),
                             r=[KTOK, VA[h2]], w=[kvl_ps])
                        p.op("vector", tt(KVFH[:], KVFH[:], kvh_ps[:], ALU.add),
                             r=[kvh_ps], w=[KVFH])
                        p.op("vector", tt(KVFL[:], KVFL[:], kvl_ps[:], ALU.add),
                             r=[kvl_ps], w=[KVFL])
                        p.op("scalar", scp(KVH[:], KVFH[:]), r=[KVFH], w=[KVH])
                        p.op("scalar", scp(KVL[:], KVFL[:]), r=[KVFL], w=[KVL])
                for (HT, o0) in ((QH, 0), (KH, 4)):
                    p.op("scalar", act(c.SCR_B[0:64, :], HT[:], AF.Square),
                         r=[HT], w=[c.SCR_B])
                    for ci in range(NCH):
                        sl2 = slice(ci * CH, (ci + 1) * CH)
                        p.op("tensor", mm(psU[0:1, 0:CH], c.ones_col[0:64, :],
                                          c.SCR_B[0:64, sl2], True, True),
                             r=[c.ones_col, c.SCR_B], w=[psU])
                        p.op("vector", rmax(c.R0[:, o0 + ci:o0 + ci + 1], psU[0:1, 0:CH]),
                             r=[psU], w=[c.TMP])
                p.op("vector", rmax(c.R0[:, 8:9], c.R0[:, 0:4]), r=[c.TMP], w=[c.TMP])
                p.op("vector", rmax(c.R0[:, 9:10], c.R0[:, 4:8]), r=[c.TMP], w=[c.TMP])
                p.op("vector", tt(c.R0[:, 10:11], c.R0[:, 8:9], c.R0[:, 9:10], ALU.mult),
                     r=[c.TMP], w=[c.TMP])
                p.op("scalar", act(c.R0[:, 11:12], c.R0[:, 10:11], AF.Sqrt),
                     r=[c.TMP], w=[c.TMP])
                p.op("vector", ts(c.R0[:, 12:13], c.R0[:, 11:12], -1.0 / 8.0, ALU.mult),
                     r=[c.TMP], w=[c.TMP])
                p.op("tensor", mm(psRepE[:, 0:1], c.ones_row, c.R0[:, 12:13], True, True),
                     r=[c._ones_rows, c.TMP], w=[psRepE])
                p.op("scalar", scp(NEGWC[:], psRepE[:, 0:1]), r=[psRepE], w=[NEGWC])
                for qc in range(NTT):
                    qsl = slice(128 * qc, 128 * (qc + 1))
                    kts = [kt for kt in (qc - 1, qc) if kt >= 0]
                    for ki, kt in enumerate(kts):
                        ksl = slice(128 * kt, 128 * (kt + 1))
                        p.op("tensor", mm(psA2[:], KH[:, ksl], QH[:, qsl], True, True),
                             r=[KH, QH], w=[psA2])
                        wm = WMSK[:, 0:128] if kt < qc else WMSK[:, 128:256]
                        p.op("vector", tt(c.SQS[1][:, 0:128], psA2[:], wm, ALU.add),
                             r=[psA2, WMSK], w=[c.SQS[1]])
                        p.op("scalar", act(PBW[:], c.SQS[1][:, 0:128], AF.Exp,
                                           bias=NEGWC[:], scale=1.0 / 8.0),
                             r=[c.SQS[1], NEGWC], w=[PBW])
                        p.op("tensor", mm(psOW[:, :], vaR[:, kt, :], PBW[:],
                                          ki == 0, ki == len(kts) - 1),
                             r=[VA[h2], PBW], w=[psOW])
                    p.op("vector", recip(c.R0[:, 0:128], psOW[64:65, :]), r=[psOW], w=[c.TMP])
                    p.op("tensor", mm(psRepE[:], c.ones_row, c.R0[:, 0:128], True, True),
                         r=[c._ones_rows, c.TMP], w=[psRepE])
                    p.op("scalar", scp(REPE[:], psRepE[:]), r=[psRepE], w=[REPE])
                    p.op("vector", tt(WINH[:, qsl], psOW[0:64, :], REPE[0:64, :], ALU.mult),
                         r=[psOW, REPE], w=[WINH])
                p.dma("gpsimd", lin_scr.ap()[64 * h:64 * (h + 1), :], LINH[:],
                      sem="linsp", r=[LINH], w=["lin_scr"])
                p.dma("gpsimd", win_scr.ap()[64 * h:64 * (h + 1), :], WINH[:],
                      sem="winsp", r=[WINH], w=["win_scr"])
        p.barrier()
        s1.close()

        s2 = ExitStack()
        sb2 = lambda n, s, d=F32: s2.enter_context(nc.sbuf_tensor(uid + n, s, d))
        ps2 = lambda n, s, d=F32: s2.enter_context(nc.psum_tensor(uid + n, s, d))
        Z = X
        LW2 = [sb2("lw2a", [128, CH], BF16), sb2("lw2b", [128, CH], BF16)]
        WOUT = [sb2(f"wob{ot}", [128, D], BF16) for ot in range(16)]
        psDE = [ps2(f"psDE{j}", [128, CH]) for j in range(DP)]
        for ot in range(16):
            tw = p.dma("gpsimd", WOUT[ot][:], wout_d.ap()[128 * ot:128 * (ot + 1), :],
                       sem="wog", w=[WOUT[ot]])
        p.retag_writers(WOUT, tw)
        for tc in range(NCH):
            tsl = slice(CH * tc, CH * (tc + 1))
            for ot in range(16):
                k = ot % 2
                src = lin_scr if ot < DP else win_scr
                ro = ot if ot < DP else ot - DP
                p.dma("scalar", LW2[k][:],
                      src.ap()[128 * ro:128 * (ro + 1), tsl], sem=f"lw2{k}",
                      r=["lin_scr" if ot < DP else "win_scr"], w=[LW2[k]])
                for j in range(DP):
                    p.op("tensor", mm(psDE[j][:], WOUT[ot][:, 128 * j:128 * (j + 1)],
                                      LW2[k][:], ot == 0, ot == 15),
                         r=[WOUT[ot], LW2[k]], w=[psDE[j]] if ot in (0, 15) else ())
            for j in range(DP):
                p.op("vector", tt(Z[j][:, tsl], Z[j][:, tsl], psDE[j][:], ALU.add),
                     r=[psDE[j]], w=[Z[j]])
        s2.close()
    p.barrier()


def build_prog_e():
    nc = bass.Bass("TRN2", target_bir_lowering=False)
    x_in = nc.dram_tensor("x_in", [D, T], F32, kind="ExternalInput")
    dr = {
        "e_wq": nc.dram_tensor("wq", [D, D], F32, kind="ExternalInput"),
        "e_wk": nc.dram_tensor("wk", [D, D], F32, kind="ExternalInput"),
        "e_wv": nc.dram_tensor("wv", [D, D], F32, kind="ExternalInput"),
        "e_wqf": nc.dram_tensor("wqf", [64, 16], F32, kind="ExternalInput"),
        "e_wkf": nc.dram_tensor("wkf", [64, 16], F32, kind="ExternalInput"),
        "e_wout": nc.dram_tensor("wout", [2 * D, D], F32, kind="ExternalInput"),
        "e_nw": nc.dram_tensor("nw", [128, DP], F32, kind="ExternalInput"),
        "e_gha": nc.dram_tensor("gha", [33, 153], BF16, kind="ExternalInput"),
        "e_ghb": nc.dram_tensor("ghb", [33, 153], BF16, kind="ExternalInput"),
        "e_triu": nc.dram_tensor("triu", [128, 128], F32, kind="ExternalInput"),
        "e_wmask": nc.dram_tensor("wmask", [128, 256], BF16, kind="ExternalInput"),
        "lin_scr": nc.dram_tensor("lin_scr", [D, T], BF16, kind="Internal"),
        "win_scr": nc.dram_tensor("win_scr", [D, T], BF16, kind="Internal"),
    }
    x_out = nc.dram_tensor("x_out", [D, T], F32, kind="ExternalOutput")
    stack = ExitStack()
    with stack:
        sb = lambda n, s, d=F32: stack.enter_context(nc.sbuf_tensor(n, s, d))
        p = Prog(nc, stack)
        c = Com(nc, stack)
        X = [sb(f"x{i}", [128, T]) for i in range(DP)]
        c.init(p)
        for i in range(DP):
            tx = p.dma("sync", X[i][:], x_in.ap()[128 * i:128 * (i + 1), :], sem="xing", w=[X[i]])
        p.retag_writers(X, tx)
        emit_e(p, c, nc, X, dr)
        for jj in range(DP):
            p.dma("sync", x_out.ap()[128 * jj:128 * (jj + 1), :], X[jj][:], sem="xout",
                  r=[X[jj]])
        p.finish()
        p.emit()
    return nc


def _prep_l1_attn(inputs):
    import math
    iu, ju = np.triu_indices(16)
    scale = np.where(iu == ju, 0.5, 1.0 / math.sqrt(2.0)).astype(np.float32)
    GA = np.zeros((33, 153), np.float32)
    GB = np.zeros((33, 153), np.float32)
    GA[32, 0] = 1.0
    GB[32, 0] = 1.0
    for i in range(16):
        GA[i, 1 + i] = 1.0
        GB[32, 1 + i] = 1.0
    for qn in range(136):
        GA[iu[qn], 17 + qn] = 1.0
        GB[ju[qn], 17 + qn] = scale[qn]
    rk = np.arange(128)[:, None]
    cq = np.arange(128)[None, :]
    wm_prev = np.where(rk >= cq + 64, 0.0, -1e30).astype(np.float32)
    wm_diag = np.where((rk <= cq) & (rk >= cq - 64), 0.0, -1e30).astype(np.float32)
    triu = np.triu(np.ones((128, 128), np.float32))
    return {
        "wq": np.ascontiguousarray(inputs["L1_attn_Wq"]),
        "wk": np.ascontiguousarray(inputs["L1_attn_Wk"]),
        "wv": np.ascontiguousarray(inputs["L1_attn_Wv"]),
        "wqf": np.ascontiguousarray(inputs["L1_attn_Wqf"]),
        "wkf": np.ascontiguousarray(inputs["L1_attn_Wkf"]),
        "wout": np.ascontiguousarray(inputs["L1_attn_Wout"]),
        "nw": np.ascontiguousarray(np.asarray(inputs["L1_attn_norm_w"]).reshape(DP, 128).T),
        "gha": np.ascontiguousarray(GA.astype(BF)),
        "ghb": np.ascontiguousarray(GB.astype(BF)),
        "triu": triu,
        "wmask": np.ascontiguousarray(
            np.concatenate([wm_prev, wm_diag], axis=1).astype(BF)),
    }


def run_e(xs, W):
    nc = get_prog("e")
    in_maps = [dict(x_in=xs[b], **W) for b in range(8)]
    res = run_bass_kernel_spmd(nc, in_maps, core_ids=list(range(8)))
    return [np.asarray(r["x_out"]) for r in res.results]



_PROG_CACHE = {}

def emit_d(p, c, nc, X, dr, uid=""):
    wg = dr["d_wg"]; wu = dr["d_wu"]; wd = dr["d_wd"]; g_scr = dr["g_scr"]
    with ExitStack() as ss:
        sbS = lambda n, s, d=F32: ss.enter_context(nc.sbuf_tensor(uid + n, s, d))
        Y = [sbS(f"y{i}", [128, T], BF16) for i in range(DP)]

        s1 = ExitStack()
        sb1 = lambda n, s, d=F32: s1.enter_context(nc.sbuf_tensor(uid + n, s, d))
        ps1 = lambda n, s, d=F32: s1.enter_context(nc.psum_tensor(uid + n, s, d))
        psA = ps1("psA", [1, CH])
        psB = ps1("psB", [1, CH])
        psR = [ps1("psR0", [128, CH]), ps1("psR1", [128, CH])]
        GPS = [ps1("gate0", [128, CH]), ps1("gate1", [128, CH])]
        UPS = [ps1("up0", [128, CH]), ps1("up1", [128, CH])]
        gst = sb1("gstage0", [128, T], BF16)
        WG = [sb1(f"wgb{i}", [128, 1536], BF16) for i in range(DP)]
        WU = [sb1(f"wub{i}", [128, 1536], BF16) for i in range(DP)]

        emit_ln(p, c, X, Y, psA, psB, psR, fold=None)

        for ih in range(2):
            csl = slice(1536 * ih, 1536 * (ih + 1))
            for d in range(DP):
                tg = p.dma("gpsimd", WG[d][:], wg.ap()[128 * d:128 * (d + 1), csl],
                           sem="wgg", w=[WG[d]])
                tu = p.dma("gpsimd", WU[d][:], wu.ap()[128 * d:128 * (d + 1), csl],
                           sem="wug", w=[WU[d]])
            p.retag_writers(WG, tg)
            p.retag_writers(WU, tu)
            for hp in range(12):
                hpg = 12 * ih + hp
                col = slice(128 * hp, 128 * (hp + 1))
                for tc in range(NCH):
                    seq = hp * NCH + tc
                    tsl = slice(CH * tc, CH * (tc + 1))
                    k = seq % 2
                    for d in range(DP):
                        p.op("tensor", mm(GPS[k][:], WG[d][:, col], Y[d][:, tsl],
                                          d == 0, d == DP - 1),
                             r=[WG[d], Y[d]], w=[GPS[k]] if d in (0, DP - 1) else ())
                    for d in range(DP):
                        p.op("tensor", mm(UPS[k][:], WU[d][:, col], Y[d][:, tsl],
                                          d == 0, d == DP - 1),
                             r=[WU[d], Y[d]], w=[UPS[k]] if d in (0, DP - 1) else ())
                    p.op("scalar", act(c.SQS[k][:], GPS[k][:], AF.Silu, scale=1.0 / 127.0),
                         r=[GPS[k]], w=[c.SQS[k]])
                    p.op("vector", tt(gst[:, tsl], c.SQS[k][:], UPS[k][:], ALU.mult),
                         r=[c.SQS[k], UPS[k]], w=[f"gst@{tc}"])
                p.dma("gpsimd", g_scr.ap()[128 * hpg:128 * (hpg + 1), :], gst[:], sem="gsp",
                      r=[f"gst@{t2}" for t2 in range(NCH)], w=["g_scr"])
            p.barrier()

        s1.close()

        s2 = ExitStack()
        sb2 = lambda n, s, d=F32: s2.enter_context(nc.sbuf_tensor(uid + n, s, d))
        ps2 = lambda n, s, d=F32: s2.enter_context(nc.psum_tensor(uid + n, s, d))
        GIN = [sb2("gin0", [128, CH], BF16), sb2("gin1", [128, CH], BF16)]
        WD = [sb2(f"wdb{i}", [128, D], BF16) for i in range(24)]
        dps = [ps2(f"dps{j}", [128, CH]) for j in range(DP)]
        for it in range(24):
            twd = p.dma("gpsimd", WD[it][:], wd.ap()[128 * it:128 * (it + 1), :],
                        sem="wdg", w=[WD[it]])
        p.retag_writers(WD, twd)
        for tc in range(NCH):
            tsl = slice(CH * tc, CH * (tc + 1))
            for it in range(24):
                k = it % 2
                p.dma("scalar", GIN[k][:], g_scr.ap()[128 * it:128 * (it + 1), tsl],
                      sem=f"gin{k}", r=["g_scr"], w=[GIN[k]])
                for j in range(DP):
                    p.op("tensor", mm(dps[j][:], WD[it][:, 128 * j:128 * (j + 1)], GIN[k][:],
                                      it == 0, it == 23),
                         r=[WD[it], GIN[k]], w=[dps[j]] if it in (0, 23) else ())
            for j in range(DP):
                p.op("vector", stt(X[j][:, tsl], dps[j][:], 0.5 / (127.0 * 127.0),
                                   X[j][:, tsl], ALU.mult, ALU.add),
                     r=[dps[j]], w=[X[j]])
        s2.close()
    p.barrier()


def build_prog_d():
    nc = bass.Bass("TRN2", target_bir_lowering=False)
    x_in = nc.dram_tensor("x_in", [D, T], F32, kind="ExternalInput")
    dr = {
        "d_wg": nc.dram_tensor("wg", [D, 3072], I32, kind="ExternalInput"),
        "d_wu": nc.dram_tensor("wu", [D, 3072], I32, kind="ExternalInput"),
        "d_wd": nc.dram_tensor("wd", [3072, D], I32, kind="ExternalInput"),
        "g_scr": nc.dram_tensor("g_scr", [3072, T], BF16, kind="Internal"),
    }
    x_out = nc.dram_tensor("x_out", [D, T], F32, kind="ExternalOutput")
    stack = ExitStack()
    with stack:
        sb = lambda n, s, d=F32: stack.enter_context(nc.sbuf_tensor(n, s, d))
        p = Prog(nc, stack)
        c = Com(nc, stack)
        X = [sb(f"x{i}", [128, T]) for i in range(DP)]
        c.init(p)
        for i in range(DP):
            tx = p.dma("sync", X[i][:], x_in.ap()[128 * i:128 * (i + 1), :], sem="xing", w=[X[i]])
        p.retag_writers(X, tx)
        emit_d(p, c, nc, X, dr)
        for j in range(DP):
            p.dma("sync", x_out.ap()[128 * j:128 * (j + 1), :], X[j][:], sem="xout", r=[X[j]])
        p.finish()
        p.emit()
    return nc



def _emit_rot_weights(p, dst, src, nh, dh, rot_half):
    rh = rot_half
    sv = src.ap().rearrange("p (h c) -> p h c", h=nh, c=dh)
    dv = dst.ap().rearrange("p (h c) -> p h c", h=nh, c=dh)
    p.op("vector", ts(dv[:, :, 0:rh], sv[:, :, rh:2 * rh], -1.0, ALU.mult),
         r=[src], w=[dst])
    p.op("vector", cp(dv[:, :, rh:2 * rh], sv[:, :, 0:rh]), r=[src], w=[dst])
    if 2 * rh < dh:
        p.op("vector", cp(dv[:, :, 2 * rh:dh], sv[:, :, 2 * rh:dh]), r=[src], w=[dst])


def emit_b(p, c, nc, X, dr, uid=""):
    wq_d = dr["b_wq"]; wk_d = dr["b_wk"]; wv_d = dr["b_wv"]; wo_d = dr["b_wo"]
    cb_d = dr["b_ropeC"]; sb_d = dr["b_ropeS"]; mask_d = dr["b_dmask"]
    o_scr = dr["o_scr"]
    NHB = 8
    GH = 2
    with ExitStack() as ss:
        sbS = lambda n, s, d=F32: ss.enter_context(nc.sbuf_tensor(uid + n, s, d))
        Y = [sbS(f"y{i}", [128, T], BF16) for i in range(DP)]

        s0 = ExitStack()
        sb0 = lambda n, s, d=F32: s0.enter_context(nc.sbuf_tensor(uid + n, s, d))
        ps0 = lambda n, s, d=F32: s0.enter_context(nc.psum_tensor(uid + n, s, d))
        psA = ps0("psA", [1, CH])
        psB = ps0("psB", [1, CH])
        psR = [ps0("psR0", [128, CH]), ps0("psR1", [128, CH])]
        emit_ln(p, c, X, Y, psA, psB, psR, fold=None)
        p.barrier()
        s0.close()

        s1 = ExitStack()
        sb1 = lambda n, s, d=F32: s1.enter_context(nc.sbuf_tensor(uid + n, s, d))
        ps1 = lambda n, s, d=F32: s1.enter_context(nc.psum_tensor(uid + n, s, d))
        CB = sb1("cb", [128, T], BF16)
        SBC = sb1("sbc", [128, T], BF16)
        MSK = sb1("msk", [128, 4 * CH], BF16)
        OSTG = [sb1("ostg0", [128, T], BF16), sb1("ostg1", [128, T], BF16)]
        p.dma("sync", CB[:], cb_d.ap(), sem="cbin", w=[CB])
        p.dma("sync", SBC[:], sb_d.ap(), sem="sbin", w=[SBC])
        p.dma("sync", MSK[:], mask_d.ap(), sem="mkin", w=[MSK])
        WQ = [sb1(f"wqb{d}", [128, GH * 128], BF16) for d in range(DP)]
        WQR = [sb1(f"wqrb{d}", [128, GH * 128], BF16) for d in range(DP)]
        WK = [sb1(f"wkb{d}", [128, GH * 128], BF16) for d in range(DP)]
        WKR = [sb1(f"wkrb{d}", [128, GH * 128], BF16) for d in range(DP)]
        WV = [sb1(f"wvb{d}", [128, GH * 128], BF16) for d in range(DP)]
        QH = [sb1(f"qh{h}", [128, T], BF16) for h in range(GH)]
        KH = [sb1(f"kh{h}", [128, T], BF16) for h in range(GH)]
        VT = [sb1(f"vt{t2}", [128, GH * 128], BF16) for t2 in range(NTT)]
        PB = [sb1("pb0", [128, CH], BF16), sb1("pb1", [128, CH], BF16)]
        REPS = sb1("reps", [128, CH])
        psS = [ps1("psS0", [128, CH]), ps1("psS1", [128, CH])]
        psO = ps1("psO", [128, CH])
        psDen = ps1("psDen", [1, CH])
        psRep = ps1("psRep", [128, CH])

        for g in range(NHB // GH):
            hbase = GH * g
            gsl = slice(128 * GH * g, 128 * GH * (g + 1))
            for d in range(DP):
                rs = slice(128 * d, 128 * (d + 1))
                tq = p.dma("gpsimd", WQ[d][:], wq_d.ap()[rs, gsl], sem="wqg", w=[WQ[d]])
                tk = p.dma("gpsimd", WK[d][:], wk_d.ap()[rs, gsl], sem="wkg", w=[WK[d]])
                tv = p.dma("gpsimd", WV[d][:], wv_d.ap()[rs, gsl], sem="wvg", w=[WV[d]])
            p.retag_writers(WQ, tq)
            p.retag_writers(WK, tk)
            p.retag_writers(WV, tv)
            for d in range(DP):
                _emit_rot_weights(p, WQR[d], WQ[d], GH, 128, 32)
                _emit_rot_weights(p, WKR[d], WK[d], GH, 128, 32)
            for h in range(GH):
                hsl = slice(128 * h, 128 * (h + 1))
                for qc in range(NCH):
                    tsl = slice(CH * qc, CH * (qc + 1))
                    for (psq, Wm, Wr, dstH) in ((psS[0], WQ, WQR, QH), (psS[1], WK, WKR, KH)):
                        for d in range(DP):
                            p.op("tensor", mm(psq[:], Wm[d][:, hsl], Y[d][:, tsl],
                                              d == 0, d == DP - 1),
                                 r=[Wm[d], Y[d]], w=[psq] if d in (0, DP - 1) else ())
                        for d in range(DP):
                            p.op("tensor", mm(psO[:], Wr[d][:, hsl], Y[d][:, tsl],
                                              d == 0, d == DP - 1),
                                 r=[Wr[d], Y[d]], w=[psO] if d in (0, DP - 1) else ())
                        p.op("vector", tt(c.SQS[0][:], psq[:], CB[:, tsl], ALU.mult),
                             r=[psq, CB], w=[c.SQS[0]])
                        p.op("vector", tt(c.SQS[1][:], psO[:], SBC[:, tsl], ALU.mult),
                             r=[psO, SBC], w=[c.SQS[1]])
                        p.op("vector", tt(dstH[h][:, tsl], c.SQS[0][:], c.SQS[1][:], ALU.add),
                             r=[c.SQS[0], c.SQS[1]], w=[f"{_key(dstH[h])}@{qc}"])
            for t2 in range(NTT):
                ssl = slice(128 * t2, 128 * (t2 + 1))
                for d in range(DP):
                    p.op("tensor", mm(psS[0][:, 0:128 * GH], Y[d][:, ssl], WV[d][:],
                                      d == 0, d == DP - 1),
                         r=[Y[d], WV[d]], w=[psS[0]] if d in (0, DP - 1) else ())
                p.op("scalar", scp(VT[t2][:], psS[0][:, 0:128 * GH]), r=[psS[0]], w=[VT[t2]])
            for h in range(GH):
                hg = hbase + h
                for qc in range(NCH):
                    tsl = slice(CH * qc, CH * (qc + 1))
                    qkeys = [f"{_key(QH[h])}@{qc}"]
                    nkt = 4 * qc + 4
                    for kt in range(nkt):
                        k2 = kt % 2
                        ksl = slice(128 * kt, 128 * (kt + 1))
                        p.op("tensor", mm(psS[k2][:], KH[h][:, ksl], QH[h][:, tsl],
                                          True, True),
                             r=[f"{_key(KH[h])}@{j}" for j in range(NCH)] + qkeys,
                             w=[psS[k2]])
                        r_rel = kt - 4 * qc
                        if r_rel >= 0:
                            p.op("vector", tt(c.SQS[k2][:], psS[k2][:],
                                              MSK[:, CH * r_rel:CH * (r_rel + 1)], ALU.add),
                                 r=[psS[k2], MSK], w=[c.SQS[k2]])
                            src = c.SQS[k2][:]
                            srck = [c.SQS[k2]]
                        else:
                            src = psS[k2][:]
                            srck = [psS[k2]]
                        p.op("scalar", act(PB[k2][:], src, AF.Exp, scale=1.0 / 128.0),
                             r=srck, w=[PB[k2]])
                        p.op("tensor", mm(psO[:], VT[kt][:, 128 * h:128 * (h + 1)], PB[k2][:],
                                          kt == 0, kt == nkt - 1),
                             r=[VT[kt], PB[k2]], w=[psO] if kt in (0, nkt - 1) else ())
                        p.op("tensor", mm(psDen[0:1, :], c.ones_col_bf[:], PB[k2][:],
                                          kt == 0, kt == nkt - 1),
                             r=[c.ones_col_bf, PB[k2]], w=[psDen] if kt in (0, nkt - 1) else ())
                    p.op("vector", recip(c.R0[:, 0:CH], psDen[0:1, :]), r=[psDen], w=[c.R0])
                    p.op("tensor", mm(psRep[:], c.ones_row, c.R0[:, 0:CH], True, True),
                         r=[c._ones_rows, c.R0], w=[psRep])
                    p.op("scalar", scp(REPS[:], psRep[:]), r=[psRep], w=[REPS])
                    p.op("vector", tt(OSTG[hg % 2][:, tsl], psO[:], REPS[:], ALU.mult),
                         r=[psO, REPS], w=[f"ostg{hg % 2}@{qc}"])
                p.dma("gpsimd", o_scr.ap()[128 * hg:128 * (hg + 1), :], OSTG[hg % 2][:],
                      sem=f"osp{hg % 2}",
                      r=[f"ostg{hg % 2}@{j}" for j in range(NCH)], w=["o_scr"])
        p.barrier()
        s1.close()

        s2 = ExitStack()
        sb2 = lambda n, s, d=F32: s2.enter_context(nc.sbuf_tensor(uid + n, s, d))
        ps2 = lambda n, s, d=F32: s2.enter_context(nc.psum_tensor(uid + n, s, d))
        OIN = [sb2(f"oin{ot}", [128, T], BF16) for ot in range(DP)]
        WO = [sb2(f"wob{ot}", [128, D], BF16) for ot in range(DP)]
        psD = [ps2("psD0", [128, CH]), ps2("psD1", [128, CH])]
        for i in range(DP):
            to = p.dma("scalar", OIN[i][:], o_scr.ap()[128 * i:128 * (i + 1), :], sem="oing",
                       r=["o_scr"], w=[OIN[i]])
        p.retag_writers(OIN, to)
        for ot in range(DP):
            tw = p.dma("gpsimd", WO[ot][:], wo_d.ap()[128 * ot:128 * (ot + 1), :], sem="wog",
                       w=[WO[ot]])
        p.retag_writers(WO, tw)
        for j in range(DP):
            jsl = slice(128 * j, 128 * (j + 1))
            for tc in range(NCH):
                tsl = slice(CH * tc, CH * (tc + 1))
                k = (j * NCH + tc) % 2
                for ot in range(DP):
                    p.op("tensor", mm(psD[k][:], WO[ot][:, jsl], OIN[ot][:, tsl],
                                      ot == 0, ot == DP - 1),
                         r=[WO[ot], OIN[ot]], w=[psD[k]] if ot in (0, DP - 1) else ())
                p.op("vector", tt(X[j][:, tsl], X[j][:, tsl], psD[k][:], ALU.add),
                     r=[psD[k]], w=[X[j]])
        s2.close()
    p.barrier()


def build_prog_b():
    nc = bass.Bass("TRN2", target_bir_lowering=False)
    x_in = nc.dram_tensor("x_in", [D, T], F32, kind="ExternalInput")
    dr = {
        "b_wq": nc.dram_tensor("wq", [D, D], F32, kind="ExternalInput"),
        "b_wk": nc.dram_tensor("wk", [D, D], F32, kind="ExternalInput"),
        "b_wv": nc.dram_tensor("wv", [D, D], F32, kind="ExternalInput"),
        "b_wo": nc.dram_tensor("wo", [D, D], F32, kind="ExternalInput"),
        "b_ropeC": nc.dram_tensor("ropeC", [128, T], BF16, kind="ExternalInput"),
        "b_ropeS": nc.dram_tensor("ropeS", [128, T], BF16, kind="ExternalInput"),
        "b_dmask": nc.dram_tensor("dmask", [128, 4 * CH], BF16, kind="ExternalInput"),
        "o_scr": nc.dram_tensor("o_scr", [D, T], BF16, kind="Internal"),
    }
    x_out = nc.dram_tensor("x_out", [D, T], F32, kind="ExternalOutput")
    stack = ExitStack()
    with stack:
        sb = lambda n, s, d=F32: stack.enter_context(nc.sbuf_tensor(n, s, d))
        p = Prog(nc, stack)
        c = Com(nc, stack)
        X = [sb(f"x{i}", [128, T]) for i in range(DP)]
        c.init(p)
        for i in range(DP):
            tx = p.dma("sync", X[i][:], x_in.ap()[128 * i:128 * (i + 1), :], sem="xing", w=[X[i]])
        p.retag_writers(X, tx)
        emit_b(p, c, nc, X, dr)
        for j in range(DP):
            p.dma("sync", x_out.ap()[128 * j:128 * (j + 1), :], X[j][:], sem="xout", r=[X[j]])
        p.finish()
        p.emit()
    return nc


def get_prog(name):
    global _NEEDED
    if name not in _PROG_CACHE:
        _NEEDED = None
        _BUILDERS[name]()
        _NEEDED = {k: frozenset(v) for k, v in Prog.last.emitted_waits.items()}
        try:
            _PROG_CACHE[name] = _BUILDERS[name]()
        finally:
            _NEEDED = None
    return _PROG_CACHE[name]



def _prep_l0_attn(inputs):
    wq = np.ascontiguousarray(np.transpose(inputs["L0_attn_wq"], (1, 0, 2)).reshape(D, D))
    wk = np.ascontiguousarray(np.transpose(inputs["L0_attn_wk"], (1, 0, 2)).reshape(D, D))
    wv = np.ascontiguousarray(np.transpose(inputs["L0_attn_wv"], (1, 0, 2)).reshape(D, D))
    wo = np.ascontiguousarray(inputs["L0_attn_wo"])
    t = np.arange(T, dtype=np.float64)[None, :]
    i = np.arange(32, dtype=np.float64)[:, None]
    rad = t * np.power(10000.0, -i / 32.0)
    cosr = np.cos(rad).astype(np.float32)
    sinr = np.sin(rad).astype(np.float32)
    ropeC = np.concatenate([cosr, cosr, np.ones((64, T), np.float32)], 0).astype(BF)
    ropeS = np.concatenate([sinr, sinr, np.zeros((64, T), np.float32)], 0).astype(BF)
    rk = np.arange(128)[:, None]
    cq = np.arange(CH)[None, :]
    dmask = np.concatenate(
        [np.where(128 * rr + rk <= cq, 0.0, -1e30).astype(np.float32) for rr in range(4)],
        axis=1).astype(BF)
    return {"wq": wq, "wk": wk, "wv": wv, "wo": wo,
            "ropeC": np.ascontiguousarray(ropeC), "ropeS": np.ascontiguousarray(ropeS),
            "dmask": np.ascontiguousarray(dmask)}


def _prep_l0_ffn_attn(inputs):
    F = np.asarray(inputs["L0_ffn_freqs_cis"])
    idx = np.repeat(np.arange(0, 64, 2), 2)
    crow = F[:, idx].T
    srow = F[:, idx + 1].T
    ropeC = np.tile(crow, (2, 1)).astype(BF)
    ropeS = np.tile(srow, (2, 1)).astype(BF)
    rk = np.arange(128)[:, None]
    cq = np.arange(CH)[None, :]
    dmask = np.concatenate(
        [np.where(128 * rr + rk <= cq, 0.0, -1e30).astype(np.float32) for rr in range(4)],
        axis=1).astype(BF)
    return {
        "wq": np.ascontiguousarray(inputs["L0_ffn_wq"]),
        "wk": np.ascontiguousarray(inputs["L0_ffn_wk"]),
        "wv": np.ascontiguousarray(inputs["L0_ffn_wv"]),
        "wo": np.ascontiguousarray(inputs["L0_ffn_wo"]),
        "anw": np.ascontiguousarray(np.asarray(inputs["L0_ffn_attn_norm_w"]).reshape(DP, 128).T),
        "ropeC": np.ascontiguousarray(ropeC),
        "ropeS": np.ascontiguousarray(ropeS),
        "dmask": np.ascontiguousarray(dmask),
    }


def run_a1(xs, W):
    nc = get_prog("a1")
    in_maps = [dict(x_in=xs[b], **W) for b in range(8)]
    res = run_bass_kernel_spmd(nc, in_maps, core_ids=list(range(8)))
    return ([np.asarray(r["z_out"]) for r in res.results],
            [np.asarray(r["h_out"]) for r in res.results])


def run_b(xs, W):
    nc = get_prog("b")
    in_maps = [dict(x_in=xs[b], **W) for b in range(8)]
    res = run_bass_kernel_spmd(nc, in_maps, core_ids=list(range(8)))
    return [np.asarray(r["x_out"]) for r in res.results]


def _prep_l1_ffn(inputs):
    return {
        "wg": np.ascontiguousarray(inputs["L1_ffn_w_gate"].T),
        "wu": np.ascontiguousarray(inputs["L1_ffn_w_up"].T),
        "wd": np.ascontiguousarray(inputs["L1_ffn_w_down"].T),
    }


def run_d(xs, W):
    nc = get_prog("d")
    in_maps = [dict(x_in=xs[b], **W) for b in range(8)]
    res = run_bass_kernel_spmd(nc, in_maps, core_ids=list(range(8)))
    return [np.asarray(r["x_out"]) for r in res.results]




def build_mega():
    nc = bass.Bass("TRN2", target_bir_lowering=False)
    x_in = nc.dram_tensor("x_in", [D, T], F32, kind="ExternalInput")
    x_out = nc.dram_tensor("x_out", [D, T], F32, kind="ExternalOutput")
    dr = {}
    for nm, shape, dt in [
        ("a1_wq", [D, D], F32), ("a1_wk", [D, D], F32), ("a1_wv", [D, D], F32),
        ("a1_wo", [D, D], F32), ("a1_anw", [128, DP], F32),
        ("a1_ropeC", [128, T], BF16), ("a1_ropeS", [128, T], BF16),
        ("a1_dmask", [128, 4 * CH], BF16),
        ("a2_fnw", [128, DP], F32), ("a2_rw", [D, 8], F32),
        ("a2_w1", [8, D, 2048], F32), ("a2_w3", [8, D, 2048], F32),
        ("a2_w2", [8, 2048, D], F32), ("a2_ident", [128, 128], F32),
        ("a2_sel8", [8, D], F32),
        ("b_wq", [D, D], F32), ("b_wk", [D, D], F32), ("b_wv", [D, D], F32),
        ("b_wo", [D, D], F32), ("b_ropeC", [128, T], BF16),
        ("b_ropeS", [128, T], BF16), ("b_dmask", [128, 4 * CH], BF16),
        ("d_wg", [D, 3072], I32), ("d_wu", [D, 3072], I32), ("d_wd", [3072, D], I32),
        ("e_wq", [D, D], F32), ("e_wk", [D, D], F32), ("e_wv", [D, D], F32),
        ("e_wqf", [64, 16], F32), ("e_wkf", [64, 16], F32),
        ("e_wout", [2 * D, D], F32), ("e_nw", [128, DP], F32),
        ("e_gha", [33, 153], BF16), ("e_ghb", [33, 153], BF16),
        ("e_triu", [128, 128], F32), ("e_wmask", [128, 256], BF16),
    ]:
        dr[nm] = nc.dram_tensor(nm, shape, dt, kind="ExternalInput")
    for nm, shape, dt in [
        ("o_scr", [D, T], BF16), ("y_scr", [D, T], BF16), ("h_scr", [D, T], BF16),
        ("g_scr", [3072, T], BF16), ("lin_scr", [D, T], BF16), ("win_scr", [D, T], BF16),
    ]:
        dr[nm] = nc.dram_tensor(nm, shape, dt, kind="Internal")

    stack = ExitStack()
    with stack:
        sb = lambda n, s, d=F32: stack.enter_context(nc.sbuf_tensor(n, s, d))
        p = Prog(nc, stack)
        c = Com(nc, stack)
        X = [sb(f"x{i}", [128, T]) for i in range(DP)]
        c.init(p)
        for i in range(DP):
            tx = p.dma("sync", X[i][:], x_in.ap()[128 * i:128 * (i + 1), :], sem="xing", w=[X[i]])
        p.retag_writers(X, tx)
        emit_a1(p, c, nc, X, dr, uid="s1_")
        emit_a2(p, c, nc, X, dr, uid="s2_")
        emit_b(p, c, nc, X, dr, uid="s3_")
        emit_a1(p, c, nc, X, dr, uid="s4_")
        emit_a2(p, c, nc, X, dr, uid="s5_")
        emit_d(p, c, nc, X, dr, uid="s6_")
        emit_e(p, c, nc, X, dr, uid="s7_")
        emit_d(p, c, nc, X, dr, uid="s8_")
        for j in range(DP):
            p.dma("sync", x_out.ap()[128 * j:128 * (j + 1), :], X[j][:], sem="xout", r=[X[j]])
        p.finish()
        p.emit()
    return nc


def _mega_inmap(inputs):
    W_a1 = _prep_l0_ffn_attn(inputs)
    W_a2 = _prep_l0_ffn_moe(inputs)
    W_b = _prep_l0_attn(inputs)
    W_d = _prep_l1_ffn(inputs)
    W_e = _prep_l1_attn(inputs)
    m = {}
    for pre, Wd in (("a1_", W_a1), ("a2_", W_a2), ("b_", W_b), ("d_", W_d), ("e_", W_e)):
        for k, v in Wd.items():
            m[pre + k] = v
    return m


FUSED = True


def kernel(**inputs):
    inputs = {k: np.asarray(v) for k, v in inputs.items()}
    xs = [np.ascontiguousarray(inputs["L0_attn_x"][b].T) for b in range(8)]
    if FUSED:
        base = _mega_inmap(inputs)
        nc = get_prog("mega")
        in_maps = [dict(x_in=xs[b], **base) for b in range(8)]
        res = run_bass_kernel_spmd(nc, in_maps, core_ids=list(range(8)))
        return np.stack([np.asarray(r["x_out"]).T for r in res.results]).astype(np.float32)
    W_a1 = _prep_l0_ffn_attn(inputs)
    W_a2 = _prep_l0_ffn_moe(inputs)
    W_b = _prep_l0_attn(inputs)
    W_d = _prep_l1_ffn(inputs)
    W_e = _prep_l1_attn(inputs)
    zs, hs = run_a1(xs, W_a1)
    xs = run_a2(hs, zs, W_a2)
    xs = run_b(xs, W_b)
    zs, hs = run_a1(xs, W_a1)
    xs = run_a2(hs, zs, W_a2)
    xs = run_d(xs, W_d)
    xs = run_e(xs, W_e)
    xs = run_d(xs, W_d)
    return np.stack([x.T for x in xs]).astype(np.float32)


_BUILDERS = {"d": build_prog_d, "b": build_prog_b, "a1": build_prog_a1, "a2": build_prog_a2, "e": build_prog_e, "mega": build_mega}


# version 05
# speedup vs baseline: 1.2374x; 1.0013x over previous
import numpy as np
import ml_dtypes
from contextlib import ExitStack

import concourse.bass as bass
import concourse.mybir as mybir
from concourse.bass_utils import run_bass_kernel_spmd

F32 = mybir.dt.float32
BF16 = mybir.dt.bfloat16
I32 = mybir.dt.int32
AX = mybir.AxisListType.X
AF = mybir.ActivationFunctionType
ALU = mybir.AluOpType
BF = ml_dtypes.bfloat16

D = 1024
DP = 8
T = 2048
NCH = 4
CH = 512
NTT = 16


def _ap(x):
    if hasattr(x, "ap") and callable(getattr(x, "ap", None)):
        return x.ap()
    return x


def _key(x):
    if isinstance(x, str):
        return x
    if hasattr(x, "name"):
        return x.name
    if hasattr(x, "tensor"):
        return x.tensor.name
    raise TypeError(f"bad key {x!r}")


_NEEDED = None


class Prog:
    ENGINES = ("tensor", "vector", "scalar", "gpsimd", "sync")
    last = None

    def __init__(self, nc, stack):
        self.nc = nc
        self.stack = stack
        self.ops = {e: [] for e in self.ENGINES}
        self.sems = {}
        self.cnt = {}
        self.waited = {e: {} for e in self.ENGINES}
        for e in self.ENGINES:
            self._mksem("p_" + e)
        self.dma_sems = set()
        self.last_w = {}
        self.readers = {}
        self.needed = _NEEDED
        self.attached = {}
        self.emitted_waits = {}
        Prog.last = self

    def _mksem(self, name):
        self.sems[name] = self.stack.enter_context(self.nc.semaphore(name))
        self.cnt[name] = 0

    def _track_pre(self, rk, wk):
        auto = []
        for k in rk:
            t = self.last_w.get(k)
            if t:
                auto.append(t)
        for k in wk:
            t = self.last_w.get(k)
            if t:
                auto.append(t)
            auto.extend(self.readers.get(k, {}).values())
        return auto

    def _track_post(self, rk, wk, slot, tok):
        for k in wk:
            self.last_w[k] = tok
            self.readers[k] = {}
        for k in rk:
            if k not in wk:
                self.readers.setdefault(k, {})[slot] = tok

    def op(self, eng, fn, waits=(), r=(), w=(), sig=False):
        rk = [_key(x) for x in r]
        wk = [_key(x) for x in w]
        auto = self._track_pre(rk, wk)
        for t in waits:
            self._wait(eng, t)
        for t in auto:
            self._wait(eng, t)
        need_sig = sig or bool(rk) or bool(wk)
        if need_sig:
            sname = "p_" + eng
            self.cnt[sname] += 1
            v = self.cnt[sname]
            tok = (sname, v)
            if self.needed is None or v in self.needed.get(sname, ()):
                delta = v - self.attached.get(sname, 0)
                self.attached[sname] = v
                sh = self.sems[sname]
                self.ops[eng].append(
                    lambda E, fn=fn, sh=sh, delta=delta: fn(E).then_inc(sh, delta))
            else:
                self.ops[eng].append(fn)
        else:
            self.ops[eng].append(fn)
            tok = None
        if tok:
            self._track_post(rk, wk, eng, tok)
        return tok

    def dma(self, q, out, in_, sem, waits=(), r=(), w=()):
        rk = [_key(x) for x in r]
        wk = [_key(x) for x in w]
        auto = self._track_pre(rk, wk)
        for t in waits:
            self._wait(q, t)
        for t in auto:
            self._wait(q, t)
        sname = "dma_" + sem
        if sname not in self.sems:
            self._mksem(sname)
            self.dma_sems.add(sname)
        self.cnt[sname] += 16
        v = self.cnt[sname]
        sh = self.sems[sname]
        self.ops[q].append(
            lambda E, out=out, in_=in_, sh=sh: E.dma_start(out=out, in_=in_).then_inc(sh, 16)
        )
        tok = (sname, v)
        self._track_post(rk, wk, sname, tok)
        return tok

    def retag_writers(self, keys, tok):
        for k in keys:
            self.last_w[_key(k)] = tok

    def _wait(self, eng, tok):
        if tok is None:
            return
        sname, v = tok
        if self.waited[eng].get(sname, 0) >= v:
            return
        self.waited[eng][sname] = v
        self.emitted_waits.setdefault(sname, set()).add(v)
        sh = self.sems[sname]
        self.ops[eng].append(lambda E, sh=sh, v=v: E.wait_ge(sh, v))

    def barrier(self):
        toks = [("p_" + e, self.cnt["p_" + e]) for e in self.ENGINES if self.cnt["p_" + e]]
        for s in sorted(self.dma_sems):
            if self.cnt[s]:
                toks.append((s, self.cnt[s]))
        for e in self.ENGINES:
            for t in toks:
                self._wait(e, t)

    def finish(self):
        for s in sorted(self.dma_sems):
            if self.cnt[s]:
                sh = self.sems[s]
                v = self.cnt[s]
                if self.waited["sync"].get(s, 0) < v:
                    self.waited["sync"][s] = v
                    self.ops["sync"].append(lambda E, sh=sh, v=v: E.wait_ge(sh, v))

    def emit(self):
        block = self.stack.enter_context(self.nc.Block())
        for e in self.ENGINES:
            ops = list(self.ops[e])

            def runner(E, ops=ops):
                for f in ops:
                    f(E)

            getattr(block, e)(runner)



def mm(out, lhsT, rhs, start, stop):
    return lambda E: E.matmul(out, lhsT, rhs, start=start, stop=stop)

def act(out, in_, func, bias=0.0, scale=1.0):
    return lambda E: E.activation(out, in_, func, bias=bias, scale=scale)

def tt(out, a, b, op):
    return lambda E: E.tensor_tensor(out, a, b, op)

def ts(out, a, s, op):
    return lambda E: E.tensor_single_scalar(out, a, s, op)

def stt(out, in0, scalar, in1, op0, op1):
    return lambda E: E.scalar_tensor_tensor(out, in0, scalar, in1, op0, op1)

def cp(out, in_):
    return lambda E: E.tensor_copy(out, in_)

def scp(out, in_):
    return lambda E: E.copy(out, in_)

def smul(out, in_, m):
    return lambda E: E.mul(out, in_, m)

def memset(ap, v):
    return lambda E: E.memset(ap, v)

def recip(out, in_):
    return lambda E: E.reciprocal(out, in_)

def rmax(out, in_, negate=False):
    return lambda E: E.reduce_max(out, in_, axis=AX, negate=negate)

def rsum(out, in_):
    return lambda E: E.reduce_sum(out, in_, axis=AX)

def max8(out, in_):
    return lambda E: E.max(out, in_)

def transp(out, in_, ident):
    return lambda E: E.transpose(out, in_, ident)



class Com:
    def __init__(self, nc, stack):
        sb = lambda n, s, d=F32: stack.enter_context(nc.sbuf_tensor(n, s, d))
        self.SCR_A = sb("com_scr_a", [128, T])
        self.SCR_B = sb("com_scr_b", [128, T])
        self.TMP = sb("com_tmp0", [128, T])
        self.R0 = self.TMP[0:1, :]
        self.R1 = self.SCR_B[0:1, :]
        self.SQS = [sb("com_sqs0", [128, CH]), sb("com_sqs1", [128, CH])]
        self.ones_col = sb("com_ones_col", [128, 1])
        self.ones_col_bf = sb("com_ones_col_bf", [128, 1], BF16)
        self._ones_rows = sb("com_ones_row", [65, 128])
        self.ones_row = self._ones_rows[0:1, :]
        self._cvals = sb("com_cvals", [128, 4])

    def cval(self, idx, base=0, n=1):
        return self._cvals[base:base + n, idx:idx + 1]

    def init(self, p):
        p.op("vector", memset(self.ones_col[:], 1.0), w=[self.ones_col])
        p.op("vector", memset(self.ones_col_bf[:], 1.0), w=[self.ones_col_bf])
        p.op("vector", memset(self._ones_rows[:], 1.0), w=[self._ones_rows])
        p.op("vector", memset(self._cvals[:, 0:1], 1e-5), w=["cval0"])
        p.op("vector", memset(self._cvals[:, 1:2], 1e-6), w=["cval1"])



def emit_ln(p, c, X, Y, psA, psB, psR, fold=None, eps=1e-5):
    for ci in range(NCH):
        sl = slice(ci * CH, (ci + 1) * CH)
        for i in range(DP):
            p.op("tensor", mm(psA[0:1, :], c.ones_col[:], X[i][:, sl], i == 0, i == DP - 1),
                 r=[c.ones_col, X[i]], w=[psA] if i in (0, DP - 1) else ())
        for i in range(DP):
            k = i % 2
            p.op("scalar", act(c.SQS[k][:], X[i][:, sl], AF.Square), r=[X[i]], w=[c.SQS[k]])
            p.op("tensor", mm(psB[0:1, :], c.ones_col[:], c.SQS[k][:], i == 0, i == DP - 1),
                 r=[c.ones_col, c.SQS[k]], w=[psB] if i in (0, DP - 1) else ())
        p.op("scalar", smul(c.R0[:, sl], psA[0:1, :], 1.0 / D), r=[psA], w=[c.R0])
        p.op("scalar", smul(c.R1[:, sl], psB[0:1, :], 1.0 / D), r=[psB], w=[c.R1])

    r_mean = c.R0[:]
    r_msq = c.R1[:]
    for ci in range(NCH):
        sl = slice(ci * CH, (ci + 1) * CH)
        p.op("scalar", act(psA[0:1, :], r_mean[:, sl], AF.Square), r=[c.R0], w=[psA])
        p.op("vector", tt(r_msq[:, sl], r_msq[:, sl], psA[0:1, :], ALU.subtract),
             r=[psA, c.R1], w=[c.R1])
    eps_idx = 0 if eps == 1e-5 else 1
    p.op("scalar", act(r_msq, r_msq, AF.Sqrt, bias=c.cval(eps_idx)),
         r=["cval0" if eps_idx == 0 else "cval1", c.R1], w=[c.R1])
    p.op("vector", recip(r_msq, r_msq), r=[c.R1], w=[c.R1])
    p.op("vector", tt(r_mean, r_mean, r_msq, ALU.mult), r=[c.R0, c.R1], w=[c.R0])

    for ci in range(NCH):
        sl = slice(ci * CH, (ci + 1) * CH)
        k = ci % 2
        p.op("tensor", mm(psR[k][:], c.ones_row, r_msq[:, sl], True, True),
             r=[c._ones_rows, c.R1], w=[psR[k]])
        p.op("scalar", scp(c.SCR_A[:, sl], psR[k][:]), r=[psR[k]], w=[c.SCR_A])
    for ci in range(NCH):
        sl = slice(ci * CH, (ci + 1) * CH)
        k = ci % 2
        p.op("tensor", mm(psR[k][:], c.ones_row, r_mean[:, sl], True, True),
             r=[c._ones_rows, c.R0], w=[psR[k]])
        p.op("scalar", scp(c.SCR_B[:, sl], psR[k][:]), r=[psR[k]], w=[c.SCR_B])

    y_toks = [None] * DP
    z_toks = [None] * DP
    for i in range(DP):
        p.op("gpsimd", tt(c.TMP[:], X[i][:], c.SCR_A[:], ALU.mult),
             r=[X[i], c.SCR_A], w=[c.TMP])
        y_toks[i] = p.op("vector", tt(Y[i][:], c.TMP[:], c.SCR_B[:], ALU.subtract),
                         r=[c.TMP, c.SCR_B], w=[Y[i]])
        if fold is not None:
            z_toks[i] = p.op("vector",
                             stt(X[i][:], Y[i][:], float(fold), X[i][:], ALU.mult, ALU.add),
                             r=[Y[i]], w=[X[i]])
    return y_toks, (z_toks if fold is not None else None)



def emit_rms(p, c, IN, psB, psR):
    for ci in range(NCH):
        sl = slice(ci * CH, (ci + 1) * CH)
        for i in range(DP):
            k = i % 2
            p.op("scalar", act(c.SQS[k][:], IN[i][:, sl], AF.Square), r=[IN[i]], w=[c.SQS[k]])
            p.op("tensor", mm(psB[0:1, :], c.ones_col[:], c.SQS[k][:], i == 0, i == DP - 1),
                 r=[c.ones_col, c.SQS[k]], w=[psB] if i in (0, DP - 1) else ())
        p.op("scalar", smul(c.R1[:, sl], psB[0:1, :], 1.0 / D), r=[psB], w=[c.R1])
    p.op("scalar", act(c.R1[:], c.R1[:], AF.Sqrt, bias=c.cval(1)),
         r=["cval1", c.R1], w=[c.R1])
    p.op("vector", recip(c.R1[:], c.R1[:]), r=[c.R1], w=[c.R1])
    for ci in range(NCH):
        sl = slice(ci * CH, (ci + 1) * CH)
        k = ci % 2
        p.op("tensor", mm(psR[k][:], c.ones_row, c.R1[:, sl], True, True),
             r=[c._ones_rows, c.R1], w=[psR[k]])
        p.op("scalar", scp(c.SCR_A[:, sl], psR[k][:]), r=[psR[k]], w=[c.SCR_A])



def emit_a1(p, c, nc, X, dr, uid=""):
    wq_d = dr["a1_wq"]; wk_d = dr["a1_wk"]; wv_d = dr["a1_wv"]; wo_d = dr["a1_wo"]
    anw_d = dr["a1_anw"]; cb_d = dr["a1_ropeC"]; sb_d = dr["a1_ropeS"]
    mask_d = dr["a1_dmask"]
    o_scr = dr["o_scr"]; y_scr = dr["y_scr"]; h_out = dr["h_scr"]
    GH = 4
    with ExitStack() as ss:
        sbS = lambda n, s, d=F32: ss.enter_context(nc.sbuf_tensor(uid + n, s, d))
        X1 = [sbS(f"x1_{i}", [128, T], BF16) for i in range(DP)]
        ANW = sbS("anw_sb", [128, DP])

        s0 = ExitStack()
        sb0 = lambda n, s, d=F32: s0.enter_context(nc.sbuf_tensor(uid + n, s, d))
        ps0 = lambda n, s, d=F32: s0.enter_context(nc.psum_tensor(uid + n, s, d))
        Y = [sb0(f"y{i}", [128, T], BF16) for i in range(DP)]
        psA = ps0("psA", [1, CH])
        psB = ps0("psB", [1, CH])
        psR = [ps0("psR0", [128, CH]), ps0("psR1", [128, CH])]
        p.dma("scalar", ANW[:], anw_d.ap(), sem="anwin", w=[ANW])
        emit_ln(p, c, X, Y, psA, psB, psR, fold=0.5)
        emit_rms(p, c, Y, psB, psR)
        for i in range(DP):
            p.op("vector", stt(X1[i][:], Y[i][:], ANW[:, i:i + 1], c.SCR_A[:],
                               ALU.mult, ALU.mult),
                 r=[Y[i], ANW, c.SCR_A], w=[X1[i]])
            p.dma("gpsimd", y_scr.ap()[128 * i:128 * (i + 1), :], Y[i][:],
                  sem=f"ysp{i % 2}", r=[Y[i]], w=["y_scr"])
        p.barrier()
        s0.close()

        s1 = ExitStack()
        sb1 = lambda n, s, d=F32: s1.enter_context(nc.sbuf_tensor(uid + n, s, d))
        ps1 = lambda n, s, d=F32: s1.enter_context(nc.psum_tensor(uid + n, s, d))
        CB = sb1("cb", [128, T], BF16)
        SBC = sb1("sbc", [128, T], BF16)
        MSK = sb1("msk", [128, 4 * CH], BF16)
        OSTG = [sb1("ostg0", [128, T], BF16), sb1("ostg1", [128, T], BF16)]
        p.dma("sync", CB[:], cb_d.ap(), sem="cbin", w=[CB])
        p.dma("sync", SBC[:], sb_d.ap(), sem="sbin", w=[SBC])
        p.dma("sync", MSK[:], mask_d.ap(), sem="mkin", w=[MSK])
        GW = GH * 64
        WQA = sb1("wqall", [128, DP * GW], BF16)
        WKA = sb1("wkall", [128, DP * GW], BF16)
        WVA = sb1("wvall", [128, DP * GW], BF16)
        WQ = [WQA[:, GW * d:GW * (d + 1)] for d in range(DP)]
        WK = [WKA[:, GW * d:GW * (d + 1)] for d in range(DP)]
        WV = [WVA[:, GW * d:GW * (d + 1)] for d in range(DP)]
        WQR = [sb1(f"wqrb{d}", [128, GW], BF16) for d in range(DP)]
        WKR = [sb1(f"wkrb{d}", [128, GW], BF16) for d in range(DP)]
        QH = [sb1(f"qh{j}", [128, T], BF16) for j in range(GH // 2)]
        KH = [sb1(f"kh{j}", [128, T], BF16) for j in range(GH // 2)]
        VT = [sb1(f"vt{t2}", [128, GW], BF16) for t2 in range(NTT)]
        PB = [sb1("pb0", [128, CH], BF16), sb1("pb1", [128, CH], BF16)]
        REPS = sb1("reps", [128, CH])
        NEGBC = sb1("negbc", [128, 1])
        psS = [ps1("psS0", [128, CH]), ps1("psS1", [128, CH])]
        psO = ps1("psO", [64, CH])
        psDen = ps1("psDen", [1, CH])
        psRep = ps1("psRep", [128, CH])

        for g in range(4):
            gsl = slice(GW * g, GW * (g + 1))
            for (WX, wsrc, snm) in ((WQ, wq_d, "wqg"), (WK, wk_d, "wkg"), (WV, wv_d, "wvg")):
                tw = p.dma("gpsimd", WX[0].tensor.ap().rearrange("p (d i) -> p d i", d=DP),
                           wsrc.ap().rearrange("(d p) i -> p d i", p=128)[:, :, gsl],
                           sem=snm, w=[WX[0]])
                p.retag_writers(WX, tw)
            for d in range(DP):
                _emit_rot_weights_il(p, WQR[d], WQ[d], GH, 64)
                _emit_rot_weights_il(p, WKR[d], WK[d], GH, 64)
            for j in range(GH // 2):
                hsl = slice(128 * j, 128 * (j + 1))
                for qc in range(NCH):
                    tsl = slice(CH * qc, CH * (qc + 1))
                    for (psq, Wm, Wr, dstH) in ((psS[0], WQ, WQR, QH), (psS[1], WK, WKR, KH)):
                        for d in range(DP):
                            p.op("tensor", mm(psq[:], Wm[d][:, hsl], X1[d][:, tsl],
                                              d == 0, d == DP - 1),
                                 r=[Wm[d], X1[d]], w=[psq] if d in (0, DP - 1) else ())
                        for d in range(DP):
                            p.op("tensor", mm(psRep[:], Wr[d][:, hsl], X1[d][:, tsl],
                                              d == 0, d == DP - 1),
                                 r=[Wr[d], X1[d]], w=[psRep] if d in (0, DP - 1) else ())
                        p.op("vector", tt(c.SQS[0][:], psq[:], CB[:, tsl], ALU.mult),
                             r=[psq, CB], w=[c.SQS[0]])
                        p.op("vector", tt(c.SQS[1][:], psRep[:], SBC[:, tsl], ALU.mult),
                             r=[psRep, SBC], w=[c.SQS[1]])
                        p.op("vector", tt(dstH[j][:, tsl], c.SQS[0][:], c.SQS[1][:], ALU.add),
                             r=[c.SQS[0], c.SQS[1]], w=[f"{_key(dstH[j])}@{qc}"])
            for t2 in range(NTT):
                ssl = slice(128 * t2, 128 * (t2 + 1))
                for d in range(DP):
                    p.op("tensor", mm(psS[0][:, 0:GW], X1[d][:, ssl], WV[d][:],
                                      d == 0, d == DP - 1),
                         r=[X1[d], WV[d]], w=[psS[0]] if d in (0, DP - 1) else ())
                p.op("scalar", scp(VT[t2][:], psS[0][:, 0:GW]), r=[psS[0]], w=[VT[t2]])
            for hh in range(GH):
                hg = GH * g + hh
                j = hh // 2
                half = hh % 2
                rows = slice(64 * half, 64 * (half + 1))
                for (HT, o0) in ((QH[j], 0), (KH[j], 4)):
                    p.op("scalar", act(c.SCR_B[:], HT[:], AF.Square),
                         r=[f"{_key(HT)}@{q2}" for q2 in range(NCH)], w=[c.SCR_B])
                    for ci in range(NCH):
                        sl2 = slice(ci * CH, (ci + 1) * CH)
                        p.op("tensor", mm(psDen[0:1, :], c.ones_col[rows, :],
                                          c.SCR_B[rows, sl2], True, True),
                             r=[c.ones_col, c.SCR_B], w=[psDen])
                        p.op("vector", rmax(c.R0[:, o0 + ci:o0 + ci + 1], psDen[0:1, :]),
                             r=[psDen], w=[c.TMP])
                p.op("vector", rmax(c.R0[:, 8:9], c.R0[:, 0:4]), r=[c.TMP], w=[c.TMP])
                p.op("vector", rmax(c.R0[:, 9:10], c.R0[:, 4:8]), r=[c.TMP], w=[c.TMP])
                p.op("vector", tt(c.R0[:, 10:11], c.R0[:, 8:9], c.R0[:, 9:10], ALU.mult),
                     r=[c.TMP], w=[c.TMP])
                p.op("scalar", act(c.R0[:, 11:12], c.R0[:, 10:11], AF.Sqrt),
                     r=[c.TMP], w=[c.TMP])
                p.op("vector", ts(c.R0[:, 12:13], c.R0[:, 11:12], -1.0 / 8.0, ALU.mult),
                     r=[c.TMP], w=[c.TMP])
                p.op("tensor", mm(psRep[:, 0:1], c.ones_row, c.R0[:, 12:13], True, True),
                     r=[c._ones_rows, c.TMP], w=[psRep])
                p.op("scalar", scp(NEGBC[:], psRep[:, 0:1]), r=[psRep], w=[NEGBC])
                for qc in range(NCH):
                    tsl = slice(CH * qc, CH * (qc + 1))
                    nkt = 4 * qc + 4
                    for kt in range(nkt):
                        k2 = kt % 2
                        ksl = slice(128 * kt, 128 * (kt + 1))
                        p.op("tensor", mm(psS[k2][:, 0:CH], KH[j][rows, ksl], QH[j][rows, tsl],
                                          True, True),
                             r=[f"{_key(KH[j])}@{q2}" for q2 in range(NCH)]
                               + [f"{_key(QH[j])}@{qc}"],
                             w=[psS[k2]])
                        r_rel = kt - 4 * qc
                        if r_rel >= 0:
                            p.op("vector", tt(c.SQS[k2][:], psS[k2][:, 0:CH],
                                              MSK[:, CH * r_rel:CH * (r_rel + 1)], ALU.add),
                                 r=[psS[k2], MSK], w=[c.SQS[k2]])
                            src = c.SQS[k2][0:128, :]
                            srck = [c.SQS[k2]]
                        else:
                            src = psS[k2][0:128, 0:CH]
                            srck = [psS[k2]]
                        p.op("scalar", act(PB[k2][:], src, AF.Exp, bias=NEGBC[:], scale=1.0 / 8.0),
                             r=srck + [NEGBC], w=[PB[k2]])
                        p.op("tensor", mm(psO[:, 0:CH], VT[kt][:, 64 * hh:64 * (hh + 1)],
                                          PB[k2][:], kt == 0, kt == nkt - 1),
                             r=[VT[kt], PB[k2]], w=[psO] if kt in (0, nkt - 1) else ())
                        p.op("tensor", mm(psDen[0:1, :], c.ones_col_bf[:], PB[k2][:],
                                          kt == 0, kt == nkt - 1),
                             r=[c.ones_col_bf, PB[k2]], w=[psDen] if kt in (0, nkt - 1) else ())
                    p.op("vector", recip(c.R0[:, 0:CH], psDen[0:1, :]), r=[psDen], w=[c.TMP])
                    p.op("tensor", mm(psRep[:], c.ones_row, c.R0[:, 0:CH], True, True),
                         r=[c._ones_rows, c.TMP], w=[psRep])
                    p.op("scalar", scp(REPS[:], psRep[:]), r=[psRep], w=[REPS])
                    p.op("vector", tt(OSTG[j % 2][rows, tsl], psO[:, 0:CH], REPS[0:64, :],
                                      ALU.mult),
                         r=[psO, REPS], w=[f"ostg{j % 2}@{half}@{qc}"])
                if half == 1:
                    hp = hg // 2
                    p.dma("gpsimd", o_scr.ap()[128 * hp:128 * (hp + 1), :], OSTG[j % 2][:],
                          sem=f"osp{j % 2}",
                          r=[f"ostg{j % 2}@{h2}@{q2}" for h2 in range(2) for q2 in range(NCH)],
                          w=["o_scr"])
        p.barrier()
        s1.close()

        s2 = ExitStack()
        sb2 = lambda n, s, d=F32: s2.enter_context(nc.sbuf_tensor(uid + n, s, d))
        ps2 = lambda n, s, d=F32: s2.enter_context(nc.psum_tensor(uid + n, s, d))
        OIN = [sb2(f"oin{ot}", [128, T], BF16) for ot in range(DP)]
        WOA = sb2("woall", [128, DP * D], BF16)
        WO = [WOA[:, D * ot:D * (ot + 1)] for ot in range(DP)]
        YW = [sb2("yw0", [128, T], BF16), sb2("yw1", [128, T], BF16)]
        HST = [sb2("hst0", [128, T], BF16), sb2("hst1", [128, T], BF16)]
        psD = [ps2("psD0", [128, CH]), ps2("psD1", [128, CH])]
        for i in range(DP):
            to = p.dma("scalar", OIN[i][:], o_scr.ap()[128 * i:128 * (i + 1), :], sem="oing",
                       r=["o_scr"], w=[OIN[i]])
        p.retag_writers(OIN, to)
        tw = p.dma("gpsimd", WO[0].tensor.ap().rearrange("p (d i) -> p d i", d=DP),
                   wo_d.ap().rearrange("(d p) i -> p d i", p=128), sem="wog", w=[WO[0]])
        p.retag_writers(WO, tw)
        for jj in range(DP):
            jsl = slice(128 * jj, 128 * (jj + 1))
            kj = jj % 2
            p.dma("sync", YW[kj][:], y_scr.ap()[jsl, :], sem=f"ywin{kj}",
                  r=["y_scr"], w=[YW[kj]])
            for tc in range(NCH):
                tsl = slice(CH * tc, CH * (tc + 1))
                k = (jj * NCH + tc) % 2
                for ot in range(DP):
                    p.op("tensor", mm(psD[k][:], WO[ot][:, jsl], OIN[ot][:, tsl],
                                      ot == 0, ot == DP - 1),
                         r=[WO[ot], OIN[ot]], w=[psD[k]] if ot in (0, DP - 1) else ())
                p.op("vector", tt(HST[kj][:, tsl], YW[kj][:, tsl], psD[k][:], ALU.add),
                     r=[YW[kj], psD[k]], w=[HST[kj]])
                p.op("vector", stt(X[jj][:, tsl], psD[k][:], 0.5, X[jj][:, tsl],
                                   ALU.mult, ALU.add),
                     r=[psD[k]], w=[X[jj]])
            p.dma("gpsimd", h_out.ap()[jsl, :], HST[kj][:], sem=f"hout{kj}",
                  r=[HST[kj]], w=["h_scr"])
        s2.close()
    p.barrier()


def build_prog_a1():
    nc = bass.Bass("TRN2", target_bir_lowering=False)
    x_in = nc.dram_tensor("x_in", [D, T], F32, kind="ExternalInput")
    dr = {
        "a1_wq": nc.dram_tensor("wq", [D, D], F32, kind="ExternalInput"),
        "a1_wk": nc.dram_tensor("wk", [D, D], F32, kind="ExternalInput"),
        "a1_wv": nc.dram_tensor("wv", [D, D], F32, kind="ExternalInput"),
        "a1_wo": nc.dram_tensor("wo", [D, D], F32, kind="ExternalInput"),
        "a1_anw": nc.dram_tensor("anw", [128, DP], F32, kind="ExternalInput"),
        "a1_ropeC": nc.dram_tensor("ropeC", [128, T], BF16, kind="ExternalInput"),
        "a1_ropeS": nc.dram_tensor("ropeS", [128, T], BF16, kind="ExternalInput"),
        "a1_dmask": nc.dram_tensor("dmask", [128, 4 * CH], BF16, kind="ExternalInput"),
        "o_scr": nc.dram_tensor("o_scr", [D, T], BF16, kind="Internal"),
        "y_scr": nc.dram_tensor("y_scr", [D, T], BF16, kind="Internal"),
        "h_scr": nc.dram_tensor("h_out", [D, T], BF16, kind="ExternalOutput"),
    }
    z_out = nc.dram_tensor("z_out", [D, T], F32, kind="ExternalOutput")
    stack = ExitStack()
    with stack:
        sb = lambda n, s, d=F32: stack.enter_context(nc.sbuf_tensor(n, s, d))
        p = Prog(nc, stack)
        c = Com(nc, stack)
        X = [sb(f"x{i}", [128, T]) for i in range(DP)]
        c.init(p)
        for i in range(DP):
            tx = p.dma("sync", X[i][:], x_in.ap()[128 * i:128 * (i + 1), :], sem="xing", w=[X[i]])
        p.retag_writers(X, tx)
        emit_a1(p, c, nc, X, dr)
        for jj in range(DP):
            p.dma("sync", z_out.ap()[128 * jj:128 * (jj + 1), :], X[jj][:], sem="zout",
                  r=[X[jj]])
        p.finish()
        p.emit()
    return nc


def _emit_rot_weights_il(p, dst, src, nh, dh):
    sv = _ap(src).rearrange("p (h i two) -> p h i two", h=nh, i=dh // 2, two=2)
    dv = _ap(dst).rearrange("p (h i two) -> p h i two", h=nh, i=dh // 2, two=2)
    p.op("vector", ts(dv[:, :, :, 0:1], sv[:, :, :, 1:2], -1.0, ALU.mult), r=[src], w=[dst])
    p.op("vector", cp(dv[:, :, :, 1:2], sv[:, :, :, 0:1]), r=[src], w=[dst])



def emit_a2(p, c, nc, Z, dr, uid=""):
    h_in = dr["h_scr"]; fnw_d = dr["a2_fnw"]; rw_d = dr["a2_rw"]
    w1_d = dr["a2_w1"]; w3_d = dr["a2_w3"]; w2_d = dr["a2_w2"]
    ident_d = dr["a2_ident"]; sel_d = dr["a2_sel8"]
    with ExitStack() as ss:
        sbS = lambda n, s, d=F32: ss.enter_context(nc.sbuf_tensor(uid + n, s, d))
        X2 = [sbS(f"x2_{i}", [128, T], BF16) for i in range(DP)]
        CTT = sbS("ctt", [8, T])
        SEL8 = sbS("sel8_sb", [8, D])
        p.dma("scalar", SEL8[:], sel_d.ap(), sem="selin", w=[SEL8])

        s0 = ExitStack()
        sb0 = lambda n, s, d=F32: s0.enter_context(nc.sbuf_tensor(uid + n, s, d))
        ps0 = lambda n, s, d=F32: s0.enter_context(nc.psum_tensor(uid + n, s, d))
        H = [sb0(f"h{i}", [128, T], BF16) for i in range(DP)]
        FNW = sb0("fnw_sb", [128, DP])
        RWB = sb0("rwb", [128, 8 * DP], BF16)
        IDENT = sb0("ident_sb", [128, 128])
        LROW = sb0("lrow", [8, CH])
        LTOK = sb0("ltok", [128, 128])
        M8 = sb0("m8", [128, 8])
        N1 = sb0("n1", [128, 1])
        E8 = sb0("e8", [128, 8])
        IND = sb0("ind", [128, 8])
        NUM = sb0("num", [128, 8])
        E2 = sb0("e2", [128, 1])
        DINV = sb0("dinv", [128, 1])
        CT = sb0("ct", [128, 128])
        psB = ps0("psB", [1, CH])
        psR = [ps0("psR0", [128, CH]), ps0("psR1", [128, CH])]
        psL = ps0("psL", [8, CH])
        psT = ps0("psT", [128, 8])
        psT2 = ps0("psT2", [8, 128])
        for i in range(DP):
            th = p.dma("sync", H[i][:], h_in.ap()[128 * i:128 * (i + 1), :], sem="hing", w=[H[i]])
        p.retag_writers(H, th)
        p.dma("scalar", FNW[:], fnw_d.ap(), sem="fnwin", w=[FNW])
        p.dma("scalar", IDENT[:], ident_d.ap(), sem="idin", w=[IDENT])
        emit_rms(p, c, H, psB, psR)
        for i in range(DP):
            p.op("vector", stt(X2[i][:], H[i][:], FNW[:, i:i + 1], c.SCR_A[:],
                               ALU.mult, ALU.mult),
                 r=[H[i], FNW, c.SCR_A], w=[X2[i]])
        for d in range(DP):
            p.dma("gpsimd", c.SQS[0][:, 0:8], rw_d.ap()[128 * d:128 * (d + 1), :],
                  sem="rwin", w=[c.SQS[0]])
            p.op("vector", cp(RWB[:, 8 * d:8 * (d + 1)], c.SQS[0][:, 0:8]),
                 r=[c.SQS[0]], w=[RWB])
        for ci in range(NCH):
            csl = slice(CH * ci, CH * (ci + 1))
            for d in range(DP):
                p.op("tensor", mm(psL[0:8, :], RWB[:, 8 * d:8 * (d + 1)], X2[d][:, csl],
                                  d == 0, d == DP - 1),
                     r=[RWB, X2[d]], w=[psL] if d in (0, DP - 1) else ())
            p.op("scalar", scp(LROW[:], psL[0:8, :]), r=[psL], w=[LROW])
            for t2 in range(4):
                tt_g = 4 * ci + t2
                p.op("tensor", transp(psT[:, 0:8], LROW[0:8, 128 * t2:128 * (t2 + 1)],
                                      IDENT[0:8, 0:8]),
                     r=[LROW, IDENT], w=[psT])
                p.op("scalar", scp(LTOK[:, 8 * tt_g:8 * (tt_g + 1)], psT[:, 0:8]),
                     r=[psT], w=[LTOK])
        for tt_g in range(NTT):
            lt = LTOK[:, 8 * tt_g:8 * (tt_g + 1)]
            p.op("vector", max8(M8[:], lt), r=[LTOK], w=[M8])
            p.op("vector", ts(N1[:], M8[:, 0:1], -1.0, ALU.mult), r=[M8], w=[N1])
            p.op("scalar", act(E8[:], lt, AF.Exp, bias=N1[:]), r=[LTOK, N1], w=[E8])
            p.op("vector", ts(IND[:], lt, M8[:, 1:2], ALU.is_ge), r=[LTOK, M8], w=[IND])
            p.op("vector", tt(NUM[:], E8[:], IND[:], ALU.mult), r=[E8, IND], w=[NUM])
            p.op("scalar", act(E2[:], M8[:, 1:2], AF.Exp, bias=N1[:]), r=[M8, N1], w=[E2])
            p.op("vector", ts(E2[:], E2[:], 1.0, ALU.add), r=[E2], w=[E2])
            p.op("vector", recip(DINV[:], E2[:]), r=[E2], w=[DINV])
            p.op("vector",
                 lambda E, NUM=NUM, DINV=DINV, tt_g=tt_g:
                     E.tensor_scalar(CT[:, 8 * tt_g:8 * (tt_g + 1)], NUM[:], DINV[:], 0.5,
                                     ALU.mult, ALU.mult),
                 r=[NUM, DINV], w=[CT])
        for tt_g in range(NTT):
            p.op("tensor", transp(psT2[0:8, :], CT[:, 8 * tt_g:8 * (tt_g + 1)], IDENT[:]),
                 r=[CT, IDENT], w=[psT2])
            p.op("scalar", scp(CTT[:, 128 * tt_g:128 * (tt_g + 1)], psT2[0:8, :]),
                 r=[psT2], w=[CTT])
        p.barrier()
        s0.close()

        s1 = ExitStack()
        sb1 = lambda n, s, d=F32: s1.enter_context(nc.sbuf_tensor(uid + n, s, d))
        ps1 = lambda n, s, d=F32: s1.enter_context(nc.psum_tensor(uid + n, s, d))
        W1A = sb1("w1all", [128, DP * D], BF16)
        W3A = sb1("w3all", [128, DP * D], BF16)
        W2A = sb1("w2all", [128, DP * D], BF16)
        W1 = [W1A[:, D * d:D * (d + 1)] for d in range(DP)]
        W3 = [W3A[:, D * d:D * (d + 1)] for d in range(DP)]
        W2H = [W2A[:, D * it:D * (it + 1)] for it in range(DP)]
        GT = [sb1(f"gt{it}", [128, CH], BF16) for it in range(DP)]
        TSC = c.SQS
        psH1 = [ps1("psH1a", [128, CH]), ps1("psH1b", [128, CH])]
        psH3 = [ps1("psH3a", [128, CH]), ps1("psH3b", [128, CH])]
        psY = [ps1("psYa", [128, CH]), ps1("psYb", [128, CH])]
        psRep = ps1("psRep2", [128, CH])

        for e in range(8):
            for ci in range(NCH):
                csl = slice(CH * ci, CH * (ci + 1))
                p.op("tensor", mm(psRep[:], SEL8[0:8, 128 * e:128 * (e + 1)],
                                  CTT[0:8, csl], True, True),
                     r=[SEL8, CTT], w=[psRep])
                p.op("scalar", scp(c.SCR_B[:, csl], psRep[:]), r=[psRep], w=[c.SCR_B])
            for ih in range(2):
                isl = slice(1024 * ih, 1024 * (ih + 1))
                src1 = w1_d.ap()[e].rearrange("(d p) i -> p d i", p=128)[:, :, isl]
                src3 = w3_d.ap()[e].rearrange("(d p) i -> p d i", p=128)[:, :, isl]
                src2 = (w2_d.ap()[e].rearrange("(h it p) j -> h p it j", h=2, p=128)[ih])
                t1 = p.dma("gpsimd", W1A.ap().rearrange("p (d i) -> p d i", d=DP),
                           src1, sem="w1g", w=[W1A])
                t3 = p.dma("gpsimd", W3A.ap().rearrange("p (d i) -> p d i", d=DP),
                           src3, sem="w3g", w=[W3A])
                t2 = p.dma("gpsimd", W2A.ap().rearrange("p (it j) -> p it j", it=DP),
                           src2, sem="w2g", w=[W2A])
                p.retag_writers(W1 + [W1A], t1)
                p.retag_writers(W3 + [W3A], t3)
                p.retag_writers(W2H + [W2A], t2)
                for tc in range(NCH):
                    tsl = slice(CH * tc, CH * (tc + 1))
                    for it in range(DP):
                        k = it % 2
                        col = slice(128 * it, 128 * (it + 1))
                        for d in range(DP):
                            p.op("tensor", mm(psH1[k][:], W1[d][:, col], X2[d][:, tsl],
                                              d == 0, d == DP - 1),
                                 r=[W1[d], X2[d]], w=[psH1[k]] if d in (0, DP - 1) else ())
                        for d in range(DP):
                            p.op("tensor", mm(psH3[k][:], W3[d][:, col], X2[d][:, tsl],
                                              d == 0, d == DP - 1),
                                 r=[W3[d], X2[d]], w=[psH3[k]] if d in (0, DP - 1) else ())
                        p.op("scalar", act(c.SQS[k][:], psH1[k][:], AF.Silu),
                             r=[psH1[k]], w=[c.SQS[k]])
                        p.op("vector", tt(GT[it][:], c.SQS[k][:], psH3[k][:], ALU.mult),
                             r=[c.SQS[k], psH3[k]], w=[GT[it]])
                    for j in range(DP):
                        k = j % 2
                        jcol = slice(128 * j, 128 * (j + 1))
                        for it in range(DP):
                            p.op("tensor", mm(psY[k][:], W2H[it][:, jcol], GT[it][:],
                                              it == 0, it == DP - 1),
                                 r=[W2H[it], GT[it]], w=[psY[k]] if it in (0, DP - 1) else ())
                        p.op("vector", tt(TSC[k][:], psY[k][:], c.SCR_B[:, tsl], ALU.mult),
                             r=[psY[k], c.SCR_B], w=[TSC[k]])
                        p.op("vector", tt(Z[j][:, tsl], Z[j][:, tsl], TSC[k][:], ALU.add),
                             r=[TSC[k]], w=[Z[j]])
        s1.close()
    p.barrier()


def build_prog_a2():
    nc = bass.Bass("TRN2", target_bir_lowering=False)
    z_in = nc.dram_tensor("z_in", [D, T], F32, kind="ExternalInput")
    dr = {
        "h_scr": nc.dram_tensor("h_in", [D, T], BF16, kind="ExternalInput"),
        "a2_fnw": nc.dram_tensor("fnw", [128, DP], F32, kind="ExternalInput"),
        "a2_rw": nc.dram_tensor("rw", [D, 8], F32, kind="ExternalInput"),
        "a2_w1": nc.dram_tensor("w1", [8, D, 2048], F32, kind="ExternalInput"),
        "a2_w3": nc.dram_tensor("w3", [8, D, 2048], F32, kind="ExternalInput"),
        "a2_w2": nc.dram_tensor("w2", [8, 2048, D], F32, kind="ExternalInput"),
        "a2_ident": nc.dram_tensor("ident", [128, 128], F32, kind="ExternalInput"),
        "a2_sel8": nc.dram_tensor("sel8", [8, D], F32, kind="ExternalInput"),
    }
    x_out = nc.dram_tensor("x_out", [D, T], F32, kind="ExternalOutput")
    stack = ExitStack()
    with stack:
        sb = lambda n, s, d=F32: stack.enter_context(nc.sbuf_tensor(n, s, d))
        p = Prog(nc, stack)
        c = Com(nc, stack)
        Z = [sb(f"z{i}", [128, T]) for i in range(DP)]
        c.init(p)
        for i in range(DP):
            tz = p.dma("sync", Z[i][:], z_in.ap()[128 * i:128 * (i + 1), :], sem="zing", w=[Z[i]])
        p.retag_writers(Z, tz)
        emit_a2(p, c, nc, Z, dr)
        for j in range(DP):
            p.dma("sync", x_out.ap()[128 * j:128 * (j + 1), :], Z[j][:], sem="xout", r=[Z[j]])
        p.finish()
        p.emit()
    return nc


def _prep_l0_ffn_moe(inputs):
    sel = np.zeros((8, D), np.float32)
    for e in range(8):
        sel[e, 128 * e:128 * (e + 1)] = 1.0
    return {
        "fnw": np.ascontiguousarray(np.asarray(inputs["L0_ffn_ffn_norm_w"]).reshape(DP, 128).T),
        "rw": np.ascontiguousarray(inputs["L0_ffn_router_w"]),
        "w1": np.ascontiguousarray(inputs["L0_ffn_w1"]),
        "w3": np.ascontiguousarray(inputs["L0_ffn_w3"]),
        "w2": np.ascontiguousarray(inputs["L0_ffn_w2"]),
        "ident": np.eye(128, dtype=np.float32),
        "sel8": sel,
    }


def run_a2(hs, zs, W):
    nc = get_prog("a2")
    in_maps = [dict(h_in=hs[b], z_in=zs[b], **W) for b in range(8)]
    res = run_bass_kernel_spmd(nc, in_maps, core_ids=list(range(8)))
    return [np.asarray(r["x_out"]) for r in res.results]



def emit_e(p, c, nc, X, dr, uid=""):
    wq_d = dr["e_wq"]; wk_d = dr["e_wk"]; wv_d = dr["e_wv"]
    wqf_d = dr["e_wqf"]; wkf_d = dr["e_wkf"]; wout_d = dr["e_wout"]
    nw_d = dr["e_nw"]; ga_d = dr["e_gha"]; gb_d = dr["e_ghb"]
    triu_d = dr["e_triu"]; wmask_d = dr["e_wmask"]
    lin_scr = dr["lin_scr"]; win_scr = dr["win_scr"]
    with ExitStack() as ss:
        sbS = lambda n, s, d=F32: ss.enter_context(nc.sbuf_tensor(uid + n, s, d))
        XH = [sbS(f"xh{i}", [128, T], BF16) for i in range(DP)]

        s0 = ExitStack()
        sb0 = lambda n, s, d=F32: s0.enter_context(nc.sbuf_tensor(uid + n, s, d))
        ps0 = lambda n, s, d=F32: s0.enter_context(nc.psum_tensor(uid + n, s, d))
        Y = [sb0(f"y{i}", [128, T], BF16) for i in range(DP)]
        NW = sb0("nw_sb", [128, DP])
        psA = ps0("psA", [1, CH])
        psB = ps0("psB", [1, CH])
        psR = [ps0("psR0", [128, CH]), ps0("psR1", [128, CH])]
        p.dma("scalar", NW[:], nw_d.ap(), sem="nwin", w=[NW])
        emit_ln(p, c, X, Y, psA, psB, psR, fold=1.0)
        emit_rms(p, c, Y, psB, psR)
        for i in range(DP):
            p.op("vector", stt(XH[i][:], Y[i][:], NW[:, i:i + 1], c.SCR_A[:],
                               ALU.mult, ALU.mult),
                 r=[Y[i], NW, c.SCR_A], w=[XH[i]])
        p.barrier()
        s0.close()

        s1 = ExitStack()
        sb1 = lambda n, s, d=F32: s1.enter_context(nc.sbuf_tensor(uid + n, s, d))
        ps1 = lambda n, s, d=F32: s1.enter_context(nc.psum_tensor(uid + n, s, d))
        WQA = sb1("wqall", [128, DP * 128], BF16)
        WKA = sb1("wkall", [128, DP * 128], BF16)
        WVA = sb1("wvall", [128, DP * 128], BF16)
        WQB = [WQA[:, 128 * d:128 * (d + 1)] for d in range(DP)]
        WKB = [WKA[:, 128 * d:128 * (d + 1)] for d in range(DP)]
        WVB = [WVA[:, 128 * d:128 * (d + 1)] for d in range(DP)]
        WQF = sb1("wqf_sb", [64, 16], BF16)
        WKF = sb1("wkf_sb", [64, 16], BF16)
        WFS = sb1("wfs", [64, 32])
        GA = sb1("gha_sb", [33, 153], BF16)
        GB = sb1("ghb_sb", [33, 153], BF16)
        TRIU = sb1("triu_sb", [128, 128])
        WMSK = sb1("wmsk_sb", [128, 256], BF16)
        QH = sb1("qh", [64, T], BF16)
        KH = sb1("kh", [64, T], BF16)
        VA = [sb1("va0", [128, NTT * 65], BF16), sb1("va1", [128, NTT * 65], BF16)]
        SQ = sb1("sq", [33, T], BF16)
        SK = sb1("sk", [33, T], BF16)
        PQ1 = sb1("pq1", [128, T], BF16)
        PQ2 = sb1("pq2", [25, T], BF16)
        PK1 = sb1("pk1", [128, T], BF16)
        PK2 = sb1("pk2", [25, T], BF16)
        KTOK = sb1("ktok", [128, NTT * 153], BF16)
        KVH = sb1("kvh", [128, 65], BF16)
        KVL = sb1("kvl", [25, 65], BF16)
        KVFH = sb1("kvfh", [128, 65])
        KVFL = sb1("kvfl", [25, 65])
        LINH = sb1("linh", [64, T], BF16)
        WINH = sb1("winh", [64, T], BF16)
        ASB = sb1("asb", [128, 128], BF16)
        PBW = sb1("pbw", [128, 128], BF16)
        REPE = sb1("repe", [128, 128])
        NEGWC = sb1("negwc", [128, 1])
        psU = ps1("psU", [128, CH])
        psW = ps1("psW", [128, CH])
        psA2 = ps1("psA2", [128, 128])
        psY2 = ps1("psY2", [65, 128])
        psOW = ps1("psOW", [65, 128])
        psRepE = ps1("psRepE", [128, 128])
        kvh_ps = ps1("kvh_ps", [128, 65])
        kvl_ps = ps1("kvl_ps", [25, 65])

        p.dma("sync", WFS[:, 0:16], wqf_d.ap(), sem="wfin", w=[WFS])
        p.dma("sync", WFS[:, 16:32], wkf_d.ap(), sem="wfin2", w=[WFS])
        p.op("vector", cp(WQF[:], WFS[:, 0:16]), r=[WFS], w=[WQF])
        p.op("vector", cp(WKF[:], WFS[:, 16:32]), r=[WFS], w=[WKF])
        p.dma("sync", GA[:], ga_d.ap(), sem="gain", w=[GA])
        p.dma("sync", GB[:], gb_d.ap(), sem="gbin", w=[GB])
        p.dma("sync", TRIU[:], triu_d.ap(), sem="trin", w=[TRIU])
        p.dma("sync", WMSK[:], wmask_d.ap(), sem="wmin", w=[WMSK])
        p.op("vector", memset(SQ[:], 0.0), w=[SQ])
        p.op("vector", memset(SQ[32:33, :], 1.0), w=[SQ])
        p.op("vector", memset(SK[:], 0.0), w=[SK])
        p.op("vector", memset(SK[32:33, :], 1.0), w=[SK])

        for hp in range(8):
            psl = slice(128 * hp, 128 * (hp + 1))
            for (WX, wsrc, snm) in ((WQB, wq_d, "wqg"), (WKB, wk_d, "wkg"),
                                    (WVB, wv_d, "wvg")):
                tw = p.dma("gpsimd", WX[0].tensor.ap().rearrange("p (d i) -> p d i", d=DP),
                           wsrc.ap().rearrange("(d p) i -> p d i", p=128)[:, :, psl],
                           sem=snm, w=[WX[0]])
                p.retag_writers(WX, tw)
            for h2 in range(2):
                p.op("vector", memset(
                    VA[h2].ap().rearrange("p (t c) -> p t c", t=NTT, c=65)[:, :, 64:65], 1.0),
                    w=[VA[h2]])
            for t2 in range(NTT):
                ssl = slice(128 * t2, 128 * (t2 + 1))
                for d in range(DP):
                    p.op("tensor", mm(psA2[:], XH[d][:, ssl], WVB[d][:],
                                      d == 0, d == DP - 1),
                         r=[XH[d], WVB[d]], w=[psA2] if d in (0, DP - 1) else ())
                for h2 in range(2):
                    p.op("vector", cp(
                        VA[h2].ap().rearrange("p (t c) -> p t c", t=NTT, c=65)[:, t2, 0:64],
                        psA2[:, 64 * h2:64 * (h2 + 1)]),
                        r=[psA2], w=[VA[h2]])
            for h2 in range(2):
                h = 2 * hp + h2
                hsl = slice(64 * h2, 64 * (h2 + 1))
                vaR = VA[h2].ap().rearrange("p (t c) -> p t c", t=NTT, c=65)
                for ci in range(NCH):
                    csl = slice(CH * ci, CH * (ci + 1))
                    for (ps_, Wm, dst) in ((psU, WQB, QH), (psW, WKB, KH)):
                        for d in range(DP):
                            p.op("tensor", mm(ps_[0:64, :], Wm[d][:, hsl], XH[d][:, csl],
                                              d == 0, d == DP - 1),
                                 r=[Wm[d], XH[d]], w=[ps_] if d in (0, DP - 1) else ())
                        p.op("scalar", scp(dst[:, csl], ps_[0:64, :]), r=[ps_], w=[dst])
                for ci in range(NCH):
                    csl = slice(CH * ci, CH * (ci + 1))
                    p.op("tensor", mm(psU[0:16, :], WQF[:], QH[:, csl], True, True),
                         r=[WQF, QH], w=[psU])
                    p.op("scalar", scp(SQ[0:16, csl], psU[0:16, :]), r=[psU], w=[SQ])
                    p.op("tensor", mm(psW[0:16, :], WKF[:], KH[:, csl], True, True),
                         r=[WKF, KH], w=[psW])
                    p.op("scalar", scp(SK[0:16, csl], psW[0:16, :]), r=[psW], w=[SK])
                for (S_, P1, P2) in ((SQ, PQ1, PQ2), (SK, PK1, PK2)):
                    for ci in range(NCH):
                        csl = slice(CH * ci, CH * (ci + 1))
                        for (P_, fsl, np_) in ((P1, slice(0, 128), 128),
                                               (P2, slice(128, 153), 25)):
                            p.op("tensor", mm(psU[0:np_, :], GA[:, fsl], S_[:, csl],
                                              True, True),
                                 r=[GA, S_], w=[psU])
                            p.op("tensor", mm(psW[0:np_, :], GB[:, fsl], S_[:, csl],
                                              True, True),
                                 r=[GB, S_], w=[psW])
                            p.op("scalar", scp(c.SQS[0][0:np_, :], psW[0:np_, :]),
                                 r=[psW], w=[c.SQS[0]])
                            p.op("vector", tt(P_[0:np_, csl], psU[0:np_, :],
                                              c.SQS[0][0:np_, :], ALU.mult),
                                 r=[psU, c.SQS[0]], w=[P_])
                for ch in range(NTT):
                    p.op("tensor", mm(psU[:, 0:153], SK[:, 128 * ch:128 * (ch + 1)], GA[:],
                                      True, True),
                         r=[SK, GA], w=[psU])
                    p.op("tensor", mm(psW[:, 0:153], SK[:, 128 * ch:128 * (ch + 1)], GB[:],
                                      True, True),
                         r=[SK, GB], w=[psW])
                    p.op("scalar", scp(c.SQS[0][:, 0:153], psW[:, 0:153]),
                         r=[psW], w=[c.SQS[0]])
                    p.op("vector", tt(KTOK[:, 153 * ch:153 * (ch + 1)], psU[:, 0:153],
                                      c.SQS[0][:, 0:153], ALU.mult),
                         r=[psU, c.SQS[0]], w=[KTOK])
                p.op("vector", memset(KVH[:], 0.0), w=[KVH])
                p.op("vector", memset(KVL[:], 0.0), w=[KVL])
                p.op("vector", memset(KVFH[:], 0.0), w=[KVFH])
                p.op("vector", memset(KVFL[:], 0.0), w=[KVFL])
                for ch in range(NTT):
                    cs128 = slice(128 * ch, 128 * (ch + 1))
                    p.op("tensor", mm(psA2[:], PK1[:, cs128], PQ1[:, cs128], True, False),
                         r=[PK1, PQ1], w=[psA2])
                    p.op("tensor", mm(psA2[:], PK2[0:25, cs128], PQ2[0:25, cs128],
                                      False, True),
                         r=[PK2, PQ2], w=[psA2])
                    p.op("vector", tt(ASB[:], psA2[:], TRIU[:], ALU.mult),
                         r=[psA2, TRIU], w=[ASB])
                    p.op("tensor", mm(psY2[:, :], vaR[:, ch, :], ASB[:], True, False),
                         r=[VA[h2], ASB], w=[psY2])
                    p.op("tensor", mm(psY2[:, :], KVH[:], PQ1[:, cs128], False, False),
                         r=[KVH, PQ1], w=[psY2])
                    p.op("tensor", mm(psY2[:, :], KVL[:], PQ2[0:25, cs128], False, True),
                         r=[KVL, PQ2], w=[psY2])
                    p.op("vector", ts(c.R0[:, 0:128], psY2[64:65, :], c.cval(1), ALU.add),
                         r=[psY2, "cval1"], w=[c.TMP])
                    p.op("vector", recip(c.R0[:, 0:128], c.R0[:, 0:128]), r=[c.TMP], w=[c.TMP])
                    p.op("tensor", mm(psRepE[:], c.ones_row, c.R0[:, 0:128], True, True),
                         r=[c._ones_rows, c.TMP], w=[psRepE])
                    p.op("scalar", scp(REPE[:], psRepE[:]), r=[psRepE], w=[REPE])
                    p.op("vector", tt(LINH[:, cs128], psY2[0:64, :], REPE[0:64, :], ALU.mult),
                         r=[psY2, REPE], w=[LINH])
                    if ch < NTT - 1:
                        p.op("tensor", mm(kvh_ps[:], KTOK[:, 153 * ch:153 * ch + 128],
                                          vaR[:, ch, :], True, True),
                             r=[KTOK, VA[h2]], w=[kvh_ps])
                        p.op("tensor", mm(kvl_ps[:], KTOK[:, 153 * ch + 128:153 * (ch + 1)],
                                          vaR[:, ch, :], True, True),
                             r=[KTOK, VA[h2]], w=[kvl_ps])
                        p.op("vector", tt(KVFH[:], KVFH[:], kvh_ps[:], ALU.add),
                             r=[kvh_ps], w=[KVFH])
                        p.op("vector", tt(KVFL[:], KVFL[:], kvl_ps[:], ALU.add),
                             r=[kvl_ps], w=[KVFL])
                        p.op("scalar", scp(KVH[:], KVFH[:]), r=[KVFH], w=[KVH])
                        p.op("scalar", scp(KVL[:], KVFL[:]), r=[KVFL], w=[KVL])
                for (HT, o0) in ((QH, 0), (KH, 4)):
                    p.op("scalar", act(c.SCR_B[0:64, :], HT[:], AF.Square),
                         r=[HT], w=[c.SCR_B])
                    for ci in range(NCH):
                        sl2 = slice(ci * CH, (ci + 1) * CH)
                        p.op("tensor", mm(psU[0:1, 0:CH], c.ones_col[0:64, :],
                                          c.SCR_B[0:64, sl2], True, True),
                             r=[c.ones_col, c.SCR_B], w=[psU])
                        p.op("vector", rmax(c.R0[:, o0 + ci:o0 + ci + 1], psU[0:1, 0:CH]),
                             r=[psU], w=[c.TMP])
                p.op("vector", rmax(c.R0[:, 8:9], c.R0[:, 0:4]), r=[c.TMP], w=[c.TMP])
                p.op("vector", rmax(c.R0[:, 9:10], c.R0[:, 4:8]), r=[c.TMP], w=[c.TMP])
                p.op("vector", tt(c.R0[:, 10:11], c.R0[:, 8:9], c.R0[:, 9:10], ALU.mult),
                     r=[c.TMP], w=[c.TMP])
                p.op("scalar", act(c.R0[:, 11:12], c.R0[:, 10:11], AF.Sqrt),
                     r=[c.TMP], w=[c.TMP])
                p.op("vector", ts(c.R0[:, 12:13], c.R0[:, 11:12], -1.0 / 8.0, ALU.mult),
                     r=[c.TMP], w=[c.TMP])
                p.op("tensor", mm(psRepE[:, 0:1], c.ones_row, c.R0[:, 12:13], True, True),
                     r=[c._ones_rows, c.TMP], w=[psRepE])
                p.op("scalar", scp(NEGWC[:], psRepE[:, 0:1]), r=[psRepE], w=[NEGWC])
                for qc in range(NTT):
                    qsl = slice(128 * qc, 128 * (qc + 1))
                    kts = [kt for kt in (qc - 1, qc) if kt >= 0]
                    for ki, kt in enumerate(kts):
                        ksl = slice(128 * kt, 128 * (kt + 1))
                        p.op("tensor", mm(psA2[:], KH[:, ksl], QH[:, qsl], True, True),
                             r=[KH, QH], w=[psA2])
                        wm = WMSK[:, 0:128] if kt < qc else WMSK[:, 128:256]
                        p.op("vector", tt(c.SQS[1][:, 0:128], psA2[:], wm, ALU.add),
                             r=[psA2, WMSK], w=[c.SQS[1]])
                        p.op("scalar", act(PBW[:], c.SQS[1][:, 0:128], AF.Exp,
                                           bias=NEGWC[:], scale=1.0 / 8.0),
                             r=[c.SQS[1], NEGWC], w=[PBW])
                        p.op("tensor", mm(psOW[:, :], vaR[:, kt, :], PBW[:],
                                          ki == 0, ki == len(kts) - 1),
                             r=[VA[h2], PBW], w=[psOW])
                    p.op("vector", recip(c.R0[:, 0:128], psOW[64:65, :]), r=[psOW], w=[c.TMP])
                    p.op("tensor", mm(psRepE[:], c.ones_row, c.R0[:, 0:128], True, True),
                         r=[c._ones_rows, c.TMP], w=[psRepE])
                    p.op("scalar", scp(REPE[:], psRepE[:]), r=[psRepE], w=[REPE])
                    p.op("vector", tt(WINH[:, qsl], psOW[0:64, :], REPE[0:64, :], ALU.mult),
                         r=[psOW, REPE], w=[WINH])
                p.dma("gpsimd", lin_scr.ap()[64 * h:64 * (h + 1), :], LINH[:],
                      sem="linsp", r=[LINH], w=["lin_scr"])
                p.dma("gpsimd", win_scr.ap()[64 * h:64 * (h + 1), :], WINH[:],
                      sem="winsp", r=[WINH], w=["win_scr"])
        p.barrier()
        s1.close()

        s2 = ExitStack()
        sb2 = lambda n, s, d=F32: s2.enter_context(nc.sbuf_tensor(uid + n, s, d))
        ps2 = lambda n, s, d=F32: s2.enter_context(nc.psum_tensor(uid + n, s, d))
        Z = X
        LW2 = [sb2("lw2a", [128, CH], BF16), sb2("lw2b", [128, CH], BF16)]
        WOUA = sb2("wouall", [128, 16 * D], BF16)
        WOUT = [WOUA[:, D * ot:D * (ot + 1)] for ot in range(16)]
        psDE = [ps2(f"psDE{j}", [128, CH]) for j in range(DP)]
        tw = p.dma("gpsimd", WOUA.ap().rearrange("p (d i) -> p d i", d=16),
                   wout_d.ap().rearrange("(d p) i -> p d i", p=128), sem="wog", w=[WOUA])
        p.retag_writers(WOUT, tw)
        for tc in range(NCH):
            tsl = slice(CH * tc, CH * (tc + 1))
            for ot in range(16):
                k = ot % 2
                src = lin_scr if ot < DP else win_scr
                ro = ot if ot < DP else ot - DP
                p.dma("scalar", LW2[k][:],
                      src.ap()[128 * ro:128 * (ro + 1), tsl], sem=f"lw2{k}",
                      r=["lin_scr" if ot < DP else "win_scr"], w=[LW2[k]])
                for j in range(DP):
                    p.op("tensor", mm(psDE[j][:], WOUT[ot][:, 128 * j:128 * (j + 1)],
                                      LW2[k][:], ot == 0, ot == 15),
                         r=[WOUT[ot], LW2[k]], w=[psDE[j]] if ot in (0, 15) else ())
            for j in range(DP):
                p.op("vector", tt(Z[j][:, tsl], Z[j][:, tsl], psDE[j][:], ALU.add),
                     r=[psDE[j]], w=[Z[j]])
        s2.close()
    p.barrier()


def build_prog_e():
    nc = bass.Bass("TRN2", target_bir_lowering=False)
    x_in = nc.dram_tensor("x_in", [D, T], F32, kind="ExternalInput")
    dr = {
        "e_wq": nc.dram_tensor("wq", [D, D], F32, kind="ExternalInput"),
        "e_wk": nc.dram_tensor("wk", [D, D], F32, kind="ExternalInput"),
        "e_wv": nc.dram_tensor("wv", [D, D], F32, kind="ExternalInput"),
        "e_wqf": nc.dram_tensor("wqf", [64, 16], F32, kind="ExternalInput"),
        "e_wkf": nc.dram_tensor("wkf", [64, 16], F32, kind="ExternalInput"),
        "e_wout": nc.dram_tensor("wout", [2 * D, D], F32, kind="ExternalInput"),
        "e_nw": nc.dram_tensor("nw", [128, DP], F32, kind="ExternalInput"),
        "e_gha": nc.dram_tensor("gha", [33, 153], BF16, kind="ExternalInput"),
        "e_ghb": nc.dram_tensor("ghb", [33, 153], BF16, kind="ExternalInput"),
        "e_triu": nc.dram_tensor("triu", [128, 128], F32, kind="ExternalInput"),
        "e_wmask": nc.dram_tensor("wmask", [128, 256], BF16, kind="ExternalInput"),
        "lin_scr": nc.dram_tensor("lin_scr", [D, T], BF16, kind="Internal"),
        "win_scr": nc.dram_tensor("win_scr", [D, T], BF16, kind="Internal"),
    }
    x_out = nc.dram_tensor("x_out", [D, T], F32, kind="ExternalOutput")
    stack = ExitStack()
    with stack:
        sb = lambda n, s, d=F32: stack.enter_context(nc.sbuf_tensor(n, s, d))
        p = Prog(nc, stack)
        c = Com(nc, stack)
        X = [sb(f"x{i}", [128, T]) for i in range(DP)]
        c.init(p)
        for i in range(DP):
            tx = p.dma("sync", X[i][:], x_in.ap()[128 * i:128 * (i + 1), :], sem="xing", w=[X[i]])
        p.retag_writers(X, tx)
        emit_e(p, c, nc, X, dr)
        for jj in range(DP):
            p.dma("sync", x_out.ap()[128 * jj:128 * (jj + 1), :], X[jj][:], sem="xout",
                  r=[X[jj]])
        p.finish()
        p.emit()
    return nc


def _prep_l1_attn(inputs):
    import math
    iu, ju = np.triu_indices(16)
    scale = np.where(iu == ju, 0.5, 1.0 / math.sqrt(2.0)).astype(np.float32)
    GA = np.zeros((33, 153), np.float32)
    GB = np.zeros((33, 153), np.float32)
    GA[32, 0] = 1.0
    GB[32, 0] = 1.0
    for i in range(16):
        GA[i, 1 + i] = 1.0
        GB[32, 1 + i] = 1.0
    for qn in range(136):
        GA[iu[qn], 17 + qn] = 1.0
        GB[ju[qn], 17 + qn] = scale[qn]
    rk = np.arange(128)[:, None]
    cq = np.arange(128)[None, :]
    wm_prev = np.where(rk >= cq + 64, 0.0, -1e30).astype(np.float32)
    wm_diag = np.where((rk <= cq) & (rk >= cq - 64), 0.0, -1e30).astype(np.float32)
    triu = np.triu(np.ones((128, 128), np.float32))
    return {
        "wq": np.ascontiguousarray(inputs["L1_attn_Wq"]),
        "wk": np.ascontiguousarray(inputs["L1_attn_Wk"]),
        "wv": np.ascontiguousarray(inputs["L1_attn_Wv"]),
        "wqf": np.ascontiguousarray(inputs["L1_attn_Wqf"]),
        "wkf": np.ascontiguousarray(inputs["L1_attn_Wkf"]),
        "wout": np.ascontiguousarray(inputs["L1_attn_Wout"]),
        "nw": np.ascontiguousarray(np.asarray(inputs["L1_attn_norm_w"]).reshape(DP, 128).T),
        "gha": np.ascontiguousarray(GA.astype(BF)),
        "ghb": np.ascontiguousarray(GB.astype(BF)),
        "triu": triu,
        "wmask": np.ascontiguousarray(
            np.concatenate([wm_prev, wm_diag], axis=1).astype(BF)),
    }


def run_e(xs, W):
    nc = get_prog("e")
    in_maps = [dict(x_in=xs[b], **W) for b in range(8)]
    res = run_bass_kernel_spmd(nc, in_maps, core_ids=list(range(8)))
    return [np.asarray(r["x_out"]) for r in res.results]



_PROG_CACHE = {}

def emit_d(p, c, nc, X, dr, uid=""):
    wg = dr["d_wg"]; wu = dr["d_wu"]; wd = dr["d_wd"]; g_scr = dr["g_scr"]
    with ExitStack() as ss:
        sbS = lambda n, s, d=F32: ss.enter_context(nc.sbuf_tensor(uid + n, s, d))
        Y = [sbS(f"y{i}", [128, T], BF16) for i in range(DP)]

        s1 = ExitStack()
        sb1 = lambda n, s, d=F32: s1.enter_context(nc.sbuf_tensor(uid + n, s, d))
        ps1 = lambda n, s, d=F32: s1.enter_context(nc.psum_tensor(uid + n, s, d))
        psA = ps1("psA", [1, CH])
        psB = ps1("psB", [1, CH])
        psR = [ps1("psR0", [128, CH]), ps1("psR1", [128, CH])]
        GPS = [ps1("gate0", [128, CH]), ps1("gate1", [128, CH])]
        UPS = [ps1("up0", [128, CH]), ps1("up1", [128, CH])]
        gst = sb1("gstage0", [128, T], BF16)
        WGA = sb1("wgall", [128, DP * 1536], BF16)
        WUA = sb1("wuall", [128, DP * 1536], BF16)
        WG = [WGA[:, 1536 * i:1536 * (i + 1)] for i in range(DP)]
        WU = [WUA[:, 1536 * i:1536 * (i + 1)] for i in range(DP)]

        emit_ln(p, c, X, Y, psA, psB, psR, fold=None)

        for ih in range(2):
            csl = slice(1536 * ih, 1536 * (ih + 1))
            p.dma("gpsimd", WGA.ap().rearrange("p (d i) -> p d i", d=DP),
                  wg.ap().rearrange("(d p) i -> p d i", p=128)[:, :, csl],
                  sem="wgg", w=[WGA])
            p.dma("gpsimd", WUA.ap().rearrange("p (d i) -> p d i", d=DP),
                  wu.ap().rearrange("(d p) i -> p d i", p=128)[:, :, csl],
                  sem="wug", w=[WUA])
            for hp in range(12):
                hpg = 12 * ih + hp
                col = slice(128 * hp, 128 * (hp + 1))
                for tc in range(NCH):
                    seq = hp * NCH + tc
                    tsl = slice(CH * tc, CH * (tc + 1))
                    k = seq % 2
                    for d in range(DP):
                        p.op("tensor", mm(GPS[k][:], WG[d][:, col], Y[d][:, tsl],
                                          d == 0, d == DP - 1),
                             r=[WG[d], Y[d]], w=[GPS[k]] if d in (0, DP - 1) else ())
                    for d in range(DP):
                        p.op("tensor", mm(UPS[k][:], WU[d][:, col], Y[d][:, tsl],
                                          d == 0, d == DP - 1),
                             r=[WU[d], Y[d]], w=[UPS[k]] if d in (0, DP - 1) else ())
                    p.op("scalar", act(c.SQS[k][:], GPS[k][:], AF.Silu, scale=1.0 / 127.0),
                         r=[GPS[k]], w=[c.SQS[k]])
                    p.op("vector", tt(gst[:, tsl], c.SQS[k][:], UPS[k][:], ALU.mult),
                         r=[c.SQS[k], UPS[k]], w=[f"gst@{tc}"])
                p.dma("gpsimd", g_scr.ap()[128 * hpg:128 * (hpg + 1), :], gst[:], sem="gsp",
                      r=[f"gst@{t2}" for t2 in range(NCH)], w=["g_scr"])
            p.barrier()

        s1.close()

        s2 = ExitStack()
        sb2 = lambda n, s, d=F32: s2.enter_context(nc.sbuf_tensor(uid + n, s, d))
        ps2 = lambda n, s, d=F32: s2.enter_context(nc.psum_tensor(uid + n, s, d))
        GIN = [sb2("gin0", [128, CH], BF16), sb2("gin1", [128, CH], BF16)]
        WDA = sb2("wdall", [128, 24 * D], BF16)
        WD = [WDA[:, D * i:D * (i + 1)] for i in range(24)]
        dps = [ps2(f"dps{j}", [128, CH]) for j in range(DP)]
        p.dma("gpsimd", WDA.ap().rearrange("p (it j) -> p it j", it=24),
              wd.ap().rearrange("(it p) j -> p it j", p=128), sem="wdg", w=[WDA])
        for tc in range(NCH):
            tsl = slice(CH * tc, CH * (tc + 1))
            for it in range(24):
                k = it % 2
                p.dma("scalar", GIN[k][:], g_scr.ap()[128 * it:128 * (it + 1), tsl],
                      sem=f"gin{k}", r=["g_scr"], w=[GIN[k]])
                for j in range(DP):
                    p.op("tensor", mm(dps[j][:], WD[it][:, 128 * j:128 * (j + 1)], GIN[k][:],
                                      it == 0, it == 23),
                         r=[WD[it], GIN[k]], w=[dps[j]] if it in (0, 23) else ())
            for j in range(DP):
                p.op("vector", stt(X[j][:, tsl], dps[j][:], 0.5 / (127.0 * 127.0),
                                   X[j][:, tsl], ALU.mult, ALU.add),
                     r=[dps[j]], w=[X[j]])
        s2.close()
    p.barrier()


def build_prog_d():
    nc = bass.Bass("TRN2", target_bir_lowering=False)
    x_in = nc.dram_tensor("x_in", [D, T], F32, kind="ExternalInput")
    dr = {
        "d_wg": nc.dram_tensor("wg", [D, 3072], I32, kind="ExternalInput"),
        "d_wu": nc.dram_tensor("wu", [D, 3072], I32, kind="ExternalInput"),
        "d_wd": nc.dram_tensor("wd", [3072, D], I32, kind="ExternalInput"),
        "g_scr": nc.dram_tensor("g_scr", [3072, T], BF16, kind="Internal"),
    }
    x_out = nc.dram_tensor("x_out", [D, T], F32, kind="ExternalOutput")
    stack = ExitStack()
    with stack:
        sb = lambda n, s, d=F32: stack.enter_context(nc.sbuf_tensor(n, s, d))
        p = Prog(nc, stack)
        c = Com(nc, stack)
        X = [sb(f"x{i}", [128, T]) for i in range(DP)]
        c.init(p)
        for i in range(DP):
            tx = p.dma("sync", X[i][:], x_in.ap()[128 * i:128 * (i + 1), :], sem="xing", w=[X[i]])
        p.retag_writers(X, tx)
        emit_d(p, c, nc, X, dr)
        for j in range(DP):
            p.dma("sync", x_out.ap()[128 * j:128 * (j + 1), :], X[j][:], sem="xout", r=[X[j]])
        p.finish()
        p.emit()
    return nc



def _emit_rot_weights(p, dst, src, nh, dh, rot_half):
    rh = rot_half
    sv = _ap(src).rearrange("p (h c) -> p h c", h=nh, c=dh)
    dv = _ap(dst).rearrange("p (h c) -> p h c", h=nh, c=dh)
    p.op("vector", ts(dv[:, :, 0:rh], sv[:, :, rh:2 * rh], -1.0, ALU.mult),
         r=[src], w=[dst])
    p.op("vector", cp(dv[:, :, rh:2 * rh], sv[:, :, 0:rh]), r=[src], w=[dst])
    if 2 * rh < dh:
        p.op("vector", cp(dv[:, :, 2 * rh:dh], sv[:, :, 2 * rh:dh]), r=[src], w=[dst])


def emit_b(p, c, nc, X, dr, uid=""):
    wq_d = dr["b_wq"]; wk_d = dr["b_wk"]; wv_d = dr["b_wv"]; wo_d = dr["b_wo"]
    cb_d = dr["b_ropeC"]; sb_d = dr["b_ropeS"]; mask_d = dr["b_dmask"]
    o_scr = dr["o_scr"]
    NHB = 8
    GH = 2
    with ExitStack() as ss:
        sbS = lambda n, s, d=F32: ss.enter_context(nc.sbuf_tensor(uid + n, s, d))
        Y = [sbS(f"y{i}", [128, T], BF16) for i in range(DP)]

        s0 = ExitStack()
        sb0 = lambda n, s, d=F32: s0.enter_context(nc.sbuf_tensor(uid + n, s, d))
        ps0 = lambda n, s, d=F32: s0.enter_context(nc.psum_tensor(uid + n, s, d))
        psA = ps0("psA", [1, CH])
        psB = ps0("psB", [1, CH])
        psR = [ps0("psR0", [128, CH]), ps0("psR1", [128, CH])]
        emit_ln(p, c, X, Y, psA, psB, psR, fold=None)
        p.barrier()
        s0.close()

        s1 = ExitStack()
        sb1 = lambda n, s, d=F32: s1.enter_context(nc.sbuf_tensor(uid + n, s, d))
        ps1 = lambda n, s, d=F32: s1.enter_context(nc.psum_tensor(uid + n, s, d))
        CB = sb1("cb", [128, T], BF16)
        SBC = sb1("sbc", [128, T], BF16)
        MSK = sb1("msk", [128, 4 * CH], BF16)
        OSTG = [sb1("ostg0", [128, T], BF16), sb1("ostg1", [128, T], BF16)]
        p.dma("sync", CB[:], cb_d.ap(), sem="cbin", w=[CB])
        p.dma("sync", SBC[:], sb_d.ap(), sem="sbin", w=[SBC])
        p.dma("sync", MSK[:], mask_d.ap(), sem="mkin", w=[MSK])
        GWB = GH * 128
        WQA = sb1("wqall", [128, DP * GWB], BF16)
        WKA = sb1("wkall", [128, DP * GWB], BF16)
        WVA = sb1("wvall", [128, DP * GWB], BF16)
        WQ = [WQA[:, GWB * d:GWB * (d + 1)] for d in range(DP)]
        WK = [WKA[:, GWB * d:GWB * (d + 1)] for d in range(DP)]
        WV = [WVA[:, GWB * d:GWB * (d + 1)] for d in range(DP)]
        WQR = [sb1(f"wqrb{d}", [128, GWB], BF16) for d in range(DP)]
        WKR = [sb1(f"wkrb{d}", [128, GWB], BF16) for d in range(DP)]
        QH = [sb1(f"qh{h}", [128, T], BF16) for h in range(GH)]
        KH = [sb1(f"kh{h}", [128, T], BF16) for h in range(GH)]
        VT = [sb1(f"vt{t2}", [128, GH * 128], BF16) for t2 in range(NTT)]
        PB = [sb1("pb0", [128, CH], BF16), sb1("pb1", [128, CH], BF16)]
        REPS = sb1("reps", [128, CH])
        psS = [ps1("psS0", [128, CH]), ps1("psS1", [128, CH])]
        psO = ps1("psO", [128, CH])
        psDen = ps1("psDen", [1, CH])
        psRep = ps1("psRep", [128, CH])

        for g in range(NHB // GH):
            hbase = GH * g
            gsl = slice(128 * GH * g, 128 * GH * (g + 1))
            for (WX, wsrc, snm) in ((WQ, wq_d, "wqg"), (WK, wk_d, "wkg"), (WV, wv_d, "wvg")):
                tw = p.dma("gpsimd", WX[0].tensor.ap().rearrange("p (d i) -> p d i", d=DP),
                           wsrc.ap().rearrange("(d p) i -> p d i", p=128)[:, :, gsl],
                           sem=snm, w=[WX[0]])
                p.retag_writers(WX, tw)
            for d in range(DP):
                _emit_rot_weights(p, WQR[d], WQ[d], GH, 128, 32)
                _emit_rot_weights(p, WKR[d], WK[d], GH, 128, 32)
            for h in range(GH):
                hsl = slice(128 * h, 128 * (h + 1))
                for qc in range(NCH):
                    tsl = slice(CH * qc, CH * (qc + 1))
                    for (psq, Wm, Wr, dstH) in ((psS[0], WQ, WQR, QH), (psS[1], WK, WKR, KH)):
                        for d in range(DP):
                            p.op("tensor", mm(psq[:], Wm[d][:, hsl], Y[d][:, tsl],
                                              d == 0, d == DP - 1),
                                 r=[Wm[d], Y[d]], w=[psq] if d in (0, DP - 1) else ())
                        for d in range(DP):
                            p.op("tensor", mm(psO[:], Wr[d][:, hsl], Y[d][:, tsl],
                                              d == 0, d == DP - 1),
                                 r=[Wr[d], Y[d]], w=[psO] if d in (0, DP - 1) else ())
                        p.op("vector", tt(c.SQS[0][:], psq[:], CB[:, tsl], ALU.mult),
                             r=[psq, CB], w=[c.SQS[0]])
                        p.op("vector", tt(c.SQS[1][:], psO[:], SBC[:, tsl], ALU.mult),
                             r=[psO, SBC], w=[c.SQS[1]])
                        p.op("vector", tt(dstH[h][:, tsl], c.SQS[0][:], c.SQS[1][:], ALU.add),
                             r=[c.SQS[0], c.SQS[1]], w=[f"{_key(dstH[h])}@{qc}"])
            for t2 in range(NTT):
                ssl = slice(128 * t2, 128 * (t2 + 1))
                for d in range(DP):
                    p.op("tensor", mm(psS[0][:, 0:128 * GH], Y[d][:, ssl], WV[d][:],
                                      d == 0, d == DP - 1),
                         r=[Y[d], WV[d]], w=[psS[0]] if d in (0, DP - 1) else ())
                p.op("scalar", scp(VT[t2][:], psS[0][:, 0:128 * GH]), r=[psS[0]], w=[VT[t2]])
            for h in range(GH):
                hg = hbase + h
                for qc in range(NCH):
                    tsl = slice(CH * qc, CH * (qc + 1))
                    qkeys = [f"{_key(QH[h])}@{qc}"]
                    nkt = 4 * qc + 4
                    for kt in range(nkt):
                        k2 = kt % 2
                        ksl = slice(128 * kt, 128 * (kt + 1))
                        p.op("tensor", mm(psS[k2][:], KH[h][:, ksl], QH[h][:, tsl],
                                          True, True),
                             r=[f"{_key(KH[h])}@{j}" for j in range(NCH)] + qkeys,
                             w=[psS[k2]])
                        r_rel = kt - 4 * qc
                        if r_rel >= 0:
                            p.op("vector", tt(c.SQS[k2][:], psS[k2][:],
                                              MSK[:, CH * r_rel:CH * (r_rel + 1)], ALU.add),
                                 r=[psS[k2], MSK], w=[c.SQS[k2]])
                            src = c.SQS[k2][:]
                            srck = [c.SQS[k2]]
                        else:
                            src = psS[k2][:]
                            srck = [psS[k2]]
                        p.op("scalar", act(PB[k2][:], src, AF.Exp, scale=1.0 / 128.0),
                             r=srck, w=[PB[k2]])
                        p.op("tensor", mm(psO[:], VT[kt][:, 128 * h:128 * (h + 1)], PB[k2][:],
                                          kt == 0, kt == nkt - 1),
                             r=[VT[kt], PB[k2]], w=[psO] if kt in (0, nkt - 1) else ())
                        p.op("tensor", mm(psDen[0:1, :], c.ones_col_bf[:], PB[k2][:],
                                          kt == 0, kt == nkt - 1),
                             r=[c.ones_col_bf, PB[k2]], w=[psDen] if kt in (0, nkt - 1) else ())
                    p.op("vector", recip(c.R0[:, 0:CH], psDen[0:1, :]), r=[psDen], w=[c.R0])
                    p.op("tensor", mm(psRep[:], c.ones_row, c.R0[:, 0:CH], True, True),
                         r=[c._ones_rows, c.R0], w=[psRep])
                    p.op("scalar", scp(REPS[:], psRep[:]), r=[psRep], w=[REPS])
                    p.op("vector", tt(OSTG[hg % 2][:, tsl], psO[:], REPS[:], ALU.mult),
                         r=[psO, REPS], w=[f"ostg{hg % 2}@{qc}"])
                p.dma("gpsimd", o_scr.ap()[128 * hg:128 * (hg + 1), :], OSTG[hg % 2][:],
                      sem=f"osp{hg % 2}",
                      r=[f"ostg{hg % 2}@{j}" for j in range(NCH)], w=["o_scr"])
        p.barrier()
        s1.close()

        s2 = ExitStack()
        sb2 = lambda n, s, d=F32: s2.enter_context(nc.sbuf_tensor(uid + n, s, d))
        ps2 = lambda n, s, d=F32: s2.enter_context(nc.psum_tensor(uid + n, s, d))
        OIN = [sb2(f"oin{ot}", [128, T], BF16) for ot in range(DP)]
        WOA = sb2("woall", [128, DP * D], BF16)
        WO = [WOA[:, D * ot:D * (ot + 1)] for ot in range(DP)]
        psD = [ps2("psD0", [128, CH]), ps2("psD1", [128, CH])]
        for i in range(DP):
            to = p.dma("scalar", OIN[i][:], o_scr.ap()[128 * i:128 * (i + 1), :], sem="oing",
                       r=["o_scr"], w=[OIN[i]])
        p.retag_writers(OIN, to)
        tw = p.dma("gpsimd", WO[0].tensor.ap().rearrange("p (d i) -> p d i", d=DP),
                   wo_d.ap().rearrange("(d p) i -> p d i", p=128), sem="wog", w=[WO[0]])
        p.retag_writers(WO, tw)
        for j in range(DP):
            jsl = slice(128 * j, 128 * (j + 1))
            for tc in range(NCH):
                tsl = slice(CH * tc, CH * (tc + 1))
                k = (j * NCH + tc) % 2
                for ot in range(DP):
                    p.op("tensor", mm(psD[k][:], WO[ot][:, jsl], OIN[ot][:, tsl],
                                      ot == 0, ot == DP - 1),
                         r=[WO[ot], OIN[ot]], w=[psD[k]] if ot in (0, DP - 1) else ())
                p.op("vector", tt(X[j][:, tsl], X[j][:, tsl], psD[k][:], ALU.add),
                     r=[psD[k]], w=[X[j]])
        s2.close()
    p.barrier()


def build_prog_b():
    nc = bass.Bass("TRN2", target_bir_lowering=False)
    x_in = nc.dram_tensor("x_in", [D, T], F32, kind="ExternalInput")
    dr = {
        "b_wq": nc.dram_tensor("wq", [D, D], F32, kind="ExternalInput"),
        "b_wk": nc.dram_tensor("wk", [D, D], F32, kind="ExternalInput"),
        "b_wv": nc.dram_tensor("wv", [D, D], F32, kind="ExternalInput"),
        "b_wo": nc.dram_tensor("wo", [D, D], F32, kind="ExternalInput"),
        "b_ropeC": nc.dram_tensor("ropeC", [128, T], BF16, kind="ExternalInput"),
        "b_ropeS": nc.dram_tensor("ropeS", [128, T], BF16, kind="ExternalInput"),
        "b_dmask": nc.dram_tensor("dmask", [128, 4 * CH], BF16, kind="ExternalInput"),
        "o_scr": nc.dram_tensor("o_scr", [D, T], BF16, kind="Internal"),
    }
    x_out = nc.dram_tensor("x_out", [D, T], F32, kind="ExternalOutput")
    stack = ExitStack()
    with stack:
        sb = lambda n, s, d=F32: stack.enter_context(nc.sbuf_tensor(n, s, d))
        p = Prog(nc, stack)
        c = Com(nc, stack)
        X = [sb(f"x{i}", [128, T]) for i in range(DP)]
        c.init(p)
        for i in range(DP):
            tx = p.dma("sync", X[i][:], x_in.ap()[128 * i:128 * (i + 1), :], sem="xing", w=[X[i]])
        p.retag_writers(X, tx)
        emit_b(p, c, nc, X, dr)
        for j in range(DP):
            p.dma("sync", x_out.ap()[128 * j:128 * (j + 1), :], X[j][:], sem="xout", r=[X[j]])
        p.finish()
        p.emit()
    return nc


def get_prog(name):
    global _NEEDED
    if name not in _PROG_CACHE:
        _NEEDED = None
        _BUILDERS[name]()
        _NEEDED = {k: frozenset(v) for k, v in Prog.last.emitted_waits.items()}
        try:
            _PROG_CACHE[name] = _BUILDERS[name]()
        finally:
            _NEEDED = None
    return _PROG_CACHE[name]



def _prep_l0_attn(inputs):
    wq = np.ascontiguousarray(np.transpose(inputs["L0_attn_wq"], (1, 0, 2)).reshape(D, D))
    wk = np.ascontiguousarray(np.transpose(inputs["L0_attn_wk"], (1, 0, 2)).reshape(D, D))
    wv = np.ascontiguousarray(np.transpose(inputs["L0_attn_wv"], (1, 0, 2)).reshape(D, D))
    wo = np.ascontiguousarray(inputs["L0_attn_wo"])
    t = np.arange(T, dtype=np.float64)[None, :]
    i = np.arange(32, dtype=np.float64)[:, None]
    rad = t * np.power(10000.0, -i / 32.0)
    cosr = np.cos(rad).astype(np.float32)
    sinr = np.sin(rad).astype(np.float32)
    ropeC = np.concatenate([cosr, cosr, np.ones((64, T), np.float32)], 0).astype(BF)
    ropeS = np.concatenate([sinr, sinr, np.zeros((64, T), np.float32)], 0).astype(BF)
    rk = np.arange(128)[:, None]
    cq = np.arange(CH)[None, :]
    dmask = np.concatenate(
        [np.where(128 * rr + rk <= cq, 0.0, -1e30).astype(np.float32) for rr in range(4)],
        axis=1).astype(BF)
    return {"wq": wq, "wk": wk, "wv": wv, "wo": wo,
            "ropeC": np.ascontiguousarray(ropeC), "ropeS": np.ascontiguousarray(ropeS),
            "dmask": np.ascontiguousarray(dmask)}


def _prep_l0_ffn_attn(inputs):
    F = np.asarray(inputs["L0_ffn_freqs_cis"])
    idx = np.repeat(np.arange(0, 64, 2), 2)
    crow = F[:, idx].T
    srow = F[:, idx + 1].T
    ropeC = np.tile(crow, (2, 1)).astype(BF)
    ropeS = np.tile(srow, (2, 1)).astype(BF)
    rk = np.arange(128)[:, None]
    cq = np.arange(CH)[None, :]
    dmask = np.concatenate(
        [np.where(128 * rr + rk <= cq, 0.0, -1e30).astype(np.float32) for rr in range(4)],
        axis=1).astype(BF)
    return {
        "wq": np.ascontiguousarray(inputs["L0_ffn_wq"]),
        "wk": np.ascontiguousarray(inputs["L0_ffn_wk"]),
        "wv": np.ascontiguousarray(inputs["L0_ffn_wv"]),
        "wo": np.ascontiguousarray(inputs["L0_ffn_wo"]),
        "anw": np.ascontiguousarray(np.asarray(inputs["L0_ffn_attn_norm_w"]).reshape(DP, 128).T),
        "ropeC": np.ascontiguousarray(ropeC),
        "ropeS": np.ascontiguousarray(ropeS),
        "dmask": np.ascontiguousarray(dmask),
    }


def run_a1(xs, W):
    nc = get_prog("a1")
    in_maps = [dict(x_in=xs[b], **W) for b in range(8)]
    res = run_bass_kernel_spmd(nc, in_maps, core_ids=list(range(8)))
    return ([np.asarray(r["z_out"]) for r in res.results],
            [np.asarray(r["h_out"]) for r in res.results])


def run_b(xs, W):
    nc = get_prog("b")
    in_maps = [dict(x_in=xs[b], **W) for b in range(8)]
    res = run_bass_kernel_spmd(nc, in_maps, core_ids=list(range(8)))
    return [np.asarray(r["x_out"]) for r in res.results]


def _prep_l1_ffn(inputs):
    return {
        "wg": np.ascontiguousarray(inputs["L1_ffn_w_gate"].T),
        "wu": np.ascontiguousarray(inputs["L1_ffn_w_up"].T),
        "wd": np.ascontiguousarray(inputs["L1_ffn_w_down"].T),
    }


def run_d(xs, W):
    nc = get_prog("d")
    in_maps = [dict(x_in=xs[b], **W) for b in range(8)]
    res = run_bass_kernel_spmd(nc, in_maps, core_ids=list(range(8)))
    return [np.asarray(r["x_out"]) for r in res.results]




def build_mega():
    nc = bass.Bass("TRN2", target_bir_lowering=False)
    x_in = nc.dram_tensor("x_in", [D, T], F32, kind="ExternalInput")
    x_out = nc.dram_tensor("x_out", [D, T], F32, kind="ExternalOutput")
    dr = {}
    for nm, shape, dt in [
        ("a1_wq", [D, D], F32), ("a1_wk", [D, D], F32), ("a1_wv", [D, D], F32),
        ("a1_wo", [D, D], F32), ("a1_anw", [128, DP], F32),
        ("a1_ropeC", [128, T], BF16), ("a1_ropeS", [128, T], BF16),
        ("a1_dmask", [128, 4 * CH], BF16),
        ("a2_fnw", [128, DP], F32), ("a2_rw", [D, 8], F32),
        ("a2_w1", [8, D, 2048], F32), ("a2_w3", [8, D, 2048], F32),
        ("a2_w2", [8, 2048, D], F32), ("a2_ident", [128, 128], F32),
        ("a2_sel8", [8, D], F32),
        ("b_wq", [D, D], F32), ("b_wk", [D, D], F32), ("b_wv", [D, D], F32),
        ("b_wo", [D, D], F32), ("b_ropeC", [128, T], BF16),
        ("b_ropeS", [128, T], BF16), ("b_dmask", [128, 4 * CH], BF16),
        ("d_wg", [D, 3072], I32), ("d_wu", [D, 3072], I32), ("d_wd", [3072, D], I32),
        ("e_wq", [D, D], F32), ("e_wk", [D, D], F32), ("e_wv", [D, D], F32),
        ("e_wqf", [64, 16], F32), ("e_wkf", [64, 16], F32),
        ("e_wout", [2 * D, D], F32), ("e_nw", [128, DP], F32),
        ("e_gha", [33, 153], BF16), ("e_ghb", [33, 153], BF16),
        ("e_triu", [128, 128], F32), ("e_wmask", [128, 256], BF16),
    ]:
        dr[nm] = nc.dram_tensor(nm, shape, dt, kind="ExternalInput")
    for nm, shape, dt in [
        ("o_scr", [D, T], BF16), ("y_scr", [D, T], BF16), ("h_scr", [D, T], BF16),
        ("g_scr", [3072, T], BF16), ("lin_scr", [D, T], BF16), ("win_scr", [D, T], BF16),
    ]:
        dr[nm] = nc.dram_tensor(nm, shape, dt, kind="Internal")

    stack = ExitStack()
    with stack:
        sb = lambda n, s, d=F32: stack.enter_context(nc.sbuf_tensor(n, s, d))
        p = Prog(nc, stack)
        c = Com(nc, stack)
        X = [sb(f"x{i}", [128, T]) for i in range(DP)]
        c.init(p)
        for i in range(DP):
            tx = p.dma("sync", X[i][:], x_in.ap()[128 * i:128 * (i + 1), :], sem="xing", w=[X[i]])
        p.retag_writers(X, tx)
        emit_a1(p, c, nc, X, dr, uid="s1_")
        emit_a2(p, c, nc, X, dr, uid="s2_")
        emit_b(p, c, nc, X, dr, uid="s3_")
        emit_a1(p, c, nc, X, dr, uid="s4_")
        emit_a2(p, c, nc, X, dr, uid="s5_")
        emit_d(p, c, nc, X, dr, uid="s6_")
        emit_e(p, c, nc, X, dr, uid="s7_")
        emit_d(p, c, nc, X, dr, uid="s8_")
        for j in range(DP):
            p.dma("sync", x_out.ap()[128 * j:128 * (j + 1), :], X[j][:], sem="xout", r=[X[j]])
        p.finish()
        p.emit()
    return nc


def _mega_inmap(inputs):
    W_a1 = _prep_l0_ffn_attn(inputs)
    W_a2 = _prep_l0_ffn_moe(inputs)
    W_b = _prep_l0_attn(inputs)
    W_d = _prep_l1_ffn(inputs)
    W_e = _prep_l1_attn(inputs)
    m = {}
    for pre, Wd in (("a1_", W_a1), ("a2_", W_a2), ("b_", W_b), ("d_", W_d), ("e_", W_e)):
        for k, v in Wd.items():
            m[pre + k] = v
    return m


FUSED = True


def kernel(**inputs):
    inputs = {k: np.asarray(v) for k, v in inputs.items()}
    xs = [np.ascontiguousarray(inputs["L0_attn_x"][b].T) for b in range(8)]
    if FUSED:
        base = _mega_inmap(inputs)
        nc = get_prog("mega")
        in_maps = [dict(x_in=xs[b], **base) for b in range(8)]
        res = run_bass_kernel_spmd(nc, in_maps, core_ids=list(range(8)))
        return np.stack([np.asarray(r["x_out"]).T for r in res.results]).astype(np.float32)
    W_a1 = _prep_l0_ffn_attn(inputs)
    W_a2 = _prep_l0_ffn_moe(inputs)
    W_b = _prep_l0_attn(inputs)
    W_d = _prep_l1_ffn(inputs)
    W_e = _prep_l1_attn(inputs)
    zs, hs = run_a1(xs, W_a1)
    xs = run_a2(hs, zs, W_a2)
    xs = run_b(xs, W_b)
    zs, hs = run_a1(xs, W_a1)
    xs = run_a2(hs, zs, W_a2)
    xs = run_d(xs, W_d)
    xs = run_e(xs, W_e)
    xs = run_d(xs, W_d)
    return np.stack([x.T for x in xs]).astype(np.float32)


_BUILDERS = {"d": build_prog_d, "b": build_prog_b, "a1": build_prog_a1, "a2": build_prog_a2, "e": build_prog_e, "mega": build_mega}


# version 06
# speedup vs baseline: 1.2488x; 1.0092x over previous
import numpy as np
import ml_dtypes
from contextlib import ExitStack

import concourse.bass as bass
import concourse.mybir as mybir
from concourse.bass_utils import run_bass_kernel_spmd

F32 = mybir.dt.float32
BF16 = mybir.dt.bfloat16
I32 = mybir.dt.int32
AX = mybir.AxisListType.X
AF = mybir.ActivationFunctionType
ALU = mybir.AluOpType
BF = ml_dtypes.bfloat16

D = 1024
DP = 8
T = 2048
NCH = 4
CH = 512
NTT = 16


def _ap(x):
    if hasattr(x, "ap") and callable(getattr(x, "ap", None)):
        return x.ap()
    return x


def _key(x):
    if isinstance(x, str):
        return x
    if hasattr(x, "name"):
        return x.name
    if hasattr(x, "tensor"):
        return x.tensor.name
    raise TypeError(f"bad key {x!r}")


_NEEDED = None


class Prog:
    ENGINES = ("tensor", "vector", "scalar", "gpsimd", "sync")
    last = None

    def __init__(self, nc, stack):
        self.nc = nc
        self.stack = stack
        self.ops = {e: [] for e in self.ENGINES}
        self.sems = {}
        self.cnt = {}
        self.waited = {e: {} for e in self.ENGINES}
        for e in self.ENGINES:
            self._mksem("p_" + e)
        self.dma_sems = set()
        self.last_w = {}
        self.readers = {}
        self.needed = _NEEDED
        self.attached = {}
        self.emitted_waits = {}
        Prog.last = self

    def _mksem(self, name):
        self.sems[name] = self.stack.enter_context(self.nc.semaphore(name))
        self.cnt[name] = 0

    def _track_pre(self, rk, wk):
        auto = []
        for k in rk:
            t = self.last_w.get(k)
            if t:
                auto.append(t)
        for k in wk:
            t = self.last_w.get(k)
            if t:
                auto.append(t)
            auto.extend(self.readers.get(k, {}).values())
        return auto

    def _track_post(self, rk, wk, slot, tok):
        for k in wk:
            self.last_w[k] = tok
            self.readers[k] = {}
        for k in rk:
            if k not in wk:
                self.readers.setdefault(k, {})[slot] = tok

    def op(self, eng, fn, waits=(), r=(), w=(), sig=False):
        rk = [_key(x) for x in r]
        wk = [_key(x) for x in w]
        auto = self._track_pre(rk, wk)
        for t in waits:
            self._wait(eng, t)
        for t in auto:
            self._wait(eng, t)
        need_sig = sig or bool(rk) or bool(wk)
        if need_sig:
            sname = "p_" + eng
            self.cnt[sname] += 1
            v = self.cnt[sname]
            tok = (sname, v)
            if self.needed is None or v in self.needed.get(sname, ()):
                delta = v - self.attached.get(sname, 0)
                self.attached[sname] = v
                sh = self.sems[sname]
                self.ops[eng].append(
                    lambda E, fn=fn, sh=sh, delta=delta: fn(E).then_inc(sh, delta))
            else:
                self.ops[eng].append(fn)
        else:
            self.ops[eng].append(fn)
            tok = None
        if tok:
            self._track_post(rk, wk, eng, tok)
        return tok

    def dma(self, q, out, in_, sem, waits=(), r=(), w=()):
        rk = [_key(x) for x in r]
        wk = [_key(x) for x in w]
        auto = self._track_pre(rk, wk)
        for t in waits:
            self._wait(q, t)
        for t in auto:
            self._wait(q, t)
        sname = "dma_" + sem
        if sname not in self.sems:
            self._mksem(sname)
            self.dma_sems.add(sname)
        self.cnt[sname] += 16
        v = self.cnt[sname]
        sh = self.sems[sname]
        self.ops[q].append(
            lambda E, out=out, in_=in_, sh=sh: E.dma_start(out=out, in_=in_).then_inc(sh, 16)
        )
        tok = (sname, v)
        self._track_post(rk, wk, sname, tok)
        return tok

    def retag_writers(self, keys, tok):
        for k in keys:
            self.last_w[_key(k)] = tok

    def _wait(self, eng, tok):
        if tok is None:
            return
        sname, v = tok
        if self.waited[eng].get(sname, 0) >= v:
            return
        self.waited[eng][sname] = v
        self.emitted_waits.setdefault(sname, set()).add(v)
        sh = self.sems[sname]
        self.ops[eng].append(lambda E, sh=sh, v=v: E.wait_ge(sh, v))

    def barrier(self):
        toks = [("p_" + e, self.cnt["p_" + e]) for e in self.ENGINES if self.cnt["p_" + e]]
        for s in sorted(self.dma_sems):
            if self.cnt[s]:
                toks.append((s, self.cnt[s]))
        for e in self.ENGINES:
            for t in toks:
                self._wait(e, t)

    def finish(self):
        for s in sorted(self.dma_sems):
            if self.cnt[s]:
                sh = self.sems[s]
                v = self.cnt[s]
                if self.waited["sync"].get(s, 0) < v:
                    self.waited["sync"][s] = v
                    self.ops["sync"].append(lambda E, sh=sh, v=v: E.wait_ge(sh, v))

    def emit(self):
        block = self.stack.enter_context(self.nc.Block())
        for e in self.ENGINES:
            ops = list(self.ops[e])

            def runner(E, ops=ops):
                for f in ops:
                    f(E)

            getattr(block, e)(runner)



def mm(out, lhsT, rhs, start, stop):
    return lambda E: E.matmul(out, lhsT, rhs, start=start, stop=stop)

def act(out, in_, func, bias=0.0, scale=1.0):
    return lambda E: E.activation(out, in_, func, bias=bias, scale=scale)

def tt(out, a, b, op):
    return lambda E: E.tensor_tensor(out, a, b, op)

def ts(out, a, s, op):
    return lambda E: E.tensor_single_scalar(out, a, s, op)

def stt(out, in0, scalar, in1, op0, op1):
    return lambda E: E.scalar_tensor_tensor(out, in0, scalar, in1, op0, op1)

def cp(out, in_):
    return lambda E: E.tensor_copy(out, in_)

def scp(out, in_):
    return lambda E: E.copy(out, in_)

def smul(out, in_, m):
    return lambda E: E.mul(out, in_, m)

def memset(ap, v):
    return lambda E: E.memset(ap, v)

def recip(out, in_):
    return lambda E: E.reciprocal(out, in_)

def rmax(out, in_, negate=False):
    return lambda E: E.reduce_max(out, in_, axis=AX, negate=negate)

def rsum(out, in_):
    return lambda E: E.reduce_sum(out, in_, axis=AX)

def max8(out, in_):
    return lambda E: E.max(out, in_)

def transp(out, in_, ident):
    return lambda E: E.transpose(out, in_, ident)



class Com:
    def __init__(self, nc, stack):
        sb = lambda n, s, d=F32: stack.enter_context(nc.sbuf_tensor(n, s, d))
        self.SCR_A = sb("com_scr_a", [128, T])
        self.SCR_B = sb("com_scr_b", [128, T])
        self.TMP = sb("com_tmp0", [128, T])
        self.R0 = self.TMP[0:1, :]
        self.R1 = self.SCR_B[0:1, :]
        self.SQS = [sb("com_sqs0", [128, CH]), sb("com_sqs1", [128, CH])]
        self.ones_col = sb("com_ones_col", [128, 1])
        self.ones_col_bf = sb("com_ones_col_bf", [128, 1], BF16)
        self._ones_rows = sb("com_ones_row", [65, 128])
        self.ones_row = self._ones_rows[0:1, :]
        self._cvals = sb("com_cvals", [128, 4])

    def cval(self, idx, base=0, n=1):
        return self._cvals[base:base + n, idx:idx + 1]

    def init(self, p):
        p.op("vector", memset(self.ones_col[:], 1.0), w=[self.ones_col])
        p.op("vector", memset(self.ones_col_bf[:], 1.0), w=[self.ones_col_bf])
        p.op("vector", memset(self._ones_rows[:], 1.0), w=[self._ones_rows])
        p.op("vector", memset(self._cvals[:, 0:1], 1e-5), w=["cval0"])
        p.op("vector", memset(self._cvals[:, 1:2], 1e-6), w=["cval1"])



def emit_ln(p, c, X, Y, psA, psB, psR, fold=None, eps=1e-5):
    for ci in range(NCH):
        sl = slice(ci * CH, (ci + 1) * CH)
        for i in range(DP):
            p.op("tensor", mm(psA[0:1, :], c.ones_col[:], X[i][:, sl], i == 0, i == DP - 1),
                 r=[c.ones_col, X[i]], w=[psA] if i in (0, DP - 1) else ())
        for i in range(DP):
            k = i % 2
            p.op("scalar", act(c.SQS[k][:], X[i][:, sl], AF.Square), r=[X[i]], w=[c.SQS[k]])
            p.op("tensor", mm(psB[0:1, :], c.ones_col[:], c.SQS[k][:], i == 0, i == DP - 1),
                 r=[c.ones_col, c.SQS[k]], w=[psB] if i in (0, DP - 1) else ())
        p.op("scalar", smul(c.R0[:, sl], psA[0:1, :], 1.0 / D), r=[psA], w=[c.R0])
        p.op("scalar", smul(c.R1[:, sl], psB[0:1, :], 1.0 / D), r=[psB], w=[c.R1])

    r_mean = c.R0[:]
    r_msq = c.R1[:]
    for ci in range(NCH):
        sl = slice(ci * CH, (ci + 1) * CH)
        p.op("scalar", act(psA[0:1, :], r_mean[:, sl], AF.Square), r=[c.R0], w=[psA])
        p.op("vector", tt(r_msq[:, sl], r_msq[:, sl], psA[0:1, :], ALU.subtract),
             r=[psA, c.R1], w=[c.R1])
    eps_idx = 0 if eps == 1e-5 else 1
    p.op("scalar", act(r_msq, r_msq, AF.Sqrt, bias=c.cval(eps_idx)),
         r=["cval0" if eps_idx == 0 else "cval1", c.R1], w=[c.R1])
    p.op("vector", recip(r_msq, r_msq), r=[c.R1], w=[c.R1])
    p.op("vector", tt(r_mean, r_mean, r_msq, ALU.mult), r=[c.R0, c.R1], w=[c.R0])

    for ci in range(NCH):
        sl = slice(ci * CH, (ci + 1) * CH)
        k = ci % 2
        p.op("tensor", mm(psR[k][:], c.ones_row, r_msq[:, sl], True, True),
             r=[c._ones_rows, c.R1], w=[psR[k]])
        p.op("scalar", scp(c.SCR_A[:, sl], psR[k][:]), r=[psR[k]], w=[c.SCR_A])
    for ci in range(NCH):
        sl = slice(ci * CH, (ci + 1) * CH)
        k = ci % 2
        p.op("tensor", mm(psR[k][:], c.ones_row, r_mean[:, sl], True, True),
             r=[c._ones_rows, c.R0], w=[psR[k]])
        p.op("scalar", scp(c.SCR_B[:, sl], psR[k][:]), r=[psR[k]], w=[c.SCR_B])

    y_toks = [None] * DP
    z_toks = [None] * DP
    for i in range(DP):
        p.op("gpsimd", tt(c.TMP[:], X[i][:], c.SCR_A[:], ALU.mult),
             r=[X[i], c.SCR_A], w=[c.TMP])
        y_toks[i] = p.op("vector", tt(Y[i][:], c.TMP[:], c.SCR_B[:], ALU.subtract),
                         r=[c.TMP, c.SCR_B], w=[Y[i]])
        if fold is not None:
            z_toks[i] = p.op("vector",
                             stt(X[i][:], Y[i][:], float(fold), X[i][:], ALU.mult, ALU.add),
                             r=[Y[i]], w=[X[i]])
    return y_toks, (z_toks if fold is not None else None)



def emit_rms(p, c, IN, psB, psR):
    for ci in range(NCH):
        sl = slice(ci * CH, (ci + 1) * CH)
        for i in range(DP):
            k = i % 2
            p.op("scalar", act(c.SQS[k][:], IN[i][:, sl], AF.Square), r=[IN[i]], w=[c.SQS[k]])
            p.op("tensor", mm(psB[0:1, :], c.ones_col[:], c.SQS[k][:], i == 0, i == DP - 1),
                 r=[c.ones_col, c.SQS[k]], w=[psB] if i in (0, DP - 1) else ())
        p.op("scalar", smul(c.R1[:, sl], psB[0:1, :], 1.0 / D), r=[psB], w=[c.R1])
    p.op("scalar", act(c.R1[:], c.R1[:], AF.Sqrt, bias=c.cval(1)),
         r=["cval1", c.R1], w=[c.R1])
    p.op("vector", recip(c.R1[:], c.R1[:]), r=[c.R1], w=[c.R1])
    for ci in range(NCH):
        sl = slice(ci * CH, (ci + 1) * CH)
        k = ci % 2
        p.op("tensor", mm(psR[k][:], c.ones_row, c.R1[:, sl], True, True),
             r=[c._ones_rows, c.R1], w=[psR[k]])
        p.op("scalar", scp(c.SCR_A[:, sl], psR[k][:]), r=[psR[k]], w=[c.SCR_A])



def emit_a1(p, c, nc, X, dr, uid=""):
    wq_d = dr["a1_wq"]; wk_d = dr["a1_wk"]; wv_d = dr["a1_wv"]; wo_d = dr["a1_wo"]
    anw_d = dr["a1_anw"]; cb_d = dr["a1_ropeC"]; sb_d = dr["a1_ropeS"]
    mask_d = dr["a1_dmask"]
    o_scr = dr["o_scr"]; y_scr = dr["y_scr"]; h_out = dr["h_scr"]
    GH = 4
    with ExitStack() as ss:
        sbS = lambda n, s, d=F32: ss.enter_context(nc.sbuf_tensor(uid + n, s, d))
        X1 = [sbS(f"x1_{i}", [128, T], BF16) for i in range(DP)]
        ANW = sbS("anw_sb", [128, DP])

        s0 = ExitStack()
        sb0 = lambda n, s, d=F32: s0.enter_context(nc.sbuf_tensor(uid + n, s, d))
        ps0 = lambda n, s, d=F32: s0.enter_context(nc.psum_tensor(uid + n, s, d))
        Y = [sb0(f"y{i}", [128, T], BF16) for i in range(DP)]
        psA = ps0("psA", [1, CH])
        psB = ps0("psB", [1, CH])
        psR = [ps0("psR0", [128, CH]), ps0("psR1", [128, CH])]
        p.dma("scalar", ANW[:], anw_d.ap(), sem="anwin", w=[ANW])
        emit_ln(p, c, X, Y, psA, psB, psR, fold=0.5)
        emit_rms(p, c, Y, psB, psR)
        for i in range(DP):
            p.op("vector", stt(X1[i][:], Y[i][:], ANW[:, i:i + 1], c.SCR_A[:],
                               ALU.mult, ALU.mult),
                 r=[Y[i], ANW, c.SCR_A], w=[X1[i]])
            p.dma("gpsimd", y_scr.ap()[128 * i:128 * (i + 1), :], Y[i][:],
                  sem=f"ysp{i % 2}", r=[Y[i]], w=["y_scr"])
        p.barrier()
        s0.close()

        s1 = ExitStack()
        sb1 = lambda n, s, d=F32: s1.enter_context(nc.sbuf_tensor(uid + n, s, d))
        ps1 = lambda n, s, d=F32: s1.enter_context(nc.psum_tensor(uid + n, s, d))
        CB = sb1("cb", [128, T], BF16)
        SBC = sb1("sbc", [128, T], BF16)
        MSK = sb1("msk", [128, 4 * CH], BF16)
        OSTG = [sb1("ostg0", [128, T], BF16), sb1("ostg1", [128, T], BF16)]
        p.dma("sync", CB[:], cb_d.ap(), sem="cbin", w=[CB])
        p.dma("sync", SBC[:], sb_d.ap(), sem="sbin", w=[SBC])
        p.dma("sync", MSK[:], mask_d.ap(), sem="mkin", w=[MSK])
        GW = GH * 64
        WQA = sb1("wqall", [128, DP * GW], BF16)
        WKA = sb1("wkall", [128, DP * GW], BF16)
        WVA = sb1("wvall", [128, DP * GW], BF16)
        WQ = [WQA[:, GW * d:GW * (d + 1)] for d in range(DP)]
        WK = [WKA[:, GW * d:GW * (d + 1)] for d in range(DP)]
        WV = [WVA[:, GW * d:GW * (d + 1)] for d in range(DP)]
        WQR = [sb1(f"wqrb{d}", [128, GW], BF16) for d in range(DP)]
        WKR = [sb1(f"wkrb{d}", [128, GW], BF16) for d in range(DP)]
        QH = [sb1(f"qh{j}", [128, T], BF16) for j in range(GH // 2)]
        KH = [sb1(f"kh{j}", [128, T], BF16) for j in range(GH // 2)]
        VT = [sb1(f"vt{t2}", [128, GW], BF16) for t2 in range(NTT)]
        PB = [sb1("pb0", [128, CH], BF16), sb1("pb1", [128, CH], BF16)]
        REPS = sb1("reps", [128, CH])
        NEGBC = sb1("negbc", [128, 1])
        psS = [ps1("psS0", [128, CH]), ps1("psS1", [128, CH])]
        psO = ps1("psO", [64, CH])
        psDen = ps1("psDen", [1, CH])
        psRep = ps1("psRep", [128, CH])

        for g in range(4):
            gsl = slice(GW * g, GW * (g + 1))
            for (WX, wsrc, snm) in ((WQ, wq_d, "wqg"), (WK, wk_d, "wkg"), (WV, wv_d, "wvg")):
                tw = p.dma("gpsimd", WX[0].tensor.ap().rearrange("p (d i) -> p d i", d=DP),
                           wsrc.ap().rearrange("(d p) i -> p d i", p=128)[:, :, gsl],
                           sem=snm, w=[WX[0]])
                p.retag_writers(WX, tw)
            for d in range(DP):
                _emit_rot_weights_il(p, WQR[d], WQ[d], GH, 64)
                _emit_rot_weights_il(p, WKR[d], WK[d], GH, 64)
            for j in range(GH // 2):
                hsl = slice(128 * j, 128 * (j + 1))
                for qc in range(NCH):
                    tsl = slice(CH * qc, CH * (qc + 1))
                    for (psq, Wm, Wr, dstH) in ((psS[0], WQ, WQR, QH), (psS[1], WK, WKR, KH)):
                        for d in range(DP):
                            p.op("tensor", mm(psq[:], Wm[d][:, hsl], X1[d][:, tsl],
                                              d == 0, d == DP - 1),
                                 r=[Wm[d], X1[d]], w=[psq] if d in (0, DP - 1) else ())
                        for d in range(DP):
                            p.op("tensor", mm(psRep[:], Wr[d][:, hsl], X1[d][:, tsl],
                                              d == 0, d == DP - 1),
                                 r=[Wr[d], X1[d]], w=[psRep] if d in (0, DP - 1) else ())
                        p.op("vector", tt(c.SQS[0][:], psq[:], CB[:, tsl], ALU.mult),
                             r=[psq, CB], w=[c.SQS[0]])
                        p.op("vector", tt(c.SQS[1][:], psRep[:], SBC[:, tsl], ALU.mult),
                             r=[psRep, SBC], w=[c.SQS[1]])
                        p.op("vector", tt(dstH[j][:, tsl], c.SQS[0][:], c.SQS[1][:], ALU.add),
                             r=[c.SQS[0], c.SQS[1]], w=[f"{_key(dstH[j])}@{qc}"])
            for t2 in range(NTT):
                ssl = slice(128 * t2, 128 * (t2 + 1))
                for d in range(DP):
                    p.op("tensor", mm(psS[0][:, 0:GW], X1[d][:, ssl], WV[d][:],
                                      d == 0, d == DP - 1),
                         r=[X1[d], WV[d]], w=[psS[0]] if d in (0, DP - 1) else ())
                p.op("scalar", scp(VT[t2][:], psS[0][:, 0:GW]), r=[psS[0]], w=[VT[t2]])
            for hh in range(GH):
                hg = GH * g + hh
                j = hh // 2
                half = hh % 2
                rows = slice(64 * half, 64 * (half + 1))
                for (HT, o0) in ((QH[j], 0), (KH[j], 4)):
                    p.op("scalar", act(c.SCR_B[:], HT[:], AF.Square),
                         r=[f"{_key(HT)}@{q2}" for q2 in range(NCH)], w=[c.SCR_B])
                    for ci in range(NCH):
                        sl2 = slice(ci * CH, (ci + 1) * CH)
                        p.op("tensor", mm(psDen[0:1, :], c.ones_col[rows, :],
                                          c.SCR_B[rows, sl2], True, True),
                             r=[c.ones_col, c.SCR_B], w=[psDen])
                        p.op("vector", rmax(c.R0[:, o0 + ci:o0 + ci + 1], psDen[0:1, :]),
                             r=[psDen], w=[c.TMP])
                p.op("vector", rmax(c.R0[:, 8:9], c.R0[:, 0:4]), r=[c.TMP], w=[c.TMP])
                p.op("vector", rmax(c.R0[:, 9:10], c.R0[:, 4:8]), r=[c.TMP], w=[c.TMP])
                p.op("vector", tt(c.R0[:, 10:11], c.R0[:, 8:9], c.R0[:, 9:10], ALU.mult),
                     r=[c.TMP], w=[c.TMP])
                p.op("scalar", act(c.R0[:, 11:12], c.R0[:, 10:11], AF.Sqrt),
                     r=[c.TMP], w=[c.TMP])
                p.op("vector", ts(c.R0[:, 12:13], c.R0[:, 11:12], -1.0 / 8.0, ALU.mult),
                     r=[c.TMP], w=[c.TMP])
                p.op("tensor", mm(psRep[:, 0:1], c.ones_row, c.R0[:, 12:13], True, True),
                     r=[c._ones_rows, c.TMP], w=[psRep])
                p.op("scalar", scp(NEGBC[:], psRep[:, 0:1]), r=[psRep], w=[NEGBC])
                for qc in range(NCH):
                    tsl = slice(CH * qc, CH * (qc + 1))
                    nkt = 4 * qc + 4
                    for kt in range(nkt):
                        k2 = kt % 2
                        ksl = slice(128 * kt, 128 * (kt + 1))
                        p.op("tensor", mm(psS[k2][:, 0:CH], KH[j][rows, ksl], QH[j][rows, tsl],
                                          True, True),
                             r=[f"{_key(KH[j])}@{q2}" for q2 in range(NCH)]
                               + [f"{_key(QH[j])}@{qc}"],
                             w=[psS[k2]])
                        r_rel = kt - 4 * qc
                        if r_rel >= 0:
                            p.op("vector", tt(c.SQS[k2][:], psS[k2][:, 0:CH],
                                              MSK[:, CH * r_rel:CH * (r_rel + 1)], ALU.add),
                                 r=[psS[k2], MSK], w=[c.SQS[k2]])
                            src = c.SQS[k2][0:128, :]
                            srck = [c.SQS[k2]]
                        else:
                            src = psS[k2][0:128, 0:CH]
                            srck = [psS[k2]]
                        p.op("scalar", act(PB[k2][:], src, AF.Exp, bias=NEGBC[:], scale=1.0 / 8.0),
                             r=srck + [NEGBC], w=[PB[k2]])
                        p.op("tensor", mm(psO[:, 0:CH], VT[kt][:, 64 * hh:64 * (hh + 1)],
                                          PB[k2][:], kt == 0, kt == nkt - 1),
                             r=[VT[kt], PB[k2]], w=[psO] if kt in (0, nkt - 1) else ())
                        p.op("tensor", mm(psDen[0:1, :], c.ones_col_bf[:], PB[k2][:],
                                          kt == 0, kt == nkt - 1),
                             r=[c.ones_col_bf, PB[k2]], w=[psDen] if kt in (0, nkt - 1) else ())
                    p.op("vector", recip(c.R0[:, 0:CH], psDen[0:1, :]), r=[psDen], w=[c.TMP])
                    p.op("tensor", mm(psRep[:], c.ones_row, c.R0[:, 0:CH], True, True),
                         r=[c._ones_rows, c.TMP], w=[psRep])
                    p.op("scalar", scp(REPS[:], psRep[:]), r=[psRep], w=[REPS])
                    p.op("vector", tt(OSTG[j % 2][rows, tsl], psO[:, 0:CH], REPS[0:64, :],
                                      ALU.mult),
                         r=[psO, REPS], w=[f"ostg{j % 2}@{half}@{qc}"])
                if half == 1:
                    hp = hg // 2
                    p.dma("gpsimd", o_scr.ap()[128 * hp:128 * (hp + 1), :], OSTG[j % 2][:],
                          sem=f"osp{j % 2}",
                          r=[f"ostg{j % 2}@{h2}@{q2}" for h2 in range(2) for q2 in range(NCH)],
                          w=["o_scr"])
        p.barrier()
        s1.close()

        s2 = ExitStack()
        sb2 = lambda n, s, d=F32: s2.enter_context(nc.sbuf_tensor(uid + n, s, d))
        ps2 = lambda n, s, d=F32: s2.enter_context(nc.psum_tensor(uid + n, s, d))
        OIN = [sb2(f"oin{ot}", [128, T], BF16) for ot in range(DP)]
        WOA = sb2("woall", [128, DP * D], BF16)
        WO = [WOA[:, D * ot:D * (ot + 1)] for ot in range(DP)]
        YW = [sb2("yw0", [128, T], BF16), sb2("yw1", [128, T], BF16)]
        HST = [sb2("hst0", [128, T], BF16), sb2("hst1", [128, T], BF16)]
        psD = [ps2("psD0", [128, CH]), ps2("psD1", [128, CH])]
        for i in range(DP):
            to = p.dma("scalar", OIN[i][:], o_scr.ap()[128 * i:128 * (i + 1), :], sem="oing",
                       r=["o_scr"], w=[OIN[i]])
        p.retag_writers(OIN, to)
        tw = p.dma("gpsimd", WO[0].tensor.ap().rearrange("p (d i) -> p d i", d=DP),
                   wo_d.ap().rearrange("(d p) i -> p d i", p=128), sem="wog", w=[WO[0]])
        p.retag_writers(WO, tw)
        for jj in range(DP):
            jsl = slice(128 * jj, 128 * (jj + 1))
            kj = jj % 2
            p.dma("sync", YW[kj][:], y_scr.ap()[jsl, :], sem=f"ywin{kj}",
                  r=["y_scr"], w=[YW[kj]])
            for tc in range(NCH):
                tsl = slice(CH * tc, CH * (tc + 1))
                k = (jj * NCH + tc) % 2
                for ot in range(DP):
                    p.op("tensor", mm(psD[k][:], WO[ot][:, jsl], OIN[ot][:, tsl],
                                      ot == 0, ot == DP - 1),
                         r=[WO[ot], OIN[ot]], w=[psD[k]] if ot in (0, DP - 1) else ())
                p.op("vector", tt(HST[kj][:, tsl], YW[kj][:, tsl], psD[k][:], ALU.add),
                     r=[YW[kj], psD[k]], w=[HST[kj]])
                p.op("vector", stt(X[jj][:, tsl], psD[k][:], 0.5, X[jj][:, tsl],
                                   ALU.mult, ALU.add),
                     r=[psD[k]], w=[X[jj]])
            p.dma("gpsimd", h_out.ap()[jsl, :], HST[kj][:], sem=f"hout{kj}",
                  r=[HST[kj]], w=["h_scr"])
        s2.close()
    p.barrier()


def build_prog_a1():
    nc = bass.Bass("TRN2", target_bir_lowering=False)
    x_in = nc.dram_tensor("x_in", [D, T], F32, kind="ExternalInput")
    dr = {
        "a1_wq": nc.dram_tensor("wq", [D, D], F32, kind="ExternalInput"),
        "a1_wk": nc.dram_tensor("wk", [D, D], F32, kind="ExternalInput"),
        "a1_wv": nc.dram_tensor("wv", [D, D], F32, kind="ExternalInput"),
        "a1_wo": nc.dram_tensor("wo", [D, D], F32, kind="ExternalInput"),
        "a1_anw": nc.dram_tensor("anw", [128, DP], F32, kind="ExternalInput"),
        "a1_ropeC": nc.dram_tensor("ropeC", [128, T], BF16, kind="ExternalInput"),
        "a1_ropeS": nc.dram_tensor("ropeS", [128, T], BF16, kind="ExternalInput"),
        "a1_dmask": nc.dram_tensor("dmask", [128, 4 * CH], BF16, kind="ExternalInput"),
        "o_scr": nc.dram_tensor("o_scr", [D, T], BF16, kind="Internal"),
        "y_scr": nc.dram_tensor("y_scr", [D, T], BF16, kind="Internal"),
        "h_scr": nc.dram_tensor("h_out", [D, T], BF16, kind="ExternalOutput"),
    }
    z_out = nc.dram_tensor("z_out", [D, T], F32, kind="ExternalOutput")
    stack = ExitStack()
    with stack:
        sb = lambda n, s, d=F32: stack.enter_context(nc.sbuf_tensor(n, s, d))
        p = Prog(nc, stack)
        c = Com(nc, stack)
        X = [sb(f"x{i}", [128, T]) for i in range(DP)]
        c.init(p)
        for i in range(DP):
            tx = p.dma("sync", X[i][:], x_in.ap()[128 * i:128 * (i + 1), :], sem="xing", w=[X[i]])
        p.retag_writers(X, tx)
        emit_a1(p, c, nc, X, dr)
        for jj in range(DP):
            p.dma("sync", z_out.ap()[128 * jj:128 * (jj + 1), :], X[jj][:], sem="zout",
                  r=[X[jj]])
        p.finish()
        p.emit()
    return nc


def _emit_rot_weights_il(p, dst, src, nh, dh):
    sv = _ap(src).rearrange("p (h i two) -> p h i two", h=nh, i=dh // 2, two=2)
    dv = _ap(dst).rearrange("p (h i two) -> p h i two", h=nh, i=dh // 2, two=2)
    p.op("vector", ts(dv[:, :, :, 0:1], sv[:, :, :, 1:2], -1.0, ALU.mult), r=[src], w=[dst])
    p.op("vector", cp(dv[:, :, :, 1:2], sv[:, :, :, 0:1]), r=[src], w=[dst])



def emit_a2(p, c, nc, Z, dr, uid=""):
    h_in = dr["h_scr"]; fnw_d = dr["a2_fnw"]; rw_d = dr["a2_rw"]
    w1_d = dr["a2_w1"]; w3_d = dr["a2_w3"]; w2_d = dr["a2_w2"]
    ident_d = dr["a2_ident"]; sel_d = dr["a2_sel8"]
    with ExitStack() as ss:
        sbS = lambda n, s, d=F32: ss.enter_context(nc.sbuf_tensor(uid + n, s, d))
        X2 = [sbS(f"x2_{i}", [128, T], BF16) for i in range(DP)]
        CTT = sbS("ctt", [8, T])
        SEL8 = sbS("sel8_sb", [8, D])
        p.dma("scalar", SEL8[:], sel_d.ap(), sem="selin", w=[SEL8])

        s0 = ExitStack()
        sb0 = lambda n, s, d=F32: s0.enter_context(nc.sbuf_tensor(uid + n, s, d))
        ps0 = lambda n, s, d=F32: s0.enter_context(nc.psum_tensor(uid + n, s, d))
        H = [sb0(f"h{i}", [128, T], BF16) for i in range(DP)]
        FNW = sb0("fnw_sb", [128, DP])
        RWB = sb0("rwb", [128, 8 * DP], BF16)
        IDENT = sb0("ident_sb", [128, 128])
        LROW = sb0("lrow", [8, CH])
        LTOK = sb0("ltok", [128, 128])
        M8 = sb0("m8", [128, 8])
        N1 = sb0("n1", [128, 1])
        E8 = sb0("e8", [128, 8])
        IND = sb0("ind", [128, 8])
        NUM = sb0("num", [128, 8])
        E2 = sb0("e2", [128, 1])
        DINV = sb0("dinv", [128, 1])
        CT = sb0("ct", [128, 128])
        psB = ps0("psB", [1, CH])
        psR = [ps0("psR0", [128, CH]), ps0("psR1", [128, CH])]
        psL = ps0("psL", [8, CH])
        psT = ps0("psT", [128, 8])
        psT2 = ps0("psT2", [8, 128])
        for i in range(DP):
            th = p.dma("sync", H[i][:], h_in.ap()[128 * i:128 * (i + 1), :], sem="hing", w=[H[i]])
        p.retag_writers(H, th)
        p.dma("scalar", FNW[:], fnw_d.ap(), sem="fnwin", w=[FNW])
        p.dma("scalar", IDENT[:], ident_d.ap(), sem="idin", w=[IDENT])
        emit_rms(p, c, H, psB, psR)
        for i in range(DP):
            p.op("vector", stt(X2[i][:], H[i][:], FNW[:, i:i + 1], c.SCR_A[:],
                               ALU.mult, ALU.mult),
                 r=[H[i], FNW, c.SCR_A], w=[X2[i]])
        for d in range(DP):
            p.dma("gpsimd", c.SQS[0][:, 0:8], rw_d.ap()[128 * d:128 * (d + 1), :],
                  sem="rwin", w=[c.SQS[0]])
            p.op("vector", cp(RWB[:, 8 * d:8 * (d + 1)], c.SQS[0][:, 0:8]),
                 r=[c.SQS[0]], w=[RWB])
        for ci in range(NCH):
            csl = slice(CH * ci, CH * (ci + 1))
            for d in range(DP):
                p.op("tensor", mm(psL[0:8, :], RWB[:, 8 * d:8 * (d + 1)], X2[d][:, csl],
                                  d == 0, d == DP - 1),
                     r=[RWB, X2[d]], w=[psL] if d in (0, DP - 1) else ())
            p.op("scalar", scp(LROW[:], psL[0:8, :]), r=[psL], w=[LROW])
            for t2 in range(4):
                tt_g = 4 * ci + t2
                p.op("tensor", transp(psT[:, 0:8], LROW[0:8, 128 * t2:128 * (t2 + 1)],
                                      IDENT[0:8, 0:8]),
                     r=[LROW, IDENT], w=[psT])
                p.op("scalar", scp(LTOK[:, 8 * tt_g:8 * (tt_g + 1)], psT[:, 0:8]),
                     r=[psT], w=[LTOK])
        for tt_g in range(NTT):
            lt = LTOK[:, 8 * tt_g:8 * (tt_g + 1)]
            p.op("vector", max8(M8[:], lt), r=[LTOK], w=[M8])
            p.op("vector", ts(N1[:], M8[:, 0:1], -1.0, ALU.mult), r=[M8], w=[N1])
            p.op("scalar", act(E8[:], lt, AF.Exp, bias=N1[:]), r=[LTOK, N1], w=[E8])
            p.op("vector", ts(IND[:], lt, M8[:, 1:2], ALU.is_ge), r=[LTOK, M8], w=[IND])
            p.op("vector", tt(NUM[:], E8[:], IND[:], ALU.mult), r=[E8, IND], w=[NUM])
            p.op("scalar", act(E2[:], M8[:, 1:2], AF.Exp, bias=N1[:]), r=[M8, N1], w=[E2])
            p.op("vector", ts(E2[:], E2[:], 1.0, ALU.add), r=[E2], w=[E2])
            p.op("vector", recip(DINV[:], E2[:]), r=[E2], w=[DINV])
            p.op("vector",
                 lambda E, NUM=NUM, DINV=DINV, tt_g=tt_g:
                     E.tensor_scalar(CT[:, 8 * tt_g:8 * (tt_g + 1)], NUM[:], DINV[:], 0.5,
                                     ALU.mult, ALU.mult),
                 r=[NUM, DINV], w=[CT])
        for tt_g in range(NTT):
            p.op("tensor", transp(psT2[0:8, :], CT[:, 8 * tt_g:8 * (tt_g + 1)], IDENT[:]),
                 r=[CT, IDENT], w=[psT2])
            p.op("scalar", scp(CTT[:, 128 * tt_g:128 * (tt_g + 1)], psT2[0:8, :]),
                 r=[psT2], w=[CTT])
        p.barrier()
        s0.close()

        s1 = ExitStack()
        sb1 = lambda n, s, d=F32: s1.enter_context(nc.sbuf_tensor(uid + n, s, d))
        ps1 = lambda n, s, d=F32: s1.enter_context(nc.psum_tensor(uid + n, s, d))
        W1A = sb1("w1all", [128, DP * D], BF16)
        W3A = sb1("w3all", [128, DP * D], BF16)
        W2A = sb1("w2all", [128, DP * D], BF16)
        W1 = [W1A[:, D * d:D * (d + 1)] for d in range(DP)]
        W3 = [W3A[:, D * d:D * (d + 1)] for d in range(DP)]
        W2H = [W2A[:, D * it:D * (it + 1)] for it in range(DP)]
        GT = [sb1(f"gt{it}", [128, CH], BF16) for it in range(DP)]
        TSC = c.SQS
        psH1 = [ps1("psH1a", [128, CH]), ps1("psH1b", [128, CH])]
        psH3 = [ps1("psH3a", [128, CH]), ps1("psH3b", [128, CH])]
        psY = [ps1("psYa", [128, CH]), ps1("psYb", [128, CH])]
        psRep = ps1("psRep2", [128, CH])

        for e in range(8):
            for ci in range(NCH):
                csl = slice(CH * ci, CH * (ci + 1))
                p.op("tensor", mm(psRep[:], SEL8[0:8, 128 * e:128 * (e + 1)],
                                  CTT[0:8, csl], True, True),
                     r=[SEL8, CTT], w=[psRep])
                p.op("scalar", scp(c.SCR_B[:, csl], psRep[:]), r=[psRep], w=[c.SCR_B])
            for ih in range(2):
                isl = slice(1024 * ih, 1024 * (ih + 1))
                src1 = w1_d.ap()[e].rearrange("(d p) i -> p d i", p=128)[:, :, isl]
                src3 = w3_d.ap()[e].rearrange("(d p) i -> p d i", p=128)[:, :, isl]
                src2 = (w2_d.ap()[e].rearrange("(h it p) j -> h p it j", h=2, p=128)[ih])
                t1 = p.dma("gpsimd", W1A.ap().rearrange("p (d i) -> p d i", d=DP),
                           src1, sem="w1g", w=[W1A])
                t3 = p.dma("gpsimd", W3A.ap().rearrange("p (d i) -> p d i", d=DP),
                           src3, sem="w3g", w=[W3A])
                t2 = p.dma("gpsimd", W2A.ap().rearrange("p (it j) -> p it j", it=DP),
                           src2, sem="w2g", w=[W2A])
                p.retag_writers(W1 + [W1A], t1)
                p.retag_writers(W3 + [W3A], t3)
                p.retag_writers(W2H + [W2A], t2)
                for tc in range(NCH):
                    tsl = slice(CH * tc, CH * (tc + 1))
                    for it in range(DP):
                        k = it % 2
                        col = slice(128 * it, 128 * (it + 1))
                        for d in range(DP):
                            p.op("tensor", mm(psH1[k][:], W1[d][:, col], X2[d][:, tsl],
                                              d == 0, d == DP - 1),
                                 r=[W1[d], X2[d]], w=[psH1[k]] if d in (0, DP - 1) else ())
                        for d in range(DP):
                            p.op("tensor", mm(psH3[k][:], W3[d][:, col], X2[d][:, tsl],
                                              d == 0, d == DP - 1),
                                 r=[W3[d], X2[d]], w=[psH3[k]] if d in (0, DP - 1) else ())
                        p.op("scalar", act(c.SQS[k][:], psH1[k][:], AF.Silu),
                             r=[psH1[k]], w=[c.SQS[k]])
                        p.op("vector", tt(GT[it][:], c.SQS[k][:], psH3[k][:], ALU.mult),
                             r=[c.SQS[k], psH3[k]], w=[GT[it]])
                    for j in range(DP):
                        k = j % 2
                        jcol = slice(128 * j, 128 * (j + 1))
                        for it in range(DP):
                            p.op("tensor", mm(psY[k][:], W2H[it][:, jcol], GT[it][:],
                                              it == 0, it == DP - 1),
                                 r=[W2H[it], GT[it]], w=[psY[k]] if it in (0, DP - 1) else ())
                        p.op("vector", tt(TSC[k][:], psY[k][:], c.SCR_B[:, tsl], ALU.mult),
                             r=[psY[k], c.SCR_B], w=[TSC[k]])
                        p.op("vector", tt(Z[j][:, tsl], Z[j][:, tsl], TSC[k][:], ALU.add),
                             r=[TSC[k]], w=[Z[j]])
        s1.close()
    p.barrier()


def build_prog_a2():
    nc = bass.Bass("TRN2", target_bir_lowering=False)
    z_in = nc.dram_tensor("z_in", [D, T], F32, kind="ExternalInput")
    dr = {
        "h_scr": nc.dram_tensor("h_in", [D, T], BF16, kind="ExternalInput"),
        "a2_fnw": nc.dram_tensor("fnw", [128, DP], F32, kind="ExternalInput"),
        "a2_rw": nc.dram_tensor("rw", [D, 8], F32, kind="ExternalInput"),
        "a2_w1": nc.dram_tensor("w1", [8, D, 2048], F32, kind="ExternalInput"),
        "a2_w3": nc.dram_tensor("w3", [8, D, 2048], F32, kind="ExternalInput"),
        "a2_w2": nc.dram_tensor("w2", [8, 2048, D], F32, kind="ExternalInput"),
        "a2_ident": nc.dram_tensor("ident", [128, 128], F32, kind="ExternalInput"),
        "a2_sel8": nc.dram_tensor("sel8", [8, D], F32, kind="ExternalInput"),
    }
    x_out = nc.dram_tensor("x_out", [D, T], F32, kind="ExternalOutput")
    stack = ExitStack()
    with stack:
        sb = lambda n, s, d=F32: stack.enter_context(nc.sbuf_tensor(n, s, d))
        p = Prog(nc, stack)
        c = Com(nc, stack)
        Z = [sb(f"z{i}", [128, T]) for i in range(DP)]
        c.init(p)
        for i in range(DP):
            tz = p.dma("sync", Z[i][:], z_in.ap()[128 * i:128 * (i + 1), :], sem="zing", w=[Z[i]])
        p.retag_writers(Z, tz)
        emit_a2(p, c, nc, Z, dr)
        for j in range(DP):
            p.dma("sync", x_out.ap()[128 * j:128 * (j + 1), :], Z[j][:], sem="xout", r=[Z[j]])
        p.finish()
        p.emit()
    return nc


def _prep_l0_ffn_moe(inputs):
    sel = np.zeros((8, D), np.float32)
    for e in range(8):
        sel[e, 128 * e:128 * (e + 1)] = 1.0
    return {
        "fnw": np.ascontiguousarray(np.asarray(inputs["L0_ffn_ffn_norm_w"]).reshape(DP, 128).T),
        "rw": np.ascontiguousarray(inputs["L0_ffn_router_w"]),
        "w1": np.ascontiguousarray(inputs["L0_ffn_w1"]),
        "w3": np.ascontiguousarray(inputs["L0_ffn_w3"]),
        "w2": np.ascontiguousarray(inputs["L0_ffn_w2"]),
        "ident": np.eye(128, dtype=np.float32),
        "sel8": sel,
    }


def run_a2(hs, zs, W):
    nc = get_prog("a2")
    in_maps = [dict(h_in=hs[b], z_in=zs[b], **W) for b in range(8)]
    res = run_bass_kernel_spmd(nc, in_maps, core_ids=list(range(8)))
    return [np.asarray(r["x_out"]) for r in res.results]



def emit_e(p, c, nc, X, dr, uid=""):
    wq_d = dr["e_wq"]; wk_d = dr["e_wk"]; wv_d = dr["e_wv"]
    wqf_d = dr["e_wqf"]; wkf_d = dr["e_wkf"]; wout_d = dr["e_wout"]
    nw_d = dr["e_nw"]; ga_d = dr["e_gha"]; gb_d = dr["e_ghb"]
    triu_d = dr["e_triu"]; wmask_d = dr["e_wmask"]
    lin_scr = dr["lin_scr"]; win_scr = dr["win_scr"]
    with ExitStack() as ss:
        sbS = lambda n, s, d=F32: ss.enter_context(nc.sbuf_tensor(uid + n, s, d))
        XH = [sbS(f"xh{i}", [128, T], BF16) for i in range(DP)]

        s0 = ExitStack()
        sb0 = lambda n, s, d=F32: s0.enter_context(nc.sbuf_tensor(uid + n, s, d))
        ps0 = lambda n, s, d=F32: s0.enter_context(nc.psum_tensor(uid + n, s, d))
        Y = [sb0(f"y{i}", [128, T], BF16) for i in range(DP)]
        NW = sb0("nw_sb", [128, DP])
        psA = ps0("psA", [1, CH])
        psB = ps0("psB", [1, CH])
        psR = [ps0("psR0", [128, CH]), ps0("psR1", [128, CH])]
        p.dma("scalar", NW[:], nw_d.ap(), sem="nwin", w=[NW])
        emit_ln(p, c, X, Y, psA, psB, psR, fold=1.0)
        emit_rms(p, c, Y, psB, psR)
        for i in range(DP):
            p.op("vector", stt(XH[i][:], Y[i][:], NW[:, i:i + 1], c.SCR_A[:],
                               ALU.mult, ALU.mult),
                 r=[Y[i], NW, c.SCR_A], w=[XH[i]])
        p.barrier()
        s0.close()

        s1 = ExitStack()
        sb1 = lambda n, s, d=F32: s1.enter_context(nc.sbuf_tensor(uid + n, s, d))
        ps1 = lambda n, s, d=F32: s1.enter_context(nc.psum_tensor(uid + n, s, d))
        WQA = sb1("wqall", [128, DP * 128], BF16)
        WKA = sb1("wkall", [128, DP * 128], BF16)
        WVA = sb1("wvall", [128, DP * 128], BF16)
        WQB = [WQA[:, 128 * d:128 * (d + 1)] for d in range(DP)]
        WKB = [WKA[:, 128 * d:128 * (d + 1)] for d in range(DP)]
        WVB = [WVA[:, 128 * d:128 * (d + 1)] for d in range(DP)]
        WQF = sb1("wqf_sb", [64, 16], BF16)
        WKF = sb1("wkf_sb", [64, 16], BF16)
        WFS = sb1("wfs", [64, 32])
        GA = sb1("gha_sb", [33, 153], BF16)
        GB = sb1("ghb_sb", [33, 153], BF16)
        TRIU = sb1("triu_sb", [128, 128])
        WMSK = sb1("wmsk_sb", [128, 256], BF16)
        QH = sb1("qh", [64, T], BF16)
        KH = sb1("kh", [64, T], BF16)
        VA = [sb1("va0", [128, NTT * 65], BF16), sb1("va1", [128, NTT * 65], BF16)]
        SQ = sb1("sq", [33, T], BF16)
        SK = sb1("sk", [33, T], BF16)
        PQ1 = sb1("pq1", [128, T], BF16)
        PQ2 = sb1("pq2", [25, T], BF16)
        PK1 = sb1("pk1", [128, T], BF16)
        PK2 = sb1("pk2", [25, T], BF16)
        KTOK = sb1("ktok", [128, NTT * 153], BF16)
        KVH = sb1("kvh", [128, 65], BF16)
        KVL = sb1("kvl", [25, 65], BF16)
        KVFH = sb1("kvfh", [128, 65])
        KVFL = sb1("kvfl", [25, 65])
        LINH = sb1("linh", [64, T], BF16)
        WINH = sb1("winh", [64, T], BF16)
        ASB = sb1("asb", [128, 128], BF16)
        PBW = sb1("pbw", [128, 128], BF16)
        REPE = sb1("repe", [128, 128])
        NEGWC = sb1("negwc", [128, 1])
        psU = ps1("psU", [128, CH])
        psW = ps1("psW", [128, CH])
        psA2 = ps1("psA2", [128, 128])
        psY2 = ps1("psY2", [65, 128])
        psOW = ps1("psOW", [65, 128])
        psRepE = ps1("psRepE", [128, 128])
        kvh_ps = ps1("kvh_ps", [128, 65])
        kvl_ps = ps1("kvl_ps", [25, 65])

        p.dma("sync", WFS[:, 0:16], wqf_d.ap(), sem="wfin", w=[WFS])
        p.dma("sync", WFS[:, 16:32], wkf_d.ap(), sem="wfin2", w=[WFS])
        p.op("vector", cp(WQF[:], WFS[:, 0:16]), r=[WFS], w=[WQF])
        p.op("vector", cp(WKF[:], WFS[:, 16:32]), r=[WFS], w=[WKF])
        p.dma("sync", GA[:], ga_d.ap(), sem="gain", w=[GA])
        p.dma("sync", GB[:], gb_d.ap(), sem="gbin", w=[GB])
        p.dma("sync", TRIU[:], triu_d.ap(), sem="trin", w=[TRIU])
        p.dma("sync", WMSK[:], wmask_d.ap(), sem="wmin", w=[WMSK])
        p.op("vector", memset(SQ[:], 0.0), w=[SQ])
        p.op("vector", memset(SQ[32:33, :], 1.0), w=[SQ])
        p.op("vector", memset(SK[:], 0.0), w=[SK])
        p.op("vector", memset(SK[32:33, :], 1.0), w=[SK])

        for hp in range(8):
            psl = slice(128 * hp, 128 * (hp + 1))
            for (WX, wsrc, snm) in ((WQB, wq_d, "wqg"), (WKB, wk_d, "wkg"),
                                    (WVB, wv_d, "wvg")):
                tw = p.dma("gpsimd", WX[0].tensor.ap().rearrange("p (d i) -> p d i", d=DP),
                           wsrc.ap().rearrange("(d p) i -> p d i", p=128)[:, :, psl],
                           sem=snm, w=[WX[0]])
                p.retag_writers(WX, tw)
            for h2 in range(2):
                p.op("vector", memset(
                    VA[h2].ap().rearrange("p (t c) -> p t c", t=NTT, c=65)[:, :, 64:65], 1.0),
                    w=[VA[h2]])
            for t2 in range(NTT):
                ssl = slice(128 * t2, 128 * (t2 + 1))
                for d in range(DP):
                    p.op("tensor", mm(psA2[:], XH[d][:, ssl], WVB[d][:],
                                      d == 0, d == DP - 1),
                         r=[XH[d], WVB[d]], w=[psA2] if d in (0, DP - 1) else ())
                for h2 in range(2):
                    p.op("vector", cp(
                        VA[h2].ap().rearrange("p (t c) -> p t c", t=NTT, c=65)[:, t2, 0:64],
                        psA2[:, 64 * h2:64 * (h2 + 1)]),
                        r=[psA2], w=[VA[h2]])
            for h2 in range(2):
                h = 2 * hp + h2
                hsl = slice(64 * h2, 64 * (h2 + 1))
                vaR = VA[h2].ap().rearrange("p (t c) -> p t c", t=NTT, c=65)
                for ci in range(NCH):
                    csl = slice(CH * ci, CH * (ci + 1))
                    for (ps_, Wm, dst) in ((psU, WQB, QH), (psW, WKB, KH)):
                        for d in range(DP):
                            p.op("tensor", mm(ps_[0:64, :], Wm[d][:, hsl], XH[d][:, csl],
                                              d == 0, d == DP - 1),
                                 r=[Wm[d], XH[d]], w=[ps_] if d in (0, DP - 1) else ())
                        p.op("scalar", scp(dst[:, csl], ps_[0:64, :]), r=[ps_], w=[dst])
                for ci in range(NCH):
                    csl = slice(CH * ci, CH * (ci + 1))
                    p.op("tensor", mm(psU[0:16, :], WQF[:], QH[:, csl], True, True),
                         r=[WQF, QH], w=[psU])
                    p.op("scalar", scp(SQ[0:16, csl], psU[0:16, :]), r=[psU], w=[SQ])
                    p.op("tensor", mm(psW[0:16, :], WKF[:], KH[:, csl], True, True),
                         r=[WKF, KH], w=[psW])
                    p.op("scalar", scp(SK[0:16, csl], psW[0:16, :]), r=[psW], w=[SK])
                for (S_, P1, P2) in ((SQ, PQ1, PQ2), (SK, PK1, PK2)):
                    for ci in range(NCH):
                        csl = slice(CH * ci, CH * (ci + 1))
                        for (P_, fsl, np_) in ((P1, slice(0, 128), 128),
                                               (P2, slice(128, 153), 25)):
                            p.op("tensor", mm(psU[0:np_, :], GA[:, fsl], S_[:, csl],
                                              True, True),
                                 r=[GA, S_], w=[psU])
                            p.op("tensor", mm(psW[0:np_, :], GB[:, fsl], S_[:, csl],
                                              True, True),
                                 r=[GB, S_], w=[psW])
                            p.op("scalar", scp(c.SQS[0][0:np_, :], psW[0:np_, :]),
                                 r=[psW], w=[c.SQS[0]])
                            p.op("vector", tt(P_[0:np_, csl], psU[0:np_, :],
                                              c.SQS[0][0:np_, :], ALU.mult),
                                 r=[psU, c.SQS[0]], w=[P_])
                for ch in range(NTT):
                    p.op("tensor", mm(psU[:, 0:153], SK[:, 128 * ch:128 * (ch + 1)], GA[:],
                                      True, True),
                         r=[SK, GA], w=[psU])
                    p.op("tensor", mm(psW[:, 0:153], SK[:, 128 * ch:128 * (ch + 1)], GB[:],
                                      True, True),
                         r=[SK, GB], w=[psW])
                    p.op("scalar", scp(c.SQS[0][:, 0:153], psW[:, 0:153]),
                         r=[psW], w=[c.SQS[0]])
                    p.op("vector", tt(KTOK[:, 153 * ch:153 * (ch + 1)], psU[:, 0:153],
                                      c.SQS[0][:, 0:153], ALU.mult),
                         r=[psU, c.SQS[0]], w=[KTOK])
                p.op("vector", memset(KVH[:], 0.0), w=[KVH])
                p.op("vector", memset(KVL[:], 0.0), w=[KVL])
                p.op("vector", memset(KVFH[:], 0.0), w=[KVFH])
                p.op("vector", memset(KVFL[:], 0.0), w=[KVFL])
                for ch in range(NTT):
                    cs128 = slice(128 * ch, 128 * (ch + 1))
                    p.op("tensor", mm(psA2[:], PK1[:, cs128], PQ1[:, cs128], True, False),
                         r=[PK1, PQ1], w=[psA2])
                    p.op("tensor", mm(psA2[:], PK2[0:25, cs128], PQ2[0:25, cs128],
                                      False, True),
                         r=[PK2, PQ2], w=[psA2])
                    p.op("vector", tt(ASB[:], psA2[:], TRIU[:], ALU.mult),
                         r=[psA2, TRIU], w=[ASB])
                    p.op("tensor", mm(psY2[:, :], vaR[:, ch, :], ASB[:], True, False),
                         r=[VA[h2], ASB], w=[psY2])
                    p.op("tensor", mm(psY2[:, :], KVH[:], PQ1[:, cs128], False, False),
                         r=[KVH, PQ1], w=[psY2])
                    p.op("tensor", mm(psY2[:, :], KVL[:], PQ2[0:25, cs128], False, True),
                         r=[KVL, PQ2], w=[psY2])
                    p.op("vector", ts(c.R0[:, 0:128], psY2[64:65, :], c.cval(1), ALU.add),
                         r=[psY2, "cval1"], w=[c.TMP])
                    p.op("vector", recip(c.R0[:, 0:128], c.R0[:, 0:128]), r=[c.TMP], w=[c.TMP])
                    p.op("tensor", mm(psRepE[:], c.ones_row, c.R0[:, 0:128], True, True),
                         r=[c._ones_rows, c.TMP], w=[psRepE])
                    p.op("scalar", scp(REPE[:], psRepE[:]), r=[psRepE], w=[REPE])
                    p.op("vector", tt(LINH[:, cs128], psY2[0:64, :], REPE[0:64, :], ALU.mult),
                         r=[psY2, REPE], w=[LINH])
                    if ch < NTT - 1:
                        p.op("tensor", mm(kvh_ps[:], KTOK[:, 153 * ch:153 * ch + 128],
                                          vaR[:, ch, :], True, True),
                             r=[KTOK, VA[h2]], w=[kvh_ps])
                        p.op("tensor", mm(kvl_ps[:], KTOK[:, 153 * ch + 128:153 * (ch + 1)],
                                          vaR[:, ch, :], True, True),
                             r=[KTOK, VA[h2]], w=[kvl_ps])
                        p.op("vector", tt(KVFH[:], KVFH[:], kvh_ps[:], ALU.add),
                             r=[kvh_ps], w=[KVFH])
                        p.op("vector", tt(KVFL[:], KVFL[:], kvl_ps[:], ALU.add),
                             r=[kvl_ps], w=[KVFL])
                        p.op("scalar", scp(KVH[:], KVFH[:]), r=[KVFH], w=[KVH])
                        p.op("scalar", scp(KVL[:], KVFL[:]), r=[KVFL], w=[KVL])
                for (HT, o0) in ((QH, 0), (KH, 4)):
                    p.op("scalar", act(c.SCR_B[0:64, :], HT[:], AF.Square),
                         r=[HT], w=[c.SCR_B])
                    for ci in range(NCH):
                        sl2 = slice(ci * CH, (ci + 1) * CH)
                        p.op("tensor", mm(psU[0:1, 0:CH], c.ones_col[0:64, :],
                                          c.SCR_B[0:64, sl2], True, True),
                             r=[c.ones_col, c.SCR_B], w=[psU])
                        p.op("vector", rmax(c.R0[:, o0 + ci:o0 + ci + 1], psU[0:1, 0:CH]),
                             r=[psU], w=[c.TMP])
                p.op("vector", rmax(c.R0[:, 8:9], c.R0[:, 0:4]), r=[c.TMP], w=[c.TMP])
                p.op("vector", rmax(c.R0[:, 9:10], c.R0[:, 4:8]), r=[c.TMP], w=[c.TMP])
                p.op("vector", tt(c.R0[:, 10:11], c.R0[:, 8:9], c.R0[:, 9:10], ALU.mult),
                     r=[c.TMP], w=[c.TMP])
                p.op("scalar", act(c.R0[:, 11:12], c.R0[:, 10:11], AF.Sqrt),
                     r=[c.TMP], w=[c.TMP])
                p.op("vector", ts(c.R0[:, 12:13], c.R0[:, 11:12], -1.0 / 8.0, ALU.mult),
                     r=[c.TMP], w=[c.TMP])
                p.op("tensor", mm(psRepE[:, 0:1], c.ones_row, c.R0[:, 12:13], True, True),
                     r=[c._ones_rows, c.TMP], w=[psRepE])
                p.op("scalar", scp(NEGWC[:], psRepE[:, 0:1]), r=[psRepE], w=[NEGWC])
                for qc in range(NTT):
                    qsl = slice(128 * qc, 128 * (qc + 1))
                    kts = [kt for kt in (qc - 1, qc) if kt >= 0]
                    for ki, kt in enumerate(kts):
                        ksl = slice(128 * kt, 128 * (kt + 1))
                        p.op("tensor", mm(psA2[:], KH[:, ksl], QH[:, qsl], True, True),
                             r=[KH, QH], w=[psA2])
                        wm = WMSK[:, 0:128] if kt < qc else WMSK[:, 128:256]
                        p.op("vector", tt(c.SQS[1][:, 0:128], psA2[:], wm, ALU.add),
                             r=[psA2, WMSK], w=[c.SQS[1]])
                        p.op("scalar", act(PBW[:], c.SQS[1][:, 0:128], AF.Exp,
                                           bias=NEGWC[:], scale=1.0 / 8.0),
                             r=[c.SQS[1], NEGWC], w=[PBW])
                        p.op("tensor", mm(psOW[:, :], vaR[:, kt, :], PBW[:],
                                          ki == 0, ki == len(kts) - 1),
                             r=[VA[h2], PBW], w=[psOW])
                    p.op("vector", recip(c.R0[:, 0:128], psOW[64:65, :]), r=[psOW], w=[c.TMP])
                    p.op("tensor", mm(psRepE[:], c.ones_row, c.R0[:, 0:128], True, True),
                         r=[c._ones_rows, c.TMP], w=[psRepE])
                    p.op("scalar", scp(REPE[:], psRepE[:]), r=[psRepE], w=[REPE])
                    p.op("vector", tt(WINH[:, qsl], psOW[0:64, :], REPE[0:64, :], ALU.mult),
                         r=[psOW, REPE], w=[WINH])
                p.dma("gpsimd", lin_scr.ap()[64 * h:64 * (h + 1), :], LINH[:],
                      sem="linsp", r=[LINH], w=["lin_scr"])
                p.dma("gpsimd", win_scr.ap()[64 * h:64 * (h + 1), :], WINH[:],
                      sem="winsp", r=[WINH], w=["win_scr"])
        p.barrier()
        s1.close()

        s2 = ExitStack()
        sb2 = lambda n, s, d=F32: s2.enter_context(nc.sbuf_tensor(uid + n, s, d))
        ps2 = lambda n, s, d=F32: s2.enter_context(nc.psum_tensor(uid + n, s, d))
        Z = X
        LW2 = [sb2("lw2a", [128, CH], BF16), sb2("lw2b", [128, CH], BF16)]
        WOUA = sb2("wouall", [128, 16 * D], BF16)
        WOUT = [WOUA[:, D * ot:D * (ot + 1)] for ot in range(16)]
        psDE = [ps2(f"psDE{j}", [128, CH]) for j in range(DP)]
        tw = p.dma("gpsimd", WOUA.ap().rearrange("p (d i) -> p d i", d=16),
                   wout_d.ap().rearrange("(d p) i -> p d i", p=128), sem="wog", w=[WOUA])
        p.retag_writers(WOUT, tw)
        for tc in range(NCH):
            tsl = slice(CH * tc, CH * (tc + 1))
            for ot in range(16):
                k = ot % 2
                src = lin_scr if ot < DP else win_scr
                ro = ot if ot < DP else ot - DP
                p.dma("scalar", LW2[k][:],
                      src.ap()[128 * ro:128 * (ro + 1), tsl], sem=f"lw2{k}",
                      r=["lin_scr" if ot < DP else "win_scr"], w=[LW2[k]])
                for j in range(DP):
                    p.op("tensor", mm(psDE[j][:], WOUT[ot][:, 128 * j:128 * (j + 1)],
                                      LW2[k][:], ot == 0, ot == 15),
                         r=[WOUT[ot], LW2[k]], w=[psDE[j]] if ot in (0, 15) else ())
            for j in range(DP):
                p.op("vector", tt(Z[j][:, tsl], Z[j][:, tsl], psDE[j][:], ALU.add),
                     r=[psDE[j]], w=[Z[j]])
        s2.close()
    p.barrier()


def build_prog_e():
    nc = bass.Bass("TRN2", target_bir_lowering=False)
    x_in = nc.dram_tensor("x_in", [D, T], F32, kind="ExternalInput")
    dr = {
        "e_wq": nc.dram_tensor("wq", [D, D], F32, kind="ExternalInput"),
        "e_wk": nc.dram_tensor("wk", [D, D], F32, kind="ExternalInput"),
        "e_wv": nc.dram_tensor("wv", [D, D], F32, kind="ExternalInput"),
        "e_wqf": nc.dram_tensor("wqf", [64, 16], F32, kind="ExternalInput"),
        "e_wkf": nc.dram_tensor("wkf", [64, 16], F32, kind="ExternalInput"),
        "e_wout": nc.dram_tensor("wout", [2 * D, D], F32, kind="ExternalInput"),
        "e_nw": nc.dram_tensor("nw", [128, DP], F32, kind="ExternalInput"),
        "e_gha": nc.dram_tensor("gha", [33, 153], BF16, kind="ExternalInput"),
        "e_ghb": nc.dram_tensor("ghb", [33, 153], BF16, kind="ExternalInput"),
        "e_triu": nc.dram_tensor("triu", [128, 128], F32, kind="ExternalInput"),
        "e_wmask": nc.dram_tensor("wmask", [128, 256], BF16, kind="ExternalInput"),
        "lin_scr": nc.dram_tensor("lin_scr", [D, T], BF16, kind="Internal"),
        "win_scr": nc.dram_tensor("win_scr", [D, T], BF16, kind="Internal"),
    }
    x_out = nc.dram_tensor("x_out", [D, T], F32, kind="ExternalOutput")
    stack = ExitStack()
    with stack:
        sb = lambda n, s, d=F32: stack.enter_context(nc.sbuf_tensor(n, s, d))
        p = Prog(nc, stack)
        c = Com(nc, stack)
        X = [sb(f"x{i}", [128, T]) for i in range(DP)]
        c.init(p)
        for i in range(DP):
            tx = p.dma("sync", X[i][:], x_in.ap()[128 * i:128 * (i + 1), :], sem="xing", w=[X[i]])
        p.retag_writers(X, tx)
        emit_e(p, c, nc, X, dr)
        for jj in range(DP):
            p.dma("sync", x_out.ap()[128 * jj:128 * (jj + 1), :], X[jj][:], sem="xout",
                  r=[X[jj]])
        p.finish()
        p.emit()
    return nc


def _prep_l1_attn(inputs):
    import math
    iu, ju = np.triu_indices(16)
    scale = np.where(iu == ju, 0.5, 1.0 / math.sqrt(2.0)).astype(np.float32)
    GA = np.zeros((33, 153), np.float32)
    GB = np.zeros((33, 153), np.float32)
    GA[32, 0] = 1.0
    GB[32, 0] = 1.0
    for i in range(16):
        GA[i, 1 + i] = 1.0
        GB[32, 1 + i] = 1.0
    for qn in range(136):
        GA[iu[qn], 17 + qn] = 1.0
        GB[ju[qn], 17 + qn] = scale[qn]
    rk = np.arange(128)[:, None]
    cq = np.arange(128)[None, :]
    wm_prev = np.where(rk >= cq + 64, 0.0, -1e30).astype(np.float32)
    wm_diag = np.where((rk <= cq) & (rk >= cq - 64), 0.0, -1e30).astype(np.float32)
    triu = np.triu(np.ones((128, 128), np.float32))
    return {
        "wq": np.ascontiguousarray(inputs["L1_attn_Wq"]),
        "wk": np.ascontiguousarray(inputs["L1_attn_Wk"]),
        "wv": np.ascontiguousarray(inputs["L1_attn_Wv"]),
        "wqf": np.ascontiguousarray(inputs["L1_attn_Wqf"]),
        "wkf": np.ascontiguousarray(inputs["L1_attn_Wkf"]),
        "wout": np.ascontiguousarray(inputs["L1_attn_Wout"]),
        "nw": np.ascontiguousarray(np.asarray(inputs["L1_attn_norm_w"]).reshape(DP, 128).T),
        "gha": np.ascontiguousarray(GA.astype(BF)),
        "ghb": np.ascontiguousarray(GB.astype(BF)),
        "triu": triu,
        "wmask": np.ascontiguousarray(
            np.concatenate([wm_prev, wm_diag], axis=1).astype(BF)),
    }


def run_e(xs, W):
    nc = get_prog("e")
    in_maps = [dict(x_in=xs[b], **W) for b in range(8)]
    res = run_bass_kernel_spmd(nc, in_maps, core_ids=list(range(8)))
    return [np.asarray(r["x_out"]) for r in res.results]



_PROG_CACHE = {}

def emit_d(p, c, nc, X, dr, uid=""):
    wg = dr["d_wg"]; wu = dr["d_wu"]; wd = dr["d_wd"]; g_scr = dr["g_scr"]
    with ExitStack() as ss:
        sbS = lambda n, s, d=F32: ss.enter_context(nc.sbuf_tensor(uid + n, s, d))
        Y = [sbS(f"y{i}", [128, T], BF16) for i in range(DP)]

        s1 = ExitStack()
        sb1 = lambda n, s, d=F32: s1.enter_context(nc.sbuf_tensor(uid + n, s, d))
        ps1 = lambda n, s, d=F32: s1.enter_context(nc.psum_tensor(uid + n, s, d))
        psA = ps1("psA", [1, CH])
        psB = ps1("psB", [1, CH])
        psR = [ps1("psR0", [128, CH]), ps1("psR1", [128, CH])]
        GPS = [ps1("gate0", [128, CH]), ps1("gate1", [128, CH])]
        UPS = [ps1("up0", [128, CH]), ps1("up1", [128, CH])]
        gst = sb1("gstage0", [128, T], BF16)
        WGA = sb1("wgall", [128, DP * 1536], BF16)
        WUA = sb1("wuall", [128, DP * 1536], BF16)
        WG = [WGA[:, 1536 * i:1536 * (i + 1)] for i in range(DP)]
        WU = [WUA[:, 1536 * i:1536 * (i + 1)] for i in range(DP)]

        emit_ln(p, c, X, Y, psA, psB, psR, fold=None)

        for ih in range(2):
            csl = slice(1536 * ih, 1536 * (ih + 1))
            p.dma("gpsimd", WGA.ap().rearrange("p (d i) -> p d i", d=DP),
                  wg.ap().rearrange("(d p) i -> p d i", p=128)[:, :, csl],
                  sem="wgg", w=[WGA])
            p.dma("gpsimd", WUA.ap().rearrange("p (d i) -> p d i", d=DP),
                  wu.ap().rearrange("(d p) i -> p d i", p=128)[:, :, csl],
                  sem="wug", w=[WUA])
            for hp in range(12):
                hpg = 12 * ih + hp
                col = slice(128 * hp, 128 * (hp + 1))
                for tc in range(NCH):
                    seq = hp * NCH + tc
                    tsl = slice(CH * tc, CH * (tc + 1))
                    k = seq % 2
                    for d in range(DP):
                        p.op("tensor", mm(GPS[k][:], WG[d][:, col], Y[d][:, tsl],
                                          d == 0, d == DP - 1),
                             r=[WG[d], Y[d]], w=[GPS[k]] if d in (0, DP - 1) else ())
                    for d in range(DP):
                        p.op("tensor", mm(UPS[k][:], WU[d][:, col], Y[d][:, tsl],
                                          d == 0, d == DP - 1),
                             r=[WU[d], Y[d]], w=[UPS[k]] if d in (0, DP - 1) else ())
                    p.op("scalar", act(c.SQS[k][:], GPS[k][:], AF.Silu, scale=1.0 / 127.0),
                         r=[GPS[k]], w=[c.SQS[k]])
                    p.op("vector", tt(gst[:, tsl], c.SQS[k][:], UPS[k][:], ALU.mult),
                         r=[c.SQS[k], UPS[k]], w=[f"gst@{tc}"])
                p.dma("gpsimd", g_scr.ap()[128 * hpg:128 * (hpg + 1), :], gst[:], sem="gsp",
                      r=[f"gst@{t2}" for t2 in range(NCH)], w=["g_scr"])
            p.barrier()

        s1.close()

        s2 = ExitStack()
        sb2 = lambda n, s, d=F32: s2.enter_context(nc.sbuf_tensor(uid + n, s, d))
        ps2 = lambda n, s, d=F32: s2.enter_context(nc.psum_tensor(uid + n, s, d))
        GIN = [sb2("gin0", [128, CH], BF16), sb2("gin1", [128, CH], BF16)]
        WDA = sb2("wdall", [128, 24 * D], BF16)
        WD = [WDA[:, D * i:D * (i + 1)] for i in range(24)]
        dps = [ps2(f"dps{j}", [128, CH]) for j in range(DP)]
        p.dma("gpsimd", WDA.ap().rearrange("p (it j) -> p it j", it=24),
              wd.ap().rearrange("(it p) j -> p it j", p=128), sem="wdg", w=[WDA])
        for tc in range(NCH):
            tsl = slice(CH * tc, CH * (tc + 1))
            for it in range(24):
                k = it % 2
                p.dma("scalar", GIN[k][:], g_scr.ap()[128 * it:128 * (it + 1), tsl],
                      sem=f"gin{k}", r=["g_scr"], w=[GIN[k]])
                for j in range(DP):
                    p.op("tensor", mm(dps[j][:], WD[it][:, 128 * j:128 * (j + 1)], GIN[k][:],
                                      it == 0, it == 23),
                         r=[WD[it], GIN[k]], w=[dps[j]] if it in (0, 23) else ())
            for j in range(DP):
                p.op("vector", stt(X[j][:, tsl], dps[j][:], 0.5 / (127.0 * 127.0),
                                   X[j][:, tsl], ALU.mult, ALU.add),
                     r=[dps[j]], w=[X[j]])
        s2.close()
    p.barrier()


def build_prog_d():
    nc = bass.Bass("TRN2", target_bir_lowering=False)
    x_in = nc.dram_tensor("x_in", [D, T], F32, kind="ExternalInput")
    dr = {
        "d_wg": nc.dram_tensor("wg", [D, 3072], I32, kind="ExternalInput"),
        "d_wu": nc.dram_tensor("wu", [D, 3072], I32, kind="ExternalInput"),
        "d_wd": nc.dram_tensor("wd", [3072, D], I32, kind="ExternalInput"),
        "g_scr": nc.dram_tensor("g_scr", [3072, T], BF16, kind="Internal"),
    }
    x_out = nc.dram_tensor("x_out", [D, T], F32, kind="ExternalOutput")
    stack = ExitStack()
    with stack:
        sb = lambda n, s, d=F32: stack.enter_context(nc.sbuf_tensor(n, s, d))
        p = Prog(nc, stack)
        c = Com(nc, stack)
        X = [sb(f"x{i}", [128, T]) for i in range(DP)]
        c.init(p)
        for i in range(DP):
            tx = p.dma("sync", X[i][:], x_in.ap()[128 * i:128 * (i + 1), :], sem="xing", w=[X[i]])
        p.retag_writers(X, tx)
        emit_d(p, c, nc, X, dr)
        for j in range(DP):
            p.dma("sync", x_out.ap()[128 * j:128 * (j + 1), :], X[j][:], sem="xout", r=[X[j]])
        p.finish()
        p.emit()
    return nc



def _emit_rot_weights(p, dst, src, nh, dh, rot_half):
    rh = rot_half
    sv = _ap(src).rearrange("p (h c) -> p h c", h=nh, c=dh)
    dv = _ap(dst).rearrange("p (h c) -> p h c", h=nh, c=dh)
    p.op("vector", ts(dv[:, :, 0:rh], sv[:, :, rh:2 * rh], -1.0, ALU.mult),
         r=[src], w=[dst])
    p.op("vector", cp(dv[:, :, rh:2 * rh], sv[:, :, 0:rh]), r=[src], w=[dst])
    if 2 * rh < dh:
        p.op("vector", cp(dv[:, :, 2 * rh:dh], sv[:, :, 2 * rh:dh]), r=[src], w=[dst])


def emit_b(p, c, nc, X, dr, uid=""):
    wq_d = dr["b_wq"]; wk_d = dr["b_wk"]; wv_d = dr["b_wv"]; wo_d = dr["b_wo"]
    cb_d = dr["b_ropeC"]; sb_d = dr["b_ropeS"]; mask_d = dr["b_dmask"]
    o_scr = dr["o_scr"]
    NHB = 8
    GH = 2
    with ExitStack() as ss:
        sbS = lambda n, s, d=F32: ss.enter_context(nc.sbuf_tensor(uid + n, s, d))
        Y = [sbS(f"y{i}", [128, T], BF16) for i in range(DP)]

        s0 = ExitStack()
        sb0 = lambda n, s, d=F32: s0.enter_context(nc.sbuf_tensor(uid + n, s, d))
        ps0 = lambda n, s, d=F32: s0.enter_context(nc.psum_tensor(uid + n, s, d))
        psA = ps0("psA", [1, CH])
        psB = ps0("psB", [1, CH])
        psR = [ps0("psR0", [128, CH]), ps0("psR1", [128, CH])]
        emit_ln(p, c, X, Y, psA, psB, psR, fold=None)
        p.barrier()
        s0.close()

        s1 = ExitStack()
        sb1 = lambda n, s, d=F32: s1.enter_context(nc.sbuf_tensor(uid + n, s, d))
        ps1 = lambda n, s, d=F32: s1.enter_context(nc.psum_tensor(uid + n, s, d))
        CB = sb1("cb", [128, T], BF16)
        SBC = sb1("sbc", [128, T], BF16)
        MSK = sb1("msk", [128, 4 * CH], BF16)
        OSTG = [sb1("ostg0", [128, T], BF16), sb1("ostg1", [128, T], BF16)]
        p.dma("sync", CB[:], cb_d.ap(), sem="cbin", w=[CB])
        p.dma("sync", SBC[:], sb_d.ap(), sem="sbin", w=[SBC])
        p.dma("sync", MSK[:], mask_d.ap(), sem="mkin", w=[MSK])
        GWB = GH * 128
        WQA = sb1("wqall", [128, DP * GWB], BF16)
        WKA = sb1("wkall", [128, DP * GWB], BF16)
        WVA = sb1("wvall", [128, DP * GWB], BF16)
        WQ = [WQA[:, GWB * d:GWB * (d + 1)] for d in range(DP)]
        WK = [WKA[:, GWB * d:GWB * (d + 1)] for d in range(DP)]
        WV = [WVA[:, GWB * d:GWB * (d + 1)] for d in range(DP)]
        WQR = [sb1(f"wqrb{d}", [128, GWB], BF16) for d in range(DP)]
        WKR = [sb1(f"wkrb{d}", [128, GWB], BF16) for d in range(DP)]
        QH = [sb1(f"qh{h}", [128, T], BF16) for h in range(GH)]
        KH = [sb1(f"kh{h}", [128, T], BF16) for h in range(GH)]
        VT = [sb1(f"vt{t2}", [128, GH * 128], BF16) for t2 in range(NTT)]
        PB = [sb1("pb0", [128, CH], BF16), sb1("pb1", [128, CH], BF16)]
        REPS = sb1("reps", [128, CH])
        psS = [ps1("psS0", [128, CH]), ps1("psS1", [128, CH])]
        psO = ps1("psO", [128, CH])
        psDen = ps1("psDen", [1, CH])
        psRep = ps1("psRep", [128, CH])

        for g in range(NHB // GH):
            hbase = GH * g
            gsl = slice(128 * GH * g, 128 * GH * (g + 1))
            for (WX, wsrc, snm) in ((WQ, wq_d, "wqg"), (WK, wk_d, "wkg"), (WV, wv_d, "wvg")):
                tw = p.dma("gpsimd", WX[0].tensor.ap().rearrange("p (d i) -> p d i", d=DP),
                           wsrc.ap().rearrange("(d p) i -> p d i", p=128)[:, :, gsl],
                           sem=snm, w=[WX[0]])
                p.retag_writers(WX, tw)
            for d in range(DP):
                _emit_rot_weights(p, WQR[d], WQ[d], GH, 128, 32)
                _emit_rot_weights(p, WKR[d], WK[d], GH, 128, 32)
            for h in range(GH):
                hsl = slice(128 * h, 128 * (h + 1))
                for qc in range(NCH):
                    tsl = slice(CH * qc, CH * (qc + 1))
                    for (psq, Wm, Wr, dstH) in ((psS[0], WQ, WQR, QH), (psS[1], WK, WKR, KH)):
                        for d in range(DP):
                            p.op("tensor", mm(psq[:], Wm[d][:, hsl], Y[d][:, tsl],
                                              d == 0, d == DP - 1),
                                 r=[Wm[d], Y[d]], w=[psq] if d in (0, DP - 1) else ())
                        for d in range(DP):
                            p.op("tensor", mm(psO[:], Wr[d][:, hsl], Y[d][:, tsl],
                                              d == 0, d == DP - 1),
                                 r=[Wr[d], Y[d]], w=[psO] if d in (0, DP - 1) else ())
                        p.op("vector", tt(c.SQS[0][:], psq[:], CB[:, tsl], ALU.mult),
                             r=[psq, CB], w=[c.SQS[0]])
                        p.op("vector", tt(c.SQS[1][:], psO[:], SBC[:, tsl], ALU.mult),
                             r=[psO, SBC], w=[c.SQS[1]])
                        p.op("vector", tt(dstH[h][:, tsl], c.SQS[0][:], c.SQS[1][:], ALU.add),
                             r=[c.SQS[0], c.SQS[1]], w=[f"{_key(dstH[h])}@{qc}"])
            for t2 in range(NTT):
                ssl = slice(128 * t2, 128 * (t2 + 1))
                for d in range(DP):
                    p.op("tensor", mm(psS[0][:, 0:128 * GH], Y[d][:, ssl], WV[d][:],
                                      d == 0, d == DP - 1),
                         r=[Y[d], WV[d]], w=[psS[0]] if d in (0, DP - 1) else ())
                p.op("scalar", scp(VT[t2][:], psS[0][:, 0:128 * GH]), r=[psS[0]], w=[VT[t2]])
            for h in range(GH):
                hg = hbase + h
                for qc in range(NCH):
                    tsl = slice(CH * qc, CH * (qc + 1))
                    qkeys = [f"{_key(QH[h])}@{qc}"]
                    nkt = 4 * qc + 4
                    for kt in range(nkt):
                        k2 = kt % 2
                        ksl = slice(128 * kt, 128 * (kt + 1))
                        p.op("tensor", mm(psS[k2][:], KH[h][:, ksl], QH[h][:, tsl],
                                          True, True),
                             r=[f"{_key(KH[h])}@{j}" for j in range(NCH)] + qkeys,
                             w=[psS[k2]])
                        r_rel = kt - 4 * qc
                        if r_rel >= 0:
                            p.op("vector", tt(c.SQS[k2][:], psS[k2][:],
                                              MSK[:, CH * r_rel:CH * (r_rel + 1)], ALU.add),
                                 r=[psS[k2], MSK], w=[c.SQS[k2]])
                            src = c.SQS[k2][:]
                            srck = [c.SQS[k2]]
                        else:
                            src = psS[k2][:]
                            srck = [psS[k2]]
                        p.op("scalar", act(PB[k2][:], src, AF.Exp, scale=1.0 / 128.0),
                             r=srck, w=[PB[k2]])
                        p.op("tensor", mm(psO[:], VT[kt][:, 128 * h:128 * (h + 1)], PB[k2][:],
                                          kt == 0, kt == nkt - 1),
                             r=[VT[kt], PB[k2]], w=[psO] if kt in (0, nkt - 1) else ())
                        p.op("tensor", mm(psDen[0:1, :], c.ones_col_bf[:], PB[k2][:],
                                          kt == 0, kt == nkt - 1),
                             r=[c.ones_col_bf, PB[k2]], w=[psDen] if kt in (0, nkt - 1) else ())
                    p.op("vector", recip(c.R0[:, 0:CH], psDen[0:1, :]), r=[psDen], w=[c.R0])
                    p.op("tensor", mm(psRep[:], c.ones_row, c.R0[:, 0:CH], True, True),
                         r=[c._ones_rows, c.R0], w=[psRep])
                    p.op("scalar", scp(REPS[:], psRep[:]), r=[psRep], w=[REPS])
                    p.op("vector", tt(OSTG[hg % 2][:, tsl], psO[:], REPS[:], ALU.mult),
                         r=[psO, REPS], w=[f"ostg{hg % 2}@{qc}"])
                p.dma("gpsimd", o_scr.ap()[128 * hg:128 * (hg + 1), :], OSTG[hg % 2][:],
                      sem=f"osp{hg % 2}",
                      r=[f"ostg{hg % 2}@{j}" for j in range(NCH)], w=["o_scr"])
        p.barrier()
        s1.close()

        s2 = ExitStack()
        sb2 = lambda n, s, d=F32: s2.enter_context(nc.sbuf_tensor(uid + n, s, d))
        ps2 = lambda n, s, d=F32: s2.enter_context(nc.psum_tensor(uid + n, s, d))
        OIN = [sb2(f"oin{ot}", [128, T], BF16) for ot in range(DP)]
        WOA = sb2("woall", [128, DP * D], BF16)
        WO = [WOA[:, D * ot:D * (ot + 1)] for ot in range(DP)]
        psD = [ps2("psD0", [128, CH]), ps2("psD1", [128, CH])]
        for i in range(DP):
            to = p.dma("scalar", OIN[i][:], o_scr.ap()[128 * i:128 * (i + 1), :], sem="oing",
                       r=["o_scr"], w=[OIN[i]])
        p.retag_writers(OIN, to)
        tw = p.dma("gpsimd", WO[0].tensor.ap().rearrange("p (d i) -> p d i", d=DP),
                   wo_d.ap().rearrange("(d p) i -> p d i", p=128), sem="wog", w=[WO[0]])
        p.retag_writers(WO, tw)
        for j in range(DP):
            jsl = slice(128 * j, 128 * (j + 1))
            for tc in range(NCH):
                tsl = slice(CH * tc, CH * (tc + 1))
                k = (j * NCH + tc) % 2
                for ot in range(DP):
                    p.op("tensor", mm(psD[k][:], WO[ot][:, jsl], OIN[ot][:, tsl],
                                      ot == 0, ot == DP - 1),
                         r=[WO[ot], OIN[ot]], w=[psD[k]] if ot in (0, DP - 1) else ())
                p.op("vector", tt(X[j][:, tsl], X[j][:, tsl], psD[k][:], ALU.add),
                     r=[psD[k]], w=[X[j]])
        s2.close()
    p.barrier()


def build_prog_b():
    nc = bass.Bass("TRN2", target_bir_lowering=False)
    x_in = nc.dram_tensor("x_in", [D, T], F32, kind="ExternalInput")
    dr = {
        "b_wq": nc.dram_tensor("wq", [D, D], F32, kind="ExternalInput"),
        "b_wk": nc.dram_tensor("wk", [D, D], F32, kind="ExternalInput"),
        "b_wv": nc.dram_tensor("wv", [D, D], F32, kind="ExternalInput"),
        "b_wo": nc.dram_tensor("wo", [D, D], F32, kind="ExternalInput"),
        "b_ropeC": nc.dram_tensor("ropeC", [128, T], BF16, kind="ExternalInput"),
        "b_ropeS": nc.dram_tensor("ropeS", [128, T], BF16, kind="ExternalInput"),
        "b_dmask": nc.dram_tensor("dmask", [128, 4 * CH], BF16, kind="ExternalInput"),
        "o_scr": nc.dram_tensor("o_scr", [D, T], BF16, kind="Internal"),
    }
    x_out = nc.dram_tensor("x_out", [D, T], F32, kind="ExternalOutput")
    stack = ExitStack()
    with stack:
        sb = lambda n, s, d=F32: stack.enter_context(nc.sbuf_tensor(n, s, d))
        p = Prog(nc, stack)
        c = Com(nc, stack)
        X = [sb(f"x{i}", [128, T]) for i in range(DP)]
        c.init(p)
        for i in range(DP):
            tx = p.dma("sync", X[i][:], x_in.ap()[128 * i:128 * (i + 1), :], sem="xing", w=[X[i]])
        p.retag_writers(X, tx)
        emit_b(p, c, nc, X, dr)
        for j in range(DP):
            p.dma("sync", x_out.ap()[128 * j:128 * (j + 1), :], X[j][:], sem="xout", r=[X[j]])
        p.finish()
        p.emit()
    return nc


COALESCE = False


def get_prog(name):
    global _NEEDED
    if name not in _PROG_CACHE:
        _NEEDED = None
        if not COALESCE:
            _PROG_CACHE[name] = _BUILDERS[name]()
            return _PROG_CACHE[name]
        _BUILDERS[name]()
        _NEEDED = {k: frozenset(v) for k, v in Prog.last.emitted_waits.items()}
        try:
            _PROG_CACHE[name] = _BUILDERS[name]()
        finally:
            _NEEDED = None
    return _PROG_CACHE[name]



def _prep_l0_attn(inputs):
    wq = np.ascontiguousarray(np.transpose(inputs["L0_attn_wq"], (1, 0, 2)).reshape(D, D))
    wk = np.ascontiguousarray(np.transpose(inputs["L0_attn_wk"], (1, 0, 2)).reshape(D, D))
    wv = np.ascontiguousarray(np.transpose(inputs["L0_attn_wv"], (1, 0, 2)).reshape(D, D))
    wo = np.ascontiguousarray(inputs["L0_attn_wo"])
    t = np.arange(T, dtype=np.float64)[None, :]
    i = np.arange(32, dtype=np.float64)[:, None]
    rad = t * np.power(10000.0, -i / 32.0)
    cosr = np.cos(rad).astype(np.float32)
    sinr = np.sin(rad).astype(np.float32)
    ropeC = np.concatenate([cosr, cosr, np.ones((64, T), np.float32)], 0).astype(BF)
    ropeS = np.concatenate([sinr, sinr, np.zeros((64, T), np.float32)], 0).astype(BF)
    rk = np.arange(128)[:, None]
    cq = np.arange(CH)[None, :]
    dmask = np.concatenate(
        [np.where(128 * rr + rk <= cq, 0.0, -1e30).astype(np.float32) for rr in range(4)],
        axis=1).astype(BF)
    return {"wq": wq, "wk": wk, "wv": wv, "wo": wo,
            "ropeC": np.ascontiguousarray(ropeC), "ropeS": np.ascontiguousarray(ropeS),
            "dmask": np.ascontiguousarray(dmask)}


def _prep_l0_ffn_attn(inputs):
    F = np.asarray(inputs["L0_ffn_freqs_cis"])
    idx = np.repeat(np.arange(0, 64, 2), 2)
    crow = F[:, idx].T
    srow = F[:, idx + 1].T
    ropeC = np.tile(crow, (2, 1)).astype(BF)
    ropeS = np.tile(srow, (2, 1)).astype(BF)
    rk = np.arange(128)[:, None]
    cq = np.arange(CH)[None, :]
    dmask = np.concatenate(
        [np.where(128 * rr + rk <= cq, 0.0, -1e30).astype(np.float32) for rr in range(4)],
        axis=1).astype(BF)
    return {
        "wq": np.ascontiguousarray(inputs["L0_ffn_wq"]),
        "wk": np.ascontiguousarray(inputs["L0_ffn_wk"]),
        "wv": np.ascontiguousarray(inputs["L0_ffn_wv"]),
        "wo": np.ascontiguousarray(inputs["L0_ffn_wo"]),
        "anw": np.ascontiguousarray(np.asarray(inputs["L0_ffn_attn_norm_w"]).reshape(DP, 128).T),
        "ropeC": np.ascontiguousarray(ropeC),
        "ropeS": np.ascontiguousarray(ropeS),
        "dmask": np.ascontiguousarray(dmask),
    }


def run_a1(xs, W):
    nc = get_prog("a1")
    in_maps = [dict(x_in=xs[b], **W) for b in range(8)]
    res = run_bass_kernel_spmd(nc, in_maps, core_ids=list(range(8)))
    return ([np.asarray(r["z_out"]) for r in res.results],
            [np.asarray(r["h_out"]) for r in res.results])


def run_b(xs, W):
    nc = get_prog("b")
    in_maps = [dict(x_in=xs[b], **W) for b in range(8)]
    res = run_bass_kernel_spmd(nc, in_maps, core_ids=list(range(8)))
    return [np.asarray(r["x_out"]) for r in res.results]


def _prep_l1_ffn(inputs):
    return {
        "wg": np.ascontiguousarray(inputs["L1_ffn_w_gate"].T),
        "wu": np.ascontiguousarray(inputs["L1_ffn_w_up"].T),
        "wd": np.ascontiguousarray(inputs["L1_ffn_w_down"].T),
    }


def run_d(xs, W):
    nc = get_prog("d")
    in_maps = [dict(x_in=xs[b], **W) for b in range(8)]
    res = run_bass_kernel_spmd(nc, in_maps, core_ids=list(range(8)))
    return [np.asarray(r["x_out"]) for r in res.results]




def build_mega():
    nc = bass.Bass("TRN2", target_bir_lowering=False)
    x_in = nc.dram_tensor("x_in", [D, T], F32, kind="ExternalInput")
    x_out = nc.dram_tensor("x_out", [D, T], F32, kind="ExternalOutput")
    dr = {}
    for nm, shape, dt in [
        ("a1_wq", [D, D], F32), ("a1_wk", [D, D], F32), ("a1_wv", [D, D], F32),
        ("a1_wo", [D, D], F32), ("a1_anw", [128, DP], F32),
        ("a1_ropeC", [128, T], BF16), ("a1_ropeS", [128, T], BF16),
        ("a1_dmask", [128, 4 * CH], BF16),
        ("a2_fnw", [128, DP], F32), ("a2_rw", [D, 8], F32),
        ("a2_w1", [8, D, 2048], F32), ("a2_w3", [8, D, 2048], F32),
        ("a2_w2", [8, 2048, D], F32), ("a2_ident", [128, 128], F32),
        ("a2_sel8", [8, D], F32),
        ("b_wq", [D, D], F32), ("b_wk", [D, D], F32), ("b_wv", [D, D], F32),
        ("b_wo", [D, D], F32), ("b_ropeC", [128, T], BF16),
        ("b_ropeS", [128, T], BF16), ("b_dmask", [128, 4 * CH], BF16),
        ("d_wg", [D, 3072], I32), ("d_wu", [D, 3072], I32), ("d_wd", [3072, D], I32),
        ("e_wq", [D, D], F32), ("e_wk", [D, D], F32), ("e_wv", [D, D], F32),
        ("e_wqf", [64, 16], F32), ("e_wkf", [64, 16], F32),
        ("e_wout", [2 * D, D], F32), ("e_nw", [128, DP], F32),
        ("e_gha", [33, 153], BF16), ("e_ghb", [33, 153], BF16),
        ("e_triu", [128, 128], F32), ("e_wmask", [128, 256], BF16),
    ]:
        dr[nm] = nc.dram_tensor(nm, shape, dt, kind="ExternalInput")
    for nm, shape, dt in [
        ("o_scr", [D, T], BF16), ("y_scr", [D, T], BF16), ("h_scr", [D, T], BF16),
        ("g_scr", [3072, T], BF16), ("lin_scr", [D, T], BF16), ("win_scr", [D, T], BF16),
    ]:
        dr[nm] = nc.dram_tensor(nm, shape, dt, kind="Internal")

    stack = ExitStack()
    with stack:
        sb = lambda n, s, d=F32: stack.enter_context(nc.sbuf_tensor(n, s, d))
        p = Prog(nc, stack)
        c = Com(nc, stack)
        X = [sb(f"x{i}", [128, T]) for i in range(DP)]
        c.init(p)
        for i in range(DP):
            tx = p.dma("sync", X[i][:], x_in.ap()[128 * i:128 * (i + 1), :], sem="xing", w=[X[i]])
        p.retag_writers(X, tx)
        emit_a1(p, c, nc, X, dr, uid="s1_")
        emit_a2(p, c, nc, X, dr, uid="s2_")
        emit_b(p, c, nc, X, dr, uid="s3_")
        emit_a1(p, c, nc, X, dr, uid="s4_")
        emit_a2(p, c, nc, X, dr, uid="s5_")
        emit_d(p, c, nc, X, dr, uid="s6_")
        emit_e(p, c, nc, X, dr, uid="s7_")
        emit_d(p, c, nc, X, dr, uid="s8_")
        for j in range(DP):
            p.dma("sync", x_out.ap()[128 * j:128 * (j + 1), :], X[j][:], sem="xout", r=[X[j]])
        p.finish()
        p.emit()
    return nc


def _mega_inmap(inputs):
    W_a1 = _prep_l0_ffn_attn(inputs)
    W_a2 = _prep_l0_ffn_moe(inputs)
    W_b = _prep_l0_attn(inputs)
    W_d = _prep_l1_ffn(inputs)
    W_e = _prep_l1_attn(inputs)
    m = {}
    for pre, Wd in (("a1_", W_a1), ("a2_", W_a2), ("b_", W_b), ("d_", W_d), ("e_", W_e)):
        for k, v in Wd.items():
            m[pre + k] = v
    return m


FUSED = True


def kernel(**inputs):
    inputs = {k: np.asarray(v) for k, v in inputs.items()}
    xs = [np.ascontiguousarray(inputs["L0_attn_x"][b].T) for b in range(8)]
    if FUSED:
        base = _mega_inmap(inputs)
        nc = get_prog("mega")
        in_maps = [dict(x_in=xs[b], **base) for b in range(8)]
        res = run_bass_kernel_spmd(nc, in_maps, core_ids=list(range(8)))
        return np.stack([np.asarray(r["x_out"]).T for r in res.results]).astype(np.float32)
    W_a1 = _prep_l0_ffn_attn(inputs)
    W_a2 = _prep_l0_ffn_moe(inputs)
    W_b = _prep_l0_attn(inputs)
    W_d = _prep_l1_ffn(inputs)
    W_e = _prep_l1_attn(inputs)
    zs, hs = run_a1(xs, W_a1)
    xs = run_a2(hs, zs, W_a2)
    xs = run_b(xs, W_b)
    zs, hs = run_a1(xs, W_a1)
    xs = run_a2(hs, zs, W_a2)
    xs = run_d(xs, W_d)
    xs = run_e(xs, W_e)
    xs = run_d(xs, W_d)
    return np.stack([x.T for x in xs]).astype(np.float32)


_BUILDERS = {"d": build_prog_d, "b": build_prog_b, "a1": build_prog_a1, "a2": build_prog_a2, "e": build_prog_e, "mega": build_mega}
